# version 41
# baseline (speedup 1.0000x reference)
import contextlib
import numpy as np
import ml_dtypes
import concourse.bass as bass
import concourse.mybir as mybir
from concourse.bass_utils import run_bass_kernel_spmd

F32 = mybir.dt.float32
F16 = mybir.dt.float16
BF16 = mybir.dt.bfloat16
AF = mybir.ActivationFunctionType
ALU = mybir.AluOpType
AX = mybir.AxisListType

NCORES = 8
TOK = 2048
HALO = 3
TOKH = TOK + HALO
D = 1024
DI = 2048
NH = 32
HD = 64
DS = 128
EXBC = 2336
NCH = 16
CH = 128
FFI = 2816
EPS_LN = 1e-5
EPS_RMS = 1e-5
M_H = float(1.5 * 2**22)
M_1 = float(1.5 * 2**23)
M_2 = float(1.5 * 2**24)
NEG = -30000.0

DEBUG_TAPS = {}
STAGE = 99
import os as _os
SSD_STEPS = int(_os.environ.get("SSD_STEPS", "99"))
SSD_NCHUNK = int(_os.environ.get("SSD_NCHUNK", "99"))


class Q:
    ENGINES = ("sync", "vector", "scalar", "tensor", "gpsimd")

    def __init__(self, nc):
        self.nc = nc
        self.steps = {e: [] for e in self.ENGINES}
        self.sems = {}
        self.cnt = {}
        self.ctx = None
        self.tag = "init"

    def sem(self, name):
        if name not in self.sems:
            self.sems[name] = self.ctx.enter_context(self.nc.semaphore(name))
            self.cnt[name] = 0
        return self.sems[name]

    def wait(self, eng, name, n=None):
        s = self.sem(name)
        tgt = self.cnt[name] if n is None else n
        if tgt > 0:
            self.steps[eng].append((self.tag, lambda e, s=s, t=tgt: e.wait_ge(s, t)))
        return tgt

    def emit(self, eng, fn, inc=None, k=1):
        if inc is not None:
            s = self.sem(inc)
            self.cnt[inc] += k
            def wrapped(e, fn=fn, s=s, k=k):
                fn(e).then_inc(s, k)
            self.steps[eng].append((self.tag, wrapped))
        else:
            self.steps[eng].append((self.tag, fn))

    def dma(self, eng, out, in_, inc, **kw):
        self.emit(eng, lambda e, o=out, i=in_, kw=kw: e.dma_start(out=o, in_=i, **kw),
                  inc=inc, k=16)

    def n(self, name):
        self.sem(name)
        return self.cnt[name]

    def run(self, block):
        nc = self.nc
        def replay(engine, steps):
            cur = None
            sid = None
            for tag, f in steps:
                if tag != cur:
                    if cur is not None:
                        nc.leave_named_scope(cur, sid, False)
                    sid, _ = nc.enter_named_scope(tag, False)
                    cur = tag
                f(engine)
            if cur is not None:
                nc.leave_named_scope(cur, sid, False)
        for eng in self.ENGINES:
            steps = self.steps[eng]
            getattr(block, eng)(lambda engine, steps=steps: replay(engine, steps))


class KB:
    def __init__(self, nc, q, ctx):
        self.nc, self.q, self.ctx = nc, q, ctx
        self.dram, self.sbuf, self.taps = {}, {}, {}

    def din(self, name, shape, dt=F32):
        self.dram[name] = self.nc.dram_tensor(name, shape, dt, kind="ExternalInput").ap()
        return self.dram[name]

    def dout(self, name, shape, dt=F32):
        self.dram[name] = self.nc.dram_tensor(name, shape, dt, kind="ExternalOutput").ap()
        return self.dram[name]

    def dint(self, name, shape, dt=F32, **kw):
        self.dram[name] = self.nc.dram_tensor(name, shape, dt, kind="Internal", **kw).ap()
        return self.dram[name]

    def sb(self, name, shape, dt=F32, scope=None):
        t = (scope or self.ctx).enter_context(self.nc.sbuf_tensor(name, shape, dt))
        self.sbuf[name] = t
        return t

    def ps(self, name, shape, dt=F32, scope=None):
        t = (scope or self.ctx).enter_context(self.nc.psum_tensor(name, shape, dt))
        self.sbuf[name] = t
        return t

    def tap(self, name, src_ap, nelem_sem="st_tap", eng="sync"):
        if name in self.taps:
            self.q.wait(eng, "vd")
            self.q.wait(eng, "ac")
            self.q.dma(eng, self.taps[name], src_ap, nelem_sem)


def build(stage=99, debug_taps=None):
    nc = bass.Bass("TRN2", target_bir_lowering=False)
    try:
        nc.detect_race_conditions = False
    except Exception:
        pass
    with contextlib.ExitStack() as ctx:
        q = Q(nc)
        q.ctx = ctx
        kb = KB(nc, q, ctx)
        declare_dram(kb)
        for tname, (tshape, tdt) in (debug_taps or {}).items():
            kb.taps[tname] = kb.dout(tname, tshape, tdt)
        body(kb, stage)
        block = ctx.enter_context(nc.Block())
        q.run(block)
    return nc


def declare_dram(kb):
    kb.din("u", [TOKH, D])
    kb.din("w_xbcT", [D, EXBC])
    kb.din("w_zT", [D, DI])
    kb.din("w_outT", [DI, D])
    kb.din("wg_sh", [FFI // NCORES, D])
    kb.din("wu_sh", [FFI // NCORES, D])
    kb.din("wd_sh", [D // NCORES, FFI])
    kb.din("conv_w4", [DI + 2 * DS, 4])
    kb.din("conv_b", [DI + 2 * DS, 1])
    kb.din("dt_bias", [NH, 1])
    kb.din("a_log", [NH, 1])
    kb.din("dcol64", [DI, 1])
    kb.din("norm_w", [DI, 1])
    kb.din("pmx", [NH, NCORES])
    kb.din("bmx", [NH, NCORES * NCORES])
    kb.dout("out", [TOK, D])
    kb.dint("d_wxb", [D, EXBC], BF16)
    kb.dint("d_wz", [D, DI], BF16)
    kb.dint("d_wout", [DI, D], BF16)
    kb.dint("d_x", [DI + DS, TOK], BF16)
    kb.dint("d_y", [TOK, DI], BF16)
    kb.dint("d_xq", [TOK, D], BF16)
    kb.dint("d_xn", [CH, 8 * TOKH], BF16)
    kb.dint("d_hq", [TOK, FFI], BF16)
    kb.dint("d_hi", [NCH, NH * CH], F16)
    kb.dint("d_lo", [NCH, NH * CH], BF16)
    kb.dint("d_rstd", [1, TOK])
    kb.dint("cc_g_in", [FFI // NCORES, D], BF16)
    kb.dint("cc_g_out", [FFI, D], BF16, addr_space="Shared")
    kb.dint("cc_u_in", [FFI // NCORES, D], BF16)
    kb.dint("cc_u_out", [FFI, D], BF16, addr_space="Shared")
    kb.dint("cc_dn_in", [D // NCORES, FFI], BF16)
    kb.dint("cc_dn_out", [D, FFI], BF16, addr_space="Shared")
    kb.dint("d_coef", [NCORES, NH])
    kb.dint("d_lcz", [NCH, NH])
    kb.dint("cc_s_in", [DS, NH * HD])
    kb.dint("cc_s_out", [NCORES * DS, NH * HD], addr_space="Shared")
    kb.dint("cc_l_in", [NH, 1])
    kb.dint("cc_l_out", [NCORES * NH, 1], addr_space="Shared")


def body(kb, stage):
    nc, q = kb.nc, kb.q
    sb, psb, dram, sbuf = kb.sb, kb.ps, kb.dram, kb.sbuf

    ident = sb("ident", [CH, CH], BF16)
    ut_incl = sb("ut_incl", [CH, CH], BF16)
    ones1_bf = sb("ones1_bf", [1, CH], BF16)
    ones1_f32 = sb("ones1_f32", [1, CH], F32)
    onescol_bf = sb("onescol_bf", [CH, 1], BF16)
    ones32 = sb("ones32", [CH, NH], BF16)
    ones_full = sb("ones_full", [CH, CH], BF16)
    identh = sb("identh", [CH, CH], F16)
    negones1 = sb("negones1", [1, CH], BF16)
    dcol = sb("dcol", [CH, DI // CH])
    dtb = sb("dtb", [NH, 1])
    acol = sb("acol", [NH, 1])
    nwcol = sb("nwcol", [CH, DI // CH])
    epscol = sb("epscol", [CH, 1])
    onecolf = sb("onecolf", [CH, 1])
    pmx = sb("pmx_s", [NH, NCORES])
    bmx = sb("bmx_s", [NH, NCORES * NCORES])

    B_fm = sb("B_fm", [DS, TOK], BF16)
    C_fm = sb("C_fm", [DS, TOK], BF16)
    a_hi = sb("a_hi", [CH, NCH * NH], F16)
    a_lo = sb("a_lo", [CH, NCH * NH], BF16)
    az_fm = sb("az_fm", [NH, NCH])
    R = sb("R", [DS, NH * HD])
    Rb = sb("Rb", [DS, NH * HD], BF16)

    lnA = contextlib.ExitStack()
    lt_strict = sb("lt_strict", [CH, CH], BF16, scope=lnA)
    maskb = sb("maskb", [CH, CH], F32, scope=lnA)
    identf = sb("identf", [CH, CH], F32, scope=lnA)
    B_tm = sb("B_tm", [CH, NCH * DS], BF16, scope=lnA)
    dt_tmb = sb("dt_tmb", [CH, NCH * NH], BF16, scope=lnA)

    g = "gpsimd"
    q.emit(g, lambda e: e.memset(ident[:], 1.0))
    q.emit(g, lambda e: e.memset(identh[:], 1.0))
    q.emit(g, lambda e: e.memset(negones1[:], -1.0))
    q.emit(g, lambda e: e.memset(identf[:], 1.0))
    q.emit(g, lambda e: e.memset(ut_incl[:], 1.0))
    q.emit(g, lambda e: e.memset(lt_strict[:], 1.0))
    q.emit(g, lambda e: e.memset(maskb[:], 0.0))
    q.emit(g, lambda e: e.memset(ones1_bf[:], 1.0))
    q.emit(g, lambda e: e.memset(ones1_f32[:], 1.0))
    q.emit(g, lambda e: e.memset(onescol_bf[:], 1.0))
    q.emit(g, lambda e: e.memset(ones32[:], 1.0))
    q.emit(g, lambda e: e.memset(ones_full[:], 1.0))
    q.emit(g, lambda e: e.memset(R[:], 0.0))
    q.emit(g, lambda e: e.memset(epscol[:], EPS_LN))
    q.emit(g, lambda e: e.memset(onecolf[:], 1.0), inc="gp")
    q.wait(g, "gp")
    q.emit(g, lambda e: e.affine_select(ident[:], ident[:], [[-1, CH]], ALU.is_equal,
                                        0.0, base=0, channel_multiplier=1))
    q.emit(g, lambda e: e.affine_select(identh[:], identh[:], [[-1, CH]], ALU.is_equal,
                                        0.0, base=0, channel_multiplier=1))
    q.emit(g, lambda e: e.affine_select(identf[:], identf[:], [[-1, CH]], ALU.is_equal,
                                        0.0, base=0, channel_multiplier=1))
    q.emit(g, lambda e: e.affine_select(ut_incl[:], ut_incl[:], [[1, CH]], ALU.is_ge,
                                        0.0, base=0, channel_multiplier=-1))
    q.emit(g, lambda e: e.affine_select(lt_strict[:], lt_strict[:], [[-1, CH]], ALU.is_gt,
                                        0.0, base=0, channel_multiplier=1))
    q.emit(g, lambda e: e.affine_select(maskb[:], maskb[:], [[1, CH]], ALU.is_ge,
                                        NEG, base=0, channel_multiplier=-1), inc="gp")
    q.wait(g, "gp")
    q.dma(g, dtb[:], dram["dt_bias"], "ldc")
    q.dma(g, acol[:], dram["a_log"], "ldc")
    for _j in range(DI // CH):
        q.dma(g, nwcol[:, _j:_j + 1], dram["norm_w"][_j * CH:(_j + 1) * CH, :], "ldc")
        q.dma(g, dcol[:, _j:_j + 1], dram["dcol64"][_j * CH:(_j + 1) * CH, :], "ldc")
    q.dma(g, pmx[:], dram["pmx"], "ldc")
    q.dma(g, bmx[:], dram["bmx"], "ldc")
    q.dma(g, dram["d_wxb"], dram["w_xbcT"], "ldw0")

    q.wait("scalar", "ldc")
    q.emit("scalar", lambda e: e.activation(acol[:], acol[:], AF.Exp), inc="ac")
    q.wait("vector", "ac")
    q.emit("vector", lambda e: e.tensor_scalar(acol[:], acol[:], -1.0, None, ALU.mult),
           inc="vd")

    sb("xn_fm", [CH, 8 * TOKH], BF16, scope=lnA)
    if stage >= 3:
        kb.q.tag = "gather"
        gsc = contextlib.ExitStack()
        phase_gather_w(kb, gsc)
    if stage >= 1:
        kb.q.tag = "ln"
        phase_ln(kb)
    if stage >= 3:
        gsc.close()
    if stage >= 2:
        kb.q.tag = "a"
        phase_a(kb)
    lnA.close()
    sb("g_fm", [CH, 8 * TOK], BF16)
    if stage >= 3:
        kb.q.tag = "gather"
        phase_gather_sl(kb)
    if stage >= 4:
        kb.q.tag = "ffn"
        phase_ffn(kb)
    if stage >= 5:
        kb.q.tag = "tail"
        phase_tail(kb)

    for s in ("st_out", "st_tap", "st_xq", "st_xn", "st_x", "st_y", "st_hi", "st_hq", "st_wq", "st_s", "st_t"):
        q.wait("sync", s)


def phase_ln(kb):
    nc, q, dram, sbuf = kb.nc, kb.q, kb.dram, kb.sbuf
    sb, psb = kb.sb, kb.ps
    u = dram["u"]
    xn_fm = sbuf["xn_fm"]

    with contextlib.ExitStack() as scope:
        utile = kb.sb("utile", [CH, 2 * D], scope=scope)
        mcol = kb.sb("mcol", [CH, 2], scope=scope)
        mraw = kb.sb("mraw", [CH, 1], scope=scope)
        ssraw = kb.sb("ssraw", [CH, 1], scope=scope)
        ssd2 = kb.sb("ssd2", [CH, 1], scope=scope)
        vcol = kb.sb("vcol", [CH, 1], scope=scope)
        rcol = kb.sb("rcol", [CH, 2], scope=scope)
        up = kb.sb("up", [CH, D], scope=scope)
        xh = kb.sb("xh", [CH, D], F16, scope=scope)
        sqs = kb.sb("sqs", [CH, D], scope=scope)
        mx = kb.sb("mx", [CH, D // 16], scope=scope)
        mxr = kb.sb("mxr", [CH, D // 16], scope=scope)
        s16 = kb.sb("s16", [CH, D // 16], scope=scope)
        w_ = kb.sb("w_", [CH, D], F16, scope=scope)
        p1 = kb.sb("p1", [CH, D], F16, scope=scope)
        t_a = kb.sb("t_a", [CH, D], F16, scope=scope)
        t_b = kb.sb("t_b", [CH, D], F16, scope=scope)
        p23 = kb.sb("p23", [CH, D], F16, scope=scope)
        xq_t = kb.sb("xq_t", [CH, D], BF16, scope=scope)
        pln = kb.ps("pln", [CH, 1024], scope=scope)

        xn3 = xn_fm[:].rearrange("p (j t) -> p j t", j=8)
        ps3 = pln[:].rearrange("p (j t) -> p j t", j=8)

        ld_tgt = {}
        a2_t = {}
        rcol_t = {}
        up_vd = {}
        diagr_vd = {}
        xn_ac = {}
        mm_pe_d = {}
        q.dma("sync", utile[0:CH, 0:D], u[0:CH, :], "ld_u")
        ld_tgt[0] = q.n("ld_u")
        q.dma("sync", utile[0:CH, D:2 * D], u[CH:2 * CH, :], "ld_u")
        ld_tgt[1] = q.n("ld_u")
        for it in range(17):
            rows = CH if it < 16 else HALO
            r0 = it * CH
            sl = slice(0, rows)
            pu = it % 2
            ut = utile[sl, pu * D:pu * D + D]
            q.wait("scalar", "ld_u", ld_tgt[it])
            if it >= 2:
                q.wait("scalar", "vd", diagr_vd[it - 2])
            q.emit("scalar", lambda e, sl=sl, ut=ut: e.activation(
                sqs[sl, :], ut, AF.Square, accum_out=ssraw[sl, :]), inc="ac")
            q.emit("scalar", lambda e, sl=sl, ut=ut: e.activation(
                sqs[sl, :], ut, AF.Copy, accum_out=mraw[sl, :]), inc="ac")
            a2_t[it] = q.n("ac")
            q.emit("scalar", lambda e: e.drain())
            q.emit("scalar", lambda e, sl=sl, pu=pu: e.activation(mcol[sl, pu:pu + 1], mraw[sl, :], AF.Copy, scale=1.0 / D))
            q.emit("scalar", lambda e, sl=sl: e.activation(ssd2[sl, :], ssraw[sl, :], AF.Copy, scale=1.0 / D, bias=EPS_LN))
            q.emit("scalar", lambda e: e.drain())
            q.emit("scalar", lambda e, sl=sl, pu=pu: e.activation(vcol[sl, :], mcol[sl, pu:pu + 1], AF.Square))
            q.emit("scalar", lambda e: e.drain())
            q.emit("scalar", lambda e, sl=sl: e.activation(vcol[sl, :], vcol[sl, :], AF.Ln, scale=-1.0, bias=ssd2[sl, :]))
            q.emit("scalar", lambda e: e.drain())
            q.emit("scalar", lambda e, sl=sl, pu=pu: e.activation(rcol[sl, pu:pu + 1], vcol[sl, :], AF.Exp, scale=-0.5))
            q.emit("scalar", lambda e: e.drain(), inc="ac")
            rcol_t[it] = q.n("ac")
            q.wait("vector", "ac", rcol_t[it])
            q.emit("vector", lambda e, sl=sl, ut=ut, pu=pu: e.tensor_scalar(up[sl, :], ut, mcol[sl, pu:pu + 1], None, ALU.subtract), inc="vd")
            up_vd[it] = q.n("vd")
            if it >= 1:
                q.wait("vector", "pe", mm_pe_d[it - 1])
            if it == 16:
                q.emit("vector", lambda e: e.memset(xh[:], 0.0), inc="vd")
            q.emit("vector", lambda e, sl=sl, pu=pu: e.tensor_scalar(xh[sl, :], up[sl, :], rcol[sl, pu:pu + 1], None, ALU.mult), inc="vd")
            diagr_vd[it] = q.n("vd")
            qr_t = equant_pre(kb, src=xh, sl=slice(0, CH), width=D, mx=mx)
            if it + 2 <= 16:
                rows2 = CH if it + 2 < 16 else HALO
                q.wait("sync", "ac", a2_t[it])
                q.wait("sync", "vd", up_vd[it])
                q.dma("sync", utile[0:rows2, pu * D:pu * D + D],
                      u[(it + 2) * CH:(it + 2) * CH + rows2, :], "ld_u")
                ld_tgt[it + 2] = q.n("ld_u")
            q.wait("tensor", "vd", diagr_vd[it])
            if it >= 1:
                q.wait("tensor", "vd", xn_ac[it - 1])
            for j in range(8):
                q.emit("tensor", lambda e, j=j, sl=sl:
                       e.matmul(pln[:, j * CH:(j + 1) * CH], xh[sl, j * CH:(j + 1) * CH],
                                kb.sbuf["identh"][sl, :], start=True, stop=True),
                       inc=("pe" if j == 7 else None))
            mm_pe_d[it] = q.n("pe")
            q.wait("vector", "pe", mm_pe_d[it])
            q.emit("vector", lambda e, it=it, rows=rows:
                   e.tensor_copy(xn3[:, :, r0_(it):r0_(it) + rows], ps3[:, :, 0:rows]), inc="vd")
            xn_ac[it] = q.n("vd")
            if it == 10:
                kb.tap("t_mcol", mcol[:])
                kb.tap("t_sscol", ssraw[:])
                kb.tap("t_rcol", rcol[:])

            p0 = HALO if it == 0 else 0
            q.wait("vector", "st_xq")
            equant_post(kb, src=xh, sl=slice(0, CH), width=D, mx=mx, mxr=mxr, s16=s16,
                        w_=w_, p1=p1, t_a=t_a, t_b=t_b, p23=p23, out=xq_t, qr_tgt=qr_t)
            q.wait("sync", "vd")
            q.dma("sync", dram["d_xq"][it * CH - HALO + p0: it * CH - HALO + rows, :],
                  xq_t[p0:rows, :], "st_xq")

        q.wait("sync", "vd")
        q.wait("sync", "ac")
        q.dma("sync", dram["d_xn"], xn_fm[:], "st_xn")
        kb.tap("t_xn", xn_fm[:])
        q.wait("sync", "st_xq")
        kb.tap("t_xq", dram["d_xq"])


def r0_(it):
    return it * CH


def equant_pre(kb, src, sl, width, mx, mx0=0, src0=0):
    q = kb.q
    nb = width // 16
    src3 = src[sl, src0:src0 + width].rearrange("p (b k) -> p b k", k=16)
    q.emit("vector", lambda e: e.tensor_reduce(mx[sl, mx0:mx0 + nb].unsqueeze(2), src3, AX.X,
                                               ALU.max, apply_absolute_value=True), inc="qr")
    return q.n("qr")


def equant_post(kb, src, sl, width, mx, mxr, s16, w_, p1, t_a, t_b, p23, out, qr_tgt, osl=None, mx0=0, src0=0):
    q = kb.q
    if osl is None:
        osl = slice(0, width)
    nb = width // 16
    rows = sl.stop - sl.start
    src3 = src[sl, src0:src0 + width].rearrange("p (b k) -> p b k", k=16)
    q.wait("vector", "qr", qr_tgt)
    q.emit("vector", lambda e: e.tensor_scalar(s16[sl, 0:nb], mx[sl, mx0:mx0 + nb], 6e-12, 1.0 / 6.0, ALU.max, ALU.mult), inc="qs")
    q.wait("vector", "qs", q.n("qs"))
    q.emit("vector", lambda e: e.reciprocal(mxr[sl, 0:nb], s16[sl, 0:nb]), inc="qs")
    q.wait("vector", "qs", q.n("qs"))
    r3 = mxr[sl, 0:nb].unsqueeze(2).broadcast_to([rows, nb, 16])
    s163 = s16[sl, 0:nb].unsqueeze(2).broadcast_to([rows, nb, 16])
    w3 = w_[sl, 0:width].rearrange("p (b k) -> p b k", k=16)
    q.emit("vector", lambda e: e.tensor_tensor(w3, src3, r3, ALU.mult), inc="vd")
    kb.srcfree_vd = q.n("vd")
    wf = w_[sl, 0:width]
    q.emit("vector", lambda e: e.tensor_scalar(p1[sl, 0:width], wf, -2.25, 2.25, ALU.max, ALU.min), inc="vd")
    q.emit("vector", lambda e: e.tensor_scalar(p1[sl, 0:width], p1[sl, 0:width], M_H, -M_H, ALU.add, ALU.add), inc="vd")
    q.emit("vector", lambda e: e.tensor_scalar(t_a[sl, 0:width], wf, M_1, -M_1, ALU.add, ALU.add), inc="vd")
    q.emit("vector", lambda e: e.tensor_scalar(t_b[sl, 0:width], t_a[sl, 0:width], 2.0, 4.0, ALU.max, ALU.min), inc="vd")
    q.emit("vector", lambda e: e.tensor_scalar(t_a[sl, 0:width], t_a[sl, 0:width], -4.0, -2.0, ALU.max, ALU.min), inc="vd")
    q.emit("vector", lambda e: e.tensor_tensor(p23[sl, 0:width], t_a[sl, 0:width], t_b[sl, 0:width], ALU.add), inc="vd")
    q.emit("vector", lambda e: e.tensor_tensor(p1[sl, 0:width], p1[sl, 0:width], p23[sl, 0:width], ALU.add), inc="vd")
    q.emit("vector", lambda e: e.tensor_scalar(t_a[sl, 0:width], wf, M_2, -M_2, ALU.add, ALU.add), inc="vd")
    q.emit("vector", lambda e: e.tensor_scalar(t_b[sl, 0:width], t_a[sl, 0:width], 4.0, 6.0, ALU.max, ALU.min), inc="vd")
    q.emit("vector", lambda e: e.tensor_scalar(t_a[sl, 0:width], t_a[sl, 0:width], -6.0, -4.0, ALU.max, ALU.min), inc="vd")
    q.emit("vector", lambda e: e.tensor_tensor(p23[sl, 0:width], t_a[sl, 0:width], t_b[sl, 0:width], ALU.add), inc="vd")
    q.emit("vector", lambda e: e.tensor_tensor(p1[sl, 0:width], p1[sl, 0:width], p23[sl, 0:width], ALU.add), inc="vd")
    o3 = out[sl, osl].rearrange("p (b k) -> p b k", k=16)
    p13 = p1[sl, 0:width].rearrange("p (b k) -> p b k", k=16)
    q.emit("vector", lambda e: e.tensor_tensor(o3, p13, s163, ALU.mult), inc="vd")


def phase_a(kb):
    nc, q, dram, sbuf = kb.nc, kb.q, kb.dram, kb.sbuf
    xn_fm = sbuf["xn_fm"]
    TH2 = 1024
    NC2 = 1027

    with contextlib.ExitStack() as sc:
        wxb = kb.sb("wxb", [CH, 2 * 8 * CH], BF16, scope=sc)
        cwall = kb.sb("cwall", [CH, 18 * 4], scope=sc)
        cball = kb.sb("cball", [CH, 18], scope=sc)
        ta = kb.sb("ta", [CH, TH2], BF16, scope=sc)
        tb = kb.sb("tb", [CH, 2 * TH2], BF16, scope=sc)
        xpost = kb.sb("xpost", [CH, 2 * TH2], BF16, scope=sc)
        dtraw = kb.sb("dtraw", [NH, TH2], scope=sc)
        dts1 = kb.sb("dts1", [NH, TH2], scope=sc)
        dts2 = kb.sb("dts2", [NH, TH2], scope=sc)
        dtv_bf = kb.sb("dtv_bf", [NH, TH2], BF16, scope=sc)
        a_f = kb.sb("a_f", [NH, TH2], scope=sc)
        x_tm = kb.sb("x_tm", [CH, 2 * DI], BF16, scope=sc)
        xfc = kb.sb("xfc", [CH, DI], BF16, scope=sc)
        xdc = kb.sb("xdc", [CH, DI], BF16, scope=sc)
        xdt = kb.sb("xdt", [CH, DI], BF16, scope=sc)
        bw = kb.sb("bw", [CH, NH * DS], BF16, scope=sc)
        mt = kb.sb("mt", [CH, NH * CH], BF16, scope=sc)
        ebuf = kb.sb("ebuf", [CH, NH * CH], BF16, scope=sc)
        d1 = kb.sb("d1", [CH, 3 * 512], scope=sc)
        rowp_hi = kb.sb("rowp_hi", [CH, NH * CH], F16, scope=sc)
        rowp_lo = kb.sb("rowp_lo", [CH, NH * CH], BF16, scope=sc)
        cumc = kb.sb("cumc", [CH, NH], scope=sc)
        wd_bf = kb.sb("wd_bf", [CH, 2 * NH], BF16, scope=sc)
        decs = kb.sb("decs", [CH, 2 * 64], scope=sc)
        rtmp = kb.sb("rtmp", [DS, NH * HD], scope=sc)
        ybt = kb.sb("ybt", [CH, 512], scope=sc)
        ytile = kb.sb("ytile", [CH, DI], BF16, scope=sc)
        scT = kb.sb("scT", [CH, CH], BF16, scope=sc)
        lsum = kb.sb("lsum", [NH, 1], scope=sc)
        ps1 = kb.ps("ps1", [CH, 2048], scope=sc)
        ps2 = kb.ps("ps2", [CH, 1536], scope=sc)
        psS = kb.ps("psS", [CH, 512], scope=sc)

        q.wait("gpsimd", "st_xq")
        q.wait("gpsimd", "st_xn")
        q.wait("gpsimd", "st_wq")
        q.dma("gpsimd", cwall[:].rearrange("p (c w) -> p c w", w=4),
              dram["conv_w4"][0:18 * CH, :].rearrange("(c p) w -> p c w", p=CH), "ldc2")
        q.dma("gpsimd", cball[:].rearrange("p (c w) -> p c w", w=1),
              dram["conv_b"][0:18 * CH, :].rearrange("(c p) w -> p c w", p=CH), "ldc2",
              allow_slow_non_contiguous=True)
        q.wait("vector", "ldc2")
        q.wait("scalar", "ldc2")
        pe_t = {}
        ldw_t = {}
        conv_vd = {}
        sil_ac = {}
        stx_t = {}
        kb.ssd_t = {}
        kb.ssd_sty = 0
        for half in range(2):
            q.tag = "a_inproj"
            t0 = half * TH2
            c0 = half * TH2
            q.wait("tensor", "vd")
            q.wait("tensor", "ac")
            q.wait("gpsimd", "ldw0", 16)
            for ec in range(19):
                width = CH if ec < 18 else NH
                e0 = ec * CH
                gec = half * 19 + ec
                wb = gec % 2
                reg = sbuf["ps1"] if gec % 2 == 0 else sbuf["ps2"]
                if gec >= 2:
                    q.wait("gpsimd", "pe", pe_t[gec - 2])
                q.dma("gpsimd", wxb[:, wb * 1024:wb * 1024 + 8 * width].rearrange("p (k w) -> p k w", k=8),
                      dram["d_wxb"][:, e0:e0 + width].rearrange("(k p) w -> p k w", p=CH),
                      "ld_w")
                ldw_t[gec] = q.n("ld_w")
                q.wait("tensor", "ld_w", ldw_t[gec])
                if half == 0 and ec == 0:
                    q.wait("sync", "ld_w")
                    kb.tap("t_wxb", wxb[:, 0:8 * CH])
                    kb.tap("t_xnfm", xn_fm[:])
                if gec >= 2 and (gec - 2) in conv_vd:
                    q.wait("tensor", "vd", conv_vd[gec - 2])
                for (tc0, tw) in [(0, 512), (512, 512), (1024, 3)]:
                    for k in range(8):
                        q.emit("tensor", lambda e, k=k, tc0=tc0, tw=tw, width=width, c0=c0, reg=reg, wb=wb:
                               e.matmul(reg[0:width, tc0:tc0 + tw],
                                        wxb[:, wb * 1024 + k * width:wb * 1024 + (k + 1) * width],
                                        xn_fm[:, k * TOKH + c0 + tc0: k * TOKH + c0 + tc0 + tw],
                                        start=(k == 0), stop=(k == 7)),
                               inc=("pe" if (k == 7 and tc0 == 1024) else None))
                pe_t[gec] = q.n("pe")
                if half == 0 and ec == 1 and "t_raw1" in kb.taps:
                    q.wait("scalar", "pe", pe_t[gec])
                    q.emit("scalar", lambda e, reg=reg: e.copy(sbuf["x_tm"][:, 0:NC2], reg[:, 0:NC2]), inc="ac")
                    q.wait("sync", "ac")
                    kb.tap("t_raw1", sbuf["x_tm"][:, 0:NC2])
                if ec < 18:
                    cwc = ec * 4
                    q.wait("vector", "pe", pe_t[gec])
                    if gec >= 2 and (gec - 2) in sil_ac:
                        q.wait("vector", "ac", sil_ac[gec - 2])
                    q.emit("vector", lambda e, reg=reg, cwc=cwc: e.tensor_scalar(
                        ta[:], reg[:, 0:TH2], cwall[:, cwc:cwc + 1], None, ALU.mult), inc="vd")
                    q.emit("vector", lambda e, reg=reg, cwc=cwc, wb=wb: e.scalar_tensor_tensor(
                        tb[:, wb * TH2:(wb + 1) * TH2], reg[:, 1:1 + TH2], cwall[:, cwc + 1:cwc + 2], ta[:], ALU.mult, ALU.add), inc="vd")
                    q.emit("vector", lambda e, reg=reg, cwc=cwc, wb=wb: e.scalar_tensor_tensor(
                        ta[:], reg[:, 2:2 + TH2], cwall[:, cwc + 2:cwc + 3], tb[:, wb * TH2:(wb + 1) * TH2], ALU.mult, ALU.add), inc="vd")
                    q.emit("vector", lambda e, reg=reg, cwc=cwc, wb=wb: e.scalar_tensor_tensor(
                        tb[:, wb * TH2:(wb + 1) * TH2], reg[:, 3:3 + TH2], cwall[:, cwc + 3:cwc + 4], ta[:], ALU.mult, ALU.add), inc="vd")
                    conv_vd[gec] = q.n("vd")
                    q.wait("scalar", "vd", conv_vd[gec])
                    if gec >= 2:
                        q.wait("scalar", "st_x", stx_t.get(gec - 2, 0))
                    q.emit("scalar", lambda e, wb=wb, ec=ec: e.activation(
                        xpost[:, wb * TH2:(wb + 1) * TH2], tb[:, wb * TH2:(wb + 1) * TH2],
                        AF.Silu, bias=cball[:, ec:ec + 1]), inc="ac")
                    sil_ac[gec] = q.n("ac")
                    if half == 0 and ec == 0:
                        q.wait("sync", "ac")
                        kb.tap("t_xpost", xpost[:, 0:TH2])
                    if ec <= 16:
                        q.wait("sync", "ac", sil_ac[gec])
                        q.dma("sync", dram["d_x"][e0:e0 + CH, t0:t0 + TH2],
                              xpost[:, wb * TH2:(wb + 1) * TH2], "st_x")
                        stx_t[gec] = q.n("st_x")
                    if ec == 16:
                        q.emit("scalar", lambda e, t0=t0, wb=wb: e.copy(
                            sbuf["B_fm"][:, t0:t0 + TH2], xpost[:, wb * TH2:(wb + 1) * TH2]), inc="ac")
                    if ec == 17:
                        q.emit("scalar", lambda e, t0=t0, wb=wb: e.copy(
                            sbuf["C_fm"][:, t0:t0 + TH2], xpost[:, wb * TH2:(wb + 1) * TH2]), inc="ac")
                else:
                    q.wait("scalar", "pe", pe_t[gec])
                    q.emit("scalar", lambda e, reg=reg: e.copy(dtraw[:], reg[0:NH, 3:3 + TH2]), inc="ac")
                    q.wait("scalar", "ac")
                    q.emit("scalar", lambda e: e.activation(dts1[:], dtraw[:], AF.Abs, bias=sbuf["dtb"][:, 0:1]))
                    q.emit("scalar", lambda e: e.drain())
                    q.emit("scalar", lambda e: e.activation(dts1[:], dts1[:], AF.Exp, scale=-1.0))
                    q.emit("scalar", lambda e: e.drain())
                    q.emit("scalar", lambda e: e.activation(dts1[:], dts1[:], AF.Ln, bias=sbuf["onecolf"][0:NH, 0:1]))
                    q.emit("scalar", lambda e: e.drain())
                    q.emit("scalar", lambda e: e.activation(dts2[:], dtraw[:], AF.Relu, bias=sbuf["dtb"][:, 0:1]), inc="ac")
                    q.wait("vector", "ac")
                    q.emit("vector", lambda e: e.tensor_tensor(dts2[:], dts1[:], dts2[:], ALU.add), inc="vd")
                    q.emit("vector", lambda e: e.tensor_copy(dtv_bf[:], dts2[:]), inc="vd")
                    q.emit("vector", lambda e: e.tensor_scalar(a_f[:], dts2[:], sbuf["acol"][0:NH, 0:1], None, ALU.mult), inc="vd")

            q.emit("vector", lambda e, half=half: e.tensor_reduce(
                sbuf["az_fm"][:, half * 8:(half + 1) * 8].unsqueeze(2),
                a_f[:].rearrange("p (c t) -> p c t", t=CH), AX.X, ALU.add))
            q.emit("vector", lambda e: e.drain(), inc="vd")
            kb.ssd_t = dict(kb.ssd_t)
            kb.ssd_t["psS_vd"] = q.n("vd")
            kb.ssd_t["psS_ac"] = q.n("ac")
            q.wait("scalar", "st_x")
            q.wait("scalar", "pe")
            q.wait("scalar", "vd")
            for tc in range(8):
                gc = half * 8 + tc
                q.dma("scalar", sbuf["B_tm"][:, gc * DS:(gc + 1) * DS],
                      dram["d_x"][DI:DI + DS, t0 + tc * CH:t0 + (tc + 1) * CH], "ld_xt",
                      transpose=True)
            ld_tgt = {}
            q.dma("scalar", x_tm[:, 0:DI],
                  dram["d_x"][0:DI, t0:t0 + CH], "ld_xc", transpose=True)
            ld_tgt[0] = q.n("ld_xc")

            q.tag = "a_ssd"
            vd_prev = 0
            for cc_ in range(8):
                if cc_ + 1 < 8:
                    if vd_prev:
                        q.wait("scalar", "vd", vd_prev)
                    q.dma("scalar", x_tm[:, ((cc_ + 1) % 2) * DI:((cc_ + 1) % 2 + 1) * DI],
                          dram["d_x"][0:DI, t0 + (cc_ + 1) * CH:t0 + (cc_ + 2) * CH],
                          "ld_xc", transpose=True)
                    ld_tgt[cc_ + 1] = q.n("ld_xc")
                if half * 8 + cc_ < SSD_NCHUNK:
                    ssd_chunk(kb, half, cc_, cc_ % 2, ld_tgt[cc_])
                vd_prev = q.n("vd")

        if SSD_NCHUNK < 16:
            kb.tap("t_dx", dram["d_x"])
            if SSD_NCHUNK > 0:
                kb.tap("t_R", sbuf["R"][:])
            return
        q.wait("sync", "vd")
        q.dma("sync", dram["cc_s_in"], sbuf["R"][:], "st_s")
        q.wait("vector", "ac")
        q.emit("vector", lambda e: e.tensor_reduce(lsum[:].unsqueeze(2), sbuf["az_fm"][:].unsqueeze(1), AX.X, ALU.add))
        q.emit("vector", lambda e: e.drain(), inc="vd")
        q.wait("sync", "vd")
        q.dma("sync", dram["cc_l_in"], lsum[:], "st_s")
        kb.tap("t_y", dram["d_y"])
        kb.tap("t_R", sbuf["R"][:])
        kb.tap("t_dx", dram["d_x"])


def ssd_chunk(kb, half, cc_, xbuf, ld_xc_tgt):
    nc, q, dram, sbuf = kb.nc, kb.q, kb.dram, kb.sbuf
    gc = half * 8 + cc_
    t0 = half * 1024
    tl0 = t0 + cc_ * CH
    ps1, ps2, psS = sbuf["ps1"], sbuf["ps2"], sbuf["psS"]
    a_f = sbuf["a_f"]
    acs = slice(cc_ * CH, (cc_ + 1) * CH)
    ahc = sbuf["a_hi"][:, gc * NH:(gc + 1) * NH]
    alc = sbuf["a_lo"][:, gc * NH:(gc + 1) * NH]
    dtc = sbuf["dt_tmb"][:, gc * NH:(gc + 1) * NH]
    par = gc % 2
    eacol = sbuf["decs"][:, par * 64:par * 64 + NH]
    pcb = sbuf["decs"][:, par * 64 + NH:par * 64 + 2 * NH]
    wd_bf = sbuf["wd_bf"][:, par * NH:(par + 1) * NH]
    T = kb.ssd_t

    q.wait("tensor", "vd", T.get("psS_vd", 0))
    q.wait("tensor", "ac", T.get("psS_ac", 0))
    q.emit("tensor", lambda e, tl0=tl0: e.matmul(psS[:, 192:320], sbuf["B_fm"][:, tl0:tl0 + CH],
                                                 sbuf["C_fm"][:, tl0:tl0 + CH], start=True, stop=True))
    q.emit("tensor", lambda e: e.transpose(psS[:, 0:NH], a_f[:, acs], sbuf["identf"][0:NH, 0:NH]))
    q.emit("tensor", lambda e: e.transpose(psS[:, 64:80].bitcast(BF16), sbuf["dtv_bf"][:, acs], sbuf["ident"][0:NH, 0:NH]), inc="pe")
    tr_pe = q.n("pe")
    q.wait("scalar", "vd", T.get("mt_vd", 0))
    q.wait("scalar", "pe", tr_pe)
    q.emit("scalar", lambda e: e.copy(sbuf["scT"][:], psS[:, 192:320]), inc="ac")
    sct_ac = q.n("ac")
    q.wait("vector", "pe", tr_pe)
    q.emit("vector", lambda e: e.tensor_copy(ahc, psS[:, 0:NH]), inc="vd")
    q.wait("vector", "vd")
    q.emit("vector", lambda e: e.tensor_tensor(alc, psS[:, 0:NH], ahc, ALU.subtract), inc="vd")
    alc_vd = q.n("vd")
    q.emit("vector", lambda e: e.tensor_copy(dtc, psS[:, 64:80].bitcast(BF16)), inc="vd")
    copies_vd = q.n("vd")

    q.wait("tensor", "vd", copies_vd)
    q.wait("tensor", "ac", sct_ac)
    q.emit("tensor", lambda e: e.matmul(psS[:, 96:96 + NH], sbuf["ut_incl"][:], ahc, start=True, stop=False))
    q.emit("tensor", lambda e: e.matmul(psS[:, 96:96 + NH], sbuf["ut_incl"][:], alc, start=False, stop=True))
    q.emit("tensor", lambda e: e.matmul(psS[:, 160:160 + NH], sbuf["lt_strict"][:], ahc, start=True, stop=False))
    q.emit("tensor", lambda e: e.matmul(psS[:, 160:160 + NH], sbuf["lt_strict"][:], alc, start=False, stop=True))
    q.emit("tensor", lambda e: e.matmul(psS[:, 128:128 + NH], sbuf["ones_full"][:], ahc, start=True, stop=False))
    q.emit("tensor", lambda e: e.matmul(psS[:, 128:128 + NH], sbuf["ones_full"][:], alc, start=False, stop=True), inc="pe")
    cum_pe = q.n("pe")
    q.wait("scalar", "pe", cum_pe)
    q.emit("scalar", lambda e, par=par: e.activation(
        sbuf["decs"][:, par * 64:(par + 1) * 64], psS[:, 96:160], AF.Exp), inc="ac")
    q.emit("scalar", lambda e, wd_bf=wd_bf: e.activation(wd_bf, psS[:, 160:160 + NH], AF.Exp), inc="ac")
    dec_ac = q.n("ac")

    q.wait("vector", "pe", T.get("diff_pe", 0))
    q.wait("vector", "pe", cum_pe)
    q.emit("vector", lambda e: e.tensor_copy(sbuf["cumc"][:], psS[:, 96:96 + NH]), inc="vd")
    q.emit("vector", lambda e: e.tensor_tensor(
        sbuf["rowp_hi"][:].rearrange("p (h l) -> p h l", l=CH),
        ahc.unsqueeze(2).broadcast_to([CH, NH, CH]),
        sbuf["ut_incl"][:].unsqueeze(1).broadcast_to([CH, NH, CH]), ALU.mult), inc="vd")
    q.emit("vector", lambda e: e.tensor_tensor(
        sbuf["rowp_lo"][:].rearrange("p (h l) -> p h l", l=CH),
        alc.unsqueeze(2).broadcast_to([CH, NH, CH]),
        sbuf["ut_incl"][:].unsqueeze(1).broadcast_to([CH, NH, CH]), ALU.mult), inc="vd")
    prod_vd = q.n("vd")

    q.wait("sync", "st_x")
    q.dma("sync", sbuf["xfc"][:].rearrange("p (j t) -> p j t", t=CH),
          dram["d_x"][0:DI, tl0:tl0 + CH].rearrange("(j p) t -> p j t", p=CH), "ld_xf")
    q.wait("vector", "ld_xf")
    q.wait("vector", "ac", dec_ac)
    q.emit("vector", lambda e: e.tensor_tensor(
        sbuf["xdc"][:].rearrange("p (j t) -> p j t", t=CH),
        sbuf["xfc"][:].rearrange("p (j t) -> p j t", t=CH),
        sbuf["dcol"][:].unsqueeze(2).broadcast_to([CH, 16, CH]), ALU.mult), inc="vd")
    q.wait("vector", "ld_xt")
    q.wait("vector", "ld_xc", ld_xc_tgt)
    btc = sbuf["B_tm"][:, gc * DS:(gc + 1) * DS]
    q.emit("vector", lambda e, wd_bf=wd_bf: e.tensor_tensor(
        sbuf["bw"][:].rearrange("p (h n) -> p h n", h=NH),
        btc.unsqueeze(1).broadcast_to([CH, NH, DS]),
        wd_bf.unsqueeze(2).broadcast_to([CH, NH, DS]), ALU.mult), inc="vd")
    xtc = sbuf["x_tm"][:, xbuf * DI:(xbuf + 1) * DI]
    q.emit("vector", lambda e: e.tensor_tensor(
        sbuf["xdt"][:].rearrange("p (h d) -> p h d", h=NH),
        xtc.rearrange("p (h d) -> p h d", h=NH),
        dtc.unsqueeze(2).broadcast_to([CH, NH, HD]), ALU.mult), inc="vd")
    build_vd = q.n("vd")

    q.wait("vector", "pe", T.get("m4_pe", 0))
    q.emit("vector", lambda e: e.tensor_copy(sbuf["Rb"][:], sbuf["R"][:]), inc="vd")
    rb_vd = q.n("vd")
    q.wait("tensor", "vd", build_vd)
    q.wait("tensor", "vd", T.get("m4_vd", 0))
    m3_pe = {}
    radd_vd = {}
    for hhalf in range(2):
        if hhalf == 1:
            q.wait("tensor", "vd", radd_vd[0])
        for hh in range(16):
            h = hhalf * 16 + hh
            q.emit("tensor", lambda e, h=h, hh=hh: e.matmul(
                ps2[:, hh * HD:(hh + 1) * HD], sbuf["bw"][:, h * DS:(h + 1) * DS],
                sbuf["xdt"][:, h * HD:(h + 1) * HD], start=(hh % 8 == 0), stop=(hh % 8 == 7)),
                inc=("pe" if hh == 15 else None))
        m3_pe[hhalf] = q.n("pe")
        q.wait("vector", "pe", m3_pe[hhalf])
        hsl = slice(hhalf * 1024, (hhalf + 1) * 1024)
        q.emit("vector", lambda e, hhalf=hhalf, hsl=hsl, pcb=pcb: e.tensor_tensor(
            sbuf["rtmp"][:, hsl].rearrange("p (h d) -> p h d", d=HD),
            sbuf["R"][:, hsl].rearrange("p (h d) -> p h d", d=HD),
            pcb[:, hhalf * 16:(hhalf + 1) * 16].unsqueeze(2).broadcast_to([DS, 16, HD]),
            ALU.mult), inc="vd")
        q.wait("vector", "vd")
        q.emit("vector", lambda e, hhalf=hhalf, hsl=hsl: e.tensor_tensor(
            sbuf["R"][:, hsl], sbuf["rtmp"][:, hsl], ps2[:, 0:1024], ALU.add), inc="vd")
        radd_vd[hhalf] = q.n("vd")
    scan_vd = q.n("vd")

    q.wait("tensor", "vd", prod_vd)
    q.wait("tensor", "vd", T.get("ytile_vd", 0))
    d1_t = {}
    exp_t = {}
    for p in range(8):
        bank = p % 3
        if p >= 3:
            q.wait("tensor", "vd", d1_t[p - 3])
        sl = slice(p * 512, (p + 1) * 512)
        q.emit("tensor", lambda e, sl=sl, bank=bank: e.matmul(
            ps1[:, bank * 512:(bank + 1) * 512], sbuf["ones_full"][:], sbuf["rowp_hi"][:, sl], start=True, stop=False))
        q.emit("tensor", lambda e, sl=sl, bank=bank: e.matmul(
            ps1[:, bank * 512:(bank + 1) * 512], sbuf["ones_full"][:], sbuf["rowp_lo"][:, sl], start=False, stop=True), inc="pe")
        pe_p = q.n("pe")
        q.wait("vector", "pe", pe_p)
        if p >= 3:
            q.wait("vector", "ac", exp_t[p - 3])
        q.emit("vector", lambda e, p=p, bank=bank: e.tensor_tensor(
            sbuf["d1"][:, bank * 512:(bank + 1) * 512].rearrange("p (a b) -> p a b", b=CH),
            ps1[:, bank * 512:(bank + 1) * 512].rearrange("p (a b) -> p a b", b=CH),
            sbuf["cumc"][:, 4 * p:4 * p + 4].unsqueeze(2).broadcast_to([CH, 4, CH]),
            ALU.subtract), inc="vd")
        q.emit("vector", lambda e, bank=bank: e.scalar_tensor_tensor(
            sbuf["d1"][:, bank * 512:(bank + 1) * 512].rearrange("p (a b) -> p a b", b=CH),
            sbuf["d1"][:, bank * 512:(bank + 1) * 512].rearrange("p (a b) -> p a b", b=CH), 0.0,
            sbuf["maskb"][:].unsqueeze(1).broadcast_to([CH, 4, CH]),
            ALU.min, ALU.add), inc="vd")
        d1_t[p] = q.n("vd")
        q.wait("scalar", "vd", d1_t[p])
        q.emit("scalar", lambda e, p=p, bank=bank: e.activation(
            sbuf["ebuf"][:, p * 512:(p + 1) * 512], sbuf["d1"][:, bank * 512:(bank + 1) * 512], AF.Exp), inc="ac")
        exp_t[p] = q.n("ac")
    diff_pe = q.n("pe")

    q.wait("vector", "ac", exp_t[7])
    q.wait("vector", "ac", sct_ac)
    q.emit("vector", lambda e: e.tensor_tensor(
        sbuf["mt"][:].rearrange("p (h l) -> p h l", h=NH),
        sbuf["ebuf"][:].rearrange("p (h l) -> p h l", h=NH),
        sbuf["scT"][:].unsqueeze(1).broadcast_to([CH, NH, CH]), ALU.mult), inc="vd")
    mt_vd = q.n("vd")

    q.wait("tensor", "vd", mt_vd)
    for h in range(NH):
        q.emit("tensor", lambda e, h=h: e.matmul(
            ps1[:, h * HD:(h + 1) * HD], sbuf["mt"][:, h * CH:(h + 1) * CH],
            sbuf["xdt"][:, h * HD:(h + 1) * HD], start=(h % 8 == 0), stop=False))
    for j in range(16):
        q.emit("tensor", lambda e, j=j: e.matmul(
            ps1[:, j * CH:(j + 1) * CH],
            sbuf["xdc"][:, j * CH:(j + 1) * CH],
            sbuf["ident"][:], start=False, stop=(j % 4 == 3)),
            inc=("pe" if j == 15 else None))
    y_pe = q.n("pe")

    q.wait("tensor", "vd", scan_vd)
    ybt_vd = {}
    m4_pe = {}

    def m4_mm(qq):
        bank = qq % 3
        q.emit("tensor", lambda e, qq=qq, bank=bank, tl0=tl0: e.matmul(
            ps2[:, bank * 512:(bank + 1) * 512], sbuf["C_fm"][:, tl0:tl0 + CH],
            sbuf["Rb"][:, qq * 512:(qq + 1) * 512], start=True, stop=True), inc="pe")
        m4_pe[qq] = q.n("pe")

    def m4_ev(qq):
        bank = qq % 3
        q.wait("vector", "pe", m4_pe[qq])
        q.emit("vector", lambda e, qq=qq, bank=bank, eacol=eacol: e.tensor_tensor(
            sbuf["ybt"][:].rearrange("p (h d) -> p h d", d=HD),
            ps2[:, bank * 512:(bank + 1) * 512].rearrange("p (h d) -> p h d", d=HD),
            eacol[:, qq * 8:(qq + 1) * 8].unsqueeze(2).broadcast_to([CH, 8, HD]),
            ALU.mult), inc="vd")
        ybt_vd[qq] = q.n("vd")
        q.emit("vector", lambda e, qq=qq: e.scalar_tensor_tensor(
            sbuf["ytile"][:, qq * 512:(qq + 1) * 512], ps1[:, qq * 512:(qq + 1) * 512], 1.0,
            sbuf["ybt"][:], ALU.mult, ALU.add), inc="vd")

    m4_mm(0)
    m4_mm(1)
    m4_mm(2)
    q.wait("vector", "pe", y_pe)
    q.wait("vector", "st_y", kb.ssd_sty)
    m4_ev(0)
    q.wait("tensor", "vd", ybt_vd[0])
    m4_mm(3)
    m4_ev(1)
    m4_ev(2)
    m4_ev(3)
    ytile_vd = q.n("vd")
    q.wait("sync", "vd", ytile_vd)
    q.dma("sync", dram["d_y"][tl0:tl0 + CH, :], sbuf["ytile"][:], "st_y")
    kb.ssd_sty = q.n("st_y")

    kb.ssd_t = dict(
        psS_vd=copies_vd, psS_ac=dec_ac, mt_vd=mt_vd, bw_vd=build_vd,
        diff_pe=diff_pe, m4_pe=m4_pe[3], m4_vd=ytile_vd, ytile_vd=ytile_vd,
    )


def phase_gather_w(kb, sc):
    nc, q, dram, sbuf = kb.nc, kb.q, kb.dram, kb.sbuf
    wsh = kb.sb("wsh", [CH, 2 * FFI], F16, scope=sc)
    wqo = kb.sb("wqo", [CH, FFI], BF16, scope=sc)
    qmx = kb.sb("qmx", [CH, 2 * (FFI // 16)], scope=sc)
    qmxr = kb.sb("qmxr", [CH, FFI // 16], scope=sc)
    qs16 = kb.sb("qs16", [CH, FFI // 16], scope=sc)
    qw = kb.sb("qw", [CH, FFI], F16, scope=sc)
    qp1 = kb.sb("qp1", [CH, FFI], F16, scope=sc)
    qta = kb.sb("qta", [CH, FFI], F16, scope=sc)
    qtb = kb.sb("qtb", [CH, FFI], F16, scope=sc)
    qp23 = kb.sb("qp23", [CH, FFI], F16, scope=sc)

    tiles = []
    for src_t, cc_t, rows, width in (
        ("wg_sh", "cc_g_in", FFI // NCORES, D),
        ("wu_sh", "cc_u_in", FFI // NCORES, D),
        ("wd_sh", "cc_dn_in", D // NCORES, FFI),
    ):
        r0 = 0
        while r0 < rows:
            rr = min(CH, rows - r0)
            tiles.append((src_t, cc_t, r0, rr, width))
            r0 += rr
    srcfree = {}
    pend = None
    for ti, (src_t, cc_t, r0, rr, width) in enumerate(tiles):
        par = ti % 2
        if ti >= 2:
            q.wait("gpsimd", "vd", srcfree[ti - 2])
        q.dma("gpsimd", wsh[0:rr, par * FFI:par * FFI + width],
              dram[src_t][r0:r0 + rr, :], "ld_wq")
        q.wait("vector", "ld_wq", q.n("ld_wq"))
        qr_t = equant_pre(kb, src=wsh, sl=slice(0, rr), width=width, mx=qmx,
                          mx0=par * (FFI // 16), src0=par * FFI)
        srcfree[ti] = q.n("vd")
        if pend is not None:
            (psrc, pcc, pr0, prr, pwidth, ppar, pqr, pti) = pend
            q.wait("vector", "st_wq")
            equant_post(kb, src=wsh, sl=slice(0, prr), width=pwidth, mx=qmx, mxr=qmxr,
                        s16=qs16, w_=qw, p1=qp1, t_a=qta, t_b=qtb, p23=qp23, out=wqo,
                        qr_tgt=pqr, mx0=ppar * (FFI // 16), src0=ppar * FFI)
            srcfree[pti] = kb.srcfree_vd
            q.wait("sync", "vd")
            q.dma("sync", dram[pcc][pr0:pr0 + prr, :], wqo[0:prr, 0:pwidth], "st_wq")
        pend = (src_t, cc_t, r0, rr, width, par, qr_t, ti)
    (psrc, pcc, pr0, prr, pwidth, ppar, pqr, pti) = pend
    q.wait("vector", "st_wq")
    equant_post(kb, src=wsh, sl=slice(0, prr), width=pwidth, mx=qmx, mxr=qmxr,
                s16=qs16, w_=qw, p1=qp1, t_a=qta, t_b=qtb, p23=qp23, out=wqo,
                qr_tgt=pqr, mx0=ppar * (FFI // 16), src0=ppar * FFI)
    q.wait("sync", "vd")
    q.dma("sync", dram[pcc][pr0:pr0 + prr, :], wqo[0:prr, 0:pwidth], "st_wq")
    q.wait("gpsimd", "st_wq")
    for kind, i, o in (("g", "cc_g_in", "cc_g_out"), ("u", "cc_u_in", "cc_u_out"),
                       ("d", "cc_dn_in", "cc_dn_out")):
        q.emit("gpsimd", lambda e, i=i, o=o: e.collective_compute(
            "AllGather", ALU.bypass, replica_groups=[list(range(NCORES))],
            ins=[dram[i]], outs=[dram[o]]), inc="ccd")


def phase_gather_sl(kb):
    nc, q, dram, sbuf = kb.nc, kb.q, kb.dram, kb.sbuf
    q.wait("gpsimd", "st_s")
    for kind, i, o in (("s", "cc_s_in", "cc_s_out"), ("l", "cc_l_in", "cc_l_out")):
        q.emit("gpsimd", lambda e, i=i, o=o: e.collective_compute(
            "AllGather", ALU.bypass, replica_groups=[list(range(NCORES))],
            ins=[dram[i]], outs=[dram[o]]), inc="ccd")
    q.wait("sync", "ccd")
    kb.tap("t_wg", dram["cc_g_out"])
    kb.tap("t_wdn", dram["cc_dn_out"])
    kb.tap("t_sg", dram["cc_s_out"])
    kb.tap("t_lg", dram["cc_l_out"])


def phase_ffn(kb):
    nc, q, dram, sbuf = kb.nc, kb.q, kb.dram, kb.sbuf
    q.dma("gpsimd", dram["d_wz"], dram["w_zT"], "ldw0")
    q.dma("gpsimd", dram["d_wout"], dram["w_outT"], "ldw0")
    OGS = [(0, 1024), (1024, 1024), (2048, 768)]
    with contextlib.ExitStack() as sc:
        xqf = kb.sb("xqf", [CH, 8 * TOK], BF16, scope=sc)
        wgt = kb.sb("wgt", [CH, 8 * 1024], BF16, scope=sc)
        wut = kb.sb("wut", [CH, 8 * 1024], BF16, scope=sc)
        sg = kb.sb("sg", [CH, 2 * 1024], F16, scope=sc)
        hbuf = kb.sb("hbuf", [CH, 2 * 1024], F16, scope=sc)
        hq = kb.sb("hq", [CH, 2 * 1024], BF16, scope=sc)
        fmx = kb.sb("fmx", [CH, 128], scope=sc)
        fmxr = kb.sb("fmxr", [CH, 64], scope=sc)
        fs16 = kb.sb("fs16", [CH, 64], scope=sc)
        fw = kb.sb("fw", [CH, 1024], F16, scope=sc)
        fp1 = kb.sb("fp1", [CH, 1024], F16, scope=sc)
        fta = kb.sb("fta", [CH, 1024], F16, scope=sc)
        ftb = kb.sb("ftb", [CH, 1024], F16, scope=sc)
        fp23 = kb.sb("fp23", [CH, 1024], F16, scope=sc)
        psg = kb.ps("psg", [CH, 2048], scope=sc)
        psu = kb.ps("psu", [CH, 2048], scope=sc)

        q.wait("scalar", "st_xq")
        q.wait("scalar", "ccd")
        q.wait("scalar", "st_wq")
        if "t_xq3" in kb.taps:
            q.wait("sync", "st_xq")
            q.dma("sync", hq[:, 0:D], dram["d_xq"][0:CH, :], "ld_dbg")
            q.wait("sync", "ld_dbg")
            kb.tap("t_xq3", hq[:, 0:D])
            q.wait("vector", "st_tap")
        for k in range(8):
            q.dma("scalar", xqf[:, k * TOK:(k + 1) * TOK],
                  dram["d_xq"][:, k * CH:(k + 1) * CH], "ld_ff", transpose=True)
        q.wait("sync", "ld_ff")
        kb.tap("t_xqf", xqf[:])
        ac_t = {}
        vd_t = {}
        pe_t = {}
        st_t = {}
        gtc = 0
        pend = None
        for og0, ogw in OGS:
            q.wait("scalar", "pe")
            for k in range(8):
                q.dma("scalar", wgt[:, k * 1024:k * 1024 + ogw],
                      dram["cc_g_out"][og0:og0 + ogw, k * CH:(k + 1) * CH], "ld_ff",
                      transpose=True)
                q.dma("scalar", wut[:, k * 1024:k * 1024 + ogw],
                      dram["cc_u_out"][og0:og0 + ogw, k * CH:(k + 1) * CH], "ld_ff",
                      transpose=True)
            q.wait("tensor", "ld_ff")
            for tc in range(16):
                p = gtc % 2
                if gtc >= 2:
                    q.wait("tensor", "ac", ac_t[gtc - 2])
                    q.wait("tensor", "vd", vd_t[gtc - 2])
                pieces = [(0, 512), (512, ogw - 512)]
                for k in range(8):
                    for f0, fw_ in pieces:
                        q.emit("tensor", lambda e, tc=tc, k=k, f0=f0, fw_=fw_, p=p:
                               e.matmul(psg[:, p * 1024 + f0:p * 1024 + f0 + fw_],
                                        xqf[:, k * TOK + tc * CH:k * TOK + (tc + 1) * CH],
                                        wgt[:, k * 1024 + f0:k * 1024 + f0 + fw_],
                                        start=(k == 0), stop=(k == 7)))
                for k in range(8):
                    for f0, fw_ in pieces:
                        q.emit("tensor", lambda e, tc=tc, k=k, f0=f0, fw_=fw_, p=p:
                               e.matmul(psu[:, p * 1024 + f0:p * 1024 + f0 + fw_],
                                        xqf[:, k * TOK + tc * CH:k * TOK + (tc + 1) * CH],
                                        wut[:, k * 1024 + f0:k * 1024 + f0 + fw_],
                                        start=(k == 0), stop=(k == 7)),
                               inc=("pe" if (k == 7 and f0 != 0) else None))
                pe_t[gtc] = q.n("pe")
                q.wait("scalar", "pe", pe_t[gtc])
                q.emit("scalar", lambda e, ogw=ogw, p=p: e.activation(
                    sg[:, p * 1024:p * 1024 + ogw], psg[:, p * 1024:p * 1024 + ogw],
                    AF.Silu), inc="ac")
                ac_t[gtc] = q.n("ac")
                q.wait("vector", "ac", ac_t[gtc])
                q.emit("vector", lambda e, ogw=ogw, p=p: e.tensor_tensor(
                    hbuf[:, p * 1024:p * 1024 + ogw], sg[:, p * 1024:p * 1024 + ogw],
                    psu[:, p * 1024:p * 1024 + ogw], ALU.mult), inc="vd")
                vd_t[gtc] = q.n("vd")
                qr_t = equant_pre(kb, src=hbuf, sl=slice(0, CH), width=ogw, mx=fmx,
                                  mx0=p * 64, src0=p * 1024)
                if pend is not None:
                    pgtc, ptc, pog0, pogw, pp, pqr = pend
                    q.wait("vector", "st_hq", st_t.get(pgtc - 2, 0))
                    equant_post(kb, src=hbuf, sl=slice(0, CH), width=pogw, mx=fmx,
                                mxr=fmxr, s16=fs16, w_=fw, p1=fp1, t_a=fta, t_b=ftb,
                                p23=fp23, out=hq, qr_tgt=pqr,
                                osl=slice(pp * 1024, pp * 1024 + pogw),
                                mx0=pp * 64, src0=pp * 1024)
                    q.wait("sync", "vd")
                    q.dma("sync", dram["d_hq"][ptc * CH:(ptc + 1) * CH, pog0:pog0 + pogw],
                          hq[0:CH, pp * 1024:pp * 1024 + pogw], "st_hq")
                    st_t[pgtc] = q.n("st_hq")
                pend = (gtc, tc, og0, ogw, p, qr_t)
                gtc += 1
        pgtc, ptc, pog0, pogw, pp, pqr = pend
        q.wait("vector", "st_hq", st_t.get(pgtc - 2, 0))
        equant_post(kb, src=hbuf, sl=slice(0, CH), width=pogw, mx=fmx,
                    mxr=fmxr, s16=fs16, w_=fw, p1=fp1, t_a=fta, t_b=ftb,
                    p23=fp23, out=hq, qr_tgt=pqr,
                    osl=slice(pp * 1024, pp * 1024 + pogw),
                    mx0=pp * 64, src0=pp * 1024)
        q.wait("sync", "vd")
        q.dma("sync", dram["d_hq"][ptc * CH:(ptc + 1) * CH, pog0:pog0 + pogw],
              hq[0:CH, pp * 1024:pp * 1024 + pogw], "st_hq")
        st_t[pgtc] = q.n("st_hq")
    q.wait("sync", "st_hq")
    kb.tap("t_hq", dram["d_hq"])

    q.tag = "ffn_dn"
    with contextlib.ExitStack() as sc:
        dnt = kb.sb("dnt", [CH, 22 * 1024], BF16, scope=sc)
        hqt = kb.sb("hqt", [CH, 2 * 512], BF16, scope=sc)
        psD = kb.ps("psD", [CH, 2048], scope=sc)
        psD2 = kb.ps("psD2", [CH, 2048], scope=sc)
        q.wait("scalar", "ccd")
        q.wait("scalar", "st_hq")
        q.wait("scalar", "vd")
        for o in range(22):
            q.dma("scalar", dnt[:, o * 1024:(o + 1) * 1024],
                  dram["cc_dn_out"][0:D, o * CH:(o + 1) * CH], "ld_dn", transpose=True)
        q.wait("tensor", "ld_dn")
        q.wait("tensor", "vd")
        dpe = {}
        dld = {}
        for t in range(4):
            q.wait("tensor", "ac")
            for o in range(22):
                buf = o % 2
                go = t * 22 + o
                if go >= 2:
                    q.wait("scalar", "pe", dpe[go - 2])
                q.dma("scalar", hqt[:, buf * 512:(buf + 1) * 512],
                      dram["d_hq"][t * 512:(t + 1) * 512, o * CH:(o + 1) * CH], "ld_hf",
                      transpose=True)
                dld[go] = q.n("ld_hf")
                q.wait("tensor", "ld_hf", dld[go])
                for ee in range(8):
                    ps = psD if ee < 4 else psD2
                    q.emit("tensor", lambda e, o=o, ee=ee, buf=buf, ps=ps:
                           e.matmul(ps[:, (ee % 4) * 512:(ee % 4 + 1) * 512],
                                    dnt[:, o * 1024 + ee * CH:o * 1024 + (ee + 1) * CH],
                                    hqt[:, buf * 512:(buf + 1) * 512],
                                    start=(o == 0), stop=(o == 21)),
                           inc=("pe" if ee == 7 else None))
                dpe[go] = q.n("pe")
            q.wait("scalar", "pe")
            for ee in range(8):
                ps = psD if ee < 4 else psD2
                q.emit("scalar", lambda e, t=t, ee=ee, ps=ps: e.activation(
                    sbuf["g_fm"][:, ee * TOK + t * 512:ee * TOK + (t + 1) * 512],
                    ps[:, (ee % 4) * 512:(ee % 4 + 1) * 512], AF.Silu), inc="ac")
    q.wait("sync", "st_hq")
    kb.tap("t_gfm", kb.sbuf["g_fm"][:])


def phase_tail(kb):
    nc, q, dram, sbuf = kb.nc, kb.q, kb.dram, kb.sbuf
    with contextlib.ExitStack() as sc:
        Lmat = kb.sb("Lmat", [NH, NCORES], scope=sc)
        etmp = kb.sb("etmp", [NH, NCORES * NCORES], scope=sc)
        esum = kb.sb("esum", [NH, NCORES], scope=sc)
        coef = kb.sb("coef", [NH, NCORES], scope=sc)
        crow = kb.sb("crow", [1, NCORES * NH], F16, scope=sc)
        cbc = kb.sb("cbc", [CH, NCORES * NH], scope=sc)
        Sj = kb.sb("Sj", [DS, 2 * NH * HD], scope=sc)
        Rtmp2 = kb.sb("Rtmp2", [DS, NH * HD], scope=sc)
        Rin = kb.sb("Rin", [DS, NH * HD], scope=sc)
        Rinb = kb.sb("Rinb", [DS, NH * HD], BF16, scope=sc)
        lcz = kb.sb("lcz2", [NH, NCH], scope=sc)
        lrow = kb.sb("lrow", [1, NCH * NH], F16, scope=sc)
        ead = kb.sb("ead", [CH, 2 * NH], scope=sc)
        ycorr = kb.sb("ycorr", [CH, 512], scope=sc)
        yload = kb.sb("yload", [CH, 2 * DI], BF16, scope=sc)
        psC = kb.ps("psC", [CH, 2048], scope=sc)
        psS2 = kb.ps("psS2", [CH, 512], scope=sc)

        q.wait("sync", "ccd")
        q.wait("sync", "pe")
        q.wait("sync", "ac")
        q.dma("sync", Lmat[:], dram["cc_l_out"].rearrange("(r h) b -> h (r b)", h=NH),
              "ld_t", allow_slow_non_contiguous=True)
        q.wait("vector", "ld_t")
        q.emit("vector", lambda e: e.tensor_tensor(
            etmp[:].rearrange("p (j i) -> p j i", j=NCORES),
            Lmat[:].unsqueeze(1).broadcast_to([NH, NCORES, NCORES]),
            sbuf["bmx_s"][:].rearrange("p (j i) -> p j i", j=NCORES), ALU.mult), inc="vd")
        q.wait("vector", "vd")
        q.emit("vector", lambda e: e.tensor_reduce(
            esum[:].unsqueeze(2), etmp[:].rearrange("p (j i) -> p j i", j=NCORES),
            AX.X, ALU.add))
        q.emit("vector", lambda e: e.drain(), inc="vd")
        q.wait("scalar", "vd")
        q.emit("scalar", lambda e: e.activation(esum[:], esum[:], AF.Exp))
        q.emit("scalar", lambda e: e.drain(), inc="ac")
        q.wait("vector", "ac")
        q.emit("vector", lambda e: e.tensor_tensor(coef[:], esum[:], sbuf["pmx_s"][:], ALU.mult), inc="vd")
        q.wait("sync", "vd")
        q.dma("sync", dram["d_coef"].rearrange("j h -> h j"), coef[:], "st_t",
              allow_slow_non_contiguous=True)
        q.wait("sync", "st_t")
        kb.tap("t_coef", dram["d_coef"])
        q.wait("gpsimd", "st_t")
        q.emit("gpsimd", lambda e: e.memset(Rin[:], 0.0), inc="gp")
        q.dma("gpsimd", crow[:], dram["d_coef"].rearrange("j h -> (j h)").unsqueeze(0),
              "ld_tg")
        q.wait("tensor", "ld_tg", q.n("ld_tg"))
        q.wait("tensor", "vd")
        for j in range(NCORES):
            q.emit("tensor", lambda e, j=j: e.matmul(
                psS2[:, j * NH:(j + 1) * NH], sbuf["ones1_bf"][:],
                crow[0:1, j * NH:(j + 1) * NH], start=True, stop=True),
                inc=("pe" if j == NCORES - 1 else None))
        cbc_pe = q.n("pe")
        sld = {}
        q.wait("sync", "vd")
        q.dma("sync", Sj[:, 0:NH * HD], dram["cc_s_out"][0:DS, :], "ld_t")
        sld[0] = q.n("ld_t")
        q.wait("vector", "pe", cbc_pe)
        q.wait("vector", "gp")
        q.emit("vector", lambda e: e.tensor_copy(cbc[:], psS2[:, 0:NCORES * NH]), inc="vd")
        rt_t = {}
        for j in range(NCORES):
            jp = j % 2
            if j + 1 < NCORES:
                if j >= 1:
                    q.wait("sync", "vd", rt_t[j - 1])
                q.dma("sync", Sj[:, (1 - jp) * NH * HD:(2 - jp) * NH * HD],
                      dram["cc_s_out"][(j + 1) * DS:(j + 2) * DS, :], "ld_t")
                sld[j + 1] = q.n("ld_t")
            q.wait("vector", "ld_t", sld[j])
            q.emit("vector", lambda e, j=j, jp=jp: e.tensor_tensor(
                Rtmp2[:].rearrange("p (h d) -> p h d", d=HD),
                Sj[:, jp * NH * HD:(jp + 1) * NH * HD].rearrange("p (h d) -> p h d", d=HD),
                cbc[:, j * NH:(j + 1) * NH].unsqueeze(2).broadcast_to([DS, NH, HD]),
                ALU.mult), inc="vd")
            rt_t[j] = q.n("vd")
            q.wait("vector", "vd")
            q.emit("vector", lambda e: e.tensor_tensor(Rin[:], Rin[:], Rtmp2[:], ALU.add), inc="vd")
        q.wait("vector", "vd")
        q.emit("vector", lambda e: e.tensor_copy(Rinb[:], Rin[:]), inc="vd")

        q.emit("vector", lambda e: e.memset(lcz[:, 0:1], 0.0), inc="vd")
        for c in range(1, NCH):
            q.emit("vector", lambda e, c=c: e.tensor_tensor(
                lcz[:, c:c + 1], lcz[:, c - 1:c], sbuf["az_fm"][:, c - 1:c], ALU.add))
            q.emit("vector", lambda e: e.drain(), inc="vd")
        q.wait("sync", "vd")
        q.dma("sync", dram["d_lcz"].rearrange("c h -> h c"), lcz[:], "st_t",
              allow_slow_non_contiguous=True)
        q.wait("sync", "st_t")
        kb.tap("t_lcz", dram["d_lcz"])
        q.wait("sync", "vd")
        kb.tap("t_rin", Rin[:])

        q.tag = "tail_corr"
        q.wait("gpsimd", "st_t")
        q.wait("gpsimd", "st_y")
        q.dma("gpsimd", lrow[:], dram["d_lcz"].rearrange("c h -> (c h)").unsqueeze(0),
              "ld_tg")
        lrow_t = q.n("ld_tg")
        q.dma("gpsimd", yload[:, 0:DI], dram["d_y"][0:CH, :], "ld_y")
        yl_t = {0: q.n("ld_y")}
        q.wait("tensor", "ld_tg", lrow_t)
        q.wait("tensor", "vd")
        exp_t = {}
        mm3_t = {}
        pc_t = {}
        yc_t = {}
        yc_half = {}
        styt = {}
        for gc in range(NCH):
            c2 = gc % 2
            tl0 = gc * CH
            ahc = sbuf["a_hi"][:, gc * NH:(gc + 1) * NH]
            alc = sbuf["a_lo"][:, gc * NH:(gc + 1) * NH]
            s0 = 256 + c2 * 64
            if gc >= 2:
                q.wait("tensor", "ac", exp_t[gc - 2])
            q.emit("tensor", lambda e, ahc=ahc, s0=s0: e.matmul(psS2[:, s0:s0 + NH], sbuf["ut_incl"][:], ahc, start=True, stop=False))
            q.emit("tensor", lambda e, alc=alc, s0=s0: e.matmul(psS2[:, s0:s0 + NH], sbuf["ut_incl"][:], alc, start=False, stop=False))
            q.emit("tensor", lambda e, gc=gc, s0=s0: e.matmul(psS2[:, s0:s0 + NH], sbuf["ones1_bf"][:], lrow[0:1, gc * NH:(gc + 1) * NH], start=False, stop=True), inc="pe")
            mm3_t[gc] = q.n("pe")
            q.wait("scalar", "pe", mm3_t[gc])
            if gc >= 2:
                q.wait("scalar", "vd", yc_t[gc - 2])
            q.emit("scalar", lambda e, c2=c2, s0=s0: e.activation(
                ead[:, c2 * NH:(c2 + 1) * NH], psS2[:, s0:s0 + NH], AF.Exp), inc="ac")
            exp_t[gc] = q.n("ac")
            if gc + 1 < NCH:
                if gc >= 1:
                    q.wait("gpsimd", "st_y", styt[gc - 1])
                q.dma("gpsimd", yload[:, (1 - c2) * DI:(2 - c2) * DI],
                      dram["d_y"][(gc + 1) * CH:(gc + 2) * CH, :], "ld_y")
                yl_t[gc + 1] = q.n("ld_y")
            for half in range(2):
                if gc >= 1:
                    q.wait("tensor", "vd", yc_half[(gc - 1, half)])
                for qq in (half * 2, half * 2 + 1):
                    q.emit("tensor", lambda e, qq=qq, tl0=tl0: e.matmul(
                        psC[:, qq * 512:(qq + 1) * 512], sbuf["C_fm"][:, tl0:tl0 + CH],
                        Rinb[:, qq * 512:(qq + 1) * 512], start=True, stop=True),
                        inc=("pe" if qq % 2 == 1 else None))
                pc_t[(gc, half)] = q.n("pe")
            q.wait("vector", "ac", exp_t[gc])
            q.wait("vector", "ld_y", yl_t[gc])
            for half in range(2):
                q.wait("vector", "pe", pc_t[(gc, half)])
                for qq in (half * 2, half * 2 + 1):
                    q.emit("vector", lambda e, qq=qq, c2=c2: e.tensor_tensor(
                        ycorr[:].rearrange("p (h d) -> p h d", d=HD),
                        psC[:, qq * 512:(qq + 1) * 512].rearrange("p (h d) -> p h d", d=HD),
                        ead[:, c2 * NH + qq * 8:c2 * NH + (qq + 1) * 8].unsqueeze(2).broadcast_to([CH, 8, HD]),
                        ALU.mult), inc="vd")
                    q.wait("vector", "vd")
                    q.emit("vector", lambda e, qq=qq, c2=c2: e.tensor_tensor(
                        yload[:, c2 * DI + qq * 512:c2 * DI + (qq + 1) * 512],
                        yload[:, c2 * DI + qq * 512:c2 * DI + (qq + 1) * 512],
                        ycorr[:], ALU.add), inc="vd")
                yc_half[(gc, half)] = q.n("vd")
            yc_t[gc] = q.n("vd")
            q.wait("sync", "vd", yc_t[gc])
            q.dma("sync", dram["d_y"][tl0:tl0 + CH, :], yload[:, c2 * DI:(c2 + 1) * DI], "st_y")
            styt[gc] = q.n("st_y")
        q.wait("sync", "st_y")
        kb.tap("t_y2", dram["d_y"])

    with contextlib.ExitStack() as sct:
        t1 = kb.sb("t1", [CH, 16 * TOK], BF16, scope=sct)
        ffff = phase_tail2(kb, sct, t1)


def phase_tail2(kb, sct, t1):
    nc, q, dram, sbuf = kb.nc, kb.q, kb.dram, kb.sbuf
    q.tag = "ztail"
    with contextlib.ExitStack() as sc:
        xn2 = kb.sb("xn2", [CH, 8 * TOKH], BF16, scope=sc)
        wz = kb.sb("wz", [CH, 2 * 8 * CH], BF16, scope=sc)
        szc = kb.sb("szc", [CH, TOK], F16, scope=sc)
        yfm = kb.sb("yfm", [CH, 2 * TOK], BF16, scope=sc)
        sq2 = kb.sb("sq2", [CH, 3 * 1024], BF16, scope=sc)
        ones32b = kb.sb("ones32b", [CH, NH], BF16, scope=sc)
        rrow = kb.sb("rrow", [NH, TOK], scope=sc)
        psZ = kb.ps("psZ", [CH, 2048], scope=sc)
        psR = kb.ps("psR", [CH, 2048], scope=sc)

        q.wait("gpsimd", "st_y")
        q.emit("gpsimd", lambda e: e.memset(ones32b[:], 1.0), inc="gp")
        q.wait("sync", "st_xn")
        q.wait("sync", "st_y")
        q.dma("sync", xn2[:], dram["d_xn"], "ld_t")
        xn2_t = q.n("ld_t")
        q.wait("sync", "ldw0", 32)
        q.wait("scalar", "st_y")
        zmm_t = {}
        silu_t = {}
        gate_t = {}
        sq2_t = {}
        rms_t = {}
        wz_t = {}
        yfm_t = {}
        for fc in range(16):
            pf = fc % 2
            if fc == 0:
                q.dma("sync", wz[:, 0:8 * CH].rearrange("p (k w) -> p k w", k=8),
                      dram["d_wz"][:, 0:CH].rearrange("(k p) w -> p k w", p=CH), "ld_wz")
                wz_t[0] = q.n("ld_wz")
                q.dma("scalar", yfm[:, 0:TOK], dram["d_y"][:, 0:CH], "ld_yf",
                      transpose=True)
                yfm_t[0] = q.n("ld_yf")
            if fc + 1 < 16:
                if fc >= 1:
                    q.wait("sync", "pe", zmm_t[2 * fc - 1])
                q.dma("sync", wz[:, (1 - pf) * 8 * CH:(2 - pf) * 8 * CH].rearrange("p (k w) -> p k w", k=8),
                      dram["d_wz"][:, (fc + 1) * CH:(fc + 2) * CH].rearrange("(k p) w -> p k w", p=CH),
                      "ld_wz")
                wz_t[fc + 1] = q.n("ld_wz")
                if fc >= 1:
                    q.wait("scalar", "vd", gate_t[2 * fc - 1])
                q.dma("scalar", yfm[:, (1 - pf) * TOK:(2 - pf) * TOK],
                      dram["d_y"][:, (fc + 1) * CH:(fc + 2) * CH], "ld_yf",
                      transpose=True)
                yfm_t[fc + 1] = q.n("ld_yf")
            for th in range(2):
                i = fc * 2 + th
                ip = i % 2
                if i == 0:
                    q.wait("tensor", "ld_t", xn2_t)
                    q.wait("tensor", "ac")
                    q.wait("tensor", "vd")
                if th == 0:
                    q.wait("tensor", "ld_wz", wz_t[fc])
                if i >= 2:
                    q.wait("tensor", "ac", silu_t[i - 2])
                for tt_ in range(2):
                    for k in range(8):
                        q.emit("tensor", lambda e, th=th, tt_=tt_, k=k, pf=pf, ip=ip:
                               e.matmul(psZ[:, ip * 1024 + tt_ * 512:ip * 1024 + (tt_ + 1) * 512],
                                        wz[:, pf * 1024 + k * CH:pf * 1024 + (k + 1) * CH],
                                        xn2[:, k * TOKH + HALO + th * 1024 + tt_ * 512: k * TOKH + HALO + th * 1024 + (tt_ + 1) * 512],
                                        start=(k == 0), stop=(k == 7)),
                               inc=("pe" if (k == 7 and tt_ == 1) else None))
                zmm_t[i] = q.n("pe")
                q.wait("scalar", "pe", zmm_t[i])
                if i >= 2:
                    q.wait("scalar", "vd", gate_t[i - 2])
                q.emit("scalar", lambda e, ip=ip: e.activation(
                    szc[:, ip * 1024:(ip + 1) * 1024], psZ[:, ip * 1024:(ip + 1) * 1024],
                    AF.Silu), inc="ac")
                silu_t[i] = q.n("ac")
                q.wait("vector", "ld_yf", yfm_t[fc])
                q.wait("vector", "ac", silu_t[i])
                q.emit("vector", lambda e, fc=fc, th=th, pf=pf, ip=ip: e.tensor_tensor(
                    t1[:, fc * TOK + th * 1024:fc * TOK + (th + 1) * 1024],
                    yfm[:, pf * TOK + th * 1024:pf * TOK + (th + 1) * 1024],
                    szc[:, ip * 1024:(ip + 1) * 1024], ALU.mult), inc="vd")
                gate_t[i] = q.n("vd")
                q.wait("scalar", "vd", gate_t[i])
                i3 = i % 3
                if i >= 3:
                    q.wait("scalar", "pe", rms_t[i - 3])
                q.emit("scalar", lambda e, fc=fc, th=th, i3=i3: e.activation(
                    sq2[:, i3 * 1024:(i3 + 1) * 1024],
                    t1[:, fc * TOK + th * 1024:fc * TOK + (th + 1) * 1024], AF.Square), inc="ac")
                sq2_t[i] = q.n("ac")
                for j in ([i - 2] if i >= 2 else []) + ([i - 1, i] if i == 31 else []):
                    jfc, jth, j3 = j // 2, j % 2, j % 3
                    q.wait("tensor", "ac", sq2_t[j])
                    for tt_ in range(2):
                        q.emit("tensor", lambda e, jfc=jfc, jth=jth, tt_=tt_, j3=j3: e.matmul(
                            psR[0:NH, jth * 1024 + tt_ * 512:jth * 1024 + (tt_ + 1) * 512],
                            ones32b[:], sq2[:, j3 * 1024 + tt_ * 512:j3 * 1024 + (tt_ + 1) * 512],
                            start=(jfc == 0), stop=(jfc == 15)),
                            inc=("pe" if tt_ == 1 else None))
                    rms_t[j] = q.n("pe")
        q.wait("vector", "pe")
        q.emit("vector", lambda e: e.tensor_copy(rrow[:], psR[0:NH, 0:TOK]), inc="vd")
        q.wait("vector", "vd")
        q.emit("vector", lambda e: e.tensor_scalar(rrow[:], rrow[:], 1.0 / DI, None, ALU.mult))
        q.emit("vector", lambda e: e.drain(), inc="vd")
        q.wait("scalar", "vd")
        q.emit("scalar", lambda e: e.activation(rrow[:], rrow[:], AF.Ln, bias=sbuf["epscol"][0:NH, 0:1]))
        q.emit("scalar", lambda e: e.drain())
        q.emit("scalar", lambda e: e.activation(rrow[:], rrow[:], AF.Exp, scale=-0.5))
        q.emit("scalar", lambda e: e.drain(), inc="ac")
        q.wait("sync", "ac")
        q.dma("sync", dram["d_rstd"], rrow[0:1, :], "st_t")
        q.wait("sync", "st_t")
        kb.tap("t_rstd", dram["d_rstd"])
        kb.tap("t_t1", t1[:])

    q.tag = "oproj"
    with contextlib.ExitStack() as sc2:
        wo = kb.sb("wo", [CH, 16 * D], BF16, scope=sc2)
        rfm = kb.sb("rfm", [CH, 8 * 256], BF16, scope=sc2)
        utile2 = kb.sb("utile2", [CH, 2 * D], scope=sc2)
        rcol2 = kb.sb("rcol2", [CH, 2], scope=sc2)
        otile = kb.sb("otile", [CH, 2 * D], scope=sc2)
        psO = kb.ps("psO", [CH, 2048], scope=sc2)
        psO2 = kb.ps("psO2", [CH, 2048], scope=sc2)
        q.wait("scalar", "st_t")
        q.wait("scalar", "ldw0", 48)
        q.dma("scalar", wo[:].rearrange("p (i w) -> p i w", i=16),
              dram["d_wout"].rearrange("(i p) w -> p i w", p=CH), "ld_wo")
        q.wait("vector", "ld_wo")
        for i in range(16):
            q.emit("vector", lambda e, i=i: e.tensor_scalar(
                wo[:, i * D:(i + 1) * D], wo[:, i * D:(i + 1) * D],
                sbuf["nwcol"][:, i:i + 1], None, ALU.mult), inc="vd")
        q.wait("vector", "ac")
        q.wait("gpsimd", "vd")
        q.wait("gpsimd", "st_t")
        u2t = {}
        for tc in (0, 1):
            q.dma("gpsimd", utile2[:, (tc % 2) * D:(tc % 2 + 1) * D],
                  dram["u"][HALO + tc * CH:HALO + (tc + 1) * CH, :], "ld_u2")
            q.dma("gpsimd", rcol2[:, tc % 2:tc % 2 + 1],
                  dram["d_rstd"][0:1, tc * CH:(tc + 1) * CH].rearrange("a p -> (a p)").unsqueeze(1), "ld_u2")
            u2t[tc] = q.n("ld_u2")
        rfm_t = {}
        trans_t = {}
        fin_t = {}
        sto_t = {}
        mmO_t = {}
        q.wait("tensor", "vd")
        q.wait("tensor", "ac")
        for s in range(8):
            if s >= 1:
                q.wait("tensor", "vd", rfm_t[s - 1])
            for ee in range(8):
                for i in range(16):
                    q.emit("tensor", lambda e, s=s, i=i, ee=ee: e.matmul(
                        psO[:, ee * 256:(ee + 1) * 256],
                        wo[:, i * D + ee * CH:i * D + (ee + 1) * CH],
                        t1[:, i * TOK + s * 256:i * TOK + (s + 1) * 256],
                        start=(i == 0), stop=(i == 15)),
                        inc=("pe" if (i == 15 and ee == 7) else None))
            mmO_t[s] = q.n("pe")
            q.wait("vector", "pe", mmO_t[s])
            for ee in range(8):
                q.emit("vector", lambda e, s=s, ee=ee: e.tensor_tensor(
                    rfm[:, ee * 256:(ee + 1) * 256],
                    psO[:, ee * 256:(ee + 1) * 256],
                    sbuf["g_fm"][:, ee * TOK + s * 256:ee * TOK + (s + 1) * 256],
                    ALU.mult), inc="vd")
            rfm_t[s] = q.n("vd")
            q.wait("tensor", "vd", rfm_t[s])
            if s >= 1:
                q.wait("tensor", "vd", fin_t[2 * s - 1])
            for tc2 in range(2):
                for ee in range(8):
                    q.emit("tensor", lambda e, tc2=tc2, ee=ee: e.matmul(
                        psO2[:, tc2 * 1024 + ee * CH:tc2 * 1024 + (ee + 1) * CH],
                        rfm[:, ee * 256 + tc2 * CH:ee * 256 + (tc2 + 1) * CH],
                        sbuf["ident"][:], start=True, stop=True),
                        inc=("pe" if ee == 7 else None))
                trans_t[(s, tc2)] = q.n("pe")
            for tc2 in range(2):
                tc = 2 * s + tc2
                q.wait("vector", "pe", trans_t[(s, tc2)])
                q.wait("vector", "ld_u2", u2t[tc])
                q.wait("vector", "st_out", sto_t.get(tc - 2, 0))
                q.emit("vector", lambda e, tc=tc, tc2=tc2: e.scalar_tensor_tensor(
                    otile[:, (tc % 2) * D:(tc % 2 + 1) * D], psO2[:, tc2 * 1024:tc2 * 1024 + D],
                    rcol2[:, tc % 2:tc % 2 + 1],
                    utile2[:, (tc % 2) * D:(tc % 2 + 1) * D], ALU.mult, ALU.add), inc="vd")
                fin_t[tc] = q.n("vd")
                q.wait("sync", "vd", fin_t[tc])
                q.dma("sync", dram["out"][tc * CH:(tc + 1) * CH, :],
                      otile[:, (tc % 2) * D:(tc % 2 + 1) * D], "st_out")
                sto_t[tc] = q.n("st_out")
                if tc + 2 < 16:
                    q.wait("gpsimd", "vd", fin_t[tc])
                    q.dma("gpsimd", utile2[:, (tc % 2) * D:(tc % 2 + 1) * D],
                          dram["u"][HALO + (tc + 2) * CH:HALO + (tc + 3) * CH, :], "ld_u2")
                    q.dma("gpsimd", rcol2[:, tc % 2:tc % 2 + 1],
                          dram["d_rstd"][0:1, (tc + 2) * CH:(tc + 3) * CH].rearrange("a p -> (a p)").unsqueeze(1), "ld_u2")
                    u2t[tc + 2] = q.n("ld_u2")


def _prep_inputs(inputs):
    u = np.asarray(inputs["L0_attn_u"])
    ipw = np.asarray(inputs["L0_attn_in_proj_w"])
    conv_w = np.asarray(inputs["L0_attn_conv_w"])
    conv_b = np.asarray(inputs["L0_attn_conv_b"])
    dt_bias = np.asarray(inputs["L0_attn_dt_bias"])
    a_log = np.asarray(inputs["L0_attn_A_log"])
    Dv = np.asarray(inputs["L0_attn_D"])
    norm_w = np.asarray(inputs["L0_attn_norm_w"])
    opw = np.asarray(inputs["L0_attn_out_proj_w"])
    wg = np.asarray(inputs["L0_ffn_w_gate"])
    wu = np.asarray(inputs["L0_ffn_w_up"])
    wdn = np.asarray(inputs["L0_ffn_w_down"])

    w_zT = np.ascontiguousarray(ipw[:DI].T.astype(np.float32))
    w_xbcT = np.ascontiguousarray(ipw[DI:].T.astype(np.float32))
    w_outT = np.ascontiguousarray(opw.T.astype(np.float32))
    conv_w4 = np.ascontiguousarray(conv_w[:, 0, :].astype(np.float32))
    dcol64 = np.ascontiguousarray(np.repeat(np.asarray(Dv, np.float32), HD).reshape(-1, 1))

    base = dict(
        w_xbcT=w_xbcT, w_zT=w_zT, w_outT=w_outT,
        conv_w4=conv_w4,
        conv_b=np.ascontiguousarray(conv_b.reshape(-1, 1), np.float32),
        dt_bias=np.ascontiguousarray(dt_bias.reshape(-1, 1), np.float32),
        a_log=np.ascontiguousarray(a_log.reshape(-1, 1), np.float32),
        dcol64=dcol64,
        norm_w=np.ascontiguousarray(norm_w.reshape(-1, 1), np.float32),
    )
    in_maps = []
    for c in range(NCORES):
        b, sidx = divmod(c, 4)
        s0 = sidx * TOK
        if s0 == 0:
            upad = np.concatenate([np.zeros((HALO, D), np.float32), u[b, :TOK]], 0)
        else:
            upad = u[b, s0 - HALO:s0 + TOK]
        grp0 = (c // 4) * 4
        prevmask = np.zeros((NCORES,), np.float32)
        binmask = np.zeros((NCORES, NCORES), np.float32)
        for j in range(grp0, c):
            prevmask[j] = 1.0
            for i in range(j + 1, c):
                binmask[j, i] = 1.0
        m = dict(base)
        m["u"] = np.ascontiguousarray(upad, np.float32)
        m["pmx"] = np.ascontiguousarray(np.repeat(prevmask.reshape(1, -1), NH, 0))
        m["bmx"] = np.ascontiguousarray(np.repeat(binmask.reshape(1, -1), NH, 0))
        shf = FFI // NCORES
        shd = D // NCORES
        m["wg_sh"] = np.ascontiguousarray(wg[c * shf:(c + 1) * shf], np.float32)
        m["wu_sh"] = np.ascontiguousarray(wu[c * shf:(c + 1) * shf], np.float32)
        m["wd_sh"] = np.ascontiguousarray(wdn[c * shd:(c + 1) * shd], np.float32)
        in_maps.append(m)
    return in_maps


def kernel(**inputs):
    in_maps = _prep_inputs(inputs)
    nc = build(stage=STAGE, debug_taps=DEBUG_TAPS)
    res = run_bass_kernel_spmd(nc, in_maps, core_ids=list(range(NCORES)))
    full = np.zeros((2, 8192, D), np.float32)
    for c in range(NCORES):
        b, sidx = divmod(c, 4)
        full[b, sidx * TOK:(sidx + 1) * TOK] = res.results[c]["out"]
    return full


# version 43
# speedup vs baseline: 1.0018x; 1.0018x over previous
import contextlib
import numpy as np
import ml_dtypes
import concourse.bass as bass
import concourse.mybir as mybir
from concourse.bass_utils import run_bass_kernel_spmd

F32 = mybir.dt.float32
F16 = mybir.dt.float16
BF16 = mybir.dt.bfloat16
AF = mybir.ActivationFunctionType
ALU = mybir.AluOpType
AX = mybir.AxisListType

NCORES = 8
TOK = 2048
HALO = 3
TOKH = TOK + HALO
D = 1024
DI = 2048
NH = 32
HD = 64
DS = 128
EXBC = 2336
NCH = 16
CH = 128
FFI = 2816
EPS_LN = 1e-5
EPS_RMS = 1e-5
M_H = float(1.5 * 2**22)
M_1 = float(1.5 * 2**23)
M_2 = float(1.5 * 2**24)
NEG = -30000.0

DEBUG_TAPS = {}
STAGE = 99
import os as _os
SSD_STEPS = int(_os.environ.get("SSD_STEPS", "99"))
SSD_NCHUNK = int(_os.environ.get("SSD_NCHUNK", "99"))


class Q:
    ENGINES = ("sync", "vector", "scalar", "tensor", "gpsimd")

    def __init__(self, nc):
        self.nc = nc
        self.steps = {e: [] for e in self.ENGINES}
        self.sems = {}
        self.cnt = {}
        self.ctx = None
        self.tag = "init"

    def sem(self, name):
        if name not in self.sems:
            self.sems[name] = self.ctx.enter_context(self.nc.semaphore(name))
            self.cnt[name] = 0
        return self.sems[name]

    def wait(self, eng, name, n=None):
        s = self.sem(name)
        tgt = self.cnt[name] if n is None else n
        if tgt > 0:
            self.steps[eng].append((self.tag, lambda e, s=s, t=tgt: e.wait_ge(s, t)))
        return tgt

    def emit(self, eng, fn, inc=None, k=1):
        if inc is not None:
            s = self.sem(inc)
            self.cnt[inc] += k
            def wrapped(e, fn=fn, s=s, k=k):
                fn(e).then_inc(s, k)
            self.steps[eng].append((self.tag, wrapped))
        else:
            self.steps[eng].append((self.tag, fn))

    def dma(self, eng, out, in_, inc, **kw):
        self.emit(eng, lambda e, o=out, i=in_, kw=kw: e.dma_start(out=o, in_=i, **kw),
                  inc=inc, k=16)

    def n(self, name):
        self.sem(name)
        return self.cnt[name]

    def run(self, block):
        nc = self.nc
        def replay(engine, steps):
            cur = None
            sid = None
            for tag, f in steps:
                if tag != cur:
                    if cur is not None:
                        nc.leave_named_scope(cur, sid, False)
                    sid, _ = nc.enter_named_scope(tag, False)
                    cur = tag
                f(engine)
            if cur is not None:
                nc.leave_named_scope(cur, sid, False)
        for eng in self.ENGINES:
            steps = self.steps[eng]
            getattr(block, eng)(lambda engine, steps=steps: replay(engine, steps))


class KB:
    def __init__(self, nc, q, ctx):
        self.nc, self.q, self.ctx = nc, q, ctx
        self.dram, self.sbuf, self.taps = {}, {}, {}

    def din(self, name, shape, dt=F32):
        self.dram[name] = self.nc.dram_tensor(name, shape, dt, kind="ExternalInput").ap()
        return self.dram[name]

    def dout(self, name, shape, dt=F32):
        self.dram[name] = self.nc.dram_tensor(name, shape, dt, kind="ExternalOutput").ap()
        return self.dram[name]

    def dint(self, name, shape, dt=F32, **kw):
        self.dram[name] = self.nc.dram_tensor(name, shape, dt, kind="Internal", **kw).ap()
        return self.dram[name]

    def sb(self, name, shape, dt=F32, scope=None):
        t = (scope or self.ctx).enter_context(self.nc.sbuf_tensor(name, shape, dt))
        self.sbuf[name] = t
        return t

    def ps(self, name, shape, dt=F32, scope=None):
        t = (scope or self.ctx).enter_context(self.nc.psum_tensor(name, shape, dt))
        self.sbuf[name] = t
        return t

    def tap(self, name, src_ap, nelem_sem="st_tap", eng="sync"):
        if name in self.taps:
            self.q.wait(eng, "vd")
            self.q.wait(eng, "ac")
            self.q.dma(eng, self.taps[name], src_ap, nelem_sem)


def build(stage=99, debug_taps=None):
    nc = bass.Bass("TRN2", target_bir_lowering=False)
    try:
        nc.detect_race_conditions = False
    except Exception:
        pass
    with contextlib.ExitStack() as ctx:
        q = Q(nc)
        q.ctx = ctx
        kb = KB(nc, q, ctx)
        declare_dram(kb)
        for tname, (tshape, tdt) in (debug_taps or {}).items():
            kb.taps[tname] = kb.dout(tname, tshape, tdt)
        body(kb, stage)
        block = ctx.enter_context(nc.Block())
        q.run(block)
    return nc


def declare_dram(kb):
    kb.din("u", [TOKH, D])
    kb.din("w_xbcT", [D, EXBC])
    kb.din("w_zT", [D, DI])
    kb.din("w_outT", [DI, D])
    kb.din("wg_sh", [FFI // NCORES, D])
    kb.din("wu_sh", [FFI // NCORES, D])
    kb.din("wd_sh", [D // NCORES, FFI])
    kb.din("conv_w4", [DI + 2 * DS, 4])
    kb.din("conv_b", [DI + 2 * DS, 1])
    kb.din("dt_bias", [NH, 1])
    kb.din("a_log", [NH, 1])
    kb.din("dcol64", [DI, 1])
    kb.din("norm_w", [DI, 1])
    kb.din("pmx", [NH, NCORES])
    kb.din("bmx", [NH, NCORES * NCORES])
    kb.dout("out", [TOK, D])
    kb.dint("d_wxb", [D, EXBC], BF16)
    kb.dint("d_wz", [D, DI], BF16)
    kb.dint("d_wout", [DI, D], BF16)
    kb.dint("d_x", [DI + DS, TOK], BF16)
    kb.dint("d_y", [TOK, DI], BF16)
    kb.dint("d_xq", [TOK, D], BF16)
    kb.dint("d_xn", [CH, 8 * TOKH], BF16)
    kb.dint("d_hq", [TOK, FFI], BF16)
    kb.dint("d_hi", [NCH, NH * CH], F16)
    kb.dint("d_lo", [NCH, NH * CH], BF16)
    kb.dint("d_rstd", [1, TOK])
    kb.dint("cc_g_in", [FFI // NCORES, D], BF16)
    kb.dint("cc_g_out", [FFI, D], BF16, addr_space="Shared")
    kb.dint("cc_u_in", [FFI // NCORES, D], BF16)
    kb.dint("cc_u_out", [FFI, D], BF16, addr_space="Shared")
    kb.dint("cc_dn_in", [D // NCORES, FFI], BF16)
    kb.dint("cc_dn_out", [D, FFI], BF16, addr_space="Shared")
    kb.dint("d_coef", [NCORES, NH])
    kb.dint("d_lcz", [NCH, NH])
    kb.dint("cc_s_in", [DS, NH * HD])
    kb.dint("cc_s_out", [NCORES * DS, NH * HD], addr_space="Shared")
    kb.dint("cc_l_in", [NH, 1])
    kb.dint("cc_l_out", [NCORES * NH, 1], addr_space="Shared")


def body(kb, stage):
    nc, q = kb.nc, kb.q
    sb, psb, dram, sbuf = kb.sb, kb.ps, kb.dram, kb.sbuf

    ident = sb("ident", [CH, CH], BF16)
    ut_incl = sb("ut_incl", [CH, CH], BF16)
    ones1_bf = sb("ones1_bf", [1, CH], BF16)
    ones1_f32 = sb("ones1_f32", [1, CH], F32)
    onescol_bf = sb("onescol_bf", [CH, 1], BF16)
    ones32 = sb("ones32", [CH, NH], BF16)
    ones_full = sb("ones_full", [CH, CH], BF16)
    identh = sb("identh", [CH, CH], F16)
    negones1 = sb("negones1", [1, CH], BF16)
    dcol = sb("dcol", [CH, DI // CH])
    dtb = sb("dtb", [NH, 1])
    acol = sb("acol", [NH, 1])
    nwcol = sb("nwcol", [CH, DI // CH])
    epscol = sb("epscol", [CH, 1])
    onecolf = sb("onecolf", [CH, 1])
    pmx = sb("pmx_s", [NH, NCORES])
    bmx = sb("bmx_s", [NH, NCORES * NCORES])

    B_fm = sb("B_fm", [DS, TOK], BF16)
    C_fm = sb("C_fm", [DS, TOK], BF16)
    a_hi = sb("a_hi", [CH, NCH * NH], F16)
    a_lo = sb("a_lo", [CH, NCH * NH], BF16)
    az_fm = sb("az_fm", [NH, NCH])
    R = sb("R", [DS, NH * HD])
    Rb = sb("Rb", [DS, NH * HD], BF16)

    lnA = contextlib.ExitStack()
    lt_strict = sb("lt_strict", [CH, CH], BF16, scope=lnA)
    maskb = sb("maskb", [CH, CH], F32, scope=lnA)
    identf = sb("identf", [CH, CH], F32, scope=lnA)
    B_tm = sb("B_tm", [CH, NCH * DS], BF16, scope=lnA)
    dt_tmb = sb("dt_tmb", [CH, NCH * NH], BF16, scope=lnA)

    g = "gpsimd"
    q.emit(g, lambda e: e.memset(ident[:], 1.0))
    q.emit(g, lambda e: e.memset(identh[:], 1.0))
    q.emit(g, lambda e: e.memset(negones1[:], -1.0))
    q.emit(g, lambda e: e.memset(identf[:], 1.0))
    q.emit(g, lambda e: e.memset(ut_incl[:], 1.0))
    q.emit(g, lambda e: e.memset(lt_strict[:], 1.0))
    q.emit(g, lambda e: e.memset(maskb[:], 0.0))
    q.emit(g, lambda e: e.memset(ones1_bf[:], 1.0))
    q.emit(g, lambda e: e.memset(ones1_f32[:], 1.0))
    q.emit(g, lambda e: e.memset(onescol_bf[:], 1.0))
    q.emit(g, lambda e: e.memset(ones32[:], 1.0))
    q.emit(g, lambda e: e.memset(ones_full[:], 1.0))
    q.emit(g, lambda e: e.memset(R[:], 0.0))
    q.emit(g, lambda e: e.memset(epscol[:], EPS_LN))
    q.emit(g, lambda e: e.memset(onecolf[:], 1.0), inc="gp")
    q.wait(g, "gp")
    q.emit(g, lambda e: e.affine_select(ident[:], ident[:], [[-1, CH]], ALU.is_equal,
                                        0.0, base=0, channel_multiplier=1))
    q.emit(g, lambda e: e.affine_select(identh[:], identh[:], [[-1, CH]], ALU.is_equal,
                                        0.0, base=0, channel_multiplier=1))
    q.emit(g, lambda e: e.affine_select(identf[:], identf[:], [[-1, CH]], ALU.is_equal,
                                        0.0, base=0, channel_multiplier=1))
    q.emit(g, lambda e: e.affine_select(ut_incl[:], ut_incl[:], [[1, CH]], ALU.is_ge,
                                        0.0, base=0, channel_multiplier=-1))
    q.emit(g, lambda e: e.affine_select(lt_strict[:], lt_strict[:], [[-1, CH]], ALU.is_gt,
                                        0.0, base=0, channel_multiplier=1))
    q.emit(g, lambda e: e.affine_select(maskb[:], maskb[:], [[1, CH]], ALU.is_ge,
                                        NEG, base=0, channel_multiplier=-1), inc="gp")
    q.wait(g, "gp")
    q.dma(g, dtb[:], dram["dt_bias"], "ldc")
    q.dma(g, acol[:], dram["a_log"], "ldc")
    for _j in range(DI // CH):
        q.dma(g, nwcol[:, _j:_j + 1], dram["norm_w"][_j * CH:(_j + 1) * CH, :], "ldc")
        q.dma(g, dcol[:, _j:_j + 1], dram["dcol64"][_j * CH:(_j + 1) * CH, :], "ldc")
    q.dma(g, pmx[:], dram["pmx"], "ldc")
    q.dma(g, bmx[:], dram["bmx"], "ldc")

    q.wait("scalar", "ldc")
    q.emit("scalar", lambda e: e.activation(acol[:], acol[:], AF.Exp), inc="ac")
    q.wait("vector", "ac")
    q.emit("vector", lambda e: e.tensor_scalar(acol[:], acol[:], -1.0, None, ALU.mult),
           inc="vd")

    sb("xn_fm", [CH, 8 * TOKH], BF16, scope=lnA)
    if stage >= 3:
        kb.q.tag = "gather"
        gsc = contextlib.ExitStack()
        phase_gather_w(kb, gsc)
    if stage >= 1:
        kb.q.tag = "ln"
        phase_ln(kb)
    if stage >= 3:
        gsc.close()
    if stage >= 2:
        kb.q.tag = "a"
        phase_a(kb)
    lnA.close()
    sb("g_fm", [CH, 8 * TOK], BF16)
    if stage >= 3:
        kb.q.tag = "gather"
        phase_gather_sl(kb)
    if stage >= 4:
        kb.q.tag = "ffn"
        phase_ffn(kb)
    if stage >= 5:
        kb.q.tag = "tail"
        phase_tail(kb)

    for s in ("st_out", "st_tap", "st_xq", "st_xn", "st_x", "st_y", "st_hi", "st_hq", "st_wq", "st_s", "st_t"):
        q.wait("sync", s)


def phase_ln(kb):
    nc, q, dram, sbuf = kb.nc, kb.q, kb.dram, kb.sbuf
    sb, psb = kb.sb, kb.ps
    u = dram["u"]
    xn_fm = sbuf["xn_fm"]

    with contextlib.ExitStack() as scope:
        utile = kb.sb("utile", [CH, 2 * D], scope=scope)
        mcol = kb.sb("mcol", [CH, 2], scope=scope)
        mraw = kb.sb("mraw", [CH, 1], scope=scope)
        ssraw = kb.sb("ssraw", [CH, 1], scope=scope)
        ssd2 = kb.sb("ssd2", [CH, 1], scope=scope)
        vcol = kb.sb("vcol", [CH, 1], scope=scope)
        rcol = kb.sb("rcol", [CH, 2], scope=scope)
        up = kb.sb("up", [CH, D], scope=scope)
        xh = kb.sb("xh", [CH, D], F16, scope=scope)
        sqs = kb.sb("sqs", [CH, D], scope=scope)
        mx = kb.sb("mx", [CH, D // 16], scope=scope)
        mxr = kb.sb("mxr", [CH, D // 16], scope=scope)
        s16 = kb.sb("s16", [CH, D // 16], scope=scope)
        w_ = kb.sb("w_", [CH, D], F16, scope=scope)
        p1 = kb.sb("p1", [CH, D], F16, scope=scope)
        t_a = kb.sb("t_a", [CH, D], F16, scope=scope)
        t_b = kb.sb("t_b", [CH, D], F16, scope=scope)
        p23 = kb.sb("p23", [CH, D], F16, scope=scope)
        xq_t = kb.sb("xq_t", [CH, D], BF16, scope=scope)
        pln = kb.ps("pln", [CH, 1024], scope=scope)

        xn3 = xn_fm[:].rearrange("p (j t) -> p j t", j=8)
        ps3 = pln[:].rearrange("p (j t) -> p j t", j=8)

        ld_tgt = {}
        a2_t = {}
        rcol_t = {}
        up_vd = {}
        diagr_vd = {}
        xn_ac = {}
        mm_pe_d = {}
        q.dma("sync", utile[0:CH, 0:D], u[0:CH, :], "ld_u")
        ld_tgt[0] = q.n("ld_u")
        q.dma("sync", utile[0:CH, D:2 * D], u[CH:2 * CH, :], "ld_u")
        ld_tgt[1] = q.n("ld_u")
        for it in range(17):
            rows = CH if it < 16 else HALO
            r0 = it * CH
            sl = slice(0, rows)
            pu = it % 2
            ut = utile[sl, pu * D:pu * D + D]
            q.wait("scalar", "ld_u", ld_tgt[it])
            if it >= 2:
                q.wait("scalar", "vd", diagr_vd[it - 2])
            q.emit("scalar", lambda e, sl=sl, ut=ut: e.activation(
                sqs[sl, :], ut, AF.Square, accum_out=ssraw[sl, :]), inc="ac")
            q.emit("scalar", lambda e, sl=sl, ut=ut: e.activation(
                sqs[sl, :], ut, AF.Copy, accum_out=mraw[sl, :]), inc="ac")
            a2_t[it] = q.n("ac")
            q.emit("scalar", lambda e: e.drain())
            q.emit("scalar", lambda e, sl=sl, pu=pu: e.activation(mcol[sl, pu:pu + 1], mraw[sl, :], AF.Copy, scale=1.0 / D))
            q.emit("scalar", lambda e, sl=sl: e.activation(ssd2[sl, :], ssraw[sl, :], AF.Copy, scale=1.0 / D, bias=EPS_LN))
            q.emit("scalar", lambda e: e.drain())
            q.emit("scalar", lambda e, sl=sl, pu=pu: e.activation(vcol[sl, :], mcol[sl, pu:pu + 1], AF.Square))
            q.emit("scalar", lambda e: e.drain())
            q.emit("scalar", lambda e, sl=sl: e.activation(vcol[sl, :], vcol[sl, :], AF.Ln, scale=-1.0, bias=ssd2[sl, :]))
            q.emit("scalar", lambda e: e.drain())
            q.emit("scalar", lambda e, sl=sl, pu=pu: e.activation(rcol[sl, pu:pu + 1], vcol[sl, :], AF.Exp, scale=-0.5))
            q.emit("scalar", lambda e: e.drain(), inc="ac")
            rcol_t[it] = q.n("ac")
            q.wait("vector", "ac", rcol_t[it])
            q.emit("vector", lambda e, sl=sl, ut=ut, pu=pu: e.tensor_scalar(up[sl, :], ut, mcol[sl, pu:pu + 1], None, ALU.subtract), inc="vd")
            up_vd[it] = q.n("vd")
            if it >= 1:
                q.wait("vector", "pe", mm_pe_d[it - 1])
            if it == 16:
                q.emit("vector", lambda e: e.memset(xh[:], 0.0), inc="vd")
            q.emit("vector", lambda e, sl=sl, pu=pu: e.tensor_scalar(xh[sl, :], up[sl, :], rcol[sl, pu:pu + 1], None, ALU.mult), inc="vd")
            diagr_vd[it] = q.n("vd")
            qr_t = equant_pre(kb, src=xh, sl=slice(0, CH), width=D, mx=mx)
            if it + 2 <= 16:
                rows2 = CH if it + 2 < 16 else HALO
                q.wait("sync", "ac", a2_t[it])
                q.wait("sync", "vd", up_vd[it])
                q.dma("sync", utile[0:rows2, pu * D:pu * D + D],
                      u[(it + 2) * CH:(it + 2) * CH + rows2, :], "ld_u")
                ld_tgt[it + 2] = q.n("ld_u")
            q.wait("tensor", "vd", diagr_vd[it])
            if it >= 1:
                q.wait("tensor", "vd", xn_ac[it - 1])
            for j in range(8):
                q.emit("tensor", lambda e, j=j, sl=sl:
                       e.matmul(pln[:, j * CH:(j + 1) * CH], xh[sl, j * CH:(j + 1) * CH],
                                kb.sbuf["identh"][sl, :], start=True, stop=True),
                       inc=("pe" if j == 7 else None))
            mm_pe_d[it] = q.n("pe")
            q.wait("vector", "pe", mm_pe_d[it])
            q.emit("vector", lambda e, it=it, rows=rows:
                   e.tensor_copy(xn3[:, :, r0_(it):r0_(it) + rows], ps3[:, :, 0:rows]), inc="vd")
            xn_ac[it] = q.n("vd")
            if it == 10:
                kb.tap("t_mcol", mcol[:])
                kb.tap("t_sscol", ssraw[:])
                kb.tap("t_rcol", rcol[:])

            p0 = HALO if it == 0 else 0
            q.wait("vector", "st_xq")
            equant_post(kb, src=xh, sl=slice(0, CH), width=D, mx=mx, mxr=mxr, s16=s16,
                        w_=w_, p1=p1, t_a=t_a, t_b=t_b, p23=p23, out=xq_t, qr_tgt=qr_t)
            q.wait("sync", "vd")
            q.dma("sync", dram["d_xq"][it * CH - HALO + p0: it * CH - HALO + rows, :],
                  xq_t[p0:rows, :], "st_xq")

        q.wait("sync", "vd")
        q.wait("sync", "ac")
        q.dma("sync", dram["d_xn"], xn_fm[:], "st_xn")
        kb.tap("t_xn", xn_fm[:])
        q.wait("sync", "st_xq")
        kb.tap("t_xq", dram["d_xq"])


def r0_(it):
    return it * CH


def equant_pre(kb, src, sl, width, mx, mx0=0, src0=0):
    q = kb.q
    nb = width // 16
    src3 = src[sl, src0:src0 + width].rearrange("p (b k) -> p b k", k=16)
    q.emit("vector", lambda e: e.tensor_reduce(mx[sl, mx0:mx0 + nb].unsqueeze(2), src3, AX.X,
                                               ALU.max, apply_absolute_value=True), inc="qr")
    return q.n("qr")


def equant_post(kb, src, sl, width, mx, mxr, s16, w_, p1, t_a, t_b, p23, out, qr_tgt, osl=None, mx0=0, src0=0):
    q = kb.q
    if osl is None:
        osl = slice(0, width)
    nb = width // 16
    rows = sl.stop - sl.start
    src3 = src[sl, src0:src0 + width].rearrange("p (b k) -> p b k", k=16)
    q.wait("vector", "qr", qr_tgt)
    q.emit("vector", lambda e: e.tensor_scalar(s16[sl, 0:nb], mx[sl, mx0:mx0 + nb], 6e-12, 1.0 / 6.0, ALU.max, ALU.mult), inc="qs")
    q.wait("vector", "qs", q.n("qs"))
    q.emit("vector", lambda e: e.reciprocal(mxr[sl, 0:nb], s16[sl, 0:nb]), inc="qs")
    q.wait("vector", "qs", q.n("qs"))
    r3 = mxr[sl, 0:nb].unsqueeze(2).broadcast_to([rows, nb, 16])
    s163 = s16[sl, 0:nb].unsqueeze(2).broadcast_to([rows, nb, 16])
    w3 = w_[sl, 0:width].rearrange("p (b k) -> p b k", k=16)
    q.emit("vector", lambda e: e.tensor_tensor(w3, src3, r3, ALU.mult), inc="vd")
    kb.srcfree_vd = q.n("vd")
    wf = w_[sl, 0:width]
    q.emit("vector", lambda e: e.tensor_scalar(p1[sl, 0:width], wf, -2.25, 2.25, ALU.max, ALU.min), inc="vd")
    q.emit("vector", lambda e: e.tensor_scalar(p1[sl, 0:width], p1[sl, 0:width], M_H, -M_H, ALU.add, ALU.add), inc="vd")
    q.emit("vector", lambda e: e.tensor_scalar(t_a[sl, 0:width], wf, M_1, -M_1, ALU.add, ALU.add), inc="vd")
    q.emit("vector", lambda e: e.tensor_scalar(t_b[sl, 0:width], t_a[sl, 0:width], 2.0, 4.0, ALU.max, ALU.min), inc="vd")
    q.emit("vector", lambda e: e.tensor_scalar(t_a[sl, 0:width], t_a[sl, 0:width], -4.0, -2.0, ALU.max, ALU.min), inc="vd")
    q.emit("vector", lambda e: e.tensor_tensor(p23[sl, 0:width], t_a[sl, 0:width], t_b[sl, 0:width], ALU.add), inc="vd")
    q.emit("vector", lambda e: e.tensor_tensor(p1[sl, 0:width], p1[sl, 0:width], p23[sl, 0:width], ALU.add), inc="vd")
    q.emit("vector", lambda e: e.tensor_scalar(t_a[sl, 0:width], wf, M_2, -M_2, ALU.add, ALU.add), inc="vd")
    q.emit("vector", lambda e: e.tensor_scalar(t_b[sl, 0:width], t_a[sl, 0:width], 4.0, 6.0, ALU.max, ALU.min), inc="vd")
    q.emit("vector", lambda e: e.tensor_scalar(t_a[sl, 0:width], t_a[sl, 0:width], -6.0, -4.0, ALU.max, ALU.min), inc="vd")
    q.emit("vector", lambda e: e.tensor_tensor(p23[sl, 0:width], t_a[sl, 0:width], t_b[sl, 0:width], ALU.add), inc="vd")
    q.emit("vector", lambda e: e.tensor_tensor(p1[sl, 0:width], p1[sl, 0:width], p23[sl, 0:width], ALU.add), inc="vd")
    o3 = out[sl, osl].rearrange("p (b k) -> p b k", k=16)
    p13 = p1[sl, 0:width].rearrange("p (b k) -> p b k", k=16)
    q.emit("vector", lambda e: e.tensor_tensor(o3, p13, s163, ALU.mult), inc="vd")


def phase_a(kb):
    nc, q, dram, sbuf = kb.nc, kb.q, kb.dram, kb.sbuf
    q.dma("gpsimd", dram["d_wz"], dram["w_zT"], "ldwz")
    q.dma("gpsimd", dram["d_wout"], dram["w_outT"], "ldwo")
    xn_fm = sbuf["xn_fm"]
    TH2 = 1024
    NC2 = 1027

    with contextlib.ExitStack() as sc:
        wxb = kb.sb("wxb", [CH, 2 * 8 * CH], BF16, scope=sc)
        cwall = kb.sb("cwall", [CH, 18 * 4], scope=sc)
        cball = kb.sb("cball", [CH, 18], scope=sc)
        ta = kb.sb("ta", [CH, TH2], BF16, scope=sc)
        tb = kb.sb("tb", [CH, 2 * TH2], BF16, scope=sc)
        xpost = kb.sb("xpost", [CH, 2 * TH2], BF16, scope=sc)
        dtraw = kb.sb("dtraw", [NH, TH2], scope=sc)
        dts1 = kb.sb("dts1", [NH, TH2], scope=sc)
        dts2 = kb.sb("dts2", [NH, TH2], scope=sc)
        dtv_bf = kb.sb("dtv_bf", [NH, TH2], BF16, scope=sc)
        a_f = kb.sb("a_f", [NH, TH2], scope=sc)
        x_tm = kb.sb("x_tm", [CH, 2 * DI], BF16, scope=sc)
        xfc = kb.sb("xfc", [CH, DI], BF16, scope=sc)
        xdc = kb.sb("xdc", [CH, DI], BF16, scope=sc)
        xdt = kb.sb("xdt", [CH, DI], BF16, scope=sc)
        bw = kb.sb("bw", [CH, NH * DS], BF16, scope=sc)
        mt = kb.sb("mt", [CH, NH * CH], BF16, scope=sc)
        ebuf = kb.sb("ebuf", [CH, NH * CH], BF16, scope=sc)
        d1 = kb.sb("d1", [CH, 3 * 512], scope=sc)
        rowp_hi = kb.sb("rowp_hi", [CH, NH * CH], F16, scope=sc)
        rowp_lo = kb.sb("rowp_lo", [CH, NH * CH], BF16, scope=sc)
        cumc = kb.sb("cumc", [CH, NH], scope=sc)
        wd_bf = kb.sb("wd_bf", [CH, 2 * NH], BF16, scope=sc)
        decs = kb.sb("decs", [CH, 2 * 64], scope=sc)
        rtmp = kb.sb("rtmp", [DS, NH * HD], scope=sc)
        ybt = kb.sb("ybt", [CH, 512], scope=sc)
        ytile = kb.sb("ytile", [CH, DI], BF16, scope=sc)
        scT = kb.sb("scT", [CH, CH], BF16, scope=sc)
        lsum = kb.sb("lsum", [NH, 1], scope=sc)
        ps1 = kb.ps("ps1", [CH, 2048], scope=sc)
        ps2 = kb.ps("ps2", [CH, 1536], scope=sc)
        psS = kb.ps("psS", [CH, 512], scope=sc)

        q.wait("gpsimd", "st_xq")
        q.wait("gpsimd", "st_xn")
        q.wait("gpsimd", "st_wq")
        q.dma("gpsimd", cwall[:].rearrange("p (c w) -> p c w", w=4),
              dram["conv_w4"][0:18 * CH, :].rearrange("(c p) w -> p c w", p=CH), "ldc2")
        q.dma("gpsimd", cball[:].rearrange("p (c w) -> p c w", w=1),
              dram["conv_b"][0:18 * CH, :].rearrange("(c p) w -> p c w", p=CH), "ldc2",
              allow_slow_non_contiguous=True)
        q.wait("vector", "ldc2")
        q.wait("scalar", "ldc2")
        pe_t = {}
        ldw_t = {}
        conv_vd = {}
        sil_ac = {}
        stx_t = {}
        kb.ssd_t = {}
        kb.ssd_sty = 0
        for half in range(2):
            q.tag = "a_inproj"
            t0 = half * TH2
            c0 = half * TH2
            q.wait("tensor", "vd")
            q.wait("tensor", "ac")
            q.wait("gpsimd", "ldw0", 16)
            for ec in range(19):
                width = CH if ec < 18 else NH
                e0 = ec * CH
                gec = half * 19 + ec
                wb = gec % 2
                reg = sbuf["ps1"] if gec % 2 == 0 else sbuf["ps2"]
                if gec >= 2:
                    q.wait("gpsimd", "pe", pe_t[gec - 2])
                q.dma("gpsimd", wxb[:, wb * 1024:wb * 1024 + 8 * width].rearrange("p (k w) -> p k w", k=8),
                      dram["d_wxb"][:, e0:e0 + width].rearrange("(k p) w -> p k w", p=CH),
                      "ld_w")
                ldw_t[gec] = q.n("ld_w")
                q.wait("tensor", "ld_w", ldw_t[gec])
                if half == 0 and ec == 0:
                    q.wait("sync", "ld_w")
                    kb.tap("t_wxb", wxb[:, 0:8 * CH])
                    kb.tap("t_xnfm", xn_fm[:])
                if gec >= 2 and (gec - 2) in conv_vd:
                    q.wait("tensor", "vd", conv_vd[gec - 2])
                for (tc0, tw) in [(0, 512), (512, 512), (1024, 3)]:
                    for k in range(8):
                        q.emit("tensor", lambda e, k=k, tc0=tc0, tw=tw, width=width, c0=c0, reg=reg, wb=wb:
                               e.matmul(reg[0:width, tc0:tc0 + tw],
                                        wxb[:, wb * 1024 + k * width:wb * 1024 + (k + 1) * width],
                                        xn_fm[:, k * TOKH + c0 + tc0: k * TOKH + c0 + tc0 + tw],
                                        start=(k == 0), stop=(k == 7)),
                               inc=("pe" if (k == 7 and tc0 == 1024) else None))
                pe_t[gec] = q.n("pe")
                if half == 0 and ec == 1 and "t_raw1" in kb.taps:
                    q.wait("scalar", "pe", pe_t[gec])
                    q.emit("scalar", lambda e, reg=reg: e.copy(sbuf["x_tm"][:, 0:NC2], reg[:, 0:NC2]), inc="ac")
                    q.wait("sync", "ac")
                    kb.tap("t_raw1", sbuf["x_tm"][:, 0:NC2])
                if ec < 18:
                    cwc = ec * 4
                    q.wait("vector", "pe", pe_t[gec])
                    if gec >= 2 and (gec - 2) in sil_ac:
                        q.wait("vector", "ac", sil_ac[gec - 2])
                    q.emit("vector", lambda e, reg=reg, cwc=cwc: e.tensor_scalar(
                        ta[:], reg[:, 0:TH2], cwall[:, cwc:cwc + 1], None, ALU.mult), inc="vd")
                    q.emit("vector", lambda e, reg=reg, cwc=cwc, wb=wb: e.scalar_tensor_tensor(
                        tb[:, wb * TH2:(wb + 1) * TH2], reg[:, 1:1 + TH2], cwall[:, cwc + 1:cwc + 2], ta[:], ALU.mult, ALU.add), inc="vd")
                    q.emit("vector", lambda e, reg=reg, cwc=cwc, wb=wb: e.scalar_tensor_tensor(
                        ta[:], reg[:, 2:2 + TH2], cwall[:, cwc + 2:cwc + 3], tb[:, wb * TH2:(wb + 1) * TH2], ALU.mult, ALU.add), inc="vd")
                    q.emit("vector", lambda e, reg=reg, cwc=cwc, wb=wb: e.scalar_tensor_tensor(
                        tb[:, wb * TH2:(wb + 1) * TH2], reg[:, 3:3 + TH2], cwall[:, cwc + 3:cwc + 4], ta[:], ALU.mult, ALU.add), inc="vd")
                    conv_vd[gec] = q.n("vd")
                    q.wait("scalar", "vd", conv_vd[gec])
                    if gec >= 2:
                        q.wait("scalar", "st_x", stx_t.get(gec - 2, 0))
                    q.emit("scalar", lambda e, wb=wb, ec=ec: e.activation(
                        xpost[:, wb * TH2:(wb + 1) * TH2], tb[:, wb * TH2:(wb + 1) * TH2],
                        AF.Silu, bias=cball[:, ec:ec + 1]), inc="ac")
                    sil_ac[gec] = q.n("ac")
                    if half == 0 and ec == 0:
                        q.wait("sync", "ac")
                        kb.tap("t_xpost", xpost[:, 0:TH2])
                    if ec <= 16:
                        q.wait("sync", "ac", sil_ac[gec])
                        q.dma("sync", dram["d_x"][e0:e0 + CH, t0:t0 + TH2],
                              xpost[:, wb * TH2:(wb + 1) * TH2], "st_x")
                        stx_t[gec] = q.n("st_x")
                    if ec == 16:
                        q.emit("scalar", lambda e, t0=t0, wb=wb: e.copy(
                            sbuf["B_fm"][:, t0:t0 + TH2], xpost[:, wb * TH2:(wb + 1) * TH2]), inc="ac")
                    if ec == 17:
                        q.emit("scalar", lambda e, t0=t0, wb=wb: e.copy(
                            sbuf["C_fm"][:, t0:t0 + TH2], xpost[:, wb * TH2:(wb + 1) * TH2]), inc="ac")
                else:
                    q.wait("scalar", "pe", pe_t[gec])
                    q.emit("scalar", lambda e, reg=reg: e.copy(dtraw[:], reg[0:NH, 3:3 + TH2]), inc="ac")
                    q.wait("scalar", "ac")
                    q.emit("scalar", lambda e: e.activation(dts1[:], dtraw[:], AF.Abs, bias=sbuf["dtb"][:, 0:1]))
                    q.emit("scalar", lambda e: e.drain())
                    q.emit("scalar", lambda e: e.activation(dts1[:], dts1[:], AF.Exp, scale=-1.0))
                    q.emit("scalar", lambda e: e.drain())
                    q.emit("scalar", lambda e: e.activation(dts1[:], dts1[:], AF.Ln, bias=sbuf["onecolf"][0:NH, 0:1]))
                    q.emit("scalar", lambda e: e.drain())
                    q.emit("scalar", lambda e: e.activation(dts2[:], dtraw[:], AF.Relu, bias=sbuf["dtb"][:, 0:1]), inc="ac")
                    q.wait("vector", "ac")
                    q.emit("vector", lambda e: e.tensor_tensor(dts2[:], dts1[:], dts2[:], ALU.add), inc="vd")
                    q.emit("vector", lambda e: e.tensor_copy(dtv_bf[:], dts2[:]), inc="vd")
                    q.emit("vector", lambda e: e.tensor_scalar(a_f[:], dts2[:], sbuf["acol"][0:NH, 0:1], None, ALU.mult), inc="vd")

            q.emit("vector", lambda e, half=half: e.tensor_reduce(
                sbuf["az_fm"][:, half * 8:(half + 1) * 8].unsqueeze(2),
                a_f[:].rearrange("p (c t) -> p c t", t=CH), AX.X, ALU.add))
            q.emit("vector", lambda e: e.drain(), inc="vd")
            kb.ssd_t = dict(kb.ssd_t)
            kb.ssd_t["psS_vd"] = q.n("vd")
            kb.ssd_t["psS_ac"] = q.n("ac")
            q.wait("scalar", "st_x")
            q.wait("scalar", "pe")
            q.wait("scalar", "vd")
            for tc in range(8):
                gc = half * 8 + tc
                q.dma("scalar", sbuf["B_tm"][:, gc * DS:(gc + 1) * DS],
                      dram["d_x"][DI:DI + DS, t0 + tc * CH:t0 + (tc + 1) * CH], "ld_xt",
                      transpose=True)
            ld_tgt = {}
            q.dma("scalar", x_tm[:, 0:DI],
                  dram["d_x"][0:DI, t0:t0 + CH], "ld_xc", transpose=True)
            ld_tgt[0] = q.n("ld_xc")

            q.tag = "a_ssd"
            vd_prev = 0
            for cc_ in range(8):
                if cc_ + 1 < 8:
                    if vd_prev:
                        q.wait("scalar", "vd", vd_prev)
                    q.dma("scalar", x_tm[:, ((cc_ + 1) % 2) * DI:((cc_ + 1) % 2 + 1) * DI],
                          dram["d_x"][0:DI, t0 + (cc_ + 1) * CH:t0 + (cc_ + 2) * CH],
                          "ld_xc", transpose=True)
                    ld_tgt[cc_ + 1] = q.n("ld_xc")
                if half * 8 + cc_ < SSD_NCHUNK:
                    ssd_chunk(kb, half, cc_, cc_ % 2, ld_tgt[cc_])
                vd_prev = q.n("vd")

        if SSD_NCHUNK < 16:
            kb.tap("t_dx", dram["d_x"])
            if SSD_NCHUNK > 0:
                kb.tap("t_R", sbuf["R"][:])
            return
        q.wait("sync", "vd")
        q.dma("sync", dram["cc_s_in"], sbuf["R"][:], "st_s")
        q.wait("vector", "ac")
        q.emit("vector", lambda e: e.tensor_reduce(lsum[:].unsqueeze(2), sbuf["az_fm"][:].unsqueeze(1), AX.X, ALU.add))
        q.emit("vector", lambda e: e.drain(), inc="vd")
        q.wait("sync", "vd")
        q.dma("sync", dram["cc_l_in"], lsum[:], "st_s")
        kb.tap("t_y", dram["d_y"])
        kb.tap("t_R", sbuf["R"][:])
        kb.tap("t_dx", dram["d_x"])


def ssd_chunk(kb, half, cc_, xbuf, ld_xc_tgt):
    nc, q, dram, sbuf = kb.nc, kb.q, kb.dram, kb.sbuf
    gc = half * 8 + cc_
    t0 = half * 1024
    tl0 = t0 + cc_ * CH
    ps1, ps2, psS = sbuf["ps1"], sbuf["ps2"], sbuf["psS"]
    a_f = sbuf["a_f"]
    acs = slice(cc_ * CH, (cc_ + 1) * CH)
    ahc = sbuf["a_hi"][:, gc * NH:(gc + 1) * NH]
    alc = sbuf["a_lo"][:, gc * NH:(gc + 1) * NH]
    dtc = sbuf["dt_tmb"][:, gc * NH:(gc + 1) * NH]
    par = gc % 2
    eacol = sbuf["decs"][:, par * 64:par * 64 + NH]
    pcb = sbuf["decs"][:, par * 64 + NH:par * 64 + 2 * NH]
    wd_bf = sbuf["wd_bf"][:, par * NH:(par + 1) * NH]
    T = kb.ssd_t

    q.wait("tensor", "vd", T.get("psS_vd", 0))
    q.wait("tensor", "ac", T.get("psS_ac", 0))
    q.emit("tensor", lambda e, tl0=tl0: e.matmul(psS[:, 192:320], sbuf["B_fm"][:, tl0:tl0 + CH],
                                                 sbuf["C_fm"][:, tl0:tl0 + CH], start=True, stop=True))
    q.emit("tensor", lambda e: e.transpose(psS[:, 0:NH], a_f[:, acs], sbuf["identf"][0:NH, 0:NH]))
    q.emit("tensor", lambda e: e.transpose(psS[:, 64:80].bitcast(BF16), sbuf["dtv_bf"][:, acs], sbuf["ident"][0:NH, 0:NH]), inc="pe")
    tr_pe = q.n("pe")
    q.wait("scalar", "vd", T.get("mt_vd", 0))
    q.wait("scalar", "pe", tr_pe)
    q.emit("scalar", lambda e: e.copy(sbuf["scT"][:], psS[:, 192:320]), inc="ac")
    sct_ac = q.n("ac")
    q.wait("vector", "pe", tr_pe)
    q.emit("vector", lambda e: e.tensor_copy(ahc, psS[:, 0:NH]), inc="vd")
    q.wait("vector", "vd")
    q.emit("vector", lambda e: e.tensor_tensor(alc, psS[:, 0:NH], ahc, ALU.subtract), inc="vd")
    alc_vd = q.n("vd")
    q.emit("vector", lambda e: e.tensor_copy(dtc, psS[:, 64:80].bitcast(BF16)), inc="vd")
    copies_vd = q.n("vd")

    q.wait("tensor", "vd", copies_vd)
    q.wait("tensor", "ac", sct_ac)
    q.emit("tensor", lambda e: e.matmul(psS[:, 96:96 + NH], sbuf["ut_incl"][:], ahc, start=True, stop=False))
    q.emit("tensor", lambda e: e.matmul(psS[:, 96:96 + NH], sbuf["ut_incl"][:], alc, start=False, stop=True))
    q.emit("tensor", lambda e: e.matmul(psS[:, 160:160 + NH], sbuf["lt_strict"][:], ahc, start=True, stop=False))
    q.emit("tensor", lambda e: e.matmul(psS[:, 160:160 + NH], sbuf["lt_strict"][:], alc, start=False, stop=True))
    q.emit("tensor", lambda e: e.matmul(psS[:, 128:128 + NH], sbuf["ones_full"][:], ahc, start=True, stop=False))
    q.emit("tensor", lambda e: e.matmul(psS[:, 128:128 + NH], sbuf["ones_full"][:], alc, start=False, stop=True), inc="pe")
    cum_pe = q.n("pe")
    q.wait("scalar", "pe", cum_pe)
    q.emit("scalar", lambda e, par=par: e.activation(
        sbuf["decs"][:, par * 64:(par + 1) * 64], psS[:, 96:160], AF.Exp), inc="ac")
    q.emit("scalar", lambda e, wd_bf=wd_bf: e.activation(wd_bf, psS[:, 160:160 + NH], AF.Exp), inc="ac")
    dec_ac = q.n("ac")

    q.wait("vector", "pe", T.get("diff_pe", 0))
    q.wait("vector", "pe", cum_pe)
    q.emit("vector", lambda e: e.tensor_copy(sbuf["cumc"][:], psS[:, 96:96 + NH]), inc="vd")
    q.emit("vector", lambda e: e.tensor_tensor(
        sbuf["rowp_hi"][:].rearrange("p (h l) -> p h l", l=CH),
        ahc.unsqueeze(2).broadcast_to([CH, NH, CH]),
        sbuf["ut_incl"][:].unsqueeze(1).broadcast_to([CH, NH, CH]), ALU.mult), inc="vd")
    q.emit("vector", lambda e: e.tensor_tensor(
        sbuf["rowp_lo"][:].rearrange("p (h l) -> p h l", l=CH),
        alc.unsqueeze(2).broadcast_to([CH, NH, CH]),
        sbuf["ut_incl"][:].unsqueeze(1).broadcast_to([CH, NH, CH]), ALU.mult), inc="vd")
    prod_vd = q.n("vd")

    q.wait("sync", "st_x")
    q.dma("sync", sbuf["xfc"][:].rearrange("p (j t) -> p j t", t=CH),
          dram["d_x"][0:DI, tl0:tl0 + CH].rearrange("(j p) t -> p j t", p=CH), "ld_xf")
    q.wait("vector", "ld_xf")
    q.wait("vector", "ac", dec_ac)
    q.emit("vector", lambda e: e.tensor_tensor(
        sbuf["xdc"][:].rearrange("p (j t) -> p j t", t=CH),
        sbuf["xfc"][:].rearrange("p (j t) -> p j t", t=CH),
        sbuf["dcol"][:].unsqueeze(2).broadcast_to([CH, 16, CH]), ALU.mult), inc="vd")
    q.wait("vector", "ld_xt")
    q.wait("vector", "ld_xc", ld_xc_tgt)
    btc = sbuf["B_tm"][:, gc * DS:(gc + 1) * DS]
    q.emit("vector", lambda e, wd_bf=wd_bf: e.tensor_tensor(
        sbuf["bw"][:].rearrange("p (h n) -> p h n", h=NH),
        btc.unsqueeze(1).broadcast_to([CH, NH, DS]),
        wd_bf.unsqueeze(2).broadcast_to([CH, NH, DS]), ALU.mult), inc="vd")
    xtc = sbuf["x_tm"][:, xbuf * DI:(xbuf + 1) * DI]
    q.emit("vector", lambda e: e.tensor_tensor(
        sbuf["xdt"][:].rearrange("p (h d) -> p h d", h=NH),
        xtc.rearrange("p (h d) -> p h d", h=NH),
        dtc.unsqueeze(2).broadcast_to([CH, NH, HD]), ALU.mult), inc="vd")
    build_vd = q.n("vd")

    q.wait("vector", "pe", T.get("m4_pe", 0))
    q.emit("vector", lambda e: e.tensor_copy(sbuf["Rb"][:], sbuf["R"][:]), inc="vd")
    rb_vd = q.n("vd")
    q.wait("tensor", "vd", build_vd)
    q.wait("tensor", "vd", T.get("m4_vd", 0))
    m3_pe = {}
    radd_vd = {}
    for hhalf in range(2):
        if hhalf == 1:
            q.wait("tensor", "vd", radd_vd[0])
        for hh in range(16):
            h = hhalf * 16 + hh
            q.emit("tensor", lambda e, h=h, hh=hh: e.matmul(
                ps2[:, hh * HD:(hh + 1) * HD], sbuf["bw"][:, h * DS:(h + 1) * DS],
                sbuf["xdt"][:, h * HD:(h + 1) * HD], start=(hh % 8 == 0), stop=(hh % 8 == 7)),
                inc=("pe" if hh == 15 else None))
        m3_pe[hhalf] = q.n("pe")
        q.wait("vector", "pe", m3_pe[hhalf])
        hsl = slice(hhalf * 1024, (hhalf + 1) * 1024)
        q.emit("vector", lambda e, hhalf=hhalf, hsl=hsl, pcb=pcb: e.tensor_tensor(
            sbuf["rtmp"][:, hsl].rearrange("p (h d) -> p h d", d=HD),
            sbuf["R"][:, hsl].rearrange("p (h d) -> p h d", d=HD),
            pcb[:, hhalf * 16:(hhalf + 1) * 16].unsqueeze(2).broadcast_to([DS, 16, HD]),
            ALU.mult), inc="vd")
        q.wait("vector", "vd")
        q.emit("vector", lambda e, hhalf=hhalf, hsl=hsl: e.tensor_tensor(
            sbuf["R"][:, hsl], sbuf["rtmp"][:, hsl], ps2[:, 0:1024], ALU.add), inc="vd")
        radd_vd[hhalf] = q.n("vd")
    scan_vd = q.n("vd")

    q.wait("tensor", "vd", prod_vd)
    q.wait("tensor", "vd", T.get("ytile_vd", 0))
    d1_t = {}
    exp_t = {}
    for p in range(8):
        bank = p % 3
        if p >= 3:
            q.wait("tensor", "vd", d1_t[p - 3])
        sl = slice(p * 512, (p + 1) * 512)
        q.emit("tensor", lambda e, sl=sl, bank=bank: e.matmul(
            ps1[:, bank * 512:(bank + 1) * 512], sbuf["ones_full"][:], sbuf["rowp_hi"][:, sl], start=True, stop=False))
        q.emit("tensor", lambda e, sl=sl, bank=bank: e.matmul(
            ps1[:, bank * 512:(bank + 1) * 512], sbuf["ones_full"][:], sbuf["rowp_lo"][:, sl], start=False, stop=True), inc="pe")
        pe_p = q.n("pe")
        q.wait("vector", "pe", pe_p)
        if p >= 3:
            q.wait("vector", "ac", exp_t[p - 3])
        q.emit("vector", lambda e, p=p, bank=bank: e.tensor_tensor(
            sbuf["d1"][:, bank * 512:(bank + 1) * 512].rearrange("p (a b) -> p a b", b=CH),
            ps1[:, bank * 512:(bank + 1) * 512].rearrange("p (a b) -> p a b", b=CH),
            sbuf["cumc"][:, 4 * p:4 * p + 4].unsqueeze(2).broadcast_to([CH, 4, CH]),
            ALU.subtract), inc="vd")
        q.emit("vector", lambda e, bank=bank: e.scalar_tensor_tensor(
            sbuf["d1"][:, bank * 512:(bank + 1) * 512].rearrange("p (a b) -> p a b", b=CH),
            sbuf["d1"][:, bank * 512:(bank + 1) * 512].rearrange("p (a b) -> p a b", b=CH), 0.0,
            sbuf["maskb"][:].unsqueeze(1).broadcast_to([CH, 4, CH]),
            ALU.min, ALU.add), inc="vd")
        d1_t[p] = q.n("vd")
        q.wait("scalar", "vd", d1_t[p])
        q.emit("scalar", lambda e, p=p, bank=bank: e.activation(
            sbuf["ebuf"][:, p * 512:(p + 1) * 512], sbuf["d1"][:, bank * 512:(bank + 1) * 512], AF.Exp), inc="ac")
        exp_t[p] = q.n("ac")
    diff_pe = q.n("pe")

    q.wait("vector", "ac", exp_t[7])
    q.wait("vector", "ac", sct_ac)
    q.emit("vector", lambda e: e.tensor_tensor(
        sbuf["mt"][:].rearrange("p (h l) -> p h l", h=NH),
        sbuf["ebuf"][:].rearrange("p (h l) -> p h l", h=NH),
        sbuf["scT"][:].unsqueeze(1).broadcast_to([CH, NH, CH]), ALU.mult), inc="vd")
    mt_vd = q.n("vd")

    q.wait("tensor", "vd", mt_vd)
    for h in range(NH):
        q.emit("tensor", lambda e, h=h: e.matmul(
            ps1[:, h * HD:(h + 1) * HD], sbuf["mt"][:, h * CH:(h + 1) * CH],
            sbuf["xdt"][:, h * HD:(h + 1) * HD], start=(h % 8 == 0), stop=False))
    for j in range(16):
        q.emit("tensor", lambda e, j=j: e.matmul(
            ps1[:, j * CH:(j + 1) * CH],
            sbuf["xdc"][:, j * CH:(j + 1) * CH],
            sbuf["ident"][:], start=False, stop=(j % 4 == 3)),
            inc=("pe" if j == 15 else None))
    y_pe = q.n("pe")

    q.wait("tensor", "vd", scan_vd)
    ybt_vd = {}
    m4_pe = {}

    def m4_mm(qq):
        bank = qq % 3
        q.emit("tensor", lambda e, qq=qq, bank=bank, tl0=tl0: e.matmul(
            ps2[:, bank * 512:(bank + 1) * 512], sbuf["C_fm"][:, tl0:tl0 + CH],
            sbuf["Rb"][:, qq * 512:(qq + 1) * 512], start=True, stop=True), inc="pe")
        m4_pe[qq] = q.n("pe")

    def m4_ev(qq):
        bank = qq % 3
        q.wait("vector", "pe", m4_pe[qq])
        q.emit("vector", lambda e, qq=qq, bank=bank, eacol=eacol: e.tensor_tensor(
            sbuf["ybt"][:].rearrange("p (h d) -> p h d", d=HD),
            ps2[:, bank * 512:(bank + 1) * 512].rearrange("p (h d) -> p h d", d=HD),
            eacol[:, qq * 8:(qq + 1) * 8].unsqueeze(2).broadcast_to([CH, 8, HD]),
            ALU.mult), inc="vd")
        ybt_vd[qq] = q.n("vd")
        q.emit("vector", lambda e, qq=qq: e.scalar_tensor_tensor(
            sbuf["ytile"][:, qq * 512:(qq + 1) * 512], ps1[:, qq * 512:(qq + 1) * 512], 1.0,
            sbuf["ybt"][:], ALU.mult, ALU.add), inc="vd")

    m4_mm(0)
    m4_mm(1)
    m4_mm(2)
    q.wait("vector", "pe", y_pe)
    q.wait("vector", "st_y", kb.ssd_sty)
    m4_ev(0)
    q.wait("tensor", "vd", ybt_vd[0])
    m4_mm(3)
    m4_ev(1)
    m4_ev(2)
    m4_ev(3)
    ytile_vd = q.n("vd")
    q.wait("sync", "vd", ytile_vd)
    q.dma("sync", dram["d_y"][tl0:tl0 + CH, :], sbuf["ytile"][:], "st_y")
    kb.ssd_sty = q.n("st_y")

    kb.ssd_t = dict(
        psS_vd=copies_vd, psS_ac=dec_ac, mt_vd=mt_vd, bw_vd=build_vd,
        diff_pe=diff_pe, m4_pe=m4_pe[3], m4_vd=ytile_vd, ytile_vd=ytile_vd,
    )


def phase_gather_w(kb, sc):
    nc, q, dram, sbuf = kb.nc, kb.q, kb.dram, kb.sbuf
    wsh = kb.sb("wsh", [CH, 2 * FFI], F16, scope=sc)
    wqo = kb.sb("wqo", [CH, FFI], BF16, scope=sc)
    qmx = kb.sb("qmx", [CH, 2 * (FFI // 16)], scope=sc)
    qmxr = kb.sb("qmxr", [CH, FFI // 16], scope=sc)
    qs16 = kb.sb("qs16", [CH, FFI // 16], scope=sc)
    qw = kb.sb("qw", [CH, FFI], F16, scope=sc)
    qp1 = kb.sb("qp1", [CH, FFI], F16, scope=sc)
    qta = kb.sb("qta", [CH, FFI], F16, scope=sc)
    qtb = kb.sb("qtb", [CH, FFI], F16, scope=sc)
    qp23 = kb.sb("qp23", [CH, FFI], F16, scope=sc)

    tiles = []
    for src_t, cc_t, rows, width in (
        ("wg_sh", "cc_g_in", FFI // NCORES, D),
        ("wu_sh", "cc_u_in", FFI // NCORES, D),
        ("wd_sh", "cc_dn_in", D // NCORES, FFI),
    ):
        r0 = 0
        while r0 < rows:
            rr = min(CH, rows - r0)
            tiles.append((src_t, cc_t, r0, rr, width))
            r0 += rr
    srcfree = {}
    pend = None
    for ti, (src_t, cc_t, r0, rr, width) in enumerate(tiles):
        par = ti % 2
        if ti >= 2:
            q.wait("gpsimd", "vd", srcfree[ti - 2])
        q.dma("gpsimd", wsh[0:rr, par * FFI:par * FFI + width],
              dram[src_t][r0:r0 + rr, :], "ld_wq")
        if ti == 1:
            q.dma("gpsimd", kb.dram["d_wxb"], kb.dram["w_xbcT"], "ldw0")
        q.wait("vector", "ld_wq", q.n("ld_wq"))
        qr_t = equant_pre(kb, src=wsh, sl=slice(0, rr), width=width, mx=qmx,
                          mx0=par * (FFI // 16), src0=par * FFI)
        srcfree[ti] = q.n("vd")
        if pend is not None:
            (psrc, pcc, pr0, prr, pwidth, ppar, pqr, pti) = pend
            q.wait("vector", "st_wq")
            equant_post(kb, src=wsh, sl=slice(0, prr), width=pwidth, mx=qmx, mxr=qmxr,
                        s16=qs16, w_=qw, p1=qp1, t_a=qta, t_b=qtb, p23=qp23, out=wqo,
                        qr_tgt=pqr, mx0=ppar * (FFI // 16), src0=ppar * FFI)
            srcfree[pti] = kb.srcfree_vd
            q.wait("sync", "vd")
            q.dma("sync", dram[pcc][pr0:pr0 + prr, :], wqo[0:prr, 0:pwidth], "st_wq")
        pend = (src_t, cc_t, r0, rr, width, par, qr_t, ti)
    (psrc, pcc, pr0, prr, pwidth, ppar, pqr, pti) = pend
    q.wait("vector", "st_wq")
    equant_post(kb, src=wsh, sl=slice(0, prr), width=pwidth, mx=qmx, mxr=qmxr,
                s16=qs16, w_=qw, p1=qp1, t_a=qta, t_b=qtb, p23=qp23, out=wqo,
                qr_tgt=pqr, mx0=ppar * (FFI // 16), src0=ppar * FFI)
    q.wait("sync", "vd")
    q.dma("sync", dram[pcc][pr0:pr0 + prr, :], wqo[0:prr, 0:pwidth], "st_wq")
    q.wait("gpsimd", "st_wq")
    for kind, i, o in (("g", "cc_g_in", "cc_g_out"), ("u", "cc_u_in", "cc_u_out"),
                       ("d", "cc_dn_in", "cc_dn_out")):
        q.emit("gpsimd", lambda e, i=i, o=o: e.collective_compute(
            "AllGather", ALU.bypass, replica_groups=[list(range(NCORES))],
            ins=[dram[i]], outs=[dram[o]]), inc="ccd")


def phase_gather_sl(kb):
    nc, q, dram, sbuf = kb.nc, kb.q, kb.dram, kb.sbuf
    q.wait("gpsimd", "st_s")
    for kind, i, o in (("s", "cc_s_in", "cc_s_out"), ("l", "cc_l_in", "cc_l_out")):
        q.emit("gpsimd", lambda e, i=i, o=o: e.collective_compute(
            "AllGather", ALU.bypass, replica_groups=[list(range(NCORES))],
            ins=[dram[i]], outs=[dram[o]]), inc="ccd")
    q.wait("sync", "ccd")
    kb.tap("t_wg", dram["cc_g_out"])
    kb.tap("t_wdn", dram["cc_dn_out"])
    kb.tap("t_sg", dram["cc_s_out"])
    kb.tap("t_lg", dram["cc_l_out"])


def phase_ffn(kb):
    nc, q, dram, sbuf = kb.nc, kb.q, kb.dram, kb.sbuf
    OGS = [(0, 1024), (1024, 1024), (2048, 768)]
    with contextlib.ExitStack() as sc:
        xqf = kb.sb("xqf", [CH, 8 * TOK], BF16, scope=sc)
        wgt = kb.sb("wgt", [CH, 8 * 1024], BF16, scope=sc)
        wut = kb.sb("wut", [CH, 8 * 1024], BF16, scope=sc)
        sg = kb.sb("sg", [CH, 2 * 1024], F16, scope=sc)
        hbuf = kb.sb("hbuf", [CH, 2 * 1024], F16, scope=sc)
        hq = kb.sb("hq", [CH, 2 * 1024], BF16, scope=sc)
        fmx = kb.sb("fmx", [CH, 128], scope=sc)
        fmxr = kb.sb("fmxr", [CH, 64], scope=sc)
        fs16 = kb.sb("fs16", [CH, 64], scope=sc)
        fw = kb.sb("fw", [CH, 1024], F16, scope=sc)
        fp1 = kb.sb("fp1", [CH, 1024], F16, scope=sc)
        fta = kb.sb("fta", [CH, 1024], F16, scope=sc)
        ftb = kb.sb("ftb", [CH, 1024], F16, scope=sc)
        fp23 = kb.sb("fp23", [CH, 1024], F16, scope=sc)
        psg = kb.ps("psg", [CH, 2048], scope=sc)
        psu = kb.ps("psu", [CH, 2048], scope=sc)

        q.wait("scalar", "st_xq")
        q.wait("scalar", "ccd")
        q.wait("scalar", "st_wq")
        if "t_xq3" in kb.taps:
            q.wait("sync", "st_xq")
            q.dma("sync", hq[:, 0:D], dram["d_xq"][0:CH, :], "ld_dbg")
            q.wait("sync", "ld_dbg")
            kb.tap("t_xq3", hq[:, 0:D])
            q.wait("vector", "st_tap")
        for k in range(8):
            q.dma("scalar", xqf[:, k * TOK:(k + 1) * TOK],
                  dram["d_xq"][:, k * CH:(k + 1) * CH], "ld_ff", transpose=True)
        q.wait("sync", "ld_ff")
        kb.tap("t_xqf", xqf[:])
        ac_t = {}
        vd_t = {}
        pe_t = {}
        st_t = {}
        gtc = 0
        pend = None
        for og0, ogw in OGS:
            q.wait("scalar", "pe")
            for k in range(8):
                q.dma("scalar", wgt[:, k * 1024:k * 1024 + ogw],
                      dram["cc_g_out"][og0:og0 + ogw, k * CH:(k + 1) * CH], "ld_ff",
                      transpose=True)
                q.dma("scalar", wut[:, k * 1024:k * 1024 + ogw],
                      dram["cc_u_out"][og0:og0 + ogw, k * CH:(k + 1) * CH], "ld_ff",
                      transpose=True)
            q.wait("tensor", "ld_ff")
            for tc in range(16):
                p = gtc % 2
                if gtc >= 2:
                    q.wait("tensor", "ac", ac_t[gtc - 2])
                    q.wait("tensor", "vd", vd_t[gtc - 2])
                pieces = [(0, 512), (512, ogw - 512)]
                for k in range(8):
                    for f0, fw_ in pieces:
                        q.emit("tensor", lambda e, tc=tc, k=k, f0=f0, fw_=fw_, p=p:
                               e.matmul(psg[:, p * 1024 + f0:p * 1024 + f0 + fw_],
                                        xqf[:, k * TOK + tc * CH:k * TOK + (tc + 1) * CH],
                                        wgt[:, k * 1024 + f0:k * 1024 + f0 + fw_],
                                        start=(k == 0), stop=(k == 7)))
                for k in range(8):
                    for f0, fw_ in pieces:
                        q.emit("tensor", lambda e, tc=tc, k=k, f0=f0, fw_=fw_, p=p:
                               e.matmul(psu[:, p * 1024 + f0:p * 1024 + f0 + fw_],
                                        xqf[:, k * TOK + tc * CH:k * TOK + (tc + 1) * CH],
                                        wut[:, k * 1024 + f0:k * 1024 + f0 + fw_],
                                        start=(k == 0), stop=(k == 7)),
                               inc=("pe" if (k == 7 and f0 != 0) else None))
                pe_t[gtc] = q.n("pe")
                q.wait("scalar", "pe", pe_t[gtc])
                q.emit("scalar", lambda e, ogw=ogw, p=p: e.activation(
                    sg[:, p * 1024:p * 1024 + ogw], psg[:, p * 1024:p * 1024 + ogw],
                    AF.Silu), inc="ac")
                ac_t[gtc] = q.n("ac")
                q.wait("vector", "ac", ac_t[gtc])
                q.emit("vector", lambda e, ogw=ogw, p=p: e.tensor_tensor(
                    hbuf[:, p * 1024:p * 1024 + ogw], sg[:, p * 1024:p * 1024 + ogw],
                    psu[:, p * 1024:p * 1024 + ogw], ALU.mult), inc="vd")
                vd_t[gtc] = q.n("vd")
                qr_t = equant_pre(kb, src=hbuf, sl=slice(0, CH), width=ogw, mx=fmx,
                                  mx0=p * 64, src0=p * 1024)
                if pend is not None:
                    pgtc, ptc, pog0, pogw, pp, pqr = pend
                    q.wait("vector", "st_hq", st_t.get(pgtc - 2, 0))
                    equant_post(kb, src=hbuf, sl=slice(0, CH), width=pogw, mx=fmx,
                                mxr=fmxr, s16=fs16, w_=fw, p1=fp1, t_a=fta, t_b=ftb,
                                p23=fp23, out=hq, qr_tgt=pqr,
                                osl=slice(pp * 1024, pp * 1024 + pogw),
                                mx0=pp * 64, src0=pp * 1024)
                    q.wait("sync", "vd")
                    q.dma("sync", dram["d_hq"][ptc * CH:(ptc + 1) * CH, pog0:pog0 + pogw],
                          hq[0:CH, pp * 1024:pp * 1024 + pogw], "st_hq")
                    st_t[pgtc] = q.n("st_hq")
                pend = (gtc, tc, og0, ogw, p, qr_t)
                gtc += 1
        pgtc, ptc, pog0, pogw, pp, pqr = pend
        q.wait("vector", "st_hq", st_t.get(pgtc - 2, 0))
        equant_post(kb, src=hbuf, sl=slice(0, CH), width=pogw, mx=fmx,
                    mxr=fmxr, s16=fs16, w_=fw, p1=fp1, t_a=fta, t_b=ftb,
                    p23=fp23, out=hq, qr_tgt=pqr,
                    osl=slice(pp * 1024, pp * 1024 + pogw),
                    mx0=pp * 64, src0=pp * 1024)
        q.wait("sync", "vd")
        q.dma("sync", dram["d_hq"][ptc * CH:(ptc + 1) * CH, pog0:pog0 + pogw],
              hq[0:CH, pp * 1024:pp * 1024 + pogw], "st_hq")
        st_t[pgtc] = q.n("st_hq")
    q.wait("sync", "st_hq")
    kb.tap("t_hq", dram["d_hq"])

    q.tag = "ffn_dn"
    with contextlib.ExitStack() as sc:
        dnt = kb.sb("dnt", [CH, 22 * 1024], BF16, scope=sc)
        hqt = kb.sb("hqt", [CH, 2 * 512], BF16, scope=sc)
        psD = kb.ps("psD", [CH, 2048], scope=sc)
        psD2 = kb.ps("psD2", [CH, 2048], scope=sc)
        q.wait("scalar", "ccd")
        q.wait("scalar", "st_hq")
        q.wait("scalar", "vd")
        for o in range(22):
            q.dma("scalar", dnt[:, o * 1024:(o + 1) * 1024],
                  dram["cc_dn_out"][0:D, o * CH:(o + 1) * CH], "ld_dn", transpose=True)
        q.wait("tensor", "ld_dn")
        q.wait("tensor", "vd")
        dpe = {}
        dld = {}
        for t in range(4):
            q.wait("tensor", "ac")
            for o in range(22):
                buf = o % 2
                go = t * 22 + o
                if go >= 2:
                    q.wait("scalar", "pe", dpe[go - 2])
                q.dma("scalar", hqt[:, buf * 512:(buf + 1) * 512],
                      dram["d_hq"][t * 512:(t + 1) * 512, o * CH:(o + 1) * CH], "ld_hf",
                      transpose=True)
                dld[go] = q.n("ld_hf")
                q.wait("tensor", "ld_hf", dld[go])
                for ee in range(8):
                    ps = psD if ee < 4 else psD2
                    q.emit("tensor", lambda e, o=o, ee=ee, buf=buf, ps=ps:
                           e.matmul(ps[:, (ee % 4) * 512:(ee % 4 + 1) * 512],
                                    dnt[:, o * 1024 + ee * CH:o * 1024 + (ee + 1) * CH],
                                    hqt[:, buf * 512:(buf + 1) * 512],
                                    start=(o == 0), stop=(o == 21)),
                           inc=("pe" if ee == 7 else None))
                dpe[go] = q.n("pe")
            q.wait("scalar", "pe")
            for ee in range(8):
                ps = psD if ee < 4 else psD2
                q.emit("scalar", lambda e, t=t, ee=ee, ps=ps: e.activation(
                    sbuf["g_fm"][:, ee * TOK + t * 512:ee * TOK + (t + 1) * 512],
                    ps[:, (ee % 4) * 512:(ee % 4 + 1) * 512], AF.Silu), inc="ac")
    q.wait("sync", "st_hq")
    kb.tap("t_gfm", kb.sbuf["g_fm"][:])


def phase_tail(kb):
    nc, q, dram, sbuf = kb.nc, kb.q, kb.dram, kb.sbuf
    with contextlib.ExitStack() as sc:
        Lmat = kb.sb("Lmat", [NH, NCORES], scope=sc)
        etmp = kb.sb("etmp", [NH, NCORES * NCORES], scope=sc)
        esum = kb.sb("esum", [NH, NCORES], scope=sc)
        coef = kb.sb("coef", [NH, NCORES], scope=sc)
        crow = kb.sb("crow", [1, NCORES * NH], F16, scope=sc)
        cbc = kb.sb("cbc", [CH, NCORES * NH], scope=sc)
        Sj = kb.sb("Sj", [DS, 2 * NH * HD], scope=sc)
        Rtmp2 = kb.sb("Rtmp2", [DS, NH * HD], scope=sc)
        Rin = kb.sb("Rin", [DS, NH * HD], scope=sc)
        Rinb = kb.sb("Rinb", [DS, NH * HD], BF16, scope=sc)
        lcz = kb.sb("lcz2", [NH, NCH], scope=sc)
        lrow = kb.sb("lrow", [1, NCH * NH], F16, scope=sc)
        ead = kb.sb("ead", [CH, 2 * NH], scope=sc)
        ycorr = kb.sb("ycorr", [CH, 512], scope=sc)
        yload = kb.sb("yload", [CH, 2 * DI], BF16, scope=sc)
        psC = kb.ps("psC", [CH, 2048], scope=sc)
        psS2 = kb.ps("psS2", [CH, 512], scope=sc)

        q.wait("sync", "ccd")
        q.wait("sync", "pe")
        q.wait("sync", "ac")
        q.dma("sync", Lmat[:], dram["cc_l_out"].rearrange("(r h) b -> h (r b)", h=NH),
              "ld_t", allow_slow_non_contiguous=True)
        q.wait("vector", "ld_t")
        q.emit("vector", lambda e: e.tensor_tensor(
            etmp[:].rearrange("p (j i) -> p j i", j=NCORES),
            Lmat[:].unsqueeze(1).broadcast_to([NH, NCORES, NCORES]),
            sbuf["bmx_s"][:].rearrange("p (j i) -> p j i", j=NCORES), ALU.mult), inc="vd")
        q.wait("vector", "vd")
        q.emit("vector", lambda e: e.tensor_reduce(
            esum[:].unsqueeze(2), etmp[:].rearrange("p (j i) -> p j i", j=NCORES),
            AX.X, ALU.add))
        q.emit("vector", lambda e: e.drain(), inc="vd")
        q.wait("scalar", "vd")
        q.emit("scalar", lambda e: e.activation(esum[:], esum[:], AF.Exp))
        q.emit("scalar", lambda e: e.drain(), inc="ac")
        q.wait("vector", "ac")
        q.emit("vector", lambda e: e.tensor_tensor(coef[:], esum[:], sbuf["pmx_s"][:], ALU.mult), inc="vd")
        q.wait("sync", "vd")
        q.dma("sync", dram["d_coef"].rearrange("j h -> h j"), coef[:], "st_t",
              allow_slow_non_contiguous=True)
        q.wait("sync", "st_t")
        kb.tap("t_coef", dram["d_coef"])
        q.wait("gpsimd", "st_t")
        q.emit("gpsimd", lambda e: e.memset(Rin[:], 0.0), inc="gp")
        q.dma("gpsimd", crow[:], dram["d_coef"].rearrange("j h -> (j h)").unsqueeze(0),
              "ld_tg")
        q.wait("tensor", "ld_tg", q.n("ld_tg"))
        q.wait("tensor", "vd")
        for j in range(NCORES):
            q.emit("tensor", lambda e, j=j: e.matmul(
                psS2[:, j * NH:(j + 1) * NH], sbuf["ones1_bf"][:],
                crow[0:1, j * NH:(j + 1) * NH], start=True, stop=True),
                inc=("pe" if j == NCORES - 1 else None))
        cbc_pe = q.n("pe")
        sld = {}
        q.wait("sync", "vd")
        q.dma("sync", Sj[:, 0:NH * HD], dram["cc_s_out"][0:DS, :], "ld_t")
        sld[0] = q.n("ld_t")
        q.wait("vector", "pe", cbc_pe)
        q.wait("vector", "gp")
        q.emit("vector", lambda e: e.tensor_copy(cbc[:], psS2[:, 0:NCORES * NH]), inc="vd")
        rt_t = {}
        for j in range(NCORES):
            jp = j % 2
            if j + 1 < NCORES:
                if j >= 1:
                    q.wait("sync", "vd", rt_t[j - 1])
                q.dma("sync", Sj[:, (1 - jp) * NH * HD:(2 - jp) * NH * HD],
                      dram["cc_s_out"][(j + 1) * DS:(j + 2) * DS, :], "ld_t")
                sld[j + 1] = q.n("ld_t")
            q.wait("vector", "ld_t", sld[j])
            q.emit("vector", lambda e, j=j, jp=jp: e.tensor_tensor(
                Rtmp2[:].rearrange("p (h d) -> p h d", d=HD),
                Sj[:, jp * NH * HD:(jp + 1) * NH * HD].rearrange("p (h d) -> p h d", d=HD),
                cbc[:, j * NH:(j + 1) * NH].unsqueeze(2).broadcast_to([DS, NH, HD]),
                ALU.mult), inc="vd")
            rt_t[j] = q.n("vd")
            q.wait("vector", "vd")
            q.emit("vector", lambda e: e.tensor_tensor(Rin[:], Rin[:], Rtmp2[:], ALU.add), inc="vd")
        q.wait("vector", "vd")
        q.emit("vector", lambda e: e.tensor_copy(Rinb[:], Rin[:]), inc="vd")

        q.emit("vector", lambda e: e.memset(lcz[:, 0:1], 0.0), inc="vd")
        for c in range(1, NCH):
            q.emit("vector", lambda e, c=c: e.tensor_tensor(
                lcz[:, c:c + 1], lcz[:, c - 1:c], sbuf["az_fm"][:, c - 1:c], ALU.add))
            q.emit("vector", lambda e: e.drain(), inc="vd")
        q.wait("sync", "vd")
        q.dma("sync", dram["d_lcz"].rearrange("c h -> h c"), lcz[:], "st_t",
              allow_slow_non_contiguous=True)
        q.wait("sync", "st_t")
        kb.tap("t_lcz", dram["d_lcz"])
        q.wait("sync", "vd")
        kb.tap("t_rin", Rin[:])

        q.tag = "tail_corr"
        q.wait("gpsimd", "st_t")
        q.wait("gpsimd", "st_y")
        q.dma("gpsimd", lrow[:], dram["d_lcz"].rearrange("c h -> (c h)").unsqueeze(0),
              "ld_tg")
        lrow_t = q.n("ld_tg")
        q.dma("gpsimd", yload[:, 0:DI], dram["d_y"][0:CH, :], "ld_y")
        yl_t = {0: q.n("ld_y")}
        q.wait("tensor", "ld_tg", lrow_t)
        q.wait("tensor", "vd")
        exp_t = {}
        mm3_t = {}
        pc_t = {}
        yc_t = {}
        yc_half = {}
        styt = {}
        for gc in range(NCH):
            c2 = gc % 2
            tl0 = gc * CH
            ahc = sbuf["a_hi"][:, gc * NH:(gc + 1) * NH]
            alc = sbuf["a_lo"][:, gc * NH:(gc + 1) * NH]
            s0 = 256 + c2 * 64
            if gc >= 2:
                q.wait("tensor", "ac", exp_t[gc - 2])
            q.emit("tensor", lambda e, ahc=ahc, s0=s0: e.matmul(psS2[:, s0:s0 + NH], sbuf["ut_incl"][:], ahc, start=True, stop=False))
            q.emit("tensor", lambda e, alc=alc, s0=s0: e.matmul(psS2[:, s0:s0 + NH], sbuf["ut_incl"][:], alc, start=False, stop=False))
            q.emit("tensor", lambda e, gc=gc, s0=s0: e.matmul(psS2[:, s0:s0 + NH], sbuf["ones1_bf"][:], lrow[0:1, gc * NH:(gc + 1) * NH], start=False, stop=True), inc="pe")
            mm3_t[gc] = q.n("pe")
            q.wait("scalar", "pe", mm3_t[gc])
            if gc >= 2:
                q.wait("scalar", "vd", yc_t[gc - 2])
            q.emit("scalar", lambda e, c2=c2, s0=s0: e.activation(
                ead[:, c2 * NH:(c2 + 1) * NH], psS2[:, s0:s0 + NH], AF.Exp), inc="ac")
            exp_t[gc] = q.n("ac")
            if gc + 1 < NCH:
                if gc >= 1:
                    q.wait("gpsimd", "st_y", styt[gc - 1])
                q.dma("gpsimd", yload[:, (1 - c2) * DI:(2 - c2) * DI],
                      dram["d_y"][(gc + 1) * CH:(gc + 2) * CH, :], "ld_y")
                yl_t[gc + 1] = q.n("ld_y")
            for half in range(2):
                if gc >= 1:
                    q.wait("tensor", "vd", yc_half[(gc - 1, half)])
                for qq in (half * 2, half * 2 + 1):
                    q.emit("tensor", lambda e, qq=qq, tl0=tl0: e.matmul(
                        psC[:, qq * 512:(qq + 1) * 512], sbuf["C_fm"][:, tl0:tl0 + CH],
                        Rinb[:, qq * 512:(qq + 1) * 512], start=True, stop=True),
                        inc=("pe" if qq % 2 == 1 else None))
                pc_t[(gc, half)] = q.n("pe")
            q.wait("vector", "ac", exp_t[gc])
            q.wait("vector", "ld_y", yl_t[gc])
            for half in range(2):
                q.wait("vector", "pe", pc_t[(gc, half)])
                for qq in (half * 2, half * 2 + 1):
                    q.emit("vector", lambda e, qq=qq, c2=c2: e.tensor_tensor(
                        ycorr[:].rearrange("p (h d) -> p h d", d=HD),
                        psC[:, qq * 512:(qq + 1) * 512].rearrange("p (h d) -> p h d", d=HD),
                        ead[:, c2 * NH + qq * 8:c2 * NH + (qq + 1) * 8].unsqueeze(2).broadcast_to([CH, 8, HD]),
                        ALU.mult), inc="vd")
                    q.wait("vector", "vd")
                    q.emit("vector", lambda e, qq=qq, c2=c2: e.tensor_tensor(
                        yload[:, c2 * DI + qq * 512:c2 * DI + (qq + 1) * 512],
                        yload[:, c2 * DI + qq * 512:c2 * DI + (qq + 1) * 512],
                        ycorr[:], ALU.add), inc="vd")
                yc_half[(gc, half)] = q.n("vd")
            yc_t[gc] = q.n("vd")
            q.wait("sync", "vd", yc_t[gc])
            q.dma("sync", dram["d_y"][tl0:tl0 + CH, :], yload[:, c2 * DI:(c2 + 1) * DI], "st_y")
            styt[gc] = q.n("st_y")
        q.wait("sync", "st_y")
        kb.tap("t_y2", dram["d_y"])

    with contextlib.ExitStack() as sct:
        t1 = kb.sb("t1", [CH, 16 * TOK], BF16, scope=sct)
        ffff = phase_tail2(kb, sct, t1)


def phase_tail2(kb, sct, t1):
    nc, q, dram, sbuf = kb.nc, kb.q, kb.dram, kb.sbuf
    q.tag = "ztail"
    with contextlib.ExitStack() as sc:
        xn2 = kb.sb("xn2", [CH, 8 * TOKH], BF16, scope=sc)
        wz = kb.sb("wz", [CH, 2 * 8 * CH], BF16, scope=sc)
        szc = kb.sb("szc", [CH, TOK], F16, scope=sc)
        yfm = kb.sb("yfm", [CH, 2 * TOK], BF16, scope=sc)
        sq2 = kb.sb("sq2", [CH, 3 * 1024], BF16, scope=sc)
        ones32b = kb.sb("ones32b", [CH, NH], BF16, scope=sc)
        rrow = kb.sb("rrow", [NH, TOK], scope=sc)
        psZ = kb.ps("psZ", [CH, 2048], scope=sc)
        psR = kb.ps("psR", [CH, 2048], scope=sc)

        q.wait("gpsimd", "st_y")
        q.emit("gpsimd", lambda e: e.memset(ones32b[:], 1.0), inc="gp")
        q.wait("sync", "st_xn")
        q.wait("sync", "st_y")
        q.dma("sync", xn2[:], dram["d_xn"], "ld_t")
        xn2_t = q.n("ld_t")
        q.wait("sync", "ldwz", 16)
        q.wait("scalar", "st_y")
        zmm_t = {}
        silu_t = {}
        gate_t = {}
        sq2_t = {}
        rms_t = {}
        wz_t = {}
        yfm_t = {}
        for fc in range(16):
            pf = fc % 2
            if fc == 0:
                q.dma("sync", wz[:, 0:8 * CH].rearrange("p (k w) -> p k w", k=8),
                      dram["d_wz"][:, 0:CH].rearrange("(k p) w -> p k w", p=CH), "ld_wz")
                wz_t[0] = q.n("ld_wz")
                q.dma("scalar", yfm[:, 0:TOK], dram["d_y"][:, 0:CH], "ld_yf",
                      transpose=True)
                yfm_t[0] = q.n("ld_yf")
            if fc + 1 < 16:
                if fc >= 1:
                    q.wait("sync", "pe", zmm_t[2 * fc - 1])
                q.dma("sync", wz[:, (1 - pf) * 8 * CH:(2 - pf) * 8 * CH].rearrange("p (k w) -> p k w", k=8),
                      dram["d_wz"][:, (fc + 1) * CH:(fc + 2) * CH].rearrange("(k p) w -> p k w", p=CH),
                      "ld_wz")
                wz_t[fc + 1] = q.n("ld_wz")
                if fc >= 1:
                    q.wait("scalar", "vd", gate_t[2 * fc - 1])
                q.dma("scalar", yfm[:, (1 - pf) * TOK:(2 - pf) * TOK],
                      dram["d_y"][:, (fc + 1) * CH:(fc + 2) * CH], "ld_yf",
                      transpose=True)
                yfm_t[fc + 1] = q.n("ld_yf")
            for th in range(2):
                i = fc * 2 + th
                ip = i % 2
                if i == 0:
                    q.wait("tensor", "ld_t", xn2_t)
                    q.wait("tensor", "ac")
                    q.wait("tensor", "vd")
                if th == 0:
                    q.wait("tensor", "ld_wz", wz_t[fc])
                if i >= 2:
                    q.wait("tensor", "ac", silu_t[i - 2])
                for tt_ in range(2):
                    for k in range(8):
                        q.emit("tensor", lambda e, th=th, tt_=tt_, k=k, pf=pf, ip=ip:
                               e.matmul(psZ[:, ip * 1024 + tt_ * 512:ip * 1024 + (tt_ + 1) * 512],
                                        wz[:, pf * 1024 + k * CH:pf * 1024 + (k + 1) * CH],
                                        xn2[:, k * TOKH + HALO + th * 1024 + tt_ * 512: k * TOKH + HALO + th * 1024 + (tt_ + 1) * 512],
                                        start=(k == 0), stop=(k == 7)),
                               inc=("pe" if (k == 7 and tt_ == 1) else None))
                zmm_t[i] = q.n("pe")
                q.wait("scalar", "pe", zmm_t[i])
                if i >= 2:
                    q.wait("scalar", "vd", gate_t[i - 2])
                q.emit("scalar", lambda e, ip=ip: e.activation(
                    szc[:, ip * 1024:(ip + 1) * 1024], psZ[:, ip * 1024:(ip + 1) * 1024],
                    AF.Silu), inc="ac")
                silu_t[i] = q.n("ac")
                q.wait("vector", "ld_yf", yfm_t[fc])
                q.wait("vector", "ac", silu_t[i])
                q.emit("vector", lambda e, fc=fc, th=th, pf=pf, ip=ip: e.tensor_tensor(
                    t1[:, fc * TOK + th * 1024:fc * TOK + (th + 1) * 1024],
                    yfm[:, pf * TOK + th * 1024:pf * TOK + (th + 1) * 1024],
                    szc[:, ip * 1024:(ip + 1) * 1024], ALU.mult), inc="vd")
                gate_t[i] = q.n("vd")
                q.wait("scalar", "vd", gate_t[i])
                i3 = i % 3
                if i >= 3:
                    q.wait("scalar", "pe", rms_t[i - 3])
                q.emit("scalar", lambda e, fc=fc, th=th, i3=i3: e.activation(
                    sq2[:, i3 * 1024:(i3 + 1) * 1024],
                    t1[:, fc * TOK + th * 1024:fc * TOK + (th + 1) * 1024], AF.Square), inc="ac")
                sq2_t[i] = q.n("ac")
                for j in ([i - 2] if i >= 2 else []) + ([i - 1, i] if i == 31 else []):
                    jfc, jth, j3 = j // 2, j % 2, j % 3
                    q.wait("tensor", "ac", sq2_t[j])
                    for tt_ in range(2):
                        q.emit("tensor", lambda e, jfc=jfc, jth=jth, tt_=tt_, j3=j3: e.matmul(
                            psR[0:NH, jth * 1024 + tt_ * 512:jth * 1024 + (tt_ + 1) * 512],
                            ones32b[:], sq2[:, j3 * 1024 + tt_ * 512:j3 * 1024 + (tt_ + 1) * 512],
                            start=(jfc == 0), stop=(jfc == 15)),
                            inc=("pe" if tt_ == 1 else None))
                    rms_t[j] = q.n("pe")
        q.wait("vector", "pe")
        q.emit("vector", lambda e: e.tensor_copy(rrow[:], psR[0:NH, 0:TOK]), inc="vd")
        q.wait("vector", "vd")
        q.emit("vector", lambda e: e.tensor_scalar(rrow[:], rrow[:], 1.0 / DI, None, ALU.mult))
        q.emit("vector", lambda e: e.drain(), inc="vd")
        q.wait("scalar", "vd")
        q.emit("scalar", lambda e: e.activation(rrow[:], rrow[:], AF.Ln, bias=sbuf["epscol"][0:NH, 0:1]))
        q.emit("scalar", lambda e: e.drain())
        q.emit("scalar", lambda e: e.activation(rrow[:], rrow[:], AF.Exp, scale=-0.5))
        q.emit("scalar", lambda e: e.drain(), inc="ac")
        q.wait("sync", "ac")
        q.dma("sync", dram["d_rstd"], rrow[0:1, :], "st_t")
        q.wait("sync", "st_t")
        kb.tap("t_rstd", dram["d_rstd"])
        kb.tap("t_t1", t1[:])

    q.tag = "oproj"
    with contextlib.ExitStack() as sc2:
        wo = kb.sb("wo", [CH, 16 * D], BF16, scope=sc2)
        rfm = kb.sb("rfm", [CH, 8 * 256], BF16, scope=sc2)
        utile2 = kb.sb("utile2", [CH, 2 * D], scope=sc2)
        rcol2 = kb.sb("rcol2", [CH, 2], scope=sc2)
        otile = kb.sb("otile", [CH, 2 * D], scope=sc2)
        psO = kb.ps("psO", [CH, 2048], scope=sc2)
        psO2 = kb.ps("psO2", [CH, 2048], scope=sc2)
        q.wait("scalar", "st_t")
        q.wait("scalar", "ldwo", 16)
        q.dma("scalar", wo[:].rearrange("p (i w) -> p i w", i=16),
              dram["d_wout"].rearrange("(i p) w -> p i w", p=CH), "ld_wo")
        q.wait("vector", "ld_wo")
        for i in range(16):
            q.emit("vector", lambda e, i=i: e.tensor_scalar(
                wo[:, i * D:(i + 1) * D], wo[:, i * D:(i + 1) * D],
                sbuf["nwcol"][:, i:i + 1], None, ALU.mult), inc="vd")
        q.wait("vector", "ac")
        q.wait("gpsimd", "vd")
        q.wait("gpsimd", "st_t")
        u2t = {}
        for tc in (0, 1):
            q.dma("gpsimd", utile2[:, (tc % 2) * D:(tc % 2 + 1) * D],
                  dram["u"][HALO + tc * CH:HALO + (tc + 1) * CH, :], "ld_u2")
            q.dma("gpsimd", rcol2[:, tc % 2:tc % 2 + 1],
                  dram["d_rstd"][0:1, tc * CH:(tc + 1) * CH].rearrange("a p -> (a p)").unsqueeze(1), "ld_u2")
            u2t[tc] = q.n("ld_u2")
        rfm_t = {}
        trans_t = {}
        fin_t = {}
        sto_t = {}
        mmO_t = {}
        q.wait("tensor", "vd")
        q.wait("tensor", "ac")
        for s in range(8):
            if s >= 1:
                q.wait("tensor", "vd", rfm_t[s - 1])
            for ee in range(8):
                for i in range(16):
                    q.emit("tensor", lambda e, s=s, i=i, ee=ee: e.matmul(
                        psO[:, ee * 256:(ee + 1) * 256],
                        wo[:, i * D + ee * CH:i * D + (ee + 1) * CH],
                        t1[:, i * TOK + s * 256:i * TOK + (s + 1) * 256],
                        start=(i == 0), stop=(i == 15)),
                        inc=("pe" if (i == 15 and ee == 7) else None))
            mmO_t[s] = q.n("pe")
            q.wait("vector", "pe", mmO_t[s])
            for ee in range(8):
                q.emit("vector", lambda e, s=s, ee=ee: e.tensor_tensor(
                    rfm[:, ee * 256:(ee + 1) * 256],
                    psO[:, ee * 256:(ee + 1) * 256],
                    sbuf["g_fm"][:, ee * TOK + s * 256:ee * TOK + (s + 1) * 256],
                    ALU.mult), inc="vd")
            rfm_t[s] = q.n("vd")
            q.wait("tensor", "vd", rfm_t[s])
            if s >= 1:
                q.wait("tensor", "vd", fin_t[2 * s - 1])
            for tc2 in range(2):
                for ee in range(8):
                    q.emit("tensor", lambda e, tc2=tc2, ee=ee: e.matmul(
                        psO2[:, tc2 * 1024 + ee * CH:tc2 * 1024 + (ee + 1) * CH],
                        rfm[:, ee * 256 + tc2 * CH:ee * 256 + (tc2 + 1) * CH],
                        sbuf["ident"][:], start=True, stop=True),
                        inc=("pe" if ee == 7 else None))
                trans_t[(s, tc2)] = q.n("pe")
            for tc2 in range(2):
                tc = 2 * s + tc2
                q.wait("vector", "pe", trans_t[(s, tc2)])
                q.wait("vector", "ld_u2", u2t[tc])
                q.wait("vector", "st_out", sto_t.get(tc - 2, 0))
                q.emit("vector", lambda e, tc=tc, tc2=tc2: e.scalar_tensor_tensor(
                    otile[:, (tc % 2) * D:(tc % 2 + 1) * D], psO2[:, tc2 * 1024:tc2 * 1024 + D],
                    rcol2[:, tc % 2:tc % 2 + 1],
                    utile2[:, (tc % 2) * D:(tc % 2 + 1) * D], ALU.mult, ALU.add), inc="vd")
                fin_t[tc] = q.n("vd")
                q.wait("sync", "vd", fin_t[tc])
                q.dma("sync", dram["out"][tc * CH:(tc + 1) * CH, :],
                      otile[:, (tc % 2) * D:(tc % 2 + 1) * D], "st_out")
                sto_t[tc] = q.n("st_out")
                if tc + 2 < 16:
                    q.wait("gpsimd", "vd", fin_t[tc])
                    q.dma("gpsimd", utile2[:, (tc % 2) * D:(tc % 2 + 1) * D],
                          dram["u"][HALO + (tc + 2) * CH:HALO + (tc + 3) * CH, :], "ld_u2")
                    q.dma("gpsimd", rcol2[:, tc % 2:tc % 2 + 1],
                          dram["d_rstd"][0:1, (tc + 2) * CH:(tc + 3) * CH].rearrange("a p -> (a p)").unsqueeze(1), "ld_u2")
                    u2t[tc + 2] = q.n("ld_u2")


def _prep_inputs(inputs):
    u = np.asarray(inputs["L0_attn_u"])
    ipw = np.asarray(inputs["L0_attn_in_proj_w"])
    conv_w = np.asarray(inputs["L0_attn_conv_w"])
    conv_b = np.asarray(inputs["L0_attn_conv_b"])
    dt_bias = np.asarray(inputs["L0_attn_dt_bias"])
    a_log = np.asarray(inputs["L0_attn_A_log"])
    Dv = np.asarray(inputs["L0_attn_D"])
    norm_w = np.asarray(inputs["L0_attn_norm_w"])
    opw = np.asarray(inputs["L0_attn_out_proj_w"])
    wg = np.asarray(inputs["L0_ffn_w_gate"])
    wu = np.asarray(inputs["L0_ffn_w_up"])
    wdn = np.asarray(inputs["L0_ffn_w_down"])

    w_zT = np.ascontiguousarray(ipw[:DI].T.astype(np.float32))
    w_xbcT = np.ascontiguousarray(ipw[DI:].T.astype(np.float32))
    w_outT = np.ascontiguousarray(opw.T.astype(np.float32))
    conv_w4 = np.ascontiguousarray(conv_w[:, 0, :].astype(np.float32))
    dcol64 = np.ascontiguousarray(np.repeat(np.asarray(Dv, np.float32), HD).reshape(-1, 1))

    base = dict(
        w_xbcT=w_xbcT, w_zT=w_zT, w_outT=w_outT,
        conv_w4=conv_w4,
        conv_b=np.ascontiguousarray(conv_b.reshape(-1, 1), np.float32),
        dt_bias=np.ascontiguousarray(dt_bias.reshape(-1, 1), np.float32),
        a_log=np.ascontiguousarray(a_log.reshape(-1, 1), np.float32),
        dcol64=dcol64,
        norm_w=np.ascontiguousarray(norm_w.reshape(-1, 1), np.float32),
    )
    in_maps = []
    for c in range(NCORES):
        b, sidx = divmod(c, 4)
        s0 = sidx * TOK
        if s0 == 0:
            upad = np.concatenate([np.zeros((HALO, D), np.float32), u[b, :TOK]], 0)
        else:
            upad = u[b, s0 - HALO:s0 + TOK]
        grp0 = (c // 4) * 4
        prevmask = np.zeros((NCORES,), np.float32)
        binmask = np.zeros((NCORES, NCORES), np.float32)
        for j in range(grp0, c):
            prevmask[j] = 1.0
            for i in range(j + 1, c):
                binmask[j, i] = 1.0
        m = dict(base)
        m["u"] = np.ascontiguousarray(upad, np.float32)
        m["pmx"] = np.ascontiguousarray(np.repeat(prevmask.reshape(1, -1), NH, 0))
        m["bmx"] = np.ascontiguousarray(np.repeat(binmask.reshape(1, -1), NH, 0))
        shf = FFI // NCORES
        shd = D // NCORES
        m["wg_sh"] = np.ascontiguousarray(wg[c * shf:(c + 1) * shf], np.float32)
        m["wu_sh"] = np.ascontiguousarray(wu[c * shf:(c + 1) * shf], np.float32)
        m["wd_sh"] = np.ascontiguousarray(wdn[c * shd:(c + 1) * shd], np.float32)
        in_maps.append(m)
    return in_maps


def kernel(**inputs):
    in_maps = _prep_inputs(inputs)
    nc = build(stage=STAGE, debug_taps=DEBUG_TAPS)
    res = run_bass_kernel_spmd(nc, in_maps, core_ids=list(range(NCORES)))
    full = np.zeros((2, 8192, D), np.float32)
    for c in range(NCORES):
        b, sidx = divmod(c, 4)
        full[b, sidx * TOK:(sidx + 1) * TOK] = res.results[c]["out"]
    return full


# version 44
# speedup vs baseline: 1.0354x; 1.0336x over previous
import contextlib
import numpy as np
import ml_dtypes
import concourse.bass as bass
import concourse.mybir as mybir
from concourse.bass_utils import run_bass_kernel_spmd

F32 = mybir.dt.float32
F16 = mybir.dt.float16
BF16 = mybir.dt.bfloat16
AF = mybir.ActivationFunctionType
ALU = mybir.AluOpType
AX = mybir.AxisListType

NCORES = 8
TOK = 2048
HALO = 3
TOKH = TOK + HALO
D = 1024
DI = 2048
NH = 32
HD = 64
DS = 128
EXBC = 2336
NCH = 16
CH = 128
FFI = 2816
EPS_LN = 1e-5
EPS_RMS = 1e-5
M_H = float(1.5 * 2**22)
M_1 = float(1.5 * 2**23)
M_2 = float(1.5 * 2**24)
NEG = -30000.0

DEBUG_TAPS = {}
STAGE = 99
import os as _os
SSD_STEPS = int(_os.environ.get("SSD_STEPS", "99"))
SSD_NCHUNK = int(_os.environ.get("SSD_NCHUNK", "99"))


class Q:
    ENGINES = ("sync", "vector", "scalar", "tensor", "gpsimd")

    def __init__(self, nc):
        self.nc = nc
        self.steps = {e: [] for e in self.ENGINES}
        self.sems = {}
        self.cnt = {}
        self.ctx = None
        self.tag = "init"

    def sem(self, name):
        if name not in self.sems:
            self.sems[name] = self.ctx.enter_context(self.nc.semaphore(name))
            self.cnt[name] = 0
        return self.sems[name]

    def wait(self, eng, name, n=None):
        s = self.sem(name)
        tgt = self.cnt[name] if n is None else n
        if tgt > 0:
            self.steps[eng].append((self.tag, lambda e, s=s, t=tgt: e.wait_ge(s, t)))
        return tgt

    def emit(self, eng, fn, inc=None, k=1):
        if inc is not None:
            s = self.sem(inc)
            self.cnt[inc] += k
            def wrapped(e, fn=fn, s=s, k=k):
                fn(e).then_inc(s, k)
            self.steps[eng].append((self.tag, wrapped))
        else:
            self.steps[eng].append((self.tag, fn))

    def dma(self, eng, out, in_, inc, **kw):
        self.emit(eng, lambda e, o=out, i=in_, kw=kw: e.dma_start(out=o, in_=i, **kw),
                  inc=inc, k=16)

    def n(self, name):
        self.sem(name)
        return self.cnt[name]

    def run(self, block):
        nc = self.nc
        def replay(engine, steps):
            cur = None
            sid = None
            for tag, f in steps:
                if tag != cur:
                    if cur is not None:
                        nc.leave_named_scope(cur, sid, False)
                    sid, _ = nc.enter_named_scope(tag, False)
                    cur = tag
                f(engine)
            if cur is not None:
                nc.leave_named_scope(cur, sid, False)
        for eng in self.ENGINES:
            steps = self.steps[eng]
            getattr(block, eng)(lambda engine, steps=steps: replay(engine, steps))


class KB:
    def __init__(self, nc, q, ctx):
        self.nc, self.q, self.ctx = nc, q, ctx
        self.dram, self.sbuf, self.taps = {}, {}, {}

    def din(self, name, shape, dt=F32):
        self.dram[name] = self.nc.dram_tensor(name, shape, dt, kind="ExternalInput").ap()
        return self.dram[name]

    def dout(self, name, shape, dt=F32):
        self.dram[name] = self.nc.dram_tensor(name, shape, dt, kind="ExternalOutput").ap()
        return self.dram[name]

    def dint(self, name, shape, dt=F32, **kw):
        self.dram[name] = self.nc.dram_tensor(name, shape, dt, kind="Internal", **kw).ap()
        return self.dram[name]

    def sb(self, name, shape, dt=F32, scope=None):
        t = (scope or self.ctx).enter_context(self.nc.sbuf_tensor(name, shape, dt))
        self.sbuf[name] = t
        return t

    def ps(self, name, shape, dt=F32, scope=None):
        t = (scope or self.ctx).enter_context(self.nc.psum_tensor(name, shape, dt))
        self.sbuf[name] = t
        return t

    def tap(self, name, src_ap, nelem_sem="st_tap", eng="sync"):
        if name in self.taps:
            self.q.wait(eng, "vd")
            self.q.wait(eng, "ac")
            self.q.dma(eng, self.taps[name], src_ap, nelem_sem)


def build(stage=99, debug_taps=None):
    nc = bass.Bass("TRN2", target_bir_lowering=False)
    try:
        nc.detect_race_conditions = False
    except Exception:
        pass
    with contextlib.ExitStack() as ctx:
        q = Q(nc)
        q.ctx = ctx
        kb = KB(nc, q, ctx)
        declare_dram(kb)
        for tname, (tshape, tdt) in (debug_taps or {}).items():
            kb.taps[tname] = kb.dout(tname, tshape, tdt)
        body(kb, stage)
        block = ctx.enter_context(nc.Block())
        q.run(block)
    return nc


def declare_dram(kb):
    kb.din("u", [TOKH, D])
    kb.din("w_xbcT", [D, EXBC])
    kb.din("w_zT", [D, DI])
    kb.din("w_outT", [DI, D])
    kb.din("wg_sh", [FFI // NCORES, D])
    kb.din("wu_sh", [FFI // NCORES, D])
    kb.din("wd_sh", [D // NCORES, FFI])
    kb.din("conv_w4", [DI + 2 * DS, 4])
    kb.din("conv_b", [DI + 2 * DS, 1])
    kb.din("dt_bias", [NH, 1])
    kb.din("a_log", [NH, 1])
    kb.din("dcol64", [DI, 1])
    kb.din("norm_w", [DI, 1])
    kb.din("pmx", [NH, NCORES])
    kb.din("bmx", [NH, NCORES * NCORES])
    kb.dout("out", [TOK, D])
    kb.dint("d_wxb", [D, EXBC], BF16)
    kb.dint("d_wz", [D, DI], BF16)
    kb.dint("d_wout", [DI, D], BF16)
    kb.dint("d_x", [DI + DS, TOK], BF16)
    kb.dint("d_y", [TOK, DI], BF16)
    kb.dint("d_xq", [TOK, D], BF16)
    kb.dint("d_xn", [CH, 8 * TOKH], BF16)
    kb.dint("d_hq", [TOK, FFI], BF16)
    kb.dint("d_hi", [NCH, NH * CH], F16)
    kb.dint("d_lo", [NCH, NH * CH], BF16)
    kb.dint("d_rstd", [1, TOK])
    kb.dint("cc_g_in", [FFI // NCORES, D], BF16)
    kb.dint("cc_g_out", [FFI, D], BF16, addr_space="Shared")
    kb.dint("cc_u_in", [FFI // NCORES, D], BF16)
    kb.dint("cc_u_out", [FFI, D], BF16, addr_space="Shared")
    kb.dint("cc_dn_in", [D // NCORES, FFI], BF16)
    kb.dint("cc_dn_out", [D, FFI], BF16, addr_space="Shared")
    kb.dint("d_coef", [NCORES, NH])
    kb.dint("d_lcz", [NCH, NH])
    kb.dint("cc_s_in", [DS, NH * HD])
    kb.dint("cc_s_out", [NCORES * DS, NH * HD], addr_space="Shared")
    kb.dint("cc_l_in", [NH, 1])
    kb.dint("cc_l_out", [NCORES * NH, 1], addr_space="Shared")


def body(kb, stage):
    nc, q = kb.nc, kb.q
    sb, psb, dram, sbuf = kb.sb, kb.ps, kb.dram, kb.sbuf

    ident = sb("ident", [CH, CH], BF16)
    ut_incl = sb("ut_incl", [CH, CH], BF16)
    ones1_bf = sb("ones1_bf", [1, CH], BF16)
    ones1_f32 = sb("ones1_f32", [1, CH], F32)
    onescol_bf = sb("onescol_bf", [CH, 1], BF16)
    ones32 = sb("ones32", [CH, NH], BF16)
    ones_full = sb("ones_full", [CH, CH], BF16)
    identh = sb("identh", [CH, CH], F16)
    negones1 = sb("negones1", [1, CH], BF16)
    dcol = sb("dcol", [CH, DI // CH])
    dtb = sb("dtb", [NH, 1])
    acol = sb("acol", [NH, 1])
    nwcol = sb("nwcol", [CH, DI // CH])
    epscol = sb("epscol", [CH, 1])
    onecolf = sb("onecolf", [CH, 1])
    pmx = sb("pmx_s", [NH, NCORES])
    bmx = sb("bmx_s", [NH, NCORES * NCORES])

    B_fm = sb("B_fm", [DS, TOK], BF16)
    C_fm = sb("C_fm", [DS, TOK], BF16)
    a_hi = sb("a_hi", [CH, NCH * NH], F16)
    a_lo = sb("a_lo", [CH, NCH * NH], BF16)
    az_fm = sb("az_fm", [NH, NCH])
    R = sb("R", [DS, NH * HD])
    Rb = sb("Rb", [DS, NH * HD], BF16)

    lnA = contextlib.ExitStack()
    lt_strict = sb("lt_strict", [CH, CH], BF16, scope=lnA)
    maskb = sb("maskb", [CH, CH], F32, scope=lnA)
    identf = sb("identf", [CH, CH], F32, scope=lnA)
    B_tm = sb("B_tm", [CH, NCH * DS], BF16, scope=lnA)
    dt_tmb = sb("dt_tmb", [CH, NCH * NH], BF16, scope=lnA)

    g = "gpsimd"
    q.emit(g, lambda e: e.memset(ident[:], 1.0))
    q.emit(g, lambda e: e.memset(identh[:], 1.0))
    q.emit(g, lambda e: e.memset(negones1[:], -1.0))
    q.emit(g, lambda e: e.memset(identf[:], 1.0))
    q.emit(g, lambda e: e.memset(ut_incl[:], 1.0))
    q.emit(g, lambda e: e.memset(lt_strict[:], 1.0))
    q.emit(g, lambda e: e.memset(maskb[:], 0.0))
    q.emit(g, lambda e: e.memset(ones1_bf[:], 1.0))
    q.emit(g, lambda e: e.memset(ones1_f32[:], 1.0))
    q.emit(g, lambda e: e.memset(onescol_bf[:], 1.0))
    q.emit(g, lambda e: e.memset(ones32[:], 1.0))
    q.emit(g, lambda e: e.memset(ones_full[:], 1.0))
    q.emit(g, lambda e: e.memset(R[:], 0.0))
    q.emit(g, lambda e: e.memset(epscol[:], EPS_LN))
    q.emit(g, lambda e: e.memset(onecolf[:], 1.0), inc="gp")
    q.wait(g, "gp")
    q.emit(g, lambda e: e.affine_select(ident[:], ident[:], [[-1, CH]], ALU.is_equal,
                                        0.0, base=0, channel_multiplier=1))
    q.emit(g, lambda e: e.affine_select(identh[:], identh[:], [[-1, CH]], ALU.is_equal,
                                        0.0, base=0, channel_multiplier=1))
    q.emit(g, lambda e: e.affine_select(identf[:], identf[:], [[-1, CH]], ALU.is_equal,
                                        0.0, base=0, channel_multiplier=1))
    q.emit(g, lambda e: e.affine_select(ut_incl[:], ut_incl[:], [[1, CH]], ALU.is_ge,
                                        0.0, base=0, channel_multiplier=-1))
    q.emit(g, lambda e: e.affine_select(lt_strict[:], lt_strict[:], [[-1, CH]], ALU.is_gt,
                                        0.0, base=0, channel_multiplier=1))
    q.emit(g, lambda e: e.affine_select(maskb[:], maskb[:], [[1, CH]], ALU.is_ge,
                                        NEG, base=0, channel_multiplier=-1), inc="gp")
    q.wait(g, "gp")
    q.dma(g, dtb[:], dram["dt_bias"], "ldc")
    q.dma(g, acol[:], dram["a_log"], "ldc")
    for _j in range(DI // CH):
        q.dma(g, nwcol[:, _j:_j + 1], dram["norm_w"][_j * CH:(_j + 1) * CH, :], "ldc")
        q.dma(g, dcol[:, _j:_j + 1], dram["dcol64"][_j * CH:(_j + 1) * CH, :], "ldc")
    q.dma(g, pmx[:], dram["pmx"], "ldc")
    q.dma(g, bmx[:], dram["bmx"], "ldc")

    q.wait("scalar", "ldc")
    q.emit("scalar", lambda e: e.activation(acol[:], acol[:], AF.Exp), inc="ac")
    q.wait("vector", "ac")
    q.emit("vector", lambda e: e.tensor_scalar(acol[:], acol[:], -1.0, None, ALU.mult),
           inc="vd")

    sb("xn_fm", [CH, 8 * TOKH], BF16, scope=lnA)
    if stage >= 3:
        kb.q.tag = "gather"
        gsc = contextlib.ExitStack()
        phase_gather_w(kb, gsc)
    if stage >= 1:
        kb.q.tag = "ln"
        phase_ln(kb)
    if stage >= 3:
        gsc.close()
    if stage >= 2:
        kb.q.tag = "a"
        phase_a(kb)
    lnA.close()
    sb("g_fm", [CH, 8 * TOK], BF16)
    if stage >= 3:
        kb.q.tag = "gather"
        phase_gather_sl(kb)
    if stage >= 4:
        kb.q.tag = "ffn"
        phase_ffn(kb)
    if stage >= 5:
        kb.q.tag = "tail"
        phase_tail(kb)

    for s in ("st_out", "st_tap", "st_xq", "st_xn", "st_x", "st_y", "st_hi", "st_hq", "st_wq", "st_s", "st_t"):
        q.wait("sync", s)


def phase_ln(kb):
    nc, q, dram, sbuf = kb.nc, kb.q, kb.dram, kb.sbuf
    sb, psb = kb.sb, kb.ps
    u = dram["u"]
    xn_fm = sbuf["xn_fm"]

    with contextlib.ExitStack() as scope:
        utile = kb.sb("utile", [CH, 2 * D], scope=scope)
        mcol = kb.sb("mcol", [CH, 2], scope=scope)
        mraw = kb.sb("mraw", [CH, 1], scope=scope)
        ssraw = kb.sb("ssraw", [CH, 1], scope=scope)
        ssd2 = kb.sb("ssd2", [CH, 1], scope=scope)
        vcol = kb.sb("vcol", [CH, 1], scope=scope)
        rcol = kb.sb("rcol", [CH, 2], scope=scope)
        up = kb.sb("up", [CH, D], scope=scope)
        xh = kb.sb("xh", [CH, D], F16, scope=scope)
        sqs = kb.sb("sqs", [CH, D], scope=scope)
        mx = kb.sb("mx", [CH, D // 16], scope=scope)
        mxr = kb.sb("mxr", [CH, D // 16], scope=scope)
        s16 = kb.sb("s16", [CH, D // 16], scope=scope)
        w_ = kb.sb("w_", [CH, D], F16, scope=scope)
        p1 = kb.sb("p1", [CH, D], F16, scope=scope)
        t_a = kb.sb("t_a", [CH, D], F16, scope=scope)
        t_b = kb.sb("t_b", [CH, D], F16, scope=scope)
        p23 = kb.sb("p23", [CH, D], F16, scope=scope)
        xq_t = kb.sb("xq_t", [CH, D], BF16, scope=scope)
        pln = kb.ps("pln", [CH, 1024], scope=scope)

        xn3 = xn_fm[:].rearrange("p (j t) -> p j t", j=8)
        ps3 = pln[:].rearrange("p (j t) -> p j t", j=8)

        ld_tgt = {}
        a2_t = {}
        rcol_t = {}
        up_vd = {}
        diagr_vd = {}
        xn_ac = {}
        mm_pe_d = {}
        q.dma("sync", utile[0:CH, 0:D], u[0:CH, :], "ld_u")
        ld_tgt[0] = q.n("ld_u")
        q.dma("sync", utile[0:CH, D:2 * D], u[CH:2 * CH, :], "ld_u")
        ld_tgt[1] = q.n("ld_u")
        for it in range(17):
            rows = CH if it < 16 else HALO
            r0 = it * CH
            sl = slice(0, rows)
            pu = it % 2
            ut = utile[sl, pu * D:pu * D + D]
            q.wait("scalar", "ld_u", ld_tgt[it])
            if it >= 2:
                q.wait("scalar", "vd", diagr_vd[it - 2])
            q.emit("scalar", lambda e, sl=sl, ut=ut: e.activation(
                sqs[sl, :], ut, AF.Square, accum_out=ssraw[sl, :]), inc="ac")
            q.emit("scalar", lambda e, sl=sl, ut=ut: e.activation(
                sqs[sl, :], ut, AF.Copy, accum_out=mraw[sl, :]), inc="ac")
            a2_t[it] = q.n("ac")
            q.emit("scalar", lambda e: e.drain())
            q.emit("scalar", lambda e, sl=sl, pu=pu: e.activation(mcol[sl, pu:pu + 1], mraw[sl, :], AF.Copy, scale=1.0 / D))
            q.emit("scalar", lambda e, sl=sl: e.activation(ssd2[sl, :], ssraw[sl, :], AF.Copy, scale=1.0 / D, bias=EPS_LN))
            q.emit("scalar", lambda e: e.drain())
            q.emit("scalar", lambda e, sl=sl, pu=pu: e.activation(vcol[sl, :], mcol[sl, pu:pu + 1], AF.Square))
            q.emit("scalar", lambda e: e.drain())
            q.emit("scalar", lambda e, sl=sl: e.activation(vcol[sl, :], vcol[sl, :], AF.Ln, scale=-1.0, bias=ssd2[sl, :]))
            q.emit("scalar", lambda e: e.drain())
            q.emit("scalar", lambda e, sl=sl, pu=pu: e.activation(rcol[sl, pu:pu + 1], vcol[sl, :], AF.Exp, scale=-0.5))
            q.emit("scalar", lambda e: e.drain(), inc="ac")
            rcol_t[it] = q.n("ac")
            q.wait("vector", "ac", rcol_t[it])
            q.emit("vector", lambda e, sl=sl, ut=ut, pu=pu: e.tensor_scalar(up[sl, :], ut, mcol[sl, pu:pu + 1], None, ALU.subtract), inc="vd")
            up_vd[it] = q.n("vd")
            if it >= 1:
                q.wait("vector", "pe", mm_pe_d[it - 1])
            if it == 16:
                q.emit("vector", lambda e: e.memset(xh[:], 0.0), inc="vd")
            q.emit("vector", lambda e, sl=sl, pu=pu: e.tensor_scalar(xh[sl, :], up[sl, :], rcol[sl, pu:pu + 1], None, ALU.mult), inc="vd")
            diagr_vd[it] = q.n("vd")
            qr_t = equant_pre(kb, src=xh, sl=slice(0, CH), width=D, mx=mx)
            if it + 2 <= 16:
                rows2 = CH if it + 2 < 16 else HALO
                q.wait("sync", "ac", a2_t[it])
                q.wait("sync", "vd", up_vd[it])
                q.dma("sync", utile[0:rows2, pu * D:pu * D + D],
                      u[(it + 2) * CH:(it + 2) * CH + rows2, :], "ld_u")
                ld_tgt[it + 2] = q.n("ld_u")
            q.wait("tensor", "vd", diagr_vd[it])
            if it >= 1:
                q.wait("tensor", "vd", xn_ac[it - 1])
            for j in range(8):
                q.emit("tensor", lambda e, j=j, sl=sl:
                       e.matmul(pln[:, j * CH:(j + 1) * CH], xh[sl, j * CH:(j + 1) * CH],
                                kb.sbuf["identh"][sl, :], start=True, stop=True),
                       inc=("pe" if j == 7 else None))
            mm_pe_d[it] = q.n("pe")
            q.wait("vector", "pe", mm_pe_d[it])
            q.emit("vector", lambda e, it=it, rows=rows:
                   e.tensor_copy(xn3[:, :, r0_(it):r0_(it) + rows], ps3[:, :, 0:rows]), inc="vd")
            xn_ac[it] = q.n("vd")
            if it == 10:
                kb.tap("t_mcol", mcol[:])
                kb.tap("t_sscol", ssraw[:])
                kb.tap("t_rcol", rcol[:])

            p0 = HALO if it == 0 else 0
            q.wait("vector", "st_xq")
            equant_post(kb, src=xh, sl=slice(0, CH), width=D, mx=mx, mxr=mxr, s16=s16,
                        w_=w_, p1=p1, t_a=t_a, t_b=t_b, p23=p23, out=xq_t, qr_tgt=qr_t)
            q.wait("sync", "vd")
            q.dma("sync", dram["d_xq"][it * CH - HALO + p0: it * CH - HALO + rows, :],
                  xq_t[p0:rows, :], "st_xq")

        q.wait("sync", "vd")
        q.wait("sync", "ac")
        q.dma("sync", dram["d_xn"], xn_fm[:], "st_xn")
        kb.tap("t_xn", xn_fm[:])
        q.wait("sync", "st_xq")
        kb.tap("t_xq", dram["d_xq"])


def r0_(it):
    return it * CH


def equant_pre(kb, src, sl, width, mx, mx0=0, src0=0):
    q = kb.q
    nb = width // 16
    src3 = src[sl, src0:src0 + width].rearrange("p (b k) -> p b k", k=16)
    q.emit("vector", lambda e: e.tensor_reduce(mx[sl, mx0:mx0 + nb].unsqueeze(2), src3, AX.X,
                                               ALU.max, apply_absolute_value=True), inc="qr")
    return q.n("qr")


def equant_post(kb, src, sl, width, mx, mxr, s16, w_, p1, t_a, t_b, p23, out, qr_tgt, osl=None, mx0=0, src0=0):
    q = kb.q
    if osl is None:
        osl = slice(0, width)
    nb = width // 16
    rows = sl.stop - sl.start
    src3 = src[sl, src0:src0 + width].rearrange("p (b k) -> p b k", k=16)
    q.wait("vector", "qr", qr_tgt)
    q.emit("vector", lambda e: e.tensor_scalar(s16[sl, 0:nb], mx[sl, mx0:mx0 + nb], 6e-12, 1.0 / 6.0, ALU.max, ALU.mult), inc="qs")
    q.wait("vector", "qs", q.n("qs"))
    q.emit("vector", lambda e: e.reciprocal(mxr[sl, 0:nb], s16[sl, 0:nb]), inc="qs")
    q.wait("vector", "qs", q.n("qs"))
    r3 = mxr[sl, 0:nb].unsqueeze(2).broadcast_to([rows, nb, 16])
    s163 = s16[sl, 0:nb].unsqueeze(2).broadcast_to([rows, nb, 16])
    w3 = w_[sl, 0:width].rearrange("p (b k) -> p b k", k=16)
    q.emit("vector", lambda e: e.tensor_tensor(w3, src3, r3, ALU.mult), inc="vd")
    kb.srcfree_vd = q.n("vd")
    wf = w_[sl, 0:width]
    q.emit("vector", lambda e: e.tensor_scalar(p1[sl, 0:width], wf, -2.25, 2.25, ALU.max, ALU.min), inc="vd")
    q.emit("vector", lambda e: e.tensor_scalar(p1[sl, 0:width], p1[sl, 0:width], M_H, -M_H, ALU.add, ALU.add), inc="vd")
    q.emit("vector", lambda e: e.tensor_scalar(t_a[sl, 0:width], wf, M_1, -M_1, ALU.add, ALU.add), inc="vd")
    q.emit("vector", lambda e: e.tensor_scalar(t_b[sl, 0:width], t_a[sl, 0:width], 2.0, 4.0, ALU.max, ALU.min), inc="vd")
    q.emit("vector", lambda e: e.tensor_scalar(t_a[sl, 0:width], t_a[sl, 0:width], -4.0, -2.0, ALU.max, ALU.min), inc="vd")
    q.emit("vector", lambda e: e.tensor_tensor(p23[sl, 0:width], t_a[sl, 0:width], t_b[sl, 0:width], ALU.add), inc="vd")
    q.emit("vector", lambda e: e.tensor_tensor(p1[sl, 0:width], p1[sl, 0:width], p23[sl, 0:width], ALU.add), inc="vd")
    q.emit("vector", lambda e: e.tensor_scalar(t_a[sl, 0:width], wf, M_2, -M_2, ALU.add, ALU.add), inc="vd")
    q.emit("vector", lambda e: e.tensor_scalar(t_b[sl, 0:width], t_a[sl, 0:width], 4.0, 6.0, ALU.max, ALU.min), inc="vd")
    q.emit("vector", lambda e: e.tensor_scalar(t_a[sl, 0:width], t_a[sl, 0:width], -6.0, -4.0, ALU.max, ALU.min), inc="vd")
    q.emit("vector", lambda e: e.tensor_tensor(p23[sl, 0:width], t_a[sl, 0:width], t_b[sl, 0:width], ALU.add), inc="vd")
    q.emit("vector", lambda e: e.tensor_tensor(p1[sl, 0:width], p1[sl, 0:width], p23[sl, 0:width], ALU.add), inc="vd")
    o3 = out[sl, osl].rearrange("p (b k) -> p b k", k=16)
    p13 = p1[sl, 0:width].rearrange("p (b k) -> p b k", k=16)
    q.emit("vector", lambda e: e.tensor_tensor(o3, p13, s163, ALU.mult), inc="vd")


def phase_a(kb):
    nc, q, dram, sbuf = kb.nc, kb.q, kb.dram, kb.sbuf
    q.dma("gpsimd", dram["d_wz"], dram["w_zT"], "ldwz")
    q.dma("gpsimd", dram["d_wout"], dram["w_outT"], "ldwo")
    xn_fm = sbuf["xn_fm"]
    TH2 = 1024
    NC2 = 1027

    with contextlib.ExitStack() as sc:
        wxb = kb.sb("wxb", [CH, 2 * 8 * CH], BF16, scope=sc)
        cwall = kb.sb("cwall", [CH, 18 * 4], scope=sc)
        cball = kb.sb("cball", [CH, 18], scope=sc)
        ta = kb.sb("ta", [CH, TH2], BF16, scope=sc)
        tb = kb.sb("tb", [CH, 2 * TH2], BF16, scope=sc)
        xpost = kb.sb("xpost", [CH, 2 * TH2], BF16, scope=sc)
        dtraw = kb.sb("dtraw", [NH, TH2], scope=sc)
        dts1 = kb.sb("dts1", [NH, TH2], scope=sc)
        dts2 = kb.sb("dts2", [NH, TH2], scope=sc)
        dtv_bf = kb.sb("dtv_bf", [NH, TH2], BF16, scope=sc)
        a_f = kb.sb("a_f", [NH, TH2], scope=sc)
        x_tm = kb.sb("x_tm", [CH, 2 * DI], BF16, scope=sc)
        xfc = kb.sb("xfc", [CH, DI], BF16, scope=sc)
        xdc = kb.sb("xdc", [CH, DI], BF16, scope=sc)
        xdt = kb.sb("xdt", [CH, DI], BF16, scope=sc)
        bw = kb.sb("bw", [CH, NH * DS], BF16, scope=sc)
        mt = kb.sb("mt", [CH, NH * CH], BF16, scope=sc)
        ebuf = kb.sb("ebuf", [CH, NH * CH], BF16, scope=sc)
        d1 = kb.sb("d1", [CH, 3 * 512], scope=sc)
        rowp_hi = kb.sb("rowp_hi", [CH, NH * CH], F16, scope=sc)
        rowp_lo = kb.sb("rowp_lo", [CH, NH * CH], BF16, scope=sc)
        cumc = kb.sb("cumc", [CH, NH], scope=sc)
        wd_bf = kb.sb("wd_bf", [CH, 2 * NH], BF16, scope=sc)
        decs = kb.sb("decs", [CH, 2 * 64], scope=sc)
        rtmp = kb.sb("rtmp", [DS, NH * HD], scope=sc)
        ybt = kb.sb("ybt", [CH, 512], scope=sc)
        ytile = kb.sb("ytile", [CH, DI], BF16, scope=sc)
        scT = kb.sb("scT", [CH, CH], BF16, scope=sc)
        lsum = kb.sb("lsum", [NH, 1], scope=sc)
        ps1 = kb.ps("ps1", [CH, 2048], scope=sc)
        ps2 = kb.ps("ps2", [CH, 1536], scope=sc)
        psS = kb.ps("psS", [CH, 512], scope=sc)

        q.wait("gpsimd", "st_xq")
        q.wait("gpsimd", "st_xn")
        q.wait("gpsimd", "st_wq")
        q.dma("gpsimd", cwall[:].rearrange("p (c w) -> p c w", w=4),
              dram["conv_w4"][0:18 * CH, :].rearrange("(c p) w -> p c w", p=CH), "ldc2")
        q.dma("gpsimd", cball[:].rearrange("p (c w) -> p c w", w=1),
              dram["conv_b"][0:18 * CH, :].rearrange("(c p) w -> p c w", p=CH), "ldc2",
              allow_slow_non_contiguous=True)
        q.wait("vector", "ldc2")
        q.wait("scalar", "ldc2")
        pe_t = {}
        ldw_t = {}
        conv_vd = {}
        sil_ac = {}
        stx_t = {}
        kb.ssd_t = {}
        kb.ssd_sty = 0
        for half in range(2):
            q.tag = "a_inproj"
            t0 = half * TH2
            c0 = half * TH2
            q.wait("tensor", "vd")
            q.wait("tensor", "ac")
            q.wait("gpsimd", "ldw0", 16)
            for ec in range(19):
                width = CH if ec < 18 else NH
                e0 = ec * CH
                gec = half * 19 + ec
                wb = gec % 2
                reg = sbuf["ps1"] if gec % 2 == 0 else sbuf["ps2"]
                if gec >= 2:
                    q.wait("gpsimd", "pe", pe_t[gec - 2])
                q.dma("gpsimd", wxb[:, wb * 1024:wb * 1024 + 8 * width].rearrange("p (k w) -> p k w", k=8),
                      dram["d_wxb"][:, e0:e0 + width].rearrange("(k p) w -> p k w", p=CH),
                      "ld_w")
                ldw_t[gec] = q.n("ld_w")
                q.wait("tensor", "ld_w", ldw_t[gec])
                if half == 0 and ec == 0:
                    q.wait("sync", "ld_w")
                    kb.tap("t_wxb", wxb[:, 0:8 * CH])
                    kb.tap("t_xnfm", xn_fm[:])
                if gec >= 2 and (gec - 2) in conv_vd:
                    q.wait("tensor", "vd", conv_vd[gec - 2])
                for (tc0, tw) in [(0, 512), (512, 512), (1024, 3)]:
                    for k in range(8):
                        q.emit("tensor", lambda e, k=k, tc0=tc0, tw=tw, width=width, c0=c0, reg=reg, wb=wb:
                               e.matmul(reg[0:width, tc0:tc0 + tw],
                                        wxb[:, wb * 1024 + k * width:wb * 1024 + (k + 1) * width],
                                        xn_fm[:, k * TOKH + c0 + tc0: k * TOKH + c0 + tc0 + tw],
                                        start=(k == 0), stop=(k == 7)),
                               inc=("pe" if (k == 7 and tc0 == 1024) else None))
                pe_t[gec] = q.n("pe")
                if half == 0 and ec == 1 and "t_raw1" in kb.taps:
                    q.wait("scalar", "pe", pe_t[gec])
                    q.emit("scalar", lambda e, reg=reg: e.copy(sbuf["x_tm"][:, 0:NC2], reg[:, 0:NC2]), inc="ac")
                    q.wait("sync", "ac")
                    kb.tap("t_raw1", sbuf["x_tm"][:, 0:NC2])
                if ec < 18:
                    cwc = ec * 4
                    q.wait("vector", "pe", pe_t[gec])
                    if gec >= 2 and (gec - 2) in sil_ac:
                        q.wait("vector", "ac", sil_ac[gec - 2])
                    q.emit("vector", lambda e, reg=reg, cwc=cwc: e.tensor_scalar(
                        ta[:], reg[:, 0:TH2], cwall[:, cwc:cwc + 1], None, ALU.mult), inc="vd")
                    q.emit("vector", lambda e, reg=reg, cwc=cwc, wb=wb: e.scalar_tensor_tensor(
                        tb[:, wb * TH2:(wb + 1) * TH2], reg[:, 1:1 + TH2], cwall[:, cwc + 1:cwc + 2], ta[:], ALU.mult, ALU.add), inc="vd")
                    q.emit("vector", lambda e, reg=reg, cwc=cwc, wb=wb: e.scalar_tensor_tensor(
                        ta[:], reg[:, 2:2 + TH2], cwall[:, cwc + 2:cwc + 3], tb[:, wb * TH2:(wb + 1) * TH2], ALU.mult, ALU.add), inc="vd")
                    q.emit("vector", lambda e, reg=reg, cwc=cwc, wb=wb: e.scalar_tensor_tensor(
                        tb[:, wb * TH2:(wb + 1) * TH2], reg[:, 3:3 + TH2], cwall[:, cwc + 3:cwc + 4], ta[:], ALU.mult, ALU.add), inc="vd")
                    conv_vd[gec] = q.n("vd")
                    q.wait("scalar", "vd", conv_vd[gec])
                    if gec >= 2:
                        q.wait("scalar", "st_x", stx_t.get(gec - 2, 0))
                    q.emit("scalar", lambda e, wb=wb, ec=ec: e.activation(
                        xpost[:, wb * TH2:(wb + 1) * TH2], tb[:, wb * TH2:(wb + 1) * TH2],
                        AF.Silu, bias=cball[:, ec:ec + 1]), inc="ac")
                    sil_ac[gec] = q.n("ac")
                    if half == 0 and ec == 0:
                        q.wait("sync", "ac")
                        kb.tap("t_xpost", xpost[:, 0:TH2])
                    if ec <= 16:
                        q.wait("sync", "ac", sil_ac[gec])
                        q.dma("sync", dram["d_x"][e0:e0 + CH, t0:t0 + TH2],
                              xpost[:, wb * TH2:(wb + 1) * TH2], "st_x")
                        stx_t[gec] = q.n("st_x")
                    if ec == 16:
                        q.emit("scalar", lambda e, t0=t0, wb=wb: e.copy(
                            sbuf["B_fm"][:, t0:t0 + TH2], xpost[:, wb * TH2:(wb + 1) * TH2]), inc="ac")
                    if ec == 17:
                        q.emit("scalar", lambda e, t0=t0, wb=wb: e.copy(
                            sbuf["C_fm"][:, t0:t0 + TH2], xpost[:, wb * TH2:(wb + 1) * TH2]), inc="ac")
                else:
                    q.wait("scalar", "pe", pe_t[gec])
                    q.emit("scalar", lambda e, reg=reg: e.copy(dtraw[:], reg[0:NH, 3:3 + TH2]), inc="ac")
                    q.wait("scalar", "ac")
                    q.emit("scalar", lambda e: e.activation(dts1[:], dtraw[:], AF.Abs, bias=sbuf["dtb"][:, 0:1]))
                    q.emit("scalar", lambda e: e.drain())
                    q.emit("scalar", lambda e: e.activation(dts1[:], dts1[:], AF.Exp, scale=-1.0))
                    q.emit("scalar", lambda e: e.drain())
                    q.emit("scalar", lambda e: e.activation(dts1[:], dts1[:], AF.Ln, bias=sbuf["onecolf"][0:NH, 0:1]))
                    q.emit("scalar", lambda e: e.drain())
                    q.emit("scalar", lambda e: e.activation(dts2[:], dtraw[:], AF.Relu, bias=sbuf["dtb"][:, 0:1]), inc="ac")
                    q.wait("vector", "ac")
                    q.emit("vector", lambda e: e.tensor_tensor(dts2[:], dts1[:], dts2[:], ALU.add), inc="vd")
                    q.emit("vector", lambda e: e.tensor_copy(dtv_bf[:], dts2[:]), inc="vd")
                    q.emit("vector", lambda e: e.tensor_scalar(a_f[:], dts2[:], sbuf["acol"][0:NH, 0:1], None, ALU.mult), inc="vd")

            q.emit("vector", lambda e, half=half: e.tensor_reduce(
                sbuf["az_fm"][:, half * 8:(half + 1) * 8].unsqueeze(2),
                a_f[:].rearrange("p (c t) -> p c t", t=CH), AX.X, ALU.add))
            q.emit("vector", lambda e: e.drain(), inc="vd")
            kb.ssd_t = dict(kb.ssd_t)
            kb.ssd_t["psS_vd"] = q.n("vd")
            kb.ssd_t["psS_ac"] = q.n("ac")
            q.wait("scalar", "st_x")
            q.wait("scalar", "pe")
            q.wait("scalar", "vd")
            for tc in range(8):
                gc = half * 8 + tc
                q.dma("scalar", sbuf["B_tm"][:, gc * DS:(gc + 1) * DS],
                      dram["d_x"][DI:DI + DS, t0 + tc * CH:t0 + (tc + 1) * CH], "ld_xt",
                      transpose=True)
            ld_tgt = {}
            q.dma("scalar", x_tm[:, 0:DI],
                  dram["d_x"][0:DI, t0:t0 + CH], "ld_xc", transpose=True)
            ld_tgt[0] = q.n("ld_xc")

            q.tag = "a_ssd"
            vd_prev = 0
            for cc_ in range(8):
                if cc_ + 1 < 8:
                    if vd_prev:
                        q.wait("scalar", "vd", vd_prev)
                    q.dma("scalar", x_tm[:, ((cc_ + 1) % 2) * DI:((cc_ + 1) % 2 + 1) * DI],
                          dram["d_x"][0:DI, t0 + (cc_ + 1) * CH:t0 + (cc_ + 2) * CH],
                          "ld_xc", transpose=True)
                    ld_tgt[cc_ + 1] = q.n("ld_xc")
                if half * 8 + cc_ < SSD_NCHUNK:
                    ssd_chunk(kb, half, cc_, cc_ % 2, ld_tgt[cc_])
                vd_prev = q.n("vd")

        if SSD_NCHUNK < 16:
            kb.tap("t_dx", dram["d_x"])
            if SSD_NCHUNK > 0:
                kb.tap("t_R", sbuf["R"][:])
            return
        q.wait("sync", "vd")
        q.dma("sync", dram["cc_s_in"], sbuf["R"][:], "st_s")
        q.wait("vector", "ac")
        q.emit("vector", lambda e: e.tensor_reduce(lsum[:].unsqueeze(2), sbuf["az_fm"][:].unsqueeze(1), AX.X, ALU.add))
        q.emit("vector", lambda e: e.drain(), inc="vd")
        q.wait("sync", "vd")
        q.dma("sync", dram["cc_l_in"], lsum[:], "st_s")
        kb.tap("t_y", dram["d_y"])
        kb.tap("t_R", sbuf["R"][:])
        kb.tap("t_dx", dram["d_x"])


def ssd_chunk(kb, half, cc_, xbuf, ld_xc_tgt):
    nc, q, dram, sbuf = kb.nc, kb.q, kb.dram, kb.sbuf
    gc = half * 8 + cc_
    t0 = half * 1024
    tl0 = t0 + cc_ * CH
    ps1, ps2, psS = sbuf["ps1"], sbuf["ps2"], sbuf["psS"]
    a_f = sbuf["a_f"]
    acs = slice(cc_ * CH, (cc_ + 1) * CH)
    ahc = sbuf["a_hi"][:, gc * NH:(gc + 1) * NH]
    alc = sbuf["a_lo"][:, gc * NH:(gc + 1) * NH]
    dtc = sbuf["dt_tmb"][:, gc * NH:(gc + 1) * NH]
    par = gc % 2
    eacol = sbuf["decs"][:, par * 64:par * 64 + NH]
    pcb = sbuf["decs"][:, par * 64 + NH:par * 64 + 2 * NH]
    wd_bf = sbuf["wd_bf"][:, par * NH:(par + 1) * NH]
    T = kb.ssd_t

    q.wait("tensor", "vd", T.get("psS_vd", 0))
    q.wait("tensor", "ac", T.get("psS_ac", 0))
    q.emit("tensor", lambda e, tl0=tl0: e.matmul(psS[:, 192:320], sbuf["B_fm"][:, tl0:tl0 + CH],
                                                 sbuf["C_fm"][:, tl0:tl0 + CH], start=True, stop=True))
    q.emit("tensor", lambda e: e.transpose(psS[:, 0:NH], a_f[:, acs], sbuf["identf"][0:NH, 0:NH]))
    q.emit("tensor", lambda e: e.transpose(psS[:, 64:80].bitcast(BF16), sbuf["dtv_bf"][:, acs], sbuf["ident"][0:NH, 0:NH]), inc="pe")
    tr_pe = q.n("pe")
    q.wait("scalar", "vd", T.get("mt_vd", 0))
    q.wait("scalar", "pe", tr_pe)
    q.emit("scalar", lambda e: e.copy(sbuf["scT"][:], psS[:, 192:320]), inc="ac")
    sct_ac = q.n("ac")
    q.wait("vector", "pe", tr_pe)
    q.emit("vector", lambda e: e.tensor_copy(ahc, psS[:, 0:NH]), inc="vd")
    q.wait("vector", "vd")
    q.emit("vector", lambda e: e.tensor_tensor(alc, psS[:, 0:NH], ahc, ALU.subtract), inc="vd")
    alc_vd = q.n("vd")
    q.emit("vector", lambda e: e.tensor_copy(dtc, psS[:, 64:80].bitcast(BF16)), inc="vd")
    copies_vd = q.n("vd")

    q.wait("tensor", "vd", copies_vd)
    q.wait("tensor", "ac", sct_ac)
    q.emit("tensor", lambda e: e.matmul(psS[:, 96:96 + NH], sbuf["ut_incl"][:], ahc, start=True, stop=False))
    q.emit("tensor", lambda e: e.matmul(psS[:, 96:96 + NH], sbuf["ut_incl"][:], alc, start=False, stop=True))
    q.emit("tensor", lambda e: e.matmul(psS[:, 160:160 + NH], sbuf["lt_strict"][:], ahc, start=True, stop=False))
    q.emit("tensor", lambda e: e.matmul(psS[:, 160:160 + NH], sbuf["lt_strict"][:], alc, start=False, stop=True))
    q.emit("tensor", lambda e: e.matmul(psS[:, 128:128 + NH], sbuf["ones_full"][:], ahc, start=True, stop=False))
    q.emit("tensor", lambda e: e.matmul(psS[:, 128:128 + NH], sbuf["ones_full"][:], alc, start=False, stop=True), inc="pe")
    cum_pe = q.n("pe")
    q.wait("scalar", "pe", cum_pe)
    q.emit("scalar", lambda e, par=par: e.activation(
        sbuf["decs"][:, par * 64:(par + 1) * 64], psS[:, 96:160], AF.Exp), inc="ac")
    q.emit("scalar", lambda e, wd_bf=wd_bf: e.activation(wd_bf, psS[:, 160:160 + NH], AF.Exp), inc="ac")
    dec_ac = q.n("ac")

    q.wait("vector", "pe", T.get("diff_pe", 0))
    q.emit("vector", lambda e: e.tensor_tensor(
        sbuf["rowp_hi"][:].rearrange("p (h l) -> p h l", l=CH),
        ahc.unsqueeze(2).broadcast_to([CH, NH, CH]),
        sbuf["ut_incl"][:].unsqueeze(1).broadcast_to([CH, NH, CH]), ALU.mult), inc="vd")
    q.emit("vector", lambda e: e.tensor_tensor(
        sbuf["rowp_lo"][:].rearrange("p (h l) -> p h l", l=CH),
        alc.unsqueeze(2).broadcast_to([CH, NH, CH]),
        sbuf["ut_incl"][:].unsqueeze(1).broadcast_to([CH, NH, CH]), ALU.mult), inc="vd")
    q.wait("vector", "pe", cum_pe)
    q.emit("vector", lambda e: e.tensor_copy(sbuf["cumc"][:], psS[:, 96:96 + NH]), inc="vd")
    prod_vd = q.n("vd")

    q.wait("sync", "st_x")
    q.dma("sync", sbuf["xfc"][:].rearrange("p (j t) -> p j t", t=CH),
          dram["d_x"][0:DI, tl0:tl0 + CH].rearrange("(j p) t -> p j t", p=CH), "ld_xf")
    q.wait("vector", "ld_xf")
    q.wait("vector", "ac", dec_ac)
    q.emit("vector", lambda e: e.tensor_tensor(
        sbuf["xdc"][:].rearrange("p (j t) -> p j t", t=CH),
        sbuf["xfc"][:].rearrange("p (j t) -> p j t", t=CH),
        sbuf["dcol"][:].unsqueeze(2).broadcast_to([CH, 16, CH]), ALU.mult), inc="vd")
    q.wait("vector", "ld_xt")
    q.wait("vector", "ld_xc", ld_xc_tgt)
    btc = sbuf["B_tm"][:, gc * DS:(gc + 1) * DS]
    q.emit("vector", lambda e, wd_bf=wd_bf: e.tensor_tensor(
        sbuf["bw"][:].rearrange("p (h n) -> p h n", h=NH),
        btc.unsqueeze(1).broadcast_to([CH, NH, DS]),
        wd_bf.unsqueeze(2).broadcast_to([CH, NH, DS]), ALU.mult), inc="vd")
    xtc = sbuf["x_tm"][:, xbuf * DI:(xbuf + 1) * DI]
    q.emit("vector", lambda e: e.tensor_tensor(
        sbuf["xdt"][:].rearrange("p (h d) -> p h d", h=NH),
        xtc.rearrange("p (h d) -> p h d", h=NH),
        dtc.unsqueeze(2).broadcast_to([CH, NH, HD]), ALU.mult), inc="vd")
    build_vd = q.n("vd")

    q.wait("vector", "pe", T.get("m4_pe", 0))
    q.emit("vector", lambda e: e.tensor_copy(sbuf["Rb"][:], sbuf["R"][:]), inc="vd")
    rb_vd = q.n("vd")
    q.wait("tensor", "vd", build_vd)
    q.wait("tensor", "vd", T.get("m4_vd", 0))
    m3_pe = {}
    radd_vd = {}
    def m3_out(hh):
        if hh < 16:
            return ps2[:, hh * HD:(hh + 1) * HD]
        if hh < 24:
            return ps2[:, 1024 + (hh - 16) * HD:1024 + (hh - 15) * HD]
        return ps1[:, 1536 + (hh - 24) * HD:1536 + (hh - 23) * HD]
    for hh in range(32):
        q.emit("tensor", lambda e, hh=hh: e.matmul(
            m3_out(hh), sbuf["bw"][:, hh * DS:(hh + 1) * DS],
            sbuf["xdt"][:, hh * HD:(hh + 1) * HD], start=(hh % 8 == 0), stop=(hh % 8 == 7)),
            inc=("pe" if hh in (15, 31) else None))
        if hh == 15:
            m3_pe[0] = q.n("pe")
    m3_pe[1] = q.n("pe")
    q.wait("vector", "ac", dec_ac)
    for hhalf in range(2):
        hsl = slice(hhalf * 1024, (hhalf + 1) * 1024)
        q.emit("vector", lambda e, hhalf=hhalf, hsl=hsl, pcb=pcb: e.tensor_tensor(
            sbuf["rtmp"][:, hsl].rearrange("p (h d) -> p h d", d=HD),
            sbuf["R"][:, hsl].rearrange("p (h d) -> p h d", d=HD),
            pcb[:, hhalf * 16:(hhalf + 1) * 16].unsqueeze(2).broadcast_to([DS, 16, HD]),
            ALU.mult), inc="vd")
    q.wait("vector", "pe", m3_pe[0])
    q.wait("vector", "vd")
    q.emit("vector", lambda e: e.tensor_tensor(
        sbuf["R"][:, 0:1024], sbuf["rtmp"][:, 0:1024], ps2[:, 0:1024], ALU.add), inc="vd")
    radd_vd[0] = q.n("vd")
    q.wait("vector", "pe", m3_pe[1])
    q.emit("vector", lambda e: e.tensor_tensor(
        sbuf["R"][:, 1024:1536], sbuf["rtmp"][:, 1024:1536], ps2[:, 1024:1536], ALU.add), inc="vd")
    q.emit("vector", lambda e: e.tensor_tensor(
        sbuf["R"][:, 1536:2048], sbuf["rtmp"][:, 1536:2048], ps1[:, 1536:2048], ALU.add), inc="vd")
    radd_vd[1] = q.n("vd")
    scan_vd = q.n("vd")

    q.wait("tensor", "vd", prod_vd)
    q.wait("tensor", "vd", T.get("ytile_vd", 0))
    d1_t = {}
    exp_t = {}
    for p in range(8):
        bank = p % 3
        if p >= 3:
            q.wait("tensor", "vd", d1_t[p - 3])
        sl = slice(p * 512, (p + 1) * 512)
        q.emit("tensor", lambda e, sl=sl, bank=bank: e.matmul(
            ps1[:, bank * 512:(bank + 1) * 512], sbuf["ones_full"][:], sbuf["rowp_hi"][:, sl], start=True, stop=False))
        q.emit("tensor", lambda e, sl=sl, bank=bank: e.matmul(
            ps1[:, bank * 512:(bank + 1) * 512], sbuf["ones_full"][:], sbuf["rowp_lo"][:, sl], start=False, stop=True), inc="pe")
        pe_p = q.n("pe")
        q.wait("vector", "pe", pe_p)
        if p >= 3:
            q.wait("vector", "ac", exp_t[p - 3])
        q.emit("vector", lambda e, p=p, bank=bank: e.tensor_tensor(
            sbuf["d1"][:, bank * 512:(bank + 1) * 512].rearrange("p (a b) -> p a b", b=CH),
            ps1[:, bank * 512:(bank + 1) * 512].rearrange("p (a b) -> p a b", b=CH),
            sbuf["cumc"][:, 4 * p:4 * p + 4].unsqueeze(2).broadcast_to([CH, 4, CH]),
            ALU.subtract), inc="vd")
        q.emit("vector", lambda e, bank=bank: e.scalar_tensor_tensor(
            sbuf["d1"][:, bank * 512:(bank + 1) * 512].rearrange("p (a b) -> p a b", b=CH),
            sbuf["d1"][:, bank * 512:(bank + 1) * 512].rearrange("p (a b) -> p a b", b=CH), 0.0,
            sbuf["maskb"][:].unsqueeze(1).broadcast_to([CH, 4, CH]),
            ALU.min, ALU.add), inc="vd")
        d1_t[p] = q.n("vd")
        q.wait("scalar", "vd", d1_t[p])
        q.emit("scalar", lambda e, p=p, bank=bank: e.activation(
            sbuf["ebuf"][:, p * 512:(p + 1) * 512], sbuf["d1"][:, bank * 512:(bank + 1) * 512], AF.Exp), inc="ac")
        exp_t[p] = q.n("ac")
    diff_pe = q.n("pe")

    q.wait("vector", "ac", exp_t[7])
    q.wait("vector", "ac", sct_ac)
    q.emit("vector", lambda e: e.tensor_tensor(
        sbuf["mt"][:].rearrange("p (h l) -> p h l", h=NH),
        sbuf["ebuf"][:].rearrange("p (h l) -> p h l", h=NH),
        sbuf["scT"][:].unsqueeze(1).broadcast_to([CH, NH, CH]), ALU.mult), inc="vd")
    mt_vd = q.n("vd")

    q.wait("tensor", "vd", mt_vd)
    for h in range(NH):
        q.emit("tensor", lambda e, h=h: e.matmul(
            ps1[:, h * HD:(h + 1) * HD], sbuf["mt"][:, h * CH:(h + 1) * CH],
            sbuf["xdt"][:, h * HD:(h + 1) * HD], start=(h % 8 == 0), stop=False))
    for j in range(16):
        q.emit("tensor", lambda e, j=j: e.matmul(
            ps1[:, j * CH:(j + 1) * CH],
            sbuf["xdc"][:, j * CH:(j + 1) * CH],
            sbuf["ident"][:], start=False, stop=(j % 4 == 3)),
            inc=("pe" if j == 15 else None))
    y_pe = q.n("pe")

    q.wait("tensor", "vd", scan_vd)
    ybt_vd = {}
    m4_pe = {}

    def m4_mm(qq):
        bank = qq % 3
        q.emit("tensor", lambda e, qq=qq, bank=bank, tl0=tl0: e.matmul(
            ps2[:, bank * 512:(bank + 1) * 512], sbuf["C_fm"][:, tl0:tl0 + CH],
            sbuf["Rb"][:, qq * 512:(qq + 1) * 512], start=True, stop=True), inc="pe")
        m4_pe[qq] = q.n("pe")

    def m4_ev(qq):
        bank = qq % 3
        q.wait("vector", "pe", m4_pe[qq])
        q.emit("vector", lambda e, qq=qq, bank=bank, eacol=eacol: e.tensor_tensor(
            sbuf["ybt"][:].rearrange("p (h d) -> p h d", d=HD),
            ps2[:, bank * 512:(bank + 1) * 512].rearrange("p (h d) -> p h d", d=HD),
            eacol[:, qq * 8:(qq + 1) * 8].unsqueeze(2).broadcast_to([CH, 8, HD]),
            ALU.mult), inc="vd")
        ybt_vd[qq] = q.n("vd")
        q.emit("vector", lambda e, qq=qq: e.scalar_tensor_tensor(
            sbuf["ytile"][:, qq * 512:(qq + 1) * 512], ps1[:, qq * 512:(qq + 1) * 512], 1.0,
            sbuf["ybt"][:], ALU.mult, ALU.add), inc="vd")

    m4_mm(0)
    m4_mm(1)
    m4_mm(2)
    q.wait("vector", "pe", y_pe)
    q.wait("vector", "st_y", kb.ssd_sty)
    m4_ev(0)
    q.wait("tensor", "vd", ybt_vd[0])
    m4_mm(3)
    m4_ev(1)
    m4_ev(2)
    m4_ev(3)
    ytile_vd = q.n("vd")
    q.wait("sync", "vd", ytile_vd)
    q.dma("sync", dram["d_y"][tl0:tl0 + CH, :], sbuf["ytile"][:], "st_y")
    kb.ssd_sty = q.n("st_y")

    kb.ssd_t = dict(
        psS_vd=copies_vd, psS_ac=dec_ac, mt_vd=mt_vd, bw_vd=build_vd,
        diff_pe=diff_pe, m4_pe=m4_pe[3], m4_vd=ytile_vd, ytile_vd=ytile_vd,
    )


def phase_gather_w(kb, sc):
    nc, q, dram, sbuf = kb.nc, kb.q, kb.dram, kb.sbuf
    wsh = kb.sb("wsh", [CH, 2 * FFI], F16, scope=sc)
    wqo = kb.sb("wqo", [CH, FFI], BF16, scope=sc)
    qmx = kb.sb("qmx", [CH, 2 * (FFI // 16)], scope=sc)
    qmxr = kb.sb("qmxr", [CH, FFI // 16], scope=sc)
    qs16 = kb.sb("qs16", [CH, FFI // 16], scope=sc)
    qw = kb.sb("qw", [CH, FFI], F16, scope=sc)
    qp1 = kb.sb("qp1", [CH, FFI], F16, scope=sc)
    qta = kb.sb("qta", [CH, FFI], F16, scope=sc)
    qtb = kb.sb("qtb", [CH, FFI], F16, scope=sc)
    qp23 = kb.sb("qp23", [CH, FFI], F16, scope=sc)

    tiles = []
    for src_t, cc_t, rows, width in (
        ("wg_sh", "cc_g_in", FFI // NCORES, D),
        ("wu_sh", "cc_u_in", FFI // NCORES, D),
        ("wd_sh", "cc_dn_in", D // NCORES, FFI),
    ):
        r0 = 0
        while r0 < rows:
            rr = min(CH, rows - r0)
            tiles.append((src_t, cc_t, r0, rr, width))
            r0 += rr
    srcfree = {}
    pend = None
    for ti, (src_t, cc_t, r0, rr, width) in enumerate(tiles):
        par = ti % 2
        if ti >= 2:
            q.wait("gpsimd", "vd", srcfree[ti - 2])
        q.dma("gpsimd", wsh[0:rr, par * FFI:par * FFI + width],
              dram[src_t][r0:r0 + rr, :], "ld_wq")
        if ti == 1:
            q.dma("gpsimd", kb.dram["d_wxb"], kb.dram["w_xbcT"], "ldw0")
        q.wait("vector", "ld_wq", q.n("ld_wq"))
        qr_t = equant_pre(kb, src=wsh, sl=slice(0, rr), width=width, mx=qmx,
                          mx0=par * (FFI // 16), src0=par * FFI)
        srcfree[ti] = q.n("vd")
        if pend is not None:
            (psrc, pcc, pr0, prr, pwidth, ppar, pqr, pti) = pend
            q.wait("vector", "st_wq")
            equant_post(kb, src=wsh, sl=slice(0, prr), width=pwidth, mx=qmx, mxr=qmxr,
                        s16=qs16, w_=qw, p1=qp1, t_a=qta, t_b=qtb, p23=qp23, out=wqo,
                        qr_tgt=pqr, mx0=ppar * (FFI // 16), src0=ppar * FFI)
            srcfree[pti] = kb.srcfree_vd
            q.wait("sync", "vd")
            q.dma("sync", dram[pcc][pr0:pr0 + prr, :], wqo[0:prr, 0:pwidth], "st_wq")
        pend = (src_t, cc_t, r0, rr, width, par, qr_t, ti)
    (psrc, pcc, pr0, prr, pwidth, ppar, pqr, pti) = pend
    q.wait("vector", "st_wq")
    equant_post(kb, src=wsh, sl=slice(0, prr), width=pwidth, mx=qmx, mxr=qmxr,
                s16=qs16, w_=qw, p1=qp1, t_a=qta, t_b=qtb, p23=qp23, out=wqo,
                qr_tgt=pqr, mx0=ppar * (FFI // 16), src0=ppar * FFI)
    q.wait("sync", "vd")
    q.dma("sync", dram[pcc][pr0:pr0 + prr, :], wqo[0:prr, 0:pwidth], "st_wq")
    q.wait("gpsimd", "st_wq")
    for kind, i, o in (("g", "cc_g_in", "cc_g_out"), ("u", "cc_u_in", "cc_u_out"),
                       ("d", "cc_dn_in", "cc_dn_out")):
        q.emit("gpsimd", lambda e, i=i, o=o: e.collective_compute(
            "AllGather", ALU.bypass, replica_groups=[list(range(NCORES))],
            ins=[dram[i]], outs=[dram[o]]), inc="ccd")


def phase_gather_sl(kb):
    nc, q, dram, sbuf = kb.nc, kb.q, kb.dram, kb.sbuf
    q.wait("gpsimd", "st_s")
    for kind, i, o in (("s", "cc_s_in", "cc_s_out"), ("l", "cc_l_in", "cc_l_out")):
        q.emit("gpsimd", lambda e, i=i, o=o: e.collective_compute(
            "AllGather", ALU.bypass, replica_groups=[list(range(NCORES))],
            ins=[dram[i]], outs=[dram[o]]), inc="ccd")
    q.wait("sync", "ccd")
    kb.tap("t_wg", dram["cc_g_out"])
    kb.tap("t_wdn", dram["cc_dn_out"])
    kb.tap("t_sg", dram["cc_s_out"])
    kb.tap("t_lg", dram["cc_l_out"])


def phase_ffn(kb):
    nc, q, dram, sbuf = kb.nc, kb.q, kb.dram, kb.sbuf
    OGS = [(0, 1024), (1024, 1024), (2048, 768)]
    with contextlib.ExitStack() as sc:
        xqf = kb.sb("xqf", [CH, 8 * TOK], BF16, scope=sc)
        wgt = kb.sb("wgt", [CH, 8 * 1024], BF16, scope=sc)
        wut = kb.sb("wut", [CH, 8 * 1024], BF16, scope=sc)
        sg = kb.sb("sg", [CH, 2 * 1024], F16, scope=sc)
        hbuf = kb.sb("hbuf", [CH, 2 * 1024], F16, scope=sc)
        hq = kb.sb("hq", [CH, 2 * 1024], BF16, scope=sc)
        fmx = kb.sb("fmx", [CH, 128], scope=sc)
        fmxr = kb.sb("fmxr", [CH, 64], scope=sc)
        fs16 = kb.sb("fs16", [CH, 64], scope=sc)
        fw = kb.sb("fw", [CH, 1024], F16, scope=sc)
        fp1 = kb.sb("fp1", [CH, 1024], F16, scope=sc)
        fta = kb.sb("fta", [CH, 1024], F16, scope=sc)
        ftb = kb.sb("ftb", [CH, 1024], F16, scope=sc)
        fp23 = kb.sb("fp23", [CH, 1024], F16, scope=sc)
        psg = kb.ps("psg", [CH, 2048], scope=sc)
        psu = kb.ps("psu", [CH, 2048], scope=sc)

        q.wait("scalar", "st_xq")
        q.wait("scalar", "ccd")
        q.wait("scalar", "st_wq")
        if "t_xq3" in kb.taps:
            q.wait("sync", "st_xq")
            q.dma("sync", hq[:, 0:D], dram["d_xq"][0:CH, :], "ld_dbg")
            q.wait("sync", "ld_dbg")
            kb.tap("t_xq3", hq[:, 0:D])
            q.wait("vector", "st_tap")
        for k in range(8):
            q.dma("scalar", xqf[:, k * TOK:(k + 1) * TOK],
                  dram["d_xq"][:, k * CH:(k + 1) * CH], "ld_ff", transpose=True)
        q.wait("sync", "ld_ff")
        kb.tap("t_xqf", xqf[:])
        ac_t = {}
        vd_t = {}
        pe_t = {}
        st_t = {}
        gtc = 0
        pend = None
        for og0, ogw in OGS:
            q.wait("scalar", "pe")
            for k in range(8):
                q.dma("scalar", wgt[:, k * 1024:k * 1024 + ogw],
                      dram["cc_g_out"][og0:og0 + ogw, k * CH:(k + 1) * CH], "ld_ff",
                      transpose=True)
                q.dma("scalar", wut[:, k * 1024:k * 1024 + ogw],
                      dram["cc_u_out"][og0:og0 + ogw, k * CH:(k + 1) * CH], "ld_ff",
                      transpose=True)
            q.wait("tensor", "ld_ff")
            for tc in range(16):
                p = gtc % 2
                if gtc >= 2:
                    q.wait("tensor", "ac", ac_t[gtc - 2])
                    q.wait("tensor", "vd", vd_t[gtc - 2])
                pieces = [(0, 512), (512, ogw - 512)]
                for k in range(8):
                    for f0, fw_ in pieces:
                        q.emit("tensor", lambda e, tc=tc, k=k, f0=f0, fw_=fw_, p=p:
                               e.matmul(psg[:, p * 1024 + f0:p * 1024 + f0 + fw_],
                                        xqf[:, k * TOK + tc * CH:k * TOK + (tc + 1) * CH],
                                        wgt[:, k * 1024 + f0:k * 1024 + f0 + fw_],
                                        start=(k == 0), stop=(k == 7)))
                for k in range(8):
                    for f0, fw_ in pieces:
                        q.emit("tensor", lambda e, tc=tc, k=k, f0=f0, fw_=fw_, p=p:
                               e.matmul(psu[:, p * 1024 + f0:p * 1024 + f0 + fw_],
                                        xqf[:, k * TOK + tc * CH:k * TOK + (tc + 1) * CH],
                                        wut[:, k * 1024 + f0:k * 1024 + f0 + fw_],
                                        start=(k == 0), stop=(k == 7)),
                               inc=("pe" if (k == 7 and f0 != 0) else None))
                pe_t[gtc] = q.n("pe")
                q.wait("scalar", "pe", pe_t[gtc])
                q.emit("scalar", lambda e, ogw=ogw, p=p: e.activation(
                    sg[:, p * 1024:p * 1024 + ogw], psg[:, p * 1024:p * 1024 + ogw],
                    AF.Silu), inc="ac")
                ac_t[gtc] = q.n("ac")
                q.wait("vector", "ac", ac_t[gtc])
                q.emit("vector", lambda e, ogw=ogw, p=p: e.tensor_tensor(
                    hbuf[:, p * 1024:p * 1024 + ogw], sg[:, p * 1024:p * 1024 + ogw],
                    psu[:, p * 1024:p * 1024 + ogw], ALU.mult), inc="vd")
                vd_t[gtc] = q.n("vd")
                qr_t = equant_pre(kb, src=hbuf, sl=slice(0, CH), width=ogw, mx=fmx,
                                  mx0=p * 64, src0=p * 1024)
                if pend is not None:
                    pgtc, ptc, pog0, pogw, pp, pqr = pend
                    q.wait("vector", "st_hq", st_t.get(pgtc - 2, 0))
                    equant_post(kb, src=hbuf, sl=slice(0, CH), width=pogw, mx=fmx,
                                mxr=fmxr, s16=fs16, w_=fw, p1=fp1, t_a=fta, t_b=ftb,
                                p23=fp23, out=hq, qr_tgt=pqr,
                                osl=slice(pp * 1024, pp * 1024 + pogw),
                                mx0=pp * 64, src0=pp * 1024)
                    q.wait("sync", "vd")
                    q.dma("sync", dram["d_hq"][ptc * CH:(ptc + 1) * CH, pog0:pog0 + pogw],
                          hq[0:CH, pp * 1024:pp * 1024 + pogw], "st_hq")
                    st_t[pgtc] = q.n("st_hq")
                pend = (gtc, tc, og0, ogw, p, qr_t)
                gtc += 1
        pgtc, ptc, pog0, pogw, pp, pqr = pend
        q.wait("vector", "st_hq", st_t.get(pgtc - 2, 0))
        equant_post(kb, src=hbuf, sl=slice(0, CH), width=pogw, mx=fmx,
                    mxr=fmxr, s16=fs16, w_=fw, p1=fp1, t_a=fta, t_b=ftb,
                    p23=fp23, out=hq, qr_tgt=pqr,
                    osl=slice(pp * 1024, pp * 1024 + pogw),
                    mx0=pp * 64, src0=pp * 1024)
        q.wait("sync", "vd")
        q.dma("sync", dram["d_hq"][ptc * CH:(ptc + 1) * CH, pog0:pog0 + pogw],
              hq[0:CH, pp * 1024:pp * 1024 + pogw], "st_hq")
        st_t[pgtc] = q.n("st_hq")
    q.wait("sync", "st_hq")
    kb.tap("t_hq", dram["d_hq"])

    q.tag = "ffn_dn"
    with contextlib.ExitStack() as sc:
        dnt = kb.sb("dnt", [CH, 22 * 1024], BF16, scope=sc)
        hqt = kb.sb("hqt", [CH, 2 * 512], BF16, scope=sc)
        psD = kb.ps("psD", [CH, 2048], scope=sc)
        psD2 = kb.ps("psD2", [CH, 2048], scope=sc)
        q.wait("scalar", "ccd")
        q.wait("scalar", "st_hq")
        q.wait("scalar", "vd")
        for o in range(22):
            q.dma("scalar", dnt[:, o * 1024:(o + 1) * 1024],
                  dram["cc_dn_out"][0:D, o * CH:(o + 1) * CH], "ld_dn", transpose=True)
        q.wait("tensor", "ld_dn")
        q.wait("tensor", "vd")
        dpe = {}
        dld = {}
        for t in range(4):
            q.wait("tensor", "ac")
            for o in range(22):
                buf = o % 2
                go = t * 22 + o
                if go >= 2:
                    q.wait("scalar", "pe", dpe[go - 2])
                q.dma("scalar", hqt[:, buf * 512:(buf + 1) * 512],
                      dram["d_hq"][t * 512:(t + 1) * 512, o * CH:(o + 1) * CH], "ld_hf",
                      transpose=True)
                dld[go] = q.n("ld_hf")
                q.wait("tensor", "ld_hf", dld[go])
                for ee in range(8):
                    ps = psD if ee < 4 else psD2
                    q.emit("tensor", lambda e, o=o, ee=ee, buf=buf, ps=ps:
                           e.matmul(ps[:, (ee % 4) * 512:(ee % 4 + 1) * 512],
                                    dnt[:, o * 1024 + ee * CH:o * 1024 + (ee + 1) * CH],
                                    hqt[:, buf * 512:(buf + 1) * 512],
                                    start=(o == 0), stop=(o == 21)),
                           inc=("pe" if ee == 7 else None))
                dpe[go] = q.n("pe")
            q.wait("scalar", "pe")
            for ee in range(8):
                ps = psD if ee < 4 else psD2
                q.emit("scalar", lambda e, t=t, ee=ee, ps=ps: e.activation(
                    sbuf["g_fm"][:, ee * TOK + t * 512:ee * TOK + (t + 1) * 512],
                    ps[:, (ee % 4) * 512:(ee % 4 + 1) * 512], AF.Silu), inc="ac")
    q.wait("sync", "st_hq")
    kb.tap("t_gfm", kb.sbuf["g_fm"][:])


def phase_tail(kb):
    nc, q, dram, sbuf = kb.nc, kb.q, kb.dram, kb.sbuf
    with contextlib.ExitStack() as sc:
        Lmat = kb.sb("Lmat", [NH, NCORES], scope=sc)
        etmp = kb.sb("etmp", [NH, NCORES * NCORES], scope=sc)
        esum = kb.sb("esum", [NH, NCORES], scope=sc)
        coef = kb.sb("coef", [NH, NCORES], scope=sc)
        crow = kb.sb("crow", [1, NCORES * NH], F16, scope=sc)
        cbc = kb.sb("cbc", [CH, NCORES * NH], scope=sc)
        Sj = kb.sb("Sj", [DS, 2 * NH * HD], scope=sc)
        Rtmp2 = kb.sb("Rtmp2", [DS, NH * HD], scope=sc)
        Rin = kb.sb("Rin", [DS, NH * HD], scope=sc)
        Rinb = kb.sb("Rinb", [DS, NH * HD], BF16, scope=sc)
        lcz = kb.sb("lcz2", [NH, NCH], scope=sc)
        lrow = kb.sb("lrow", [1, NCH * NH], F16, scope=sc)
        ead = kb.sb("ead", [CH, 2 * NH], scope=sc)
        ycorr = kb.sb("ycorr", [CH, 512], scope=sc)
        yload = kb.sb("yload", [CH, 2 * DI], BF16, scope=sc)
        psC = kb.ps("psC", [CH, 2048], scope=sc)
        psS2 = kb.ps("psS2", [CH, 512], scope=sc)

        q.wait("sync", "ccd")
        q.wait("sync", "pe")
        q.wait("sync", "ac")
        q.dma("sync", Lmat[:], dram["cc_l_out"].rearrange("(r h) b -> h (r b)", h=NH),
              "ld_t", allow_slow_non_contiguous=True)
        q.wait("vector", "ld_t")
        q.emit("vector", lambda e: e.tensor_tensor(
            etmp[:].rearrange("p (j i) -> p j i", j=NCORES),
            Lmat[:].unsqueeze(1).broadcast_to([NH, NCORES, NCORES]),
            sbuf["bmx_s"][:].rearrange("p (j i) -> p j i", j=NCORES), ALU.mult), inc="vd")
        q.wait("vector", "vd")
        q.emit("vector", lambda e: e.tensor_reduce(
            esum[:].unsqueeze(2), etmp[:].rearrange("p (j i) -> p j i", j=NCORES),
            AX.X, ALU.add))
        q.emit("vector", lambda e: e.drain(), inc="vd")
        q.wait("scalar", "vd")
        q.emit("scalar", lambda e: e.activation(esum[:], esum[:], AF.Exp))
        q.emit("scalar", lambda e: e.drain(), inc="ac")
        q.wait("vector", "ac")
        q.emit("vector", lambda e: e.tensor_tensor(coef[:], esum[:], sbuf["pmx_s"][:], ALU.mult), inc="vd")
        q.wait("sync", "vd")
        q.dma("sync", dram["d_coef"].rearrange("j h -> h j"), coef[:], "st_t",
              allow_slow_non_contiguous=True)
        q.wait("sync", "st_t")
        kb.tap("t_coef", dram["d_coef"])
        q.wait("gpsimd", "st_t")
        q.emit("gpsimd", lambda e: e.memset(Rin[:], 0.0), inc="gp")
        q.dma("gpsimd", crow[:], dram["d_coef"].rearrange("j h -> (j h)").unsqueeze(0),
              "ld_tg")
        q.wait("tensor", "ld_tg", q.n("ld_tg"))
        q.wait("tensor", "vd")
        for j in range(NCORES):
            q.emit("tensor", lambda e, j=j: e.matmul(
                psS2[:, j * NH:(j + 1) * NH], sbuf["ones1_bf"][:],
                crow[0:1, j * NH:(j + 1) * NH], start=True, stop=True),
                inc=("pe" if j == NCORES - 1 else None))
        cbc_pe = q.n("pe")
        sld = {}
        q.wait("sync", "vd")
        q.dma("sync", Sj[:, 0:NH * HD], dram["cc_s_out"][0:DS, :], "ld_t")
        sld[0] = q.n("ld_t")
        q.wait("vector", "pe", cbc_pe)
        q.wait("vector", "gp")
        q.emit("vector", lambda e: e.tensor_copy(cbc[:], psS2[:, 0:NCORES * NH]), inc="vd")
        rt_t = {}
        for j in range(NCORES):
            jp = j % 2
            if j + 1 < NCORES:
                if j >= 1:
                    q.wait("sync", "vd", rt_t[j - 1])
                q.dma("sync", Sj[:, (1 - jp) * NH * HD:(2 - jp) * NH * HD],
                      dram["cc_s_out"][(j + 1) * DS:(j + 2) * DS, :], "ld_t")
                sld[j + 1] = q.n("ld_t")
            q.wait("vector", "ld_t", sld[j])
            q.emit("vector", lambda e, j=j, jp=jp: e.tensor_tensor(
                Rtmp2[:].rearrange("p (h d) -> p h d", d=HD),
                Sj[:, jp * NH * HD:(jp + 1) * NH * HD].rearrange("p (h d) -> p h d", d=HD),
                cbc[:, j * NH:(j + 1) * NH].unsqueeze(2).broadcast_to([DS, NH, HD]),
                ALU.mult), inc="vd")
            rt_t[j] = q.n("vd")
            q.wait("vector", "vd")
            q.emit("vector", lambda e: e.tensor_tensor(Rin[:], Rin[:], Rtmp2[:], ALU.add), inc="vd")
        q.wait("vector", "vd")
        q.emit("vector", lambda e: e.tensor_copy(Rinb[:], Rin[:]), inc="vd")

        q.emit("vector", lambda e: e.memset(lcz[:, 0:1], 0.0), inc="vd")
        for c in range(1, NCH):
            q.emit("vector", lambda e, c=c: e.tensor_tensor(
                lcz[:, c:c + 1], lcz[:, c - 1:c], sbuf["az_fm"][:, c - 1:c], ALU.add))
            q.emit("vector", lambda e: e.drain(), inc="vd")
        q.wait("sync", "vd")
        q.dma("sync", dram["d_lcz"].rearrange("c h -> h c"), lcz[:], "st_t",
              allow_slow_non_contiguous=True)
        q.wait("sync", "st_t")
        kb.tap("t_lcz", dram["d_lcz"])
        q.wait("sync", "vd")
        kb.tap("t_rin", Rin[:])

        q.tag = "tail_corr"
        q.wait("gpsimd", "st_t")
        q.wait("gpsimd", "st_y")
        q.dma("gpsimd", lrow[:], dram["d_lcz"].rearrange("c h -> (c h)").unsqueeze(0),
              "ld_tg")
        lrow_t = q.n("ld_tg")
        q.dma("gpsimd", yload[:, 0:DI], dram["d_y"][0:CH, :], "ld_y")
        yl_t = {0: q.n("ld_y")}
        q.wait("tensor", "ld_tg", lrow_t)
        q.wait("tensor", "vd")
        exp_t = {}
        mm3_t = {}
        pc_t = {}
        yc_t = {}
        yc_half = {}
        styt = {}
        for gc in range(NCH):
            c2 = gc % 2
            tl0 = gc * CH
            ahc = sbuf["a_hi"][:, gc * NH:(gc + 1) * NH]
            alc = sbuf["a_lo"][:, gc * NH:(gc + 1) * NH]
            s0 = 256 + c2 * 64
            if gc >= 2:
                q.wait("tensor", "ac", exp_t[gc - 2])
            q.emit("tensor", lambda e, ahc=ahc, s0=s0: e.matmul(psS2[:, s0:s0 + NH], sbuf["ut_incl"][:], ahc, start=True, stop=False))
            q.emit("tensor", lambda e, alc=alc, s0=s0: e.matmul(psS2[:, s0:s0 + NH], sbuf["ut_incl"][:], alc, start=False, stop=False))
            q.emit("tensor", lambda e, gc=gc, s0=s0: e.matmul(psS2[:, s0:s0 + NH], sbuf["ones1_bf"][:], lrow[0:1, gc * NH:(gc + 1) * NH], start=False, stop=True), inc="pe")
            mm3_t[gc] = q.n("pe")
            q.wait("scalar", "pe", mm3_t[gc])
            if gc >= 2:
                q.wait("scalar", "vd", yc_t[gc - 2])
            q.emit("scalar", lambda e, c2=c2, s0=s0: e.activation(
                ead[:, c2 * NH:(c2 + 1) * NH], psS2[:, s0:s0 + NH], AF.Exp), inc="ac")
            exp_t[gc] = q.n("ac")
            if gc + 1 < NCH:
                if gc >= 1:
                    q.wait("gpsimd", "st_y", styt[gc - 1])
                q.dma("gpsimd", yload[:, (1 - c2) * DI:(2 - c2) * DI],
                      dram["d_y"][(gc + 1) * CH:(gc + 2) * CH, :], "ld_y")
                yl_t[gc + 1] = q.n("ld_y")
            for half in range(2):
                if gc >= 1:
                    q.wait("tensor", "vd", yc_half[(gc - 1, half)])
                for qq in (half * 2, half * 2 + 1):
                    q.emit("tensor", lambda e, qq=qq, tl0=tl0: e.matmul(
                        psC[:, qq * 512:(qq + 1) * 512], sbuf["C_fm"][:, tl0:tl0 + CH],
                        Rinb[:, qq * 512:(qq + 1) * 512], start=True, stop=True),
                        inc=("pe" if qq % 2 == 1 else None))
                pc_t[(gc, half)] = q.n("pe")
            q.wait("vector", "ac", exp_t[gc])
            q.wait("vector", "ld_y", yl_t[gc])
            for half in range(2):
                q.wait("vector", "pe", pc_t[(gc, half)])
                for qq in (half * 2, half * 2 + 1):
                    q.emit("vector", lambda e, qq=qq, c2=c2: e.tensor_tensor(
                        ycorr[:].rearrange("p (h d) -> p h d", d=HD),
                        psC[:, qq * 512:(qq + 1) * 512].rearrange("p (h d) -> p h d", d=HD),
                        ead[:, c2 * NH + qq * 8:c2 * NH + (qq + 1) * 8].unsqueeze(2).broadcast_to([CH, 8, HD]),
                        ALU.mult), inc="vd")
                    q.wait("vector", "vd")
                    q.emit("vector", lambda e, qq=qq, c2=c2: e.tensor_tensor(
                        yload[:, c2 * DI + qq * 512:c2 * DI + (qq + 1) * 512],
                        yload[:, c2 * DI + qq * 512:c2 * DI + (qq + 1) * 512],
                        ycorr[:], ALU.add), inc="vd")
                yc_half[(gc, half)] = q.n("vd")
            yc_t[gc] = q.n("vd")
            q.wait("sync", "vd", yc_t[gc])
            q.dma("sync", dram["d_y"][tl0:tl0 + CH, :], yload[:, c2 * DI:(c2 + 1) * DI], "st_y")
            styt[gc] = q.n("st_y")
        q.wait("sync", "st_y")
        kb.tap("t_y2", dram["d_y"])

    with contextlib.ExitStack() as sct:
        t1 = kb.sb("t1", [CH, 16 * TOK], BF16, scope=sct)
        ffff = phase_tail2(kb, sct, t1)


def phase_tail2(kb, sct, t1):
    nc, q, dram, sbuf = kb.nc, kb.q, kb.dram, kb.sbuf
    q.tag = "ztail"
    with contextlib.ExitStack() as sc:
        xn2 = kb.sb("xn2", [CH, 8 * TOKH], BF16, scope=sc)
        wz = kb.sb("wz", [CH, 2 * 8 * CH], BF16, scope=sc)
        szc = kb.sb("szc", [CH, TOK], F16, scope=sc)
        yfm = kb.sb("yfm", [CH, 2 * TOK], BF16, scope=sc)
        sq2 = kb.sb("sq2", [CH, 3 * 1024], BF16, scope=sc)
        ones32b = kb.sb("ones32b", [CH, NH], BF16, scope=sc)
        rrow = kb.sb("rrow", [NH, TOK], scope=sc)
        psZ = kb.ps("psZ", [CH, 2048], scope=sc)
        psR = kb.ps("psR", [CH, 2048], scope=sc)

        q.wait("gpsimd", "st_y")
        q.emit("gpsimd", lambda e: e.memset(ones32b[:], 1.0), inc="gp")
        q.wait("sync", "st_xn")
        q.wait("sync", "st_y")
        q.dma("sync", xn2[:], dram["d_xn"], "ld_t")
        xn2_t = q.n("ld_t")
        q.wait("sync", "ldwz", 16)
        q.wait("scalar", "st_y")
        zmm_t = {}
        silu_t = {}
        gate_t = {}
        sq2_t = {}
        rms_t = {}
        wz_t = {}
        yfm_t = {}
        for fc in range(16):
            pf = fc % 2
            if fc == 0:
                q.dma("sync", wz[:, 0:8 * CH].rearrange("p (k w) -> p k w", k=8),
                      dram["d_wz"][:, 0:CH].rearrange("(k p) w -> p k w", p=CH), "ld_wz")
                wz_t[0] = q.n("ld_wz")
                q.dma("scalar", yfm[:, 0:TOK], dram["d_y"][:, 0:CH], "ld_yf",
                      transpose=True)
                yfm_t[0] = q.n("ld_yf")
            if fc + 1 < 16:
                if fc >= 1:
                    q.wait("sync", "pe", zmm_t[2 * fc - 1])
                q.dma("sync", wz[:, (1 - pf) * 8 * CH:(2 - pf) * 8 * CH].rearrange("p (k w) -> p k w", k=8),
                      dram["d_wz"][:, (fc + 1) * CH:(fc + 2) * CH].rearrange("(k p) w -> p k w", p=CH),
                      "ld_wz")
                wz_t[fc + 1] = q.n("ld_wz")
                if fc >= 1:
                    q.wait("scalar", "vd", gate_t[2 * fc - 1])
                q.dma("scalar", yfm[:, (1 - pf) * TOK:(2 - pf) * TOK],
                      dram["d_y"][:, (fc + 1) * CH:(fc + 2) * CH], "ld_yf",
                      transpose=True)
                yfm_t[fc + 1] = q.n("ld_yf")
            for th in range(2):
                i = fc * 2 + th
                ip = i % 2
                if i == 0:
                    q.wait("tensor", "ld_t", xn2_t)
                    q.wait("tensor", "ac")
                    q.wait("tensor", "vd")
                if th == 0:
                    q.wait("tensor", "ld_wz", wz_t[fc])
                if i >= 2:
                    q.wait("tensor", "ac", silu_t[i - 2])
                for tt_ in range(2):
                    for k in range(8):
                        q.emit("tensor", lambda e, th=th, tt_=tt_, k=k, pf=pf, ip=ip:
                               e.matmul(psZ[:, ip * 1024 + tt_ * 512:ip * 1024 + (tt_ + 1) * 512],
                                        wz[:, pf * 1024 + k * CH:pf * 1024 + (k + 1) * CH],
                                        xn2[:, k * TOKH + HALO + th * 1024 + tt_ * 512: k * TOKH + HALO + th * 1024 + (tt_ + 1) * 512],
                                        start=(k == 0), stop=(k == 7)),
                               inc=("pe" if (k == 7 and tt_ == 1) else None))
                zmm_t[i] = q.n("pe")
                q.wait("scalar", "pe", zmm_t[i])
                if i >= 2:
                    q.wait("scalar", "vd", gate_t[i - 2])
                q.emit("scalar", lambda e, ip=ip: e.activation(
                    szc[:, ip * 1024:(ip + 1) * 1024], psZ[:, ip * 1024:(ip + 1) * 1024],
                    AF.Silu), inc="ac")
                silu_t[i] = q.n("ac")
                q.wait("vector", "ld_yf", yfm_t[fc])
                q.wait("vector", "ac", silu_t[i])
                q.emit("vector", lambda e, fc=fc, th=th, pf=pf, ip=ip: e.tensor_tensor(
                    t1[:, fc * TOK + th * 1024:fc * TOK + (th + 1) * 1024],
                    yfm[:, pf * TOK + th * 1024:pf * TOK + (th + 1) * 1024],
                    szc[:, ip * 1024:(ip + 1) * 1024], ALU.mult), inc="vd")
                gate_t[i] = q.n("vd")
                q.wait("scalar", "vd", gate_t[i])
                i3 = i % 3
                if i >= 3:
                    q.wait("scalar", "pe", rms_t[i - 3])
                q.emit("scalar", lambda e, fc=fc, th=th, i3=i3: e.activation(
                    sq2[:, i3 * 1024:(i3 + 1) * 1024],
                    t1[:, fc * TOK + th * 1024:fc * TOK + (th + 1) * 1024], AF.Square), inc="ac")
                sq2_t[i] = q.n("ac")
                for j in ([i - 2] if i >= 2 else []) + ([i - 1, i] if i == 31 else []):
                    jfc, jth, j3 = j // 2, j % 2, j % 3
                    q.wait("tensor", "ac", sq2_t[j])
                    for tt_ in range(2):
                        q.emit("tensor", lambda e, jfc=jfc, jth=jth, tt_=tt_, j3=j3: e.matmul(
                            psR[0:NH, jth * 1024 + tt_ * 512:jth * 1024 + (tt_ + 1) * 512],
                            ones32b[:], sq2[:, j3 * 1024 + tt_ * 512:j3 * 1024 + (tt_ + 1) * 512],
                            start=(jfc == 0), stop=(jfc == 15)),
                            inc=("pe" if tt_ == 1 else None))
                    rms_t[j] = q.n("pe")
        q.wait("vector", "pe")
        q.emit("vector", lambda e: e.tensor_copy(rrow[:], psR[0:NH, 0:TOK]), inc="vd")
        q.wait("vector", "vd")
        q.emit("vector", lambda e: e.tensor_scalar(rrow[:], rrow[:], 1.0 / DI, None, ALU.mult))
        q.emit("vector", lambda e: e.drain(), inc="vd")
        q.wait("scalar", "vd")
        q.emit("scalar", lambda e: e.activation(rrow[:], rrow[:], AF.Ln, bias=sbuf["epscol"][0:NH, 0:1]))
        q.emit("scalar", lambda e: e.drain())
        q.emit("scalar", lambda e: e.activation(rrow[:], rrow[:], AF.Exp, scale=-0.5))
        q.emit("scalar", lambda e: e.drain(), inc="ac")
        q.wait("sync", "ac")
        q.dma("sync", dram["d_rstd"], rrow[0:1, :], "st_t")
        q.wait("sync", "st_t")
        kb.tap("t_rstd", dram["d_rstd"])
        kb.tap("t_t1", t1[:])

    q.tag = "oproj"
    with contextlib.ExitStack() as sc2:
        wo = kb.sb("wo", [CH, 16 * D], BF16, scope=sc2)
        rfm = kb.sb("rfm", [CH, 8 * 256], BF16, scope=sc2)
        utile2 = kb.sb("utile2", [CH, 2 * D], scope=sc2)
        rcol2 = kb.sb("rcol2", [CH, 2], scope=sc2)
        otile = kb.sb("otile", [CH, 2 * D], scope=sc2)
        psO = kb.ps("psO", [CH, 2048], scope=sc2)
        psO2 = kb.ps("psO2", [CH, 2048], scope=sc2)
        q.wait("scalar", "st_t")
        q.wait("scalar", "ldwo", 16)
        q.dma("scalar", wo[:].rearrange("p (i w) -> p i w", i=16),
              dram["d_wout"].rearrange("(i p) w -> p i w", p=CH), "ld_wo")
        q.wait("vector", "ld_wo")
        for i in range(16):
            q.emit("vector", lambda e, i=i: e.tensor_scalar(
                wo[:, i * D:(i + 1) * D], wo[:, i * D:(i + 1) * D],
                sbuf["nwcol"][:, i:i + 1], None, ALU.mult), inc="vd")
        q.wait("vector", "ac")
        q.wait("gpsimd", "vd")
        q.wait("gpsimd", "st_t")
        u2t = {}
        for tc in (0, 1):
            q.dma("gpsimd", utile2[:, (tc % 2) * D:(tc % 2 + 1) * D],
                  dram["u"][HALO + tc * CH:HALO + (tc + 1) * CH, :], "ld_u2")
            q.dma("gpsimd", rcol2[:, tc % 2:tc % 2 + 1],
                  dram["d_rstd"][0:1, tc * CH:(tc + 1) * CH].rearrange("a p -> (a p)").unsqueeze(1), "ld_u2")
            u2t[tc] = q.n("ld_u2")
        rfm_t = {}
        trans_t = {}
        fin_t = {}
        sto_t = {}
        mmO_t = {}
        q.wait("tensor", "vd")
        q.wait("tensor", "ac")
        for s in range(8):
            if s >= 1:
                q.wait("tensor", "vd", rfm_t[s - 1])
            for ee in range(8):
                for i in range(16):
                    q.emit("tensor", lambda e, s=s, i=i, ee=ee: e.matmul(
                        psO[:, ee * 256:(ee + 1) * 256],
                        wo[:, i * D + ee * CH:i * D + (ee + 1) * CH],
                        t1[:, i * TOK + s * 256:i * TOK + (s + 1) * 256],
                        start=(i == 0), stop=(i == 15)),
                        inc=("pe" if (i == 15 and ee == 7) else None))
            mmO_t[s] = q.n("pe")
            q.wait("vector", "pe", mmO_t[s])
            for ee in range(8):
                q.emit("vector", lambda e, s=s, ee=ee: e.tensor_tensor(
                    rfm[:, ee * 256:(ee + 1) * 256],
                    psO[:, ee * 256:(ee + 1) * 256],
                    sbuf["g_fm"][:, ee * TOK + s * 256:ee * TOK + (s + 1) * 256],
                    ALU.mult), inc="vd")
            rfm_t[s] = q.n("vd")
            q.wait("tensor", "vd", rfm_t[s])
            if s >= 1:
                q.wait("tensor", "vd", fin_t[2 * s - 1])
            for tc2 in range(2):
                for ee in range(8):
                    q.emit("tensor", lambda e, tc2=tc2, ee=ee: e.matmul(
                        psO2[:, tc2 * 1024 + ee * CH:tc2 * 1024 + (ee + 1) * CH],
                        rfm[:, ee * 256 + tc2 * CH:ee * 256 + (tc2 + 1) * CH],
                        sbuf["ident"][:], start=True, stop=True),
                        inc=("pe" if ee == 7 else None))
                trans_t[(s, tc2)] = q.n("pe")
            for tc2 in range(2):
                tc = 2 * s + tc2
                q.wait("vector", "pe", trans_t[(s, tc2)])
                q.wait("vector", "ld_u2", u2t[tc])
                q.wait("vector", "st_out", sto_t.get(tc - 2, 0))
                q.emit("vector", lambda e, tc=tc, tc2=tc2: e.scalar_tensor_tensor(
                    otile[:, (tc % 2) * D:(tc % 2 + 1) * D], psO2[:, tc2 * 1024:tc2 * 1024 + D],
                    rcol2[:, tc % 2:tc % 2 + 1],
                    utile2[:, (tc % 2) * D:(tc % 2 + 1) * D], ALU.mult, ALU.add), inc="vd")
                fin_t[tc] = q.n("vd")
                q.wait("sync", "vd", fin_t[tc])
                q.dma("sync", dram["out"][tc * CH:(tc + 1) * CH, :],
                      otile[:, (tc % 2) * D:(tc % 2 + 1) * D], "st_out")
                sto_t[tc] = q.n("st_out")
                if tc + 2 < 16:
                    q.wait("gpsimd", "vd", fin_t[tc])
                    q.dma("gpsimd", utile2[:, (tc % 2) * D:(tc % 2 + 1) * D],
                          dram["u"][HALO + (tc + 2) * CH:HALO + (tc + 3) * CH, :], "ld_u2")
                    q.dma("gpsimd", rcol2[:, tc % 2:tc % 2 + 1],
                          dram["d_rstd"][0:1, (tc + 2) * CH:(tc + 3) * CH].rearrange("a p -> (a p)").unsqueeze(1), "ld_u2")
                    u2t[tc + 2] = q.n("ld_u2")


def _prep_inputs(inputs):
    u = np.asarray(inputs["L0_attn_u"])
    ipw = np.asarray(inputs["L0_attn_in_proj_w"])
    conv_w = np.asarray(inputs["L0_attn_conv_w"])
    conv_b = np.asarray(inputs["L0_attn_conv_b"])
    dt_bias = np.asarray(inputs["L0_attn_dt_bias"])
    a_log = np.asarray(inputs["L0_attn_A_log"])
    Dv = np.asarray(inputs["L0_attn_D"])
    norm_w = np.asarray(inputs["L0_attn_norm_w"])
    opw = np.asarray(inputs["L0_attn_out_proj_w"])
    wg = np.asarray(inputs["L0_ffn_w_gate"])
    wu = np.asarray(inputs["L0_ffn_w_up"])
    wdn = np.asarray(inputs["L0_ffn_w_down"])

    w_zT = np.ascontiguousarray(ipw[:DI].T.astype(np.float32))
    w_xbcT = np.ascontiguousarray(ipw[DI:].T.astype(np.float32))
    w_outT = np.ascontiguousarray(opw.T.astype(np.float32))
    conv_w4 = np.ascontiguousarray(conv_w[:, 0, :].astype(np.float32))
    dcol64 = np.ascontiguousarray(np.repeat(np.asarray(Dv, np.float32), HD).reshape(-1, 1))

    base = dict(
        w_xbcT=w_xbcT, w_zT=w_zT, w_outT=w_outT,
        conv_w4=conv_w4,
        conv_b=np.ascontiguousarray(conv_b.reshape(-1, 1), np.float32),
        dt_bias=np.ascontiguousarray(dt_bias.reshape(-1, 1), np.float32),
        a_log=np.ascontiguousarray(a_log.reshape(-1, 1), np.float32),
        dcol64=dcol64,
        norm_w=np.ascontiguousarray(norm_w.reshape(-1, 1), np.float32),
    )
    in_maps = []
    for c in range(NCORES):
        b, sidx = divmod(c, 4)
        s0 = sidx * TOK
        if s0 == 0:
            upad = np.concatenate([np.zeros((HALO, D), np.float32), u[b, :TOK]], 0)
        else:
            upad = u[b, s0 - HALO:s0 + TOK]
        grp0 = (c // 4) * 4
        prevmask = np.zeros((NCORES,), np.float32)
        binmask = np.zeros((NCORES, NCORES), np.float32)
        for j in range(grp0, c):
            prevmask[j] = 1.0
            for i in range(j + 1, c):
                binmask[j, i] = 1.0
        m = dict(base)
        m["u"] = np.ascontiguousarray(upad, np.float32)
        m["pmx"] = np.ascontiguousarray(np.repeat(prevmask.reshape(1, -1), NH, 0))
        m["bmx"] = np.ascontiguousarray(np.repeat(binmask.reshape(1, -1), NH, 0))
        shf = FFI // NCORES
        shd = D // NCORES
        m["wg_sh"] = np.ascontiguousarray(wg[c * shf:(c + 1) * shf], np.float32)
        m["wu_sh"] = np.ascontiguousarray(wu[c * shf:(c + 1) * shf], np.float32)
        m["wd_sh"] = np.ascontiguousarray(wdn[c * shd:(c + 1) * shd], np.float32)
        in_maps.append(m)
    return in_maps


def kernel(**inputs):
    in_maps = _prep_inputs(inputs)
    nc = build(stage=STAGE, debug_taps=DEBUG_TAPS)
    res = run_bass_kernel_spmd(nc, in_maps, core_ids=list(range(NCORES)))
    full = np.zeros((2, 8192, D), np.float32)
    for c in range(NCORES):
        b, sidx = divmod(c, 4)
        full[b, sidx * TOK:(sidx + 1) * TOK] = res.results[c]["out"]
    return full


# version 47
# speedup vs baseline: 1.0400x; 1.0045x over previous
import contextlib
import numpy as np
import ml_dtypes
import concourse.bass as bass
import concourse.mybir as mybir
from concourse.bass_utils import run_bass_kernel_spmd

F32 = mybir.dt.float32
F16 = mybir.dt.float16
BF16 = mybir.dt.bfloat16
AF = mybir.ActivationFunctionType
ALU = mybir.AluOpType
AX = mybir.AxisListType

NCORES = 8
TOK = 2048
HALO = 3
TOKH = TOK + HALO
D = 1024
DI = 2048
NH = 32
HD = 64
DS = 128
EXBC = 2336
NCH = 16
CH = 128
FFI = 2816
EPS_LN = 1e-5
EPS_RMS = 1e-5
M_H = float(1.5 * 2**22)
M_1 = float(1.5 * 2**23)
M_2 = float(1.5 * 2**24)
NEG = -30000.0

DEBUG_TAPS = {}
STAGE = 99
import os as _os
SSD_STEPS = int(_os.environ.get("SSD_STEPS", "99"))
SSD_NCHUNK = int(_os.environ.get("SSD_NCHUNK", "99"))


class Q:
    ENGINES = ("sync", "vector", "scalar", "tensor", "gpsimd")

    def __init__(self, nc):
        self.nc = nc
        self.steps = {e: [] for e in self.ENGINES}
        self.sems = {}
        self.cnt = {}
        self.ctx = None
        self.tag = "init"

    def sem(self, name):
        if name not in self.sems:
            self.sems[name] = self.ctx.enter_context(self.nc.semaphore(name))
            self.cnt[name] = 0
        return self.sems[name]

    def wait(self, eng, name, n=None):
        s = self.sem(name)
        tgt = self.cnt[name] if n is None else n
        if tgt > 0:
            self.steps[eng].append((self.tag, lambda e, s=s, t=tgt: e.wait_ge(s, t)))
        return tgt

    def emit(self, eng, fn, inc=None, k=1):
        if inc is not None:
            s = self.sem(inc)
            self.cnt[inc] += k
            def wrapped(e, fn=fn, s=s, k=k):
                fn(e).then_inc(s, k)
            self.steps[eng].append((self.tag, wrapped))
        else:
            self.steps[eng].append((self.tag, fn))

    def dma(self, eng, out, in_, inc, **kw):
        self.emit(eng, lambda e, o=out, i=in_, kw=kw: e.dma_start(out=o, in_=i, **kw),
                  inc=inc, k=16)

    def n(self, name):
        self.sem(name)
        return self.cnt[name]

    def run(self, block):
        nc = self.nc
        def replay(engine, steps):
            cur = None
            sid = None
            for tag, f in steps:
                if tag != cur:
                    if cur is not None:
                        nc.leave_named_scope(cur, sid, False)
                    sid, _ = nc.enter_named_scope(tag, False)
                    cur = tag
                f(engine)
            if cur is not None:
                nc.leave_named_scope(cur, sid, False)
        for eng in self.ENGINES:
            steps = self.steps[eng]
            getattr(block, eng)(lambda engine, steps=steps: replay(engine, steps))


class KB:
    def __init__(self, nc, q, ctx):
        self.nc, self.q, self.ctx = nc, q, ctx
        self.dram, self.sbuf, self.taps = {}, {}, {}

    def din(self, name, shape, dt=F32):
        self.dram[name] = self.nc.dram_tensor(name, shape, dt, kind="ExternalInput").ap()
        return self.dram[name]

    def dout(self, name, shape, dt=F32):
        self.dram[name] = self.nc.dram_tensor(name, shape, dt, kind="ExternalOutput").ap()
        return self.dram[name]

    def dint(self, name, shape, dt=F32, **kw):
        self.dram[name] = self.nc.dram_tensor(name, shape, dt, kind="Internal", **kw).ap()
        return self.dram[name]

    def sb(self, name, shape, dt=F32, scope=None):
        t = (scope or self.ctx).enter_context(self.nc.sbuf_tensor(name, shape, dt))
        self.sbuf[name] = t
        return t

    def ps(self, name, shape, dt=F32, scope=None):
        t = (scope or self.ctx).enter_context(self.nc.psum_tensor(name, shape, dt))
        self.sbuf[name] = t
        return t

    def tap(self, name, src_ap, nelem_sem="st_tap", eng="sync"):
        if name in self.taps:
            self.q.wait(eng, "vd")
            self.q.wait(eng, "ac")
            self.q.dma(eng, self.taps[name], src_ap, nelem_sem)


def build(stage=99, debug_taps=None):
    nc = bass.Bass("TRN2", target_bir_lowering=False)
    try:
        nc.detect_race_conditions = False
    except Exception:
        pass
    with contextlib.ExitStack() as ctx:
        q = Q(nc)
        q.ctx = ctx
        kb = KB(nc, q, ctx)
        declare_dram(kb)
        for tname, (tshape, tdt) in (debug_taps or {}).items():
            kb.taps[tname] = kb.dout(tname, tshape, tdt)
        body(kb, stage)
        block = ctx.enter_context(nc.Block())
        q.run(block)
    return nc


def declare_dram(kb):
    kb.din("u", [TOKH, D])
    kb.din("w_xbcT", [D, EXBC])
    kb.din("w_zT", [D, DI])
    kb.din("w_outT", [DI, D])
    kb.din("wg_sh", [FFI // NCORES, D])
    kb.din("wu_sh", [FFI // NCORES, D])
    kb.din("wd_sh", [D // NCORES, FFI])
    kb.din("conv_w4", [DI + 2 * DS, 4])
    kb.din("conv_b", [DI + 2 * DS, 1])
    kb.din("dt_bias", [NH, 1])
    kb.din("a_log", [NH, 1])
    kb.din("dcol64", [DI, 1])
    kb.din("norm_w", [DI, 1])
    kb.din("pmx", [NH, NCORES])
    kb.din("bmx", [NH, NCORES * NCORES])
    kb.dout("out", [TOK, D])
    kb.dint("d_wxb", [D, EXBC], BF16)
    kb.dint("d_wz", [D, DI], BF16)
    kb.dint("d_wout", [DI, D], BF16)
    kb.dint("d_x", [DI + DS, TOK], BF16)
    kb.dint("d_y", [TOK, DI], BF16)
    kb.dint("d_xq", [TOK, D], BF16)
    kb.dint("d_xn", [CH, 8 * TOKH], BF16)
    kb.dint("d_hq", [TOK, FFI], BF16)
    kb.dint("d_hi", [NCH, NH * CH], F16)
    kb.dint("d_lo", [NCH, NH * CH], BF16)
    kb.dint("d_rstd", [1, TOK])
    kb.dint("cc_g_in", [FFI // NCORES, D], BF16)
    kb.dint("cc_g_out", [FFI, D], BF16, addr_space="Shared")
    kb.dint("cc_u_in", [FFI // NCORES, D], BF16)
    kb.dint("cc_u_out", [FFI, D], BF16, addr_space="Shared")
    kb.dint("cc_dn_in", [D // NCORES, FFI], BF16)
    kb.dint("cc_dn_out", [D, FFI], BF16, addr_space="Shared")
    kb.dint("d_coef", [NCORES, NH])
    kb.dint("d_lcz", [NCH, NH])
    kb.dint("cc_s_in", [DS, NH * HD])
    kb.dint("cc_s_out", [NCORES * DS, NH * HD], addr_space="Shared")
    kb.dint("cc_l_in", [NH, 1])
    kb.dint("cc_l_out", [NCORES * NH, 1], addr_space="Shared")


def body(kb, stage):
    nc, q = kb.nc, kb.q
    sb, psb, dram, sbuf = kb.sb, kb.ps, kb.dram, kb.sbuf

    ident = sb("ident", [CH, CH], BF16)
    ut_incl = sb("ut_incl", [CH, CH], BF16)
    ones1_bf = sb("ones1_bf", [1, CH], BF16)
    ones1_f32 = sb("ones1_f32", [1, CH], F32)
    onescol_bf = sb("onescol_bf", [CH, 1], BF16)
    ones32 = sb("ones32", [CH, NH], BF16)
    ones_full = sb("ones_full", [CH, CH], BF16)
    identh = sb("identh", [CH, CH], F16)
    negones1 = sb("negones1", [1, CH], BF16)
    dcol = sb("dcol", [CH, DI // CH])
    dtb = sb("dtb", [NH, 1])
    acol = sb("acol", [NH, 1])
    nwcol = sb("nwcol", [CH, DI // CH])
    epscol = sb("epscol", [CH, 1])
    onecolf = sb("onecolf", [CH, 1])
    pmx = sb("pmx_s", [NH, NCORES])
    bmx = sb("bmx_s", [NH, NCORES * NCORES])

    B_fm = sb("B_fm", [DS, TOK], BF16)
    C_fm = sb("C_fm", [DS, TOK], BF16)
    a_hi = sb("a_hi", [CH, NCH * NH], F16)
    a_lo = sb("a_lo", [CH, NCH * NH], BF16)
    az_fm = sb("az_fm", [NH, NCH])
    R = sb("R", [DS, NH * HD])
    Rb = sb("Rb", [DS, NH * HD], BF16)

    lnA = contextlib.ExitStack()
    lt_strict = sb("lt_strict", [CH, CH], BF16, scope=lnA)
    maskb = sb("maskb", [CH, CH], F32, scope=lnA)
    identf = sb("identf", [CH, CH], F32, scope=lnA)
    B_tm = sb("B_tm", [CH, NCH * DS], BF16, scope=lnA)
    dt_tmb = sb("dt_tmb", [CH, NCH * NH], BF16, scope=lnA)

    g = "gpsimd"
    q.emit(g, lambda e: e.memset(ident[:], 1.0))
    q.emit(g, lambda e: e.memset(identh[:], 1.0))
    q.emit(g, lambda e: e.memset(negones1[:], -1.0))
    q.emit(g, lambda e: e.memset(identf[:], 1.0))
    q.emit(g, lambda e: e.memset(ut_incl[:], 1.0))
    q.emit(g, lambda e: e.memset(lt_strict[:], 1.0))
    q.emit(g, lambda e: e.memset(maskb[:], 0.0))
    q.emit(g, lambda e: e.memset(ones1_bf[:], 1.0))
    q.emit(g, lambda e: e.memset(ones1_f32[:], 1.0))
    q.emit(g, lambda e: e.memset(onescol_bf[:], 1.0))
    q.emit(g, lambda e: e.memset(ones32[:], 1.0))
    q.emit(g, lambda e: e.memset(ones_full[:], 1.0))
    q.emit(g, lambda e: e.memset(R[:], 0.0))
    q.emit(g, lambda e: e.memset(epscol[:], EPS_LN))
    q.emit(g, lambda e: e.memset(onecolf[:], 1.0), inc="gp")
    q.wait(g, "gp")
    q.emit(g, lambda e: e.affine_select(ident[:], ident[:], [[-1, CH]], ALU.is_equal,
                                        0.0, base=0, channel_multiplier=1))
    q.emit(g, lambda e: e.affine_select(identh[:], identh[:], [[-1, CH]], ALU.is_equal,
                                        0.0, base=0, channel_multiplier=1))
    q.emit(g, lambda e: e.affine_select(identf[:], identf[:], [[-1, CH]], ALU.is_equal,
                                        0.0, base=0, channel_multiplier=1))
    q.emit(g, lambda e: e.affine_select(ut_incl[:], ut_incl[:], [[1, CH]], ALU.is_ge,
                                        0.0, base=0, channel_multiplier=-1))
    q.emit(g, lambda e: e.affine_select(lt_strict[:], lt_strict[:], [[-1, CH]], ALU.is_gt,
                                        0.0, base=0, channel_multiplier=1))
    q.emit(g, lambda e: e.affine_select(maskb[:], maskb[:], [[1, CH]], ALU.is_ge,
                                        NEG, base=0, channel_multiplier=-1), inc="gp")
    q.wait(g, "gp")
    q.dma(g, dtb[:], dram["dt_bias"], "ldc")
    q.dma(g, acol[:], dram["a_log"], "ldc")
    for _j in range(DI // CH):
        q.dma(g, nwcol[:, _j:_j + 1], dram["norm_w"][_j * CH:(_j + 1) * CH, :], "ldc")
        q.dma(g, dcol[:, _j:_j + 1], dram["dcol64"][_j * CH:(_j + 1) * CH, :], "ldc")
    q.dma(g, pmx[:], dram["pmx"], "ldc")
    q.dma(g, bmx[:], dram["bmx"], "ldc")

    q.wait("scalar", "ldc")
    q.emit("scalar", lambda e: e.activation(acol[:], acol[:], AF.Exp), inc="ac")
    q.wait("vector", "ac")
    q.emit("vector", lambda e: e.tensor_scalar(acol[:], acol[:], -1.0, None, ALU.mult),
           inc="vd")

    sb("xn_fm", [CH, 8 * TOKH], BF16, scope=lnA)
    if stage >= 3:
        kb.q.tag = "gather"
        gsc = contextlib.ExitStack()
        phase_gather_w(kb, gsc)
    if stage >= 1:
        kb.q.tag = "ln"
        phase_ln(kb)
    if stage >= 3:
        gsc.close()
    if stage >= 2:
        kb.q.tag = "a"
        phase_a(kb)
    lnA.close()
    sb("g_fm", [CH, 8 * TOK], BF16)
    if stage >= 3:
        kb.q.tag = "gather"
        phase_gather_sl(kb)
    if stage >= 4:
        kb.q.tag = "ffn"
        phase_ffn(kb)
    if stage >= 5:
        kb.q.tag = "tail"
        phase_tail(kb)

    for s in ("st_out", "st_tap", "st_xq", "st_xn", "st_x", "st_y", "st_hi", "st_hq", "st_wq", "st_s", "st_t"):
        q.wait("sync", s)


def phase_ln(kb):
    nc, q, dram, sbuf = kb.nc, kb.q, kb.dram, kb.sbuf
    sb, psb = kb.sb, kb.ps
    u = dram["u"]
    xn_fm = sbuf["xn_fm"]

    with contextlib.ExitStack() as scope:
        utile = kb.sb("utile", [CH, 2 * D], scope=scope)
        mcol = kb.sb("mcol", [CH, 2], scope=scope)
        mraw = kb.sb("mraw", [CH, 1], scope=scope)
        ssraw = kb.sb("ssraw", [CH, 1], scope=scope)
        ssd2 = kb.sb("ssd2", [CH, 1], scope=scope)
        vcol = kb.sb("vcol", [CH, 1], scope=scope)
        rcol = kb.sb("rcol", [CH, 2], scope=scope)
        up = kb.sb("up", [CH, D], scope=scope)
        xh = kb.sb("xh", [CH, D], F16, scope=scope)
        sqs = kb.sb("sqs", [CH, D], scope=scope)
        mx = kb.sb("mx", [CH, D // 16], scope=scope)
        mxr = kb.sb("mxr", [CH, D // 16], scope=scope)
        s16 = kb.sb("s16", [CH, D // 16], scope=scope)
        w_ = kb.sb("w_", [CH, D], F16, scope=scope)
        p1 = kb.sb("p1", [CH, D], F16, scope=scope)
        t_a = kb.sb("t_a", [CH, D], F16, scope=scope)
        t_b = kb.sb("t_b", [CH, D], F16, scope=scope)
        p23 = kb.sb("p23", [CH, D], F16, scope=scope)
        xq_t = kb.sb("xq_t", [CH, D], BF16, scope=scope)
        pln = kb.ps("pln", [CH, 1024], scope=scope)

        xn3 = xn_fm[:].rearrange("p (j t) -> p j t", j=8)
        ps3 = pln[:].rearrange("p (j t) -> p j t", j=8)

        ld_tgt = {}
        a2_t = {}
        rcol_t = {}
        up_vd = {}
        diagr_vd = {}
        xn_ac = {}
        mm_pe_d = {}
        q.dma("sync", utile[0:CH, 0:D], u[0:CH, :], "ld_u")
        ld_tgt[0] = q.n("ld_u")
        q.dma("sync", utile[0:CH, D:2 * D], u[CH:2 * CH, :], "ld_u")
        ld_tgt[1] = q.n("ld_u")
        for it in range(17):
            rows = CH if it < 16 else HALO
            r0 = it * CH
            sl = slice(0, rows)
            pu = it % 2
            ut = utile[sl, pu * D:pu * D + D]
            q.wait("scalar", "ld_u", ld_tgt[it])
            if it >= 2:
                q.wait("scalar", "vd", diagr_vd[it - 2])
            q.emit("scalar", lambda e, sl=sl, ut=ut: e.activation(
                sqs[sl, :], ut, AF.Square, accum_out=ssraw[sl, :]), inc="ac")
            q.emit("scalar", lambda e, sl=sl, ut=ut: e.activation(
                sqs[sl, :], ut, AF.Copy, accum_out=mraw[sl, :]), inc="ac")
            a2_t[it] = q.n("ac")
            q.emit("scalar", lambda e: e.drain())
            q.emit("scalar", lambda e, sl=sl, pu=pu: e.activation(mcol[sl, pu:pu + 1], mraw[sl, :], AF.Copy, scale=1.0 / D))
            q.emit("scalar", lambda e, sl=sl: e.activation(ssd2[sl, :], ssraw[sl, :], AF.Copy, scale=1.0 / D, bias=EPS_LN))
            q.emit("scalar", lambda e: e.drain())
            q.emit("scalar", lambda e, sl=sl, pu=pu: e.activation(vcol[sl, :], mcol[sl, pu:pu + 1], AF.Square))
            q.emit("scalar", lambda e: e.drain())
            q.emit("scalar", lambda e, sl=sl: e.activation(vcol[sl, :], vcol[sl, :], AF.Ln, scale=-1.0, bias=ssd2[sl, :]))
            q.emit("scalar", lambda e: e.drain())
            q.emit("scalar", lambda e, sl=sl, pu=pu: e.activation(rcol[sl, pu:pu + 1], vcol[sl, :], AF.Exp, scale=-0.5))
            q.emit("scalar", lambda e: e.drain(), inc="ac")
            rcol_t[it] = q.n("ac")
            q.wait("vector", "ac", rcol_t[it])
            q.emit("vector", lambda e, sl=sl, ut=ut, pu=pu: e.tensor_scalar(up[sl, :], ut, mcol[sl, pu:pu + 1], None, ALU.subtract), inc="vd")
            up_vd[it] = q.n("vd")
            if it >= 1:
                q.wait("vector", "pe", mm_pe_d[it - 1])
            if it == 16:
                q.emit("vector", lambda e: e.memset(xh[:], 0.0), inc="vd")
            q.emit("vector", lambda e, sl=sl, pu=pu: e.tensor_scalar(xh[sl, :], up[sl, :], rcol[sl, pu:pu + 1], None, ALU.mult), inc="vd")
            diagr_vd[it] = q.n("vd")
            qr_t = equant_pre(kb, src=xh, sl=slice(0, CH), width=D, mx=mx)
            if it + 2 <= 16:
                rows2 = CH if it + 2 < 16 else HALO
                q.wait("sync", "ac", a2_t[it])
                q.wait("sync", "vd", up_vd[it])
                q.dma("sync", utile[0:rows2, pu * D:pu * D + D],
                      u[(it + 2) * CH:(it + 2) * CH + rows2, :], "ld_u")
                ld_tgt[it + 2] = q.n("ld_u")
            q.wait("tensor", "vd", diagr_vd[it])
            if it >= 1:
                q.wait("tensor", "vd", xn_ac[it - 1])
            for j in range(8):
                q.emit("tensor", lambda e, j=j, sl=sl:
                       e.matmul(pln[:, j * CH:(j + 1) * CH], xh[sl, j * CH:(j + 1) * CH],
                                kb.sbuf["identh"][sl, :], start=True, stop=True),
                       inc=("pe" if j == 7 else None))
            mm_pe_d[it] = q.n("pe")
            q.wait("vector", "pe", mm_pe_d[it])
            q.emit("vector", lambda e, it=it, rows=rows:
                   e.tensor_copy(xn3[:, :, r0_(it):r0_(it) + rows], ps3[:, :, 0:rows]), inc="vd")
            xn_ac[it] = q.n("vd")
            if it == 10:
                kb.tap("t_mcol", mcol[:])
                kb.tap("t_sscol", ssraw[:])
                kb.tap("t_rcol", rcol[:])

            p0 = HALO if it == 0 else 0
            q.wait("vector", "st_xq")
            equant_post(kb, src=xh, sl=slice(0, CH), width=D, mx=mx, mxr=mxr, s16=s16,
                        w_=w_, p1=p1, t_a=t_a, t_b=t_b, p23=p23, out=xq_t, qr_tgt=qr_t)
            q.wait("sync", "vd")
            q.dma("sync", dram["d_xq"][it * CH - HALO + p0: it * CH - HALO + rows, :],
                  xq_t[p0:rows, :], "st_xq")

        q.wait("sync", "vd")
        q.wait("sync", "ac")
        q.dma("sync", dram["d_xn"], xn_fm[:], "st_xn")
        kb.tap("t_xn", xn_fm[:])
        q.wait("sync", "st_xq")
        kb.tap("t_xq", dram["d_xq"])


def r0_(it):
    return it * CH


def equant_pre(kb, src, sl, width, mx, mx0=0, src0=0):
    q = kb.q
    nb = width // 16
    src3 = src[sl, src0:src0 + width].rearrange("p (b k) -> p b k", k=16)
    q.emit("vector", lambda e: e.tensor_reduce(mx[sl, mx0:mx0 + nb].unsqueeze(2), src3, AX.X,
                                               ALU.max, apply_absolute_value=True), inc="qr")
    return q.n("qr")


def equant_post(kb, src, sl, width, mx, mxr, s16, w_, p1, t_a, t_b, p23, out, qr_tgt, osl=None, mx0=0, src0=0):
    q = kb.q
    if osl is None:
        osl = slice(0, width)
    nb = width // 16
    rows = sl.stop - sl.start
    src3 = src[sl, src0:src0 + width].rearrange("p (b k) -> p b k", k=16)
    q.wait("vector", "qr", qr_tgt)
    q.emit("vector", lambda e: e.tensor_scalar(s16[sl, 0:nb], mx[sl, mx0:mx0 + nb], 6e-12, 1.0 / 6.0, ALU.max, ALU.mult), inc="qs")
    q.wait("vector", "qs", q.n("qs"))
    q.emit("vector", lambda e: e.reciprocal(mxr[sl, 0:nb], s16[sl, 0:nb]), inc="qs")
    q.wait("vector", "qs", q.n("qs"))
    r3 = mxr[sl, 0:nb].unsqueeze(2).broadcast_to([rows, nb, 16])
    s163 = s16[sl, 0:nb].unsqueeze(2).broadcast_to([rows, nb, 16])
    w3 = w_[sl, 0:width].rearrange("p (b k) -> p b k", k=16)
    q.emit("vector", lambda e: e.tensor_tensor(w3, src3, r3, ALU.mult), inc="vd")
    kb.srcfree_vd = q.n("vd")
    wf = w_[sl, 0:width]
    q.emit("vector", lambda e: e.tensor_scalar(p1[sl, 0:width], wf, -2.25, 2.25, ALU.max, ALU.min), inc="vd")
    q.emit("vector", lambda e: e.tensor_scalar(p1[sl, 0:width], p1[sl, 0:width], M_H, -M_H, ALU.add, ALU.add), inc="vd")
    q.emit("vector", lambda e: e.tensor_scalar(t_a[sl, 0:width], wf, M_1, -M_1, ALU.add, ALU.add), inc="vd")
    q.emit("vector", lambda e: e.tensor_scalar(t_b[sl, 0:width], t_a[sl, 0:width], 2.0, 4.0, ALU.max, ALU.min), inc="vd")
    q.emit("vector", lambda e: e.tensor_scalar(t_a[sl, 0:width], t_a[sl, 0:width], -4.0, -2.0, ALU.max, ALU.min), inc="vd")
    q.emit("vector", lambda e: e.tensor_tensor(p23[sl, 0:width], t_a[sl, 0:width], t_b[sl, 0:width], ALU.add), inc="vd")
    q.emit("vector", lambda e: e.tensor_tensor(p1[sl, 0:width], p1[sl, 0:width], p23[sl, 0:width], ALU.add), inc="vd")
    q.emit("vector", lambda e: e.tensor_scalar(t_a[sl, 0:width], wf, M_2, -M_2, ALU.add, ALU.add), inc="vd")
    q.emit("vector", lambda e: e.tensor_scalar(t_b[sl, 0:width], t_a[sl, 0:width], 4.0, 6.0, ALU.max, ALU.min), inc="vd")
    q.emit("vector", lambda e: e.tensor_scalar(t_a[sl, 0:width], t_a[sl, 0:width], -6.0, -4.0, ALU.max, ALU.min), inc="vd")
    q.emit("vector", lambda e: e.tensor_tensor(p23[sl, 0:width], t_a[sl, 0:width], t_b[sl, 0:width], ALU.add), inc="vd")
    q.emit("vector", lambda e: e.tensor_tensor(p1[sl, 0:width], p1[sl, 0:width], p23[sl, 0:width], ALU.add), inc="vd")
    o3 = out[sl, osl].rearrange("p (b k) -> p b k", k=16)
    p13 = p1[sl, 0:width].rearrange("p (b k) -> p b k", k=16)
    q.emit("vector", lambda e: e.tensor_tensor(o3, p13, s163, ALU.mult), inc="vd")


def phase_a(kb):
    nc, q, dram, sbuf = kb.nc, kb.q, kb.dram, kb.sbuf
    q.dma("gpsimd", dram["d_wz"], dram["w_zT"], "ldwz")
    q.dma("gpsimd", dram["d_wout"], dram["w_outT"], "ldwo")
    xn_fm = sbuf["xn_fm"]
    TH2 = 1024
    NC2 = 1027

    with contextlib.ExitStack() as sc:
        wxb = kb.sb("wxb", [CH, 2 * 8 * CH], BF16, scope=sc)
        cwall = kb.sb("cwall", [CH, 18 * 4], scope=sc)
        cball = kb.sb("cball", [CH, 18], scope=sc)
        ta = kb.sb("ta", [CH, TH2], BF16, scope=sc)
        tb = kb.sb("tb", [CH, 2 * TH2], BF16, scope=sc)
        xpost = kb.sb("xpost", [CH, 2 * TH2], BF16, scope=sc)
        dtraw = kb.sb("dtraw", [NH, TH2], scope=sc)
        dts1 = kb.sb("dts1", [NH, TH2], scope=sc)
        dts2 = kb.sb("dts2", [NH, TH2], scope=sc)
        dtv_bf = kb.sb("dtv_bf", [NH, TH2], BF16, scope=sc)
        a_f = kb.sb("a_f", [NH, TH2], scope=sc)
        x_tm = kb.sb("x_tm", [CH, 2 * DI], BF16, scope=sc)
        xfc = kb.sb("xfc", [CH, DI], BF16, scope=sc)
        xdc = kb.sb("xdc", [CH, DI], BF16, scope=sc)
        xdt = kb.sb("xdt", [CH, DI], BF16, scope=sc)
        bw = kb.sb("bw", [CH, NH * DS], BF16, scope=sc)
        mt = kb.sb("mt", [CH, NH * CH], BF16, scope=sc)
        ebuf = kb.sb("ebuf", [CH, NH * CH], BF16, scope=sc)
        d1 = kb.sb("d1", [CH, 3 * 512], scope=sc)
        rowp_hi = kb.sb("rowp_hi", [CH, NH * CH], F16, scope=sc)
        rowp_lo = kb.sb("rowp_lo", [CH, NH * CH], BF16, scope=sc)
        cumc = kb.sb("cumc", [CH, NH], scope=sc)
        wd_bf = kb.sb("wd_bf", [CH, 2 * NH], BF16, scope=sc)
        decs = kb.sb("decs", [CH, 2 * 64], scope=sc)
        rtmp = kb.sb("rtmp", [DS, NH * HD], scope=sc)
        ybt = kb.sb("ybt", [CH, 512], scope=sc)
        ytile = kb.sb("ytile", [CH, DI], BF16, scope=sc)
        scT = kb.sb("scT", [CH, CH], BF16, scope=sc)
        lsum = kb.sb("lsum", [NH, 1], scope=sc)
        ps1 = kb.ps("ps1", [CH, 2048], scope=sc)
        ps2 = kb.ps("ps2", [CH, 1536], scope=sc)
        psS = kb.ps("psS", [CH, 512], scope=sc)

        q.wait("gpsimd", "st_xq")
        q.wait("gpsimd", "st_xn")
        q.wait("gpsimd", "st_wq")
        q.dma("gpsimd", cwall[:].rearrange("p (c w) -> p c w", w=4),
              dram["conv_w4"][0:18 * CH, :].rearrange("(c p) w -> p c w", p=CH), "ldc2")
        q.dma("gpsimd", cball[:].rearrange("p (c w) -> p c w", w=1),
              dram["conv_b"][0:18 * CH, :].rearrange("(c p) w -> p c w", p=CH), "ldc2",
              allow_slow_non_contiguous=True)
        q.wait("vector", "ldc2")
        q.wait("scalar", "ldc2")
        pe_t = {}
        ldw_t = {}
        conv_vd = {}
        sil_ac = {}
        stx_t = {}
        kb.ssd_t = {}
        kb.ssd_sty = 0
        for half in range(2):
            q.tag = "a_inproj"
            t0 = half * TH2
            c0 = half * TH2
            q.wait("tensor", "vd")
            q.wait("tensor", "ac")
            q.wait("gpsimd", "ldw0", 16)
            for ec in range(19):
                width = CH if ec < 18 else NH
                e0 = ec * CH
                gec = half * 19 + ec
                wb = gec % 2
                reg = sbuf["ps1"] if gec % 2 == 0 else sbuf["ps2"]
                if gec >= 2:
                    q.wait("gpsimd", "pe", pe_t[gec - 2])
                q.dma("gpsimd", wxb[:, wb * 1024:wb * 1024 + 8 * width].rearrange("p (k w) -> p k w", k=8),
                      dram["d_wxb"][:, e0:e0 + width].rearrange("(k p) w -> p k w", p=CH),
                      "ld_w")
                ldw_t[gec] = q.n("ld_w")
                q.wait("tensor", "ld_w", ldw_t[gec])
                if half == 0 and ec == 0:
                    q.wait("sync", "ld_w")
                    kb.tap("t_wxb", wxb[:, 0:8 * CH])
                    kb.tap("t_xnfm", xn_fm[:])
                if gec >= 2 and (gec - 2) in conv_vd:
                    q.wait("tensor", "vd", conv_vd[gec - 2])
                for (tc0, tw) in [(0, 512), (512, 512), (1024, 3)]:
                    for k in range(8):
                        q.emit("tensor", lambda e, k=k, tc0=tc0, tw=tw, width=width, c0=c0, reg=reg, wb=wb:
                               e.matmul(reg[0:width, tc0:tc0 + tw],
                                        wxb[:, wb * 1024 + k * width:wb * 1024 + (k + 1) * width],
                                        xn_fm[:, k * TOKH + c0 + tc0: k * TOKH + c0 + tc0 + tw],
                                        start=(k == 0), stop=(k == 7)),
                               inc=("pe" if (k == 7 and tc0 == 1024) else None))
                pe_t[gec] = q.n("pe")
                if half == 0 and ec == 1 and "t_raw1" in kb.taps:
                    q.wait("scalar", "pe", pe_t[gec])
                    q.emit("scalar", lambda e, reg=reg: e.copy(sbuf["x_tm"][:, 0:NC2], reg[:, 0:NC2]), inc="ac")
                    q.wait("sync", "ac")
                    kb.tap("t_raw1", sbuf["x_tm"][:, 0:NC2])
                if ec < 18:
                    cwc = ec * 4
                    q.wait("vector", "pe", pe_t[gec])
                    if gec >= 2 and (gec - 2) in sil_ac:
                        q.wait("vector", "ac", sil_ac[gec - 2])
                    q.emit("vector", lambda e, reg=reg, cwc=cwc: e.tensor_scalar(
                        ta[:], reg[:, 0:TH2], cwall[:, cwc:cwc + 1], None, ALU.mult), inc="vd")
                    q.emit("vector", lambda e, reg=reg, cwc=cwc, wb=wb: e.scalar_tensor_tensor(
                        tb[:, wb * TH2:(wb + 1) * TH2], reg[:, 1:1 + TH2], cwall[:, cwc + 1:cwc + 2], ta[:], ALU.mult, ALU.add), inc="vd")
                    q.emit("vector", lambda e, reg=reg, cwc=cwc, wb=wb: e.scalar_tensor_tensor(
                        ta[:], reg[:, 2:2 + TH2], cwall[:, cwc + 2:cwc + 3], tb[:, wb * TH2:(wb + 1) * TH2], ALU.mult, ALU.add), inc="vd")
                    q.emit("vector", lambda e, reg=reg, cwc=cwc, wb=wb: e.scalar_tensor_tensor(
                        tb[:, wb * TH2:(wb + 1) * TH2], reg[:, 3:3 + TH2], cwall[:, cwc + 3:cwc + 4], ta[:], ALU.mult, ALU.add), inc="vd")
                    conv_vd[gec] = q.n("vd")
                    q.wait("scalar", "vd", conv_vd[gec])
                    if gec >= 2:
                        q.wait("scalar", "st_x", stx_t.get(gec - 2, 0))
                    q.emit("scalar", lambda e, wb=wb, ec=ec: e.activation(
                        xpost[:, wb * TH2:(wb + 1) * TH2], tb[:, wb * TH2:(wb + 1) * TH2],
                        AF.Silu, bias=cball[:, ec:ec + 1]), inc="ac")
                    sil_ac[gec] = q.n("ac")
                    if half == 0 and ec == 0:
                        q.wait("sync", "ac")
                        kb.tap("t_xpost", xpost[:, 0:TH2])
                    if ec <= 16:
                        q.wait("sync", "ac", sil_ac[gec])
                        q.dma("sync", dram["d_x"][e0:e0 + CH, t0:t0 + TH2],
                              xpost[:, wb * TH2:(wb + 1) * TH2], "st_x")
                        stx_t[gec] = q.n("st_x")
                    if ec == 16:
                        q.emit("scalar", lambda e, t0=t0, wb=wb: e.copy(
                            sbuf["B_fm"][:, t0:t0 + TH2], xpost[:, wb * TH2:(wb + 1) * TH2]), inc="ac")
                    if ec == 17:
                        q.emit("scalar", lambda e, t0=t0, wb=wb: e.copy(
                            sbuf["C_fm"][:, t0:t0 + TH2], xpost[:, wb * TH2:(wb + 1) * TH2]), inc="ac")
                else:
                    q.wait("scalar", "pe", pe_t[gec])
                    q.emit("scalar", lambda e, reg=reg: e.copy(dtraw[:], reg[0:NH, 3:3 + TH2]), inc="ac")
                    q.wait("scalar", "ac")
                    q.emit("scalar", lambda e: e.activation(dts1[:], dtraw[:], AF.Abs, bias=sbuf["dtb"][:, 0:1]))
                    q.emit("scalar", lambda e: e.drain())
                    q.emit("scalar", lambda e: e.activation(dts1[:], dts1[:], AF.Exp, scale=-1.0))
                    q.emit("scalar", lambda e: e.drain())
                    q.emit("scalar", lambda e: e.activation(dts1[:], dts1[:], AF.Ln, bias=sbuf["onecolf"][0:NH, 0:1]))
                    q.emit("scalar", lambda e: e.drain())
                    q.emit("scalar", lambda e: e.activation(dts2[:], dtraw[:], AF.Relu, bias=sbuf["dtb"][:, 0:1]), inc="ac")
                    q.wait("vector", "ac")
                    q.emit("vector", lambda e: e.tensor_tensor(dts2[:], dts1[:], dts2[:], ALU.add), inc="vd")
                    q.emit("vector", lambda e: e.tensor_copy(dtv_bf[:], dts2[:]), inc="vd")
                    q.emit("vector", lambda e: e.tensor_scalar(a_f[:], dts2[:], sbuf["acol"][0:NH, 0:1], None, ALU.mult), inc="vd")

            q.emit("vector", lambda e, half=half: e.tensor_reduce(
                sbuf["az_fm"][:, half * 8:(half + 1) * 8].unsqueeze(2),
                a_f[:].rearrange("p (c t) -> p c t", t=CH), AX.X, ALU.add))
            q.emit("vector", lambda e: e.drain(), inc="vd")
            kb.ssd_t = dict(kb.ssd_t)
            kb.ssd_t["psS_vd"] = q.n("vd")
            kb.ssd_t["psS_ac"] = q.n("ac")
            q.wait("scalar", "st_x")
            q.wait("scalar", "pe")
            q.wait("scalar", "vd")
            for tc in range(8):
                gc = half * 8 + tc
                q.dma("scalar", sbuf["B_tm"][:, gc * DS:(gc + 1) * DS],
                      dram["d_x"][DI:DI + DS, t0 + tc * CH:t0 + (tc + 1) * CH], "ld_xt",
                      transpose=True)
            ld_tgt = {}
            q.dma("scalar", x_tm[:, 0:DI],
                  dram["d_x"][0:DI, t0:t0 + CH], "ld_xc", transpose=True)
            ld_tgt[0] = q.n("ld_xc")

            q.tag = "a_ssd"
            vd_prev = 0
            for cc_ in range(8):
                if cc_ + 1 < 8:
                    if vd_prev:
                        q.wait("scalar", "vd", vd_prev)
                    q.dma("scalar", x_tm[:, ((cc_ + 1) % 2) * DI:((cc_ + 1) % 2 + 1) * DI],
                          dram["d_x"][0:DI, t0 + (cc_ + 1) * CH:t0 + (cc_ + 2) * CH],
                          "ld_xc", transpose=True)
                    ld_tgt[cc_ + 1] = q.n("ld_xc")
                if half * 8 + cc_ < SSD_NCHUNK:
                    ssd_chunk(kb, half, cc_, cc_ % 2, ld_tgt[cc_])
                vd_prev = q.n("vd")

        if SSD_NCHUNK < 16:
            kb.tap("t_dx", dram["d_x"])
            if SSD_NCHUNK > 0:
                kb.tap("t_R", sbuf["R"][:])
            return
        q.wait("sync", "vd")
        q.dma("sync", dram["cc_s_in"], sbuf["R"][:], "st_s")
        q.wait("vector", "ac")
        q.emit("vector", lambda e: e.tensor_reduce(lsum[:].unsqueeze(2), sbuf["az_fm"][:].unsqueeze(1), AX.X, ALU.add))
        q.emit("vector", lambda e: e.drain(), inc="vd")
        q.wait("sync", "vd")
        q.dma("sync", dram["cc_l_in"], lsum[:], "st_s")
        kb.tap("t_y", dram["d_y"])
        kb.tap("t_R", sbuf["R"][:])
        kb.tap("t_dx", dram["d_x"])


def ssd_chunk(kb, half, cc_, xbuf, ld_xc_tgt):
    nc, q, dram, sbuf = kb.nc, kb.q, kb.dram, kb.sbuf
    gc = half * 8 + cc_
    t0 = half * 1024
    tl0 = t0 + cc_ * CH
    ps1, ps2, psS = sbuf["ps1"], sbuf["ps2"], sbuf["psS"]
    a_f = sbuf["a_f"]
    acs = slice(cc_ * CH, (cc_ + 1) * CH)
    ahc = sbuf["a_hi"][:, gc * NH:(gc + 1) * NH]
    alc = sbuf["a_lo"][:, gc * NH:(gc + 1) * NH]
    dtc = sbuf["dt_tmb"][:, gc * NH:(gc + 1) * NH]
    par = gc % 2
    eacol = sbuf["decs"][:, par * 64:par * 64 + NH]
    pcb = sbuf["decs"][:, par * 64 + NH:par * 64 + 2 * NH]
    wd_bf = sbuf["wd_bf"][:, par * NH:(par + 1) * NH]
    T = kb.ssd_t

    q.wait("tensor", "vd", T.get("psS_vd", 0))
    q.wait("tensor", "ac", T.get("psS_ac", 0))
    q.emit("tensor", lambda e, tl0=tl0: e.matmul(psS[:, 192:320], sbuf["B_fm"][:, tl0:tl0 + CH],
                                                 sbuf["C_fm"][:, tl0:tl0 + CH], start=True, stop=True))
    q.emit("tensor", lambda e: e.transpose(psS[:, 0:NH], a_f[:, acs], sbuf["identf"][0:NH, 0:NH]))
    q.emit("tensor", lambda e: e.transpose(psS[:, 64:80].bitcast(BF16), sbuf["dtv_bf"][:, acs], sbuf["ident"][0:NH, 0:NH]), inc="pe")
    tr_pe = q.n("pe")
    q.wait("scalar", "vd", T.get("mt_vd", 0))
    q.wait("scalar", "pe", tr_pe)
    q.emit("scalar", lambda e: e.copy(sbuf["scT"][:], psS[:, 192:320]), inc="ac")
    sct_ac = q.n("ac")
    q.wait("vector", "pe", tr_pe)
    q.emit("vector", lambda e: e.tensor_copy(ahc, psS[:, 0:NH]), inc="vd")
    q.wait("vector", "vd")
    q.emit("vector", lambda e: e.tensor_tensor(alc, psS[:, 0:NH], ahc, ALU.subtract), inc="vd")
    alc_vd = q.n("vd")
    q.emit("vector", lambda e: e.tensor_copy(dtc, psS[:, 64:80].bitcast(BF16)), inc="vd")
    copies_vd = q.n("vd")

    q.wait("tensor", "vd", copies_vd)
    q.wait("tensor", "ac", sct_ac)
    q.emit("tensor", lambda e: e.matmul(psS[:, 96:96 + NH], sbuf["ut_incl"][:], ahc, start=True, stop=False))
    q.emit("tensor", lambda e: e.matmul(psS[:, 96:96 + NH], sbuf["ut_incl"][:], alc, start=False, stop=True))
    q.emit("tensor", lambda e: e.matmul(psS[:, 160:160 + NH], sbuf["lt_strict"][:], ahc, start=True, stop=False))
    q.emit("tensor", lambda e: e.matmul(psS[:, 160:160 + NH], sbuf["lt_strict"][:], alc, start=False, stop=True))
    q.emit("tensor", lambda e: e.matmul(psS[:, 128:128 + NH], sbuf["ones_full"][:], ahc, start=True, stop=False))
    q.emit("tensor", lambda e: e.matmul(psS[:, 128:128 + NH], sbuf["ones_full"][:], alc, start=False, stop=True), inc="pe")
    cum_pe = q.n("pe")
    q.wait("scalar", "pe", cum_pe)
    q.emit("scalar", lambda e, par=par: e.activation(
        sbuf["decs"][:, par * 64:(par + 1) * 64], psS[:, 96:160], AF.Exp), inc="ac")
    q.emit("scalar", lambda e, wd_bf=wd_bf: e.activation(wd_bf, psS[:, 160:160 + NH], AF.Exp), inc="ac")
    dec_ac = q.n("ac")

    q.wait("vector", "pe", T.get("diff_pe", 0))
    q.emit("vector", lambda e: e.tensor_tensor(
        sbuf["rowp_hi"][:].rearrange("p (h l) -> p h l", l=CH),
        ahc.unsqueeze(2).broadcast_to([CH, NH, CH]),
        sbuf["ut_incl"][:].unsqueeze(1).broadcast_to([CH, NH, CH]), ALU.mult), inc="vd")
    q.emit("vector", lambda e: e.tensor_tensor(
        sbuf["rowp_lo"][:].rearrange("p (h l) -> p h l", l=CH),
        alc.unsqueeze(2).broadcast_to([CH, NH, CH]),
        sbuf["ut_incl"][:].unsqueeze(1).broadcast_to([CH, NH, CH]), ALU.mult), inc="vd")
    q.wait("vector", "pe", cum_pe)
    q.emit("vector", lambda e: e.tensor_copy(sbuf["cumc"][:], psS[:, 96:96 + NH]), inc="vd")
    prod_vd = q.n("vd")

    q.wait("sync", "st_x")
    q.dma("sync", sbuf["xfc"][:].rearrange("p (j t) -> p j t", t=CH),
          dram["d_x"][0:DI, tl0:tl0 + CH].rearrange("(j p) t -> p j t", p=CH), "ld_xf")
    q.wait("vector", "ld_xf")
    q.wait("vector", "ac", dec_ac)
    q.emit("vector", lambda e: e.tensor_tensor(
        sbuf["xdc"][:].rearrange("p (j t) -> p j t", t=CH),
        sbuf["xfc"][:].rearrange("p (j t) -> p j t", t=CH),
        sbuf["dcol"][:].unsqueeze(2).broadcast_to([CH, 16, CH]), ALU.mult), inc="vd")
    q.wait("vector", "ld_xt")
    q.wait("vector", "ld_xc", ld_xc_tgt)
    btc = sbuf["B_tm"][:, gc * DS:(gc + 1) * DS]
    q.emit("vector", lambda e, wd_bf=wd_bf: e.tensor_tensor(
        sbuf["bw"][:].rearrange("p (h n) -> p h n", h=NH),
        btc.unsqueeze(1).broadcast_to([CH, NH, DS]),
        wd_bf.unsqueeze(2).broadcast_to([CH, NH, DS]), ALU.mult), inc="vd")
    xtc = sbuf["x_tm"][:, xbuf * DI:(xbuf + 1) * DI]
    q.emit("vector", lambda e: e.tensor_tensor(
        sbuf["xdt"][:].rearrange("p (h d) -> p h d", h=NH),
        xtc.rearrange("p (h d) -> p h d", h=NH),
        dtc.unsqueeze(2).broadcast_to([CH, NH, HD]), ALU.mult), inc="vd")
    build_vd = q.n("vd")

    q.wait("vector", "pe", T.get("m4_pe", 0))
    q.emit("vector", lambda e: e.tensor_copy(sbuf["Rb"][:], sbuf["R"][:]), inc="vd")
    rb_vd = q.n("vd")
    q.wait("tensor", "vd", build_vd)
    q.wait("tensor", "vd", T.get("m4_vd", 0))
    m3_pe = {}
    radd_vd = {}
    def m3_out(hh):
        if hh < 16:
            return ps2[:, hh * HD:(hh + 1) * HD]
        if hh < 24:
            return ps2[:, 1024 + (hh - 16) * HD:1024 + (hh - 15) * HD]
        return ps1[:, 1536 + (hh - 24) * HD:1536 + (hh - 23) * HD]
    for hh in range(32):
        q.emit("tensor", lambda e, hh=hh: e.matmul(
            m3_out(hh), sbuf["bw"][:, hh * DS:(hh + 1) * DS],
            sbuf["xdt"][:, hh * HD:(hh + 1) * HD], start=(hh % 8 == 0), stop=(hh % 8 == 7)),
            inc=("pe" if hh in (15, 31) else None))
        if hh == 15:
            m3_pe[0] = q.n("pe")
    m3_pe[1] = q.n("pe")
    q.wait("vector", "ac", dec_ac)
    for hhalf in range(2):
        hsl = slice(hhalf * 1024, (hhalf + 1) * 1024)
        q.emit("vector", lambda e, hhalf=hhalf, hsl=hsl, pcb=pcb: e.tensor_tensor(
            sbuf["rtmp"][:, hsl].rearrange("p (h d) -> p h d", d=HD),
            sbuf["R"][:, hsl].rearrange("p (h d) -> p h d", d=HD),
            pcb[:, hhalf * 16:(hhalf + 1) * 16].unsqueeze(2).broadcast_to([DS, 16, HD]),
            ALU.mult), inc="vd")
    q.wait("vector", "pe", m3_pe[0])
    q.wait("vector", "vd")
    q.emit("vector", lambda e: e.tensor_tensor(
        sbuf["R"][:, 0:1024], sbuf["rtmp"][:, 0:1024], ps2[:, 0:1024], ALU.add), inc="vd")
    radd_vd[0] = q.n("vd")
    q.wait("vector", "pe", m3_pe[1])
    q.emit("vector", lambda e: e.tensor_tensor(
        sbuf["R"][:, 1024:1536], sbuf["rtmp"][:, 1024:1536], ps2[:, 1024:1536], ALU.add), inc="vd")
    q.emit("vector", lambda e: e.tensor_tensor(
        sbuf["R"][:, 1536:2048], sbuf["rtmp"][:, 1536:2048], ps1[:, 1536:2048], ALU.add), inc="vd")
    radd_vd[1] = q.n("vd")
    scan_vd = q.n("vd")

    q.wait("tensor", "vd", prod_vd)
    q.wait("tensor", "vd", T.get("ytile_vd", 0))
    d1_t = {}
    exp_t = {}
    for p in range(8):
        bank = p % 3
        if p >= 3:
            q.wait("tensor", "vd", d1_t[p - 3])
        sl = slice(p * 512, (p + 1) * 512)
        q.emit("tensor", lambda e, sl=sl, bank=bank: e.matmul(
            ps1[:, bank * 512:(bank + 1) * 512], sbuf["ones_full"][:], sbuf["rowp_hi"][:, sl], start=True, stop=False))
        q.emit("tensor", lambda e, sl=sl, bank=bank: e.matmul(
            ps1[:, bank * 512:(bank + 1) * 512], sbuf["ones_full"][:], sbuf["rowp_lo"][:, sl], start=False, stop=True), inc="pe")
        pe_p = q.n("pe")
        q.wait("vector", "pe", pe_p)
        if p >= 3:
            q.wait("vector", "ac", exp_t[p - 3])
        q.emit("vector", lambda e, p=p, bank=bank: e.tensor_tensor(
            sbuf["d1"][:, bank * 512:(bank + 1) * 512].rearrange("p (a b) -> p a b", b=CH),
            ps1[:, bank * 512:(bank + 1) * 512].rearrange("p (a b) -> p a b", b=CH),
            sbuf["cumc"][:, 4 * p:4 * p + 4].unsqueeze(2).broadcast_to([CH, 4, CH]),
            ALU.subtract), inc="vd")
        q.emit("vector", lambda e, bank=bank: e.scalar_tensor_tensor(
            sbuf["d1"][:, bank * 512:(bank + 1) * 512].rearrange("p (a b) -> p a b", b=CH),
            sbuf["d1"][:, bank * 512:(bank + 1) * 512].rearrange("p (a b) -> p a b", b=CH), 0.0,
            sbuf["maskb"][:].unsqueeze(1).broadcast_to([CH, 4, CH]),
            ALU.min, ALU.add), inc="vd")
        d1_t[p] = q.n("vd")
        q.wait("scalar", "vd", d1_t[p])
        q.emit("scalar", lambda e, p=p, bank=bank: e.activation(
            sbuf["ebuf"][:, p * 512:(p + 1) * 512], sbuf["d1"][:, bank * 512:(bank + 1) * 512], AF.Exp), inc="ac")
        exp_t[p] = q.n("ac")
    diff_pe = q.n("pe")

    q.wait("vector", "ac", exp_t[7])
    q.wait("vector", "ac", sct_ac)
    q.emit("vector", lambda e: e.tensor_tensor(
        sbuf["mt"][:].rearrange("p (h l) -> p h l", h=NH),
        sbuf["ebuf"][:].rearrange("p (h l) -> p h l", h=NH),
        sbuf["scT"][:].unsqueeze(1).broadcast_to([CH, NH, CH]), ALU.mult), inc="vd")
    mt_vd = q.n("vd")

    q.wait("tensor", "vd", mt_vd)
    for h in range(NH):
        q.emit("tensor", lambda e, h=h: e.matmul(
            ps1[:, h * HD:(h + 1) * HD], sbuf["mt"][:, h * CH:(h + 1) * CH],
            sbuf["xdt"][:, h * HD:(h + 1) * HD], start=(h % 8 == 0), stop=False))
    for j in range(16):
        q.emit("tensor", lambda e, j=j: e.matmul(
            ps1[:, j * CH:(j + 1) * CH],
            sbuf["xdc"][:, j * CH:(j + 1) * CH],
            sbuf["ident"][:], start=False, stop=(j % 4 == 3)),
            inc=("pe" if j == 15 else None))
    y_pe = q.n("pe")

    q.wait("tensor", "vd", scan_vd)
    ybt_vd = {}
    m4_pe = {}

    def m4_mm(qq):
        bank = qq % 3
        q.emit("tensor", lambda e, qq=qq, bank=bank, tl0=tl0: e.matmul(
            ps2[:, bank * 512:(bank + 1) * 512], sbuf["C_fm"][:, tl0:tl0 + CH],
            sbuf["Rb"][:, qq * 512:(qq + 1) * 512], start=True, stop=True), inc="pe")
        m4_pe[qq] = q.n("pe")

    def m4_ev(qq):
        bank = qq % 3
        q.wait("vector", "pe", m4_pe[qq])
        q.emit("vector", lambda e, qq=qq, bank=bank, eacol=eacol: e.tensor_tensor(
            sbuf["ybt"][:].rearrange("p (h d) -> p h d", d=HD),
            ps2[:, bank * 512:(bank + 1) * 512].rearrange("p (h d) -> p h d", d=HD),
            eacol[:, qq * 8:(qq + 1) * 8].unsqueeze(2).broadcast_to([CH, 8, HD]),
            ALU.mult), inc="vd")
        ybt_vd[qq] = q.n("vd")
        q.emit("vector", lambda e, qq=qq: e.scalar_tensor_tensor(
            sbuf["ytile"][:, qq * 512:(qq + 1) * 512], ps1[:, qq * 512:(qq + 1) * 512], 1.0,
            sbuf["ybt"][:], ALU.mult, ALU.add), inc="vd")

    m4_mm(0)
    m4_mm(1)
    m4_mm(2)
    q.wait("vector", "pe", y_pe)
    q.wait("vector", "st_y", kb.ssd_sty)
    m4_ev(0)
    q.wait("tensor", "vd", ybt_vd[0])
    m4_mm(3)
    m4_ev(1)
    m4_ev(2)
    m4_ev(3)
    ytile_vd = q.n("vd")
    q.wait("sync", "vd", ytile_vd)
    q.dma("sync", dram["d_y"][tl0:tl0 + CH, :], sbuf["ytile"][:], "st_y")
    kb.ssd_sty = q.n("st_y")

    kb.ssd_t = dict(
        psS_vd=copies_vd, psS_ac=dec_ac, mt_vd=mt_vd, bw_vd=build_vd,
        diff_pe=diff_pe, m4_pe=m4_pe[3], m4_vd=ytile_vd, ytile_vd=ytile_vd,
    )


def phase_gather_w(kb, sc):
    nc, q, dram, sbuf = kb.nc, kb.q, kb.dram, kb.sbuf
    wsh = kb.sb("wsh", [CH, 2 * FFI], F16, scope=sc)
    wqo = kb.sb("wqo", [CH, FFI], BF16, scope=sc)
    qmx = kb.sb("qmx", [CH, 2 * (FFI // 16)], scope=sc)
    qmxr = kb.sb("qmxr", [CH, FFI // 16], scope=sc)
    qs16 = kb.sb("qs16", [CH, FFI // 16], scope=sc)
    qw = kb.sb("qw", [CH, FFI], F16, scope=sc)
    qp1 = kb.sb("qp1", [CH, FFI], F16, scope=sc)
    qta = kb.sb("qta", [CH, FFI], F16, scope=sc)
    qtb = kb.sb("qtb", [CH, FFI], F16, scope=sc)
    qp23 = kb.sb("qp23", [CH, FFI], F16, scope=sc)

    tiles = []
    for src_t, cc_t, rows, width in (
        ("wg_sh", "cc_g_in", FFI // NCORES, D),
        ("wu_sh", "cc_u_in", FFI // NCORES, D),
        ("wd_sh", "cc_dn_in", D // NCORES, FFI),
    ):
        r0 = 0
        while r0 < rows:
            rr = min(CH, rows - r0)
            tiles.append((src_t, cc_t, r0, rr, width))
            r0 += rr
    srcfree = {}
    pend = None
    for ti, (src_t, cc_t, r0, rr, width) in enumerate(tiles):
        par = ti % 2
        if ti >= 2:
            q.wait("gpsimd", "vd", srcfree[ti - 2])
        q.dma("gpsimd", wsh[0:rr, par * FFI:par * FFI + width],
              dram[src_t][r0:r0 + rr, :], "ld_wq")
        if ti == 1:
            q.dma("gpsimd", kb.dram["d_wxb"], kb.dram["w_xbcT"], "ldw0")
        q.wait("vector", "ld_wq", q.n("ld_wq"))
        qr_t = equant_pre(kb, src=wsh, sl=slice(0, rr), width=width, mx=qmx,
                          mx0=par * (FFI // 16), src0=par * FFI)
        srcfree[ti] = q.n("vd")
        if pend is not None:
            (psrc, pcc, pr0, prr, pwidth, ppar, pqr, pti) = pend
            q.wait("vector", "st_wq")
            equant_post(kb, src=wsh, sl=slice(0, prr), width=pwidth, mx=qmx, mxr=qmxr,
                        s16=qs16, w_=qw, p1=qp1, t_a=qta, t_b=qtb, p23=qp23, out=wqo,
                        qr_tgt=pqr, mx0=ppar * (FFI // 16), src0=ppar * FFI)
            srcfree[pti] = kb.srcfree_vd
            q.wait("sync", "vd")
            q.dma("sync", dram[pcc][pr0:pr0 + prr, :], wqo[0:prr, 0:pwidth], "st_wq")
        pend = (src_t, cc_t, r0, rr, width, par, qr_t, ti)
    (psrc, pcc, pr0, prr, pwidth, ppar, pqr, pti) = pend
    q.wait("vector", "st_wq")
    equant_post(kb, src=wsh, sl=slice(0, prr), width=pwidth, mx=qmx, mxr=qmxr,
                s16=qs16, w_=qw, p1=qp1, t_a=qta, t_b=qtb, p23=qp23, out=wqo,
                qr_tgt=pqr, mx0=ppar * (FFI // 16), src0=ppar * FFI)
    q.wait("sync", "vd")
    q.dma("sync", dram[pcc][pr0:pr0 + prr, :], wqo[0:prr, 0:pwidth], "st_wq")
    q.wait("gpsimd", "st_wq")
    for kind, i, o in (("g", "cc_g_in", "cc_g_out"), ("u", "cc_u_in", "cc_u_out"),
                       ("d", "cc_dn_in", "cc_dn_out")):
        q.emit("gpsimd", lambda e, i=i, o=o: e.collective_compute(
            "AllGather", ALU.bypass, replica_groups=[list(range(NCORES))],
            ins=[dram[i]], outs=[dram[o]]), inc="ccd")


def phase_gather_sl(kb):
    nc, q, dram, sbuf = kb.nc, kb.q, kb.dram, kb.sbuf
    q.wait("gpsimd", "st_s")
    for kind, i, o in (("s", "cc_s_in", "cc_s_out"), ("l", "cc_l_in", "cc_l_out")):
        q.emit("gpsimd", lambda e, i=i, o=o: e.collective_compute(
            "AllGather", ALU.bypass, replica_groups=[list(range(NCORES))],
            ins=[dram[i]], outs=[dram[o]]), inc="ccd")
    q.wait("sync", "ccd")
    kb.tap("t_wg", dram["cc_g_out"])
    kb.tap("t_wdn", dram["cc_dn_out"])
    kb.tap("t_sg", dram["cc_s_out"])
    kb.tap("t_lg", dram["cc_l_out"])


def phase_ffn(kb):
    nc, q, dram, sbuf = kb.nc, kb.q, kb.dram, kb.sbuf
    OGS = [(0, 1024), (1024, 1024), (2048, 768)]
    with contextlib.ExitStack() as sc:
        xqf = kb.sb("xqf", [CH, 8 * TOK], BF16, scope=sc)
        wgt = kb.sb("wgt", [CH, 8 * 1024], BF16, scope=sc)
        wut = kb.sb("wut", [CH, 8 * 1024], BF16, scope=sc)
        sg = kb.sb("sg", [CH, 2 * 1024], F16, scope=sc)
        hbuf = kb.sb("hbuf", [CH, 2 * 1024], F16, scope=sc)
        hq = kb.sb("hq", [CH, 2 * 1024], BF16, scope=sc)
        fmx = kb.sb("fmx", [CH, 128], scope=sc)
        fmxr = kb.sb("fmxr", [CH, 64], scope=sc)
        fs16 = kb.sb("fs16", [CH, 64], scope=sc)
        fw = kb.sb("fw", [CH, 1024], F16, scope=sc)
        fp1 = kb.sb("fp1", [CH, 1024], F16, scope=sc)
        fta = kb.sb("fta", [CH, 1024], F16, scope=sc)
        ftb = kb.sb("ftb", [CH, 1024], F16, scope=sc)
        fp23 = kb.sb("fp23", [CH, 1024], F16, scope=sc)
        psg = kb.ps("psg", [CH, 2048], scope=sc)
        psu = kb.ps("psu", [CH, 2048], scope=sc)

        q.wait("scalar", "st_xq")
        q.wait("scalar", "ccd")
        q.wait("scalar", "st_wq")
        if "t_xq3" in kb.taps:
            q.wait("sync", "st_xq")
            q.dma("sync", hq[:, 0:D], dram["d_xq"][0:CH, :], "ld_dbg")
            q.wait("sync", "ld_dbg")
            kb.tap("t_xq3", hq[:, 0:D])
            q.wait("vector", "st_tap")
        for k in range(8):
            q.dma("scalar", xqf[:, k * TOK:(k + 1) * TOK],
                  dram["d_xq"][:, k * CH:(k + 1) * CH], "ld_ff", transpose=True)
        q.wait("sync", "ld_ff")
        kb.tap("t_xqf", xqf[:])
        ac_t = {}
        vd_t = {}
        pe_t = {}
        st_t = {}
        gtc = 0
        pend = None
        for og0, ogw in OGS:
            q.wait("scalar", "pe")
            for k in range(8):
                q.dma("scalar", wgt[:, k * 1024:k * 1024 + ogw],
                      dram["cc_g_out"][og0:og0 + ogw, k * CH:(k + 1) * CH], "ld_ff",
                      transpose=True)
                q.dma("scalar", wut[:, k * 1024:k * 1024 + ogw],
                      dram["cc_u_out"][og0:og0 + ogw, k * CH:(k + 1) * CH], "ld_ff",
                      transpose=True)
            q.wait("tensor", "ld_ff")
            for tc in range(16):
                p = gtc % 2
                if gtc >= 2:
                    q.wait("tensor", "ac", ac_t[gtc - 2])
                    q.wait("tensor", "vd", vd_t[gtc - 2])
                pieces = [(0, 512), (512, ogw - 512)]
                for k in range(8):
                    for f0, fw_ in pieces:
                        q.emit("tensor", lambda e, tc=tc, k=k, f0=f0, fw_=fw_, p=p:
                               e.matmul(psg[:, p * 1024 + f0:p * 1024 + f0 + fw_],
                                        xqf[:, k * TOK + tc * CH:k * TOK + (tc + 1) * CH],
                                        wgt[:, k * 1024 + f0:k * 1024 + f0 + fw_],
                                        start=(k == 0), stop=(k == 7)))
                for k in range(8):
                    for f0, fw_ in pieces:
                        q.emit("tensor", lambda e, tc=tc, k=k, f0=f0, fw_=fw_, p=p:
                               e.matmul(psu[:, p * 1024 + f0:p * 1024 + f0 + fw_],
                                        xqf[:, k * TOK + tc * CH:k * TOK + (tc + 1) * CH],
                                        wut[:, k * 1024 + f0:k * 1024 + f0 + fw_],
                                        start=(k == 0), stop=(k == 7)),
                               inc=("pe" if (k == 7 and f0 != 0) else None))
                pe_t[gtc] = q.n("pe")
                q.wait("scalar", "pe", pe_t[gtc])
                q.emit("scalar", lambda e, ogw=ogw, p=p: e.activation(
                    sg[:, p * 1024:p * 1024 + ogw], psg[:, p * 1024:p * 1024 + ogw],
                    AF.Silu), inc="ac")
                ac_t[gtc] = q.n("ac")
                q.wait("vector", "ac", ac_t[gtc])
                q.emit("vector", lambda e, ogw=ogw, p=p: e.tensor_tensor(
                    hbuf[:, p * 1024:p * 1024 + ogw], sg[:, p * 1024:p * 1024 + ogw],
                    psu[:, p * 1024:p * 1024 + ogw], ALU.mult), inc="vd")
                vd_t[gtc] = q.n("vd")
                qr_t = equant_pre(kb, src=hbuf, sl=slice(0, CH), width=ogw, mx=fmx,
                                  mx0=p * 64, src0=p * 1024)
                if pend is not None:
                    pgtc, ptc, pog0, pogw, pp, pqr = pend
                    q.wait("vector", "st_hq", st_t.get(pgtc - 2, 0))
                    equant_post(kb, src=hbuf, sl=slice(0, CH), width=pogw, mx=fmx,
                                mxr=fmxr, s16=fs16, w_=fw, p1=fp1, t_a=fta, t_b=ftb,
                                p23=fp23, out=hq, qr_tgt=pqr,
                                osl=slice(pp * 1024, pp * 1024 + pogw),
                                mx0=pp * 64, src0=pp * 1024)
                    q.wait("sync", "vd")
                    q.dma("sync", dram["d_hq"][ptc * CH:(ptc + 1) * CH, pog0:pog0 + pogw],
                          hq[0:CH, pp * 1024:pp * 1024 + pogw], "st_hq")
                    st_t[pgtc] = q.n("st_hq")
                pend = (gtc, tc, og0, ogw, p, qr_t)
                gtc += 1
        pgtc, ptc, pog0, pogw, pp, pqr = pend
        q.wait("vector", "st_hq", st_t.get(pgtc - 2, 0))
        equant_post(kb, src=hbuf, sl=slice(0, CH), width=pogw, mx=fmx,
                    mxr=fmxr, s16=fs16, w_=fw, p1=fp1, t_a=fta, t_b=ftb,
                    p23=fp23, out=hq, qr_tgt=pqr,
                    osl=slice(pp * 1024, pp * 1024 + pogw),
                    mx0=pp * 64, src0=pp * 1024)
        q.wait("sync", "vd")
        q.dma("sync", dram["d_hq"][ptc * CH:(ptc + 1) * CH, pog0:pog0 + pogw],
              hq[0:CH, pp * 1024:pp * 1024 + pogw], "st_hq")
        st_t[pgtc] = q.n("st_hq")
    q.wait("sync", "st_hq")
    kb.tap("t_hq", dram["d_hq"])

    q.tag = "ffn_dn"
    with contextlib.ExitStack() as sc:
        dnt = kb.sb("dnt", [CH, 22 * 1024], BF16, scope=sc)
        hqt = kb.sb("hqt", [CH, 4 * 512], BF16, scope=sc)
        psD = kb.ps("psD", [CH, 2048], scope=sc)
        psD2 = kb.ps("psD2", [CH, 2048], scope=sc)
        q.wait("scalar", "ccd")
        q.wait("scalar", "st_hq")
        q.wait("scalar", "vd")
        for o in range(22):
            q.dma("scalar", dnt[:, o * 1024:(o + 1) * 1024],
                  dram["cc_dn_out"][0:D, o * CH:(o + 1) * CH], "ld_dn", transpose=True)
        q.wait("tensor", "ld_dn")
        q.wait("tensor", "vd")
        dpe = {}
        dld = {}
        for t in range(4):
            q.wait("tensor", "ac")
            for o in range(22):
                buf = o % 2
                go = t * 22 + o
                if go >= 2:
                    q.wait("scalar", "pe", dpe[go - 2])
                q.dma("scalar", hqt[:, buf * 512:(buf + 1) * 512],
                      dram["d_hq"][t * 512:(t + 1) * 512, o * CH:(o + 1) * CH], "ld_hf",
                      transpose=True)
                dld[go] = q.n("ld_hf")
                q.wait("tensor", "ld_hf", dld[go])
                for ee in range(8):
                    ps = psD if ee < 4 else psD2
                    q.emit("tensor", lambda e, o=o, ee=ee, buf=buf, ps=ps:
                           e.matmul(ps[:, (ee % 4) * 512:(ee % 4 + 1) * 512],
                                    dnt[:, o * 1024 + ee * CH:o * 1024 + (ee + 1) * CH],
                                    hqt[:, buf * 512:(buf + 1) * 512],
                                    start=(o == 0), stop=(o == 21)),
                           inc=("pe" if ee == 7 else None))
                dpe[go] = q.n("pe")
            q.wait("scalar", "pe")
            for ee in range(8):
                ps = psD if ee < 4 else psD2
                q.emit("scalar", lambda e, t=t, ee=ee, ps=ps: e.activation(
                    sbuf["g_fm"][:, ee * TOK + t * 512:ee * TOK + (t + 1) * 512],
                    ps[:, (ee % 4) * 512:(ee % 4 + 1) * 512], AF.Silu), inc="ac")
    q.wait("sync", "st_hq")
    kb.tap("t_gfm", kb.sbuf["g_fm"][:])


def phase_tail(kb):
    nc, q, dram, sbuf = kb.nc, kb.q, kb.dram, kb.sbuf
    with contextlib.ExitStack() as sc:
        Lmat = kb.sb("Lmat", [NH, NCORES], scope=sc)
        etmp = kb.sb("etmp", [NH, NCORES * NCORES], scope=sc)
        esum = kb.sb("esum", [NH, NCORES], scope=sc)
        coef = kb.sb("coef", [NH, NCORES], scope=sc)
        crow = kb.sb("crow", [1, NCORES * NH], F16, scope=sc)
        cbc = kb.sb("cbc", [CH, NCORES * NH], scope=sc)
        Sj = kb.sb("Sj", [DS, 2 * NH * HD], scope=sc)
        Rtmp2 = kb.sb("Rtmp2", [DS, NH * HD], scope=sc)
        Rin = kb.sb("Rin", [DS, NH * HD], scope=sc)
        Rinb = kb.sb("Rinb", [DS, NH * HD], BF16, scope=sc)
        lcz = kb.sb("lcz2", [NH, NCH], scope=sc)
        lrow = kb.sb("lrow", [1, NCH * NH], F16, scope=sc)
        ead = kb.sb("ead", [CH, 2 * NH], scope=sc)
        ycorr = kb.sb("ycorr", [CH, 512], scope=sc)
        yload = kb.sb("yload", [CH, 2 * DI], BF16, scope=sc)
        psC = kb.ps("psC", [CH, 2048], scope=sc)
        psS2 = kb.ps("psS2", [CH, 512], scope=sc)

        q.wait("sync", "ccd")
        q.wait("sync", "pe")
        q.wait("sync", "ac")
        q.dma("sync", Lmat[:], dram["cc_l_out"].rearrange("(r h) b -> h (r b)", h=NH),
              "ld_t", allow_slow_non_contiguous=True)
        q.wait("vector", "ld_t")
        q.emit("vector", lambda e: e.tensor_tensor(
            etmp[:].rearrange("p (j i) -> p j i", j=NCORES),
            Lmat[:].unsqueeze(1).broadcast_to([NH, NCORES, NCORES]),
            sbuf["bmx_s"][:].rearrange("p (j i) -> p j i", j=NCORES), ALU.mult), inc="vd")
        q.wait("vector", "vd")
        q.emit("vector", lambda e: e.tensor_reduce(
            esum[:].unsqueeze(2), etmp[:].rearrange("p (j i) -> p j i", j=NCORES),
            AX.X, ALU.add))
        q.emit("vector", lambda e: e.drain(), inc="vd")
        q.wait("scalar", "vd")
        q.emit("scalar", lambda e: e.activation(esum[:], esum[:], AF.Exp))
        q.emit("scalar", lambda e: e.drain(), inc="ac")
        q.wait("vector", "ac")
        q.emit("vector", lambda e: e.tensor_tensor(coef[:], esum[:], sbuf["pmx_s"][:], ALU.mult), inc="vd")
        q.wait("sync", "vd")
        q.dma("sync", dram["d_coef"].rearrange("j h -> h j"), coef[:], "st_t",
              allow_slow_non_contiguous=True)
        q.wait("sync", "st_t")
        kb.tap("t_coef", dram["d_coef"])
        q.wait("gpsimd", "st_t")
        q.emit("gpsimd", lambda e: e.memset(Rin[:], 0.0), inc="gp")
        q.dma("gpsimd", crow[:], dram["d_coef"].rearrange("j h -> (j h)").unsqueeze(0),
              "ld_tg")
        q.wait("tensor", "ld_tg", q.n("ld_tg"))
        q.wait("tensor", "vd")
        for j in range(NCORES):
            q.emit("tensor", lambda e, j=j: e.matmul(
                psS2[:, j * NH:(j + 1) * NH], sbuf["ones1_bf"][:],
                crow[0:1, j * NH:(j + 1) * NH], start=True, stop=True),
                inc=("pe" if j == NCORES - 1 else None))
        cbc_pe = q.n("pe")
        sld = {}
        q.wait("sync", "vd")
        q.dma("sync", Sj[:, 0:NH * HD], dram["cc_s_out"][0:DS, :], "ld_t")
        sld[0] = q.n("ld_t")
        q.wait("vector", "pe", cbc_pe)
        q.wait("vector", "gp")
        q.emit("vector", lambda e: e.tensor_copy(cbc[:], psS2[:, 0:NCORES * NH]), inc="vd")
        rt_t = {}
        for j in range(NCORES):
            jp = j % 2
            if j + 1 < NCORES:
                if j >= 1:
                    q.wait("sync", "vd", rt_t[j - 1])
                q.dma("sync", Sj[:, (1 - jp) * NH * HD:(2 - jp) * NH * HD],
                      dram["cc_s_out"][(j + 1) * DS:(j + 2) * DS, :], "ld_t")
                sld[j + 1] = q.n("ld_t")
            q.wait("vector", "ld_t", sld[j])
            q.emit("vector", lambda e, j=j, jp=jp: e.tensor_tensor(
                Rtmp2[:].rearrange("p (h d) -> p h d", d=HD),
                Sj[:, jp * NH * HD:(jp + 1) * NH * HD].rearrange("p (h d) -> p h d", d=HD),
                cbc[:, j * NH:(j + 1) * NH].unsqueeze(2).broadcast_to([DS, NH, HD]),
                ALU.mult), inc="vd")
            rt_t[j] = q.n("vd")
            q.wait("vector", "vd")
            q.emit("vector", lambda e: e.tensor_tensor(Rin[:], Rin[:], Rtmp2[:], ALU.add), inc="vd")
        q.wait("vector", "vd")
        q.emit("vector", lambda e: e.tensor_copy(Rinb[:], Rin[:]), inc="vd")

        q.emit("vector", lambda e: e.memset(lcz[:, 0:1], 0.0), inc="vd")
        for c in range(1, NCH):
            q.emit("vector", lambda e, c=c: e.tensor_tensor(
                lcz[:, c:c + 1], lcz[:, c - 1:c], sbuf["az_fm"][:, c - 1:c], ALU.add))
            q.emit("vector", lambda e: e.drain(), inc="vd")
        q.wait("sync", "vd")
        q.dma("sync", dram["d_lcz"].rearrange("c h -> h c"), lcz[:], "st_t",
              allow_slow_non_contiguous=True)
        q.wait("sync", "st_t")
        kb.tap("t_lcz", dram["d_lcz"])
        q.wait("sync", "vd")
        kb.tap("t_rin", Rin[:])

        q.tag = "tail_corr"
        q.wait("gpsimd", "st_t")
        q.wait("gpsimd", "st_y")
        q.dma("gpsimd", lrow[:], dram["d_lcz"].rearrange("c h -> (c h)").unsqueeze(0),
              "ld_tg")
        lrow_t = q.n("ld_tg")
        q.dma("gpsimd", yload[:, 0:DI], dram["d_y"][0:CH, :], "ld_y")
        yl_t = {0: q.n("ld_y")}
        q.wait("tensor", "ld_tg", lrow_t)
        q.wait("tensor", "vd")
        exp_t = {}
        mm3_t = {}
        pc_t = {}
        yc_t = {}
        yc_half = {}
        styt = {}
        for gc in range(NCH):
            c2 = gc % 2
            tl0 = gc * CH
            ahc = sbuf["a_hi"][:, gc * NH:(gc + 1) * NH]
            alc = sbuf["a_lo"][:, gc * NH:(gc + 1) * NH]
            s0 = 256 + c2 * 64
            if gc >= 2:
                q.wait("tensor", "ac", exp_t[gc - 2])
            q.emit("tensor", lambda e, ahc=ahc, s0=s0: e.matmul(psS2[:, s0:s0 + NH], sbuf["ut_incl"][:], ahc, start=True, stop=False))
            q.emit("tensor", lambda e, alc=alc, s0=s0: e.matmul(psS2[:, s0:s0 + NH], sbuf["ut_incl"][:], alc, start=False, stop=False))
            q.emit("tensor", lambda e, gc=gc, s0=s0: e.matmul(psS2[:, s0:s0 + NH], sbuf["ones1_bf"][:], lrow[0:1, gc * NH:(gc + 1) * NH], start=False, stop=True), inc="pe")
            mm3_t[gc] = q.n("pe")
            q.wait("scalar", "pe", mm3_t[gc])
            if gc >= 2:
                q.wait("scalar", "vd", yc_t[gc - 2])
            q.emit("scalar", lambda e, c2=c2, s0=s0: e.activation(
                ead[:, c2 * NH:(c2 + 1) * NH], psS2[:, s0:s0 + NH], AF.Exp), inc="ac")
            exp_t[gc] = q.n("ac")
            if gc + 1 < NCH:
                if gc >= 1:
                    q.wait("gpsimd", "st_y", styt[gc - 1])
                q.dma("gpsimd", yload[:, (1 - c2) * DI:(2 - c2) * DI],
                      dram["d_y"][(gc + 1) * CH:(gc + 2) * CH, :], "ld_y")
                yl_t[gc + 1] = q.n("ld_y")
            for half in range(2):
                if gc >= 1:
                    q.wait("tensor", "vd", yc_half[(gc - 1, half)])
                for qq in (half * 2, half * 2 + 1):
                    q.emit("tensor", lambda e, qq=qq, tl0=tl0: e.matmul(
                        psC[:, qq * 512:(qq + 1) * 512], sbuf["C_fm"][:, tl0:tl0 + CH],
                        Rinb[:, qq * 512:(qq + 1) * 512], start=True, stop=True),
                        inc=("pe" if qq % 2 == 1 else None))
                pc_t[(gc, half)] = q.n("pe")
            q.wait("vector", "ac", exp_t[gc])
            q.wait("vector", "ld_y", yl_t[gc])
            for half in range(2):
                q.wait("vector", "pe", pc_t[(gc, half)])
                for qq in (half * 2, half * 2 + 1):
                    q.emit("vector", lambda e, qq=qq, c2=c2: e.tensor_tensor(
                        ycorr[:].rearrange("p (h d) -> p h d", d=HD),
                        psC[:, qq * 512:(qq + 1) * 512].rearrange("p (h d) -> p h d", d=HD),
                        ead[:, c2 * NH + qq * 8:c2 * NH + (qq + 1) * 8].unsqueeze(2).broadcast_to([CH, 8, HD]),
                        ALU.mult), inc="vd")
                    q.wait("vector", "vd")
                    q.emit("vector", lambda e, qq=qq, c2=c2: e.tensor_tensor(
                        yload[:, c2 * DI + qq * 512:c2 * DI + (qq + 1) * 512],
                        yload[:, c2 * DI + qq * 512:c2 * DI + (qq + 1) * 512],
                        ycorr[:], ALU.add), inc="vd")
                yc_half[(gc, half)] = q.n("vd")
            yc_t[gc] = q.n("vd")
            q.wait("sync", "vd", yc_t[gc])
            q.dma("sync", dram["d_y"][tl0:tl0 + CH, :], yload[:, c2 * DI:(c2 + 1) * DI], "st_y")
            styt[gc] = q.n("st_y")
        q.wait("sync", "st_y")
        kb.tap("t_y2", dram["d_y"])

    with contextlib.ExitStack() as sct:
        t1 = kb.sb("t1", [CH, 16 * TOK], BF16, scope=sct)
        ffff = phase_tail2(kb, sct, t1)


def phase_tail2(kb, sct, t1):
    nc, q, dram, sbuf = kb.nc, kb.q, kb.dram, kb.sbuf
    q.tag = "ztail"
    with contextlib.ExitStack() as sc:
        xn2 = kb.sb("xn2", [CH, 8 * TOKH], BF16, scope=sc)
        wz = kb.sb("wz", [CH, 2 * 8 * CH], BF16, scope=sc)
        szc = kb.sb("szc", [CH, TOK], F16, scope=sc)
        yfm = kb.sb("yfm", [CH, 3 * TOK], BF16, scope=sc)
        sq2 = kb.sb("sq2", [CH, 3 * 1024], BF16, scope=sc)
        ones32b = kb.sb("ones32b", [CH, NH], BF16, scope=sc)
        rrow = kb.sb("rrow", [NH, TOK], scope=sc)
        psZ = kb.ps("psZ", [CH, 2048], scope=sc)
        psR = kb.ps("psR", [CH, 2048], scope=sc)

        q.wait("gpsimd", "st_y")
        q.emit("gpsimd", lambda e: e.memset(ones32b[:], 1.0), inc="gp")
        q.wait("sync", "st_xn")
        q.wait("sync", "st_y")
        q.dma("sync", xn2[:], dram["d_xn"], "ld_t")
        xn2_t = q.n("ld_t")
        q.wait("sync", "ldwz", 16)
        q.wait("scalar", "st_y")
        zmm_t = {}
        silu_t = {}
        gate_t = {}
        sq2_t = {}
        rms_t = {}
        wz_t = {}
        yfm_t = {}
        for fc in range(16):
            pf = fc % 2
            if fc == 0:
                q.dma("sync", wz[:, 0:8 * CH].rearrange("p (k w) -> p k w", k=8),
                      dram["d_wz"][:, 0:CH].rearrange("(k p) w -> p k w", p=CH), "ld_wz")
                wz_t[0] = q.n("ld_wz")
                q.dma("scalar", yfm[:, 0:TOK], dram["d_y"][:, 0:CH], "ld_yf",
                      transpose=True)
                yfm_t[0] = q.n("ld_yf")
            if fc + 1 < 16:
                if fc >= 1:
                    q.wait("sync", "pe", zmm_t[2 * fc - 1])
                q.dma("sync", wz[:, (1 - pf) * 8 * CH:(2 - pf) * 8 * CH].rearrange("p (k w) -> p k w", k=8),
                      dram["d_wz"][:, (fc + 1) * CH:(fc + 2) * CH].rearrange("(k p) w -> p k w", p=CH),
                      "ld_wz")
                wz_t[fc + 1] = q.n("ld_wz")
                if fc >= 1:
                    q.wait("scalar", "vd", gate_t[2 * fc - 1])
                q.dma("scalar", yfm[:, ((fc + 1) % 3) * TOK:((fc + 1) % 3 + 1) * TOK],
                      dram["d_y"][:, (fc + 1) * CH:(fc + 2) * CH], "ld_yf",
                      transpose=True)
                yfm_t[fc + 1] = q.n("ld_yf")
            for th in range(2):
                i = fc * 2 + th
                ip = i % 2
                if i == 0:
                    q.wait("tensor", "ld_t", xn2_t)
                    q.wait("tensor", "ac")
                    q.wait("tensor", "vd")
                if th == 0:
                    q.wait("tensor", "ld_wz", wz_t[fc])
                if i >= 2:
                    q.wait("tensor", "ac", silu_t[i - 2])
                for tt_ in range(2):
                    for k in range(8):
                        q.emit("tensor", lambda e, th=th, tt_=tt_, k=k, pf=pf, ip=ip:
                               e.matmul(psZ[:, ip * 1024 + tt_ * 512:ip * 1024 + (tt_ + 1) * 512],
                                        wz[:, pf * 1024 + k * CH:pf * 1024 + (k + 1) * CH],
                                        xn2[:, k * TOKH + HALO + th * 1024 + tt_ * 512: k * TOKH + HALO + th * 1024 + (tt_ + 1) * 512],
                                        start=(k == 0), stop=(k == 7)),
                               inc=("pe" if (k == 7 and tt_ == 1) else None))
                zmm_t[i] = q.n("pe")
                q.wait("scalar", "pe", zmm_t[i])
                if i >= 2:
                    q.wait("scalar", "vd", gate_t[i - 2])
                q.emit("scalar", lambda e, ip=ip: e.activation(
                    szc[:, ip * 1024:(ip + 1) * 1024], psZ[:, ip * 1024:(ip + 1) * 1024],
                    AF.Silu), inc="ac")
                silu_t[i] = q.n("ac")
                q.wait("vector", "ld_yf", yfm_t[fc])
                q.wait("vector", "ac", silu_t[i])
                q.emit("vector", lambda e, fc=fc, th=th, pf=pf, ip=ip: e.tensor_tensor(
                    t1[:, fc * TOK + th * 1024:fc * TOK + (th + 1) * 1024],
                    yfm[:, (fc % 3) * TOK + th * 1024:(fc % 3) * TOK + (th + 1) * 1024],
                    szc[:, ip * 1024:(ip + 1) * 1024], ALU.mult), inc="vd")
                gate_t[i] = q.n("vd")
                q.wait("scalar", "vd", gate_t[i])
                i3 = i % 3
                if i >= 3:
                    q.wait("scalar", "pe", rms_t[i - 3])
                q.emit("scalar", lambda e, fc=fc, th=th, i3=i3: e.activation(
                    sq2[:, i3 * 1024:(i3 + 1) * 1024],
                    t1[:, fc * TOK + th * 1024:fc * TOK + (th + 1) * 1024], AF.Square), inc="ac")
                sq2_t[i] = q.n("ac")
                for j in ([i - 2] if i >= 2 else []) + ([i - 1, i] if i == 31 else []):
                    jfc, jth, j3 = j // 2, j % 2, j % 3
                    q.wait("tensor", "ac", sq2_t[j])
                    for tt_ in range(2):
                        q.emit("tensor", lambda e, jfc=jfc, jth=jth, tt_=tt_, j3=j3: e.matmul(
                            psR[0:NH, jth * 1024 + tt_ * 512:jth * 1024 + (tt_ + 1) * 512],
                            ones32b[:], sq2[:, j3 * 1024 + tt_ * 512:j3 * 1024 + (tt_ + 1) * 512],
                            start=(jfc == 0), stop=(jfc == 15)),
                            inc=("pe" if tt_ == 1 else None))
                    rms_t[j] = q.n("pe")
        q.wait("vector", "pe")
        q.emit("vector", lambda e: e.tensor_copy(rrow[:], psR[0:NH, 0:TOK]), inc="vd")
        q.wait("vector", "vd")
        q.emit("vector", lambda e: e.tensor_scalar(rrow[:], rrow[:], 1.0 / DI, None, ALU.mult))
        q.emit("vector", lambda e: e.drain(), inc="vd")
        q.wait("scalar", "vd")
        q.emit("scalar", lambda e: e.activation(rrow[:], rrow[:], AF.Ln, bias=sbuf["epscol"][0:NH, 0:1]))
        q.emit("scalar", lambda e: e.drain())
        q.emit("scalar", lambda e: e.activation(rrow[:], rrow[:], AF.Exp, scale=-0.5))
        q.emit("scalar", lambda e: e.drain(), inc="ac")
        q.wait("sync", "ac")
        q.dma("sync", dram["d_rstd"], rrow[0:1, :], "st_t")
        q.wait("sync", "st_t")
        kb.tap("t_rstd", dram["d_rstd"])
        kb.tap("t_t1", t1[:])

    q.tag = "oproj"
    with contextlib.ExitStack() as sc2:
        wo = kb.sb("wo", [CH, 16 * D], BF16, scope=sc2)
        rfm = kb.sb("rfm", [CH, 8 * 256], BF16, scope=sc2)
        utile2 = kb.sb("utile2", [CH, 2 * D], scope=sc2)
        rcol2 = kb.sb("rcol2", [CH, 2], scope=sc2)
        otile = kb.sb("otile", [CH, 2 * D], scope=sc2)
        psO = kb.ps("psO", [CH, 2048], scope=sc2)
        psO2 = kb.ps("psO2", [CH, 2048], scope=sc2)
        q.wait("scalar", "st_t")
        q.wait("scalar", "ldwo", 16)
        q.dma("scalar", wo[:].rearrange("p (i w) -> p i w", i=16),
              dram["d_wout"].rearrange("(i p) w -> p i w", p=CH), "ld_wo")
        q.wait("vector", "ld_wo")
        for i in range(16):
            q.emit("vector", lambda e, i=i: e.tensor_scalar(
                wo[:, i * D:(i + 1) * D], wo[:, i * D:(i + 1) * D],
                sbuf["nwcol"][:, i:i + 1], None, ALU.mult), inc="vd")
        q.wait("vector", "ac")
        q.wait("gpsimd", "vd")
        q.wait("gpsimd", "st_t")
        u2t = {}
        for tc in (0, 1):
            q.dma("gpsimd", utile2[:, (tc % 2) * D:(tc % 2 + 1) * D],
                  dram["u"][HALO + tc * CH:HALO + (tc + 1) * CH, :], "ld_u2")
            q.dma("gpsimd", rcol2[:, tc % 2:tc % 2 + 1],
                  dram["d_rstd"][0:1, tc * CH:(tc + 1) * CH].rearrange("a p -> (a p)").unsqueeze(1), "ld_u2")
            u2t[tc] = q.n("ld_u2")
        rfm_t = {}
        trans_t = {}
        fin_t = {}
        sto_t = {}
        mmO_t = {}
        q.wait("tensor", "vd")
        q.wait("tensor", "ac")
        for s in range(8):
            if s >= 1:
                q.wait("tensor", "vd", rfm_t[s - 1])
            for ee in range(8):
                for i in range(16):
                    q.emit("tensor", lambda e, s=s, i=i, ee=ee: e.matmul(
                        psO[:, ee * 256:(ee + 1) * 256],
                        wo[:, i * D + ee * CH:i * D + (ee + 1) * CH],
                        t1[:, i * TOK + s * 256:i * TOK + (s + 1) * 256],
                        start=(i == 0), stop=(i == 15)),
                        inc=("pe" if (i == 15 and ee == 7) else None))
            mmO_t[s] = q.n("pe")
            q.wait("vector", "pe", mmO_t[s])
            for ee in range(8):
                q.emit("vector", lambda e, s=s, ee=ee: e.tensor_tensor(
                    rfm[:, ee * 256:(ee + 1) * 256],
                    psO[:, ee * 256:(ee + 1) * 256],
                    sbuf["g_fm"][:, ee * TOK + s * 256:ee * TOK + (s + 1) * 256],
                    ALU.mult), inc="vd")
            rfm_t[s] = q.n("vd")
            q.wait("tensor", "vd", rfm_t[s])
            if s >= 1:
                q.wait("tensor", "vd", fin_t[2 * s - 1])
            for tc2 in range(2):
                for ee in range(8):
                    q.emit("tensor", lambda e, tc2=tc2, ee=ee: e.matmul(
                        psO2[:, tc2 * 1024 + ee * CH:tc2 * 1024 + (ee + 1) * CH],
                        rfm[:, ee * 256 + tc2 * CH:ee * 256 + (tc2 + 1) * CH],
                        sbuf["ident"][:], start=True, stop=True),
                        inc=("pe" if ee == 7 else None))
                trans_t[(s, tc2)] = q.n("pe")
            for tc2 in range(2):
                tc = 2 * s + tc2
                q.wait("vector", "pe", trans_t[(s, tc2)])
                q.wait("vector", "ld_u2", u2t[tc])
                q.wait("vector", "st_out", sto_t.get(tc - 2, 0))
                q.emit("vector", lambda e, tc=tc, tc2=tc2: e.scalar_tensor_tensor(
                    otile[:, (tc % 2) * D:(tc % 2 + 1) * D], psO2[:, tc2 * 1024:tc2 * 1024 + D],
                    rcol2[:, tc % 2:tc % 2 + 1],
                    utile2[:, (tc % 2) * D:(tc % 2 + 1) * D], ALU.mult, ALU.add), inc="vd")
                fin_t[tc] = q.n("vd")
                q.wait("sync", "vd", fin_t[tc])
                q.dma("sync", dram["out"][tc * CH:(tc + 1) * CH, :],
                      otile[:, (tc % 2) * D:(tc % 2 + 1) * D], "st_out")
                sto_t[tc] = q.n("st_out")
                if tc + 2 < 16:
                    q.wait("gpsimd", "vd", fin_t[tc])
                    q.dma("gpsimd", utile2[:, (tc % 2) * D:(tc % 2 + 1) * D],
                          dram["u"][HALO + (tc + 2) * CH:HALO + (tc + 3) * CH, :], "ld_u2")
                    q.dma("gpsimd", rcol2[:, tc % 2:tc % 2 + 1],
                          dram["d_rstd"][0:1, (tc + 2) * CH:(tc + 3) * CH].rearrange("a p -> (a p)").unsqueeze(1), "ld_u2")
                    u2t[tc + 2] = q.n("ld_u2")


def _prep_inputs(inputs):
    u = np.asarray(inputs["L0_attn_u"])
    ipw = np.asarray(inputs["L0_attn_in_proj_w"])
    conv_w = np.asarray(inputs["L0_attn_conv_w"])
    conv_b = np.asarray(inputs["L0_attn_conv_b"])
    dt_bias = np.asarray(inputs["L0_attn_dt_bias"])
    a_log = np.asarray(inputs["L0_attn_A_log"])
    Dv = np.asarray(inputs["L0_attn_D"])
    norm_w = np.asarray(inputs["L0_attn_norm_w"])
    opw = np.asarray(inputs["L0_attn_out_proj_w"])
    wg = np.asarray(inputs["L0_ffn_w_gate"])
    wu = np.asarray(inputs["L0_ffn_w_up"])
    wdn = np.asarray(inputs["L0_ffn_w_down"])

    w_zT = np.ascontiguousarray(ipw[:DI].T.astype(np.float32))
    w_xbcT = np.ascontiguousarray(ipw[DI:].T.astype(np.float32))
    w_outT = np.ascontiguousarray(opw.T.astype(np.float32))
    conv_w4 = np.ascontiguousarray(conv_w[:, 0, :].astype(np.float32))
    dcol64 = np.ascontiguousarray(np.repeat(np.asarray(Dv, np.float32), HD).reshape(-1, 1))

    base = dict(
        w_xbcT=w_xbcT, w_zT=w_zT, w_outT=w_outT,
        conv_w4=conv_w4,
        conv_b=np.ascontiguousarray(conv_b.reshape(-1, 1), np.float32),
        dt_bias=np.ascontiguousarray(dt_bias.reshape(-1, 1), np.float32),
        a_log=np.ascontiguousarray(a_log.reshape(-1, 1), np.float32),
        dcol64=dcol64,
        norm_w=np.ascontiguousarray(norm_w.reshape(-1, 1), np.float32),
    )
    in_maps = []
    for c in range(NCORES):
        b, sidx = divmod(c, 4)
        s0 = sidx * TOK
        if s0 == 0:
            upad = np.concatenate([np.zeros((HALO, D), np.float32), u[b, :TOK]], 0)
        else:
            upad = u[b, s0 - HALO:s0 + TOK]
        grp0 = (c // 4) * 4
        prevmask = np.zeros((NCORES,), np.float32)
        binmask = np.zeros((NCORES, NCORES), np.float32)
        for j in range(grp0, c):
            prevmask[j] = 1.0
            for i in range(j + 1, c):
                binmask[j, i] = 1.0
        m = dict(base)
        m["u"] = np.ascontiguousarray(upad, np.float32)
        m["pmx"] = np.ascontiguousarray(np.repeat(prevmask.reshape(1, -1), NH, 0))
        m["bmx"] = np.ascontiguousarray(np.repeat(binmask.reshape(1, -1), NH, 0))
        shf = FFI // NCORES
        shd = D // NCORES
        m["wg_sh"] = np.ascontiguousarray(wg[c * shf:(c + 1) * shf], np.float32)
        m["wu_sh"] = np.ascontiguousarray(wu[c * shf:(c + 1) * shf], np.float32)
        m["wd_sh"] = np.ascontiguousarray(wdn[c * shd:(c + 1) * shd], np.float32)
        in_maps.append(m)
    return in_maps


def kernel(**inputs):
    in_maps = _prep_inputs(inputs)
    nc = build(stage=STAGE, debug_taps=DEBUG_TAPS)
    res = run_bass_kernel_spmd(nc, in_maps, core_ids=list(range(NCORES)))
    full = np.zeros((2, 8192, D), np.float32)
    for c in range(NCORES):
        b, sidx = divmod(c, 4)
        full[b, sidx * TOK:(sidx + 1) * TOK] = res.results[c]["out"]
    return full


# version 49
# speedup vs baseline: 1.0481x; 1.0077x over previous
import contextlib
import numpy as np
import ml_dtypes
import concourse.bass as bass
import concourse.mybir as mybir
from concourse.bass_utils import run_bass_kernel_spmd

F32 = mybir.dt.float32
F16 = mybir.dt.float16
BF16 = mybir.dt.bfloat16
AF = mybir.ActivationFunctionType
ALU = mybir.AluOpType
AX = mybir.AxisListType

NCORES = 8
TOK = 2048
HALO = 3
TOKH = TOK + HALO
D = 1024
DI = 2048
NH = 32
HD = 64
DS = 128
EXBC = 2336
NCH = 16
CH = 128
FFI = 2816
EPS_LN = 1e-5
EPS_RMS = 1e-5
M_H = float(1.5 * 2**22)
M_1 = float(1.5 * 2**23)
M_2 = float(1.5 * 2**24)
NEG = -30000.0

DEBUG_TAPS = {}
STAGE = 99
import os as _os
SSD_STEPS = int(_os.environ.get("SSD_STEPS", "99"))
SSD_NCHUNK = int(_os.environ.get("SSD_NCHUNK", "99"))


class Q:
    ENGINES = ("sync", "vector", "scalar", "tensor", "gpsimd")

    def __init__(self, nc):
        self.nc = nc
        self.steps = {e: [] for e in self.ENGINES}
        self.sems = {}
        self.cnt = {}
        self.ctx = None
        self.tag = "init"

    def sem(self, name):
        if name not in self.sems:
            self.sems[name] = self.ctx.enter_context(self.nc.semaphore(name))
            self.cnt[name] = 0
        return self.sems[name]

    def wait(self, eng, name, n=None):
        s = self.sem(name)
        tgt = self.cnt[name] if n is None else n
        if tgt > 0:
            self.steps[eng].append((self.tag, lambda e, s=s, t=tgt: e.wait_ge(s, t)))
        return tgt

    def emit(self, eng, fn, inc=None, k=1):
        if inc is not None:
            s = self.sem(inc)
            self.cnt[inc] += k
            def wrapped(e, fn=fn, s=s, k=k):
                fn(e).then_inc(s, k)
            self.steps[eng].append((self.tag, wrapped))
        else:
            self.steps[eng].append((self.tag, fn))

    def dma(self, eng, out, in_, inc, **kw):
        self.emit(eng, lambda e, o=out, i=in_, kw=kw: e.dma_start(out=o, in_=i, **kw),
                  inc=inc, k=16)

    def n(self, name):
        self.sem(name)
        return self.cnt[name]

    def run(self, block):
        nc = self.nc
        def replay(engine, steps):
            cur = None
            sid = None
            for tag, f in steps:
                if tag != cur:
                    if cur is not None:
                        nc.leave_named_scope(cur, sid, False)
                    sid, _ = nc.enter_named_scope(tag, False)
                    cur = tag
                f(engine)
            if cur is not None:
                nc.leave_named_scope(cur, sid, False)
        for eng in self.ENGINES:
            steps = self.steps[eng]
            getattr(block, eng)(lambda engine, steps=steps: replay(engine, steps))


class KB:
    def __init__(self, nc, q, ctx):
        self.nc, self.q, self.ctx = nc, q, ctx
        self.dram, self.sbuf, self.taps = {}, {}, {}

    def din(self, name, shape, dt=F32):
        self.dram[name] = self.nc.dram_tensor(name, shape, dt, kind="ExternalInput").ap()
        return self.dram[name]

    def dout(self, name, shape, dt=F32):
        self.dram[name] = self.nc.dram_tensor(name, shape, dt, kind="ExternalOutput").ap()
        return self.dram[name]

    def dint(self, name, shape, dt=F32, **kw):
        self.dram[name] = self.nc.dram_tensor(name, shape, dt, kind="Internal", **kw).ap()
        return self.dram[name]

    def sb(self, name, shape, dt=F32, scope=None):
        t = (scope or self.ctx).enter_context(self.nc.sbuf_tensor(name, shape, dt))
        self.sbuf[name] = t
        return t

    def ps(self, name, shape, dt=F32, scope=None):
        t = (scope or self.ctx).enter_context(self.nc.psum_tensor(name, shape, dt))
        self.sbuf[name] = t
        return t

    def tap(self, name, src_ap, nelem_sem="st_tap", eng="sync"):
        if name in self.taps:
            self.q.wait(eng, "vd")
            self.q.wait(eng, "ac")
            self.q.dma(eng, self.taps[name], src_ap, nelem_sem)


def build(stage=99, debug_taps=None):
    nc = bass.Bass("TRN2", target_bir_lowering=False)
    try:
        nc.detect_race_conditions = False
    except Exception:
        pass
    with contextlib.ExitStack() as ctx:
        q = Q(nc)
        q.ctx = ctx
        kb = KB(nc, q, ctx)
        declare_dram(kb)
        for tname, (tshape, tdt) in (debug_taps or {}).items():
            kb.taps[tname] = kb.dout(tname, tshape, tdt)
        body(kb, stage)
        block = ctx.enter_context(nc.Block())
        q.run(block)
    return nc


def declare_dram(kb):
    kb.din("u", [TOKH, D])
    kb.din("w_xbcT", [D, EXBC])
    kb.din("w_zT", [D, DI])
    kb.din("w_outT", [DI, D])
    kb.din("wg_sh", [FFI // NCORES, D])
    kb.din("wu_sh", [FFI // NCORES, D])
    kb.din("wd_sh", [D // NCORES, FFI])
    kb.din("conv_w4", [DI + 2 * DS, 4])
    kb.din("conv_b", [DI + 2 * DS, 1])
    kb.din("dt_bias", [NH, 1])
    kb.din("a_log", [NH, 1])
    kb.din("dcol64", [DI, 1])
    kb.din("norm_w", [DI, 1])
    kb.din("pmx", [NH, NCORES])
    kb.din("bmx", [NH, NCORES * NCORES])
    kb.dout("out", [TOK, D])
    kb.dint("d_wxb", [D, EXBC], BF16)
    kb.dint("d_wz", [D, DI], BF16)
    kb.dint("d_wout", [DI, D], BF16)
    kb.dint("d_x", [DI + DS, TOK], BF16)
    kb.dint("d_y", [TOK, DI], BF16)
    kb.dint("d_xq", [TOK, D], BF16)
    kb.dint("d_xn", [CH, 8 * TOKH], BF16)
    kb.dint("d_hq", [TOK, FFI], BF16)
    kb.dint("d_hi", [NCH, NH * CH], F16)
    kb.dint("d_lo", [NCH, NH * CH], BF16)
    kb.dint("d_rstd", [1, TOK])
    kb.dint("cc_g_in", [FFI // NCORES, D], BF16)
    kb.dint("cc_g_out", [FFI, D], BF16, addr_space="Shared")
    kb.dint("cc_u_in", [FFI // NCORES, D], BF16)
    kb.dint("cc_u_out", [FFI, D], BF16, addr_space="Shared")
    kb.dint("cc_dn_in", [D // NCORES, FFI], BF16)
    kb.dint("cc_dn_out", [D, FFI], BF16, addr_space="Shared")
    kb.dint("d_coef", [NCORES, NH])
    kb.dint("d_lcz", [NCH, NH])
    kb.dint("cc_s_in", [DS, NH * HD])
    kb.dint("cc_s_out", [NCORES * DS, NH * HD], addr_space="Shared")
    kb.dint("cc_l_in", [NH, 1])
    kb.dint("cc_l_out", [NCORES * NH, 1], addr_space="Shared")


def body(kb, stage):
    nc, q = kb.nc, kb.q
    sb, psb, dram, sbuf = kb.sb, kb.ps, kb.dram, kb.sbuf

    ident = sb("ident", [CH, CH], BF16)
    ut_incl = sb("ut_incl", [CH, CH], BF16)
    ones1_bf = sb("ones1_bf", [1, CH], BF16)
    ones1_f32 = sb("ones1_f32", [1, CH], F32)
    onescol_bf = sb("onescol_bf", [CH, 1], BF16)
    ones32 = sb("ones32", [CH, NH], BF16)
    ones_full = sb("ones_full", [CH, CH], BF16)
    identh = sb("identh", [CH, CH], F16)
    negones1 = sb("negones1", [1, CH], BF16)
    dcol = sb("dcol", [CH, DI // CH])
    dtb = sb("dtb", [NH, 1])
    acol = sb("acol", [NH, 1])
    nwcol = sb("nwcol", [CH, DI // CH])
    epscol = sb("epscol", [CH, 1])
    onecolf = sb("onecolf", [CH, 1])
    pmx = sb("pmx_s", [NH, NCORES])
    bmx = sb("bmx_s", [NH, NCORES * NCORES])

    B_fm = sb("B_fm", [DS, TOK], BF16)
    C_fm = sb("C_fm", [DS, TOK], BF16)
    a_hi = sb("a_hi", [CH, NCH * NH], F16)
    a_lo = sb("a_lo", [CH, NCH * NH], BF16)
    az_fm = sb("az_fm", [NH, NCH])
    R = sb("R", [DS, NH * HD])
    Rb = sb("Rb", [DS, NH * HD], BF16)

    lnA = contextlib.ExitStack()
    lt_strict = sb("lt_strict", [CH, CH], BF16, scope=lnA)
    maskb = sb("maskb", [CH, CH], F32, scope=lnA)
    identf = sb("identf", [CH, CH], F32, scope=lnA)
    B_tm = sb("B_tm", [CH, NCH * DS], BF16, scope=lnA)
    dt_tmb = sb("dt_tmb", [CH, NCH * NH], BF16, scope=lnA)

    g = "gpsimd"
    q.emit(g, lambda e: e.memset(ident[:], 1.0))
    q.emit(g, lambda e: e.memset(identh[:], 1.0))
    q.emit(g, lambda e: e.memset(negones1[:], -1.0))
    q.emit(g, lambda e: e.memset(identf[:], 1.0))
    q.emit(g, lambda e: e.memset(ut_incl[:], 1.0))
    q.emit(g, lambda e: e.memset(lt_strict[:], 1.0))
    q.emit(g, lambda e: e.memset(maskb[:], 0.0))
    q.emit(g, lambda e: e.memset(ones1_bf[:], 1.0))
    q.emit(g, lambda e: e.memset(ones1_f32[:], 1.0))
    q.emit(g, lambda e: e.memset(onescol_bf[:], 1.0))
    q.emit(g, lambda e: e.memset(ones32[:], 1.0))
    q.emit(g, lambda e: e.memset(ones_full[:], 1.0))
    q.emit(g, lambda e: e.memset(R[:], 0.0))
    q.emit(g, lambda e: e.memset(epscol[:], EPS_LN))
    q.emit(g, lambda e: e.memset(onecolf[:], 1.0), inc="gp")
    q.wait(g, "gp")
    q.emit(g, lambda e: e.affine_select(ident[:], ident[:], [[-1, CH]], ALU.is_equal,
                                        0.0, base=0, channel_multiplier=1))
    q.emit(g, lambda e: e.affine_select(identh[:], identh[:], [[-1, CH]], ALU.is_equal,
                                        0.0, base=0, channel_multiplier=1))
    q.emit(g, lambda e: e.affine_select(identf[:], identf[:], [[-1, CH]], ALU.is_equal,
                                        0.0, base=0, channel_multiplier=1))
    q.emit(g, lambda e: e.affine_select(ut_incl[:], ut_incl[:], [[1, CH]], ALU.is_ge,
                                        0.0, base=0, channel_multiplier=-1))
    q.emit(g, lambda e: e.affine_select(lt_strict[:], lt_strict[:], [[-1, CH]], ALU.is_gt,
                                        0.0, base=0, channel_multiplier=1))
    q.emit(g, lambda e: e.affine_select(maskb[:], maskb[:], [[1, CH]], ALU.is_ge,
                                        NEG, base=0, channel_multiplier=-1), inc="gp")
    q.wait(g, "gp")
    q.dma(g, dtb[:], dram["dt_bias"], "ldc")
    q.dma(g, acol[:], dram["a_log"], "ldc")
    for _j in range(DI // CH):
        q.dma(g, nwcol[:, _j:_j + 1], dram["norm_w"][_j * CH:(_j + 1) * CH, :], "ldc")
        q.dma(g, dcol[:, _j:_j + 1], dram["dcol64"][_j * CH:(_j + 1) * CH, :], "ldc")
    q.dma(g, pmx[:], dram["pmx"], "ldc")
    q.dma(g, bmx[:], dram["bmx"], "ldc")

    q.wait("scalar", "ldc")
    q.emit("scalar", lambda e: e.activation(acol[:], acol[:], AF.Exp), inc="ac")
    q.wait("vector", "ac")
    q.emit("vector", lambda e: e.tensor_scalar(acol[:], acol[:], -1.0, None, ALU.mult),
           inc="vd")

    sb("xn_fm", [CH, 8 * TOKH], BF16, scope=lnA)
    if stage >= 3:
        kb.q.tag = "gather"
        gsc = contextlib.ExitStack()
        phase_gather_w(kb, gsc)
    if stage >= 1:
        kb.q.tag = "ln"
        phase_ln(kb)
    if stage >= 3:
        gsc.close()
    if stage >= 2:
        kb.q.tag = "a"
        phase_a(kb)
    lnA.close()
    sb("g_fm", [CH, 8 * TOK], BF16)
    if stage >= 3:
        kb.q.tag = "gather"
        phase_gather_sl(kb)
    if stage >= 4:
        kb.q.tag = "ffn"
        phase_ffn(kb)
    if stage >= 5:
        kb.q.tag = "tail"
        phase_tail(kb)

    for s in ("st_out", "st_tap", "st_xq", "st_xn", "st_x", "st_y", "st_hi", "st_hq", "st_wq", "st_s", "st_t"):
        q.wait("sync", s)


def phase_ln(kb):
    nc, q, dram, sbuf = kb.nc, kb.q, kb.dram, kb.sbuf
    sb, psb = kb.sb, kb.ps
    u = dram["u"]
    xn_fm = sbuf["xn_fm"]

    with contextlib.ExitStack() as scope:
        utile = kb.sb("utile", [CH, 2 * D], scope=scope)
        mcol = kb.sb("mcol", [CH, 2], scope=scope)
        mraw = kb.sb("mraw", [CH, 1], scope=scope)
        ssraw = kb.sb("ssraw", [CH, 1], scope=scope)
        ssd2 = kb.sb("ssd2", [CH, 1], scope=scope)
        vcol = kb.sb("vcol", [CH, 1], scope=scope)
        rcol = kb.sb("rcol", [CH, 2], scope=scope)
        up = kb.sb("up", [CH, D], scope=scope)
        xh = kb.sb("xh", [CH, D], F16, scope=scope)
        sqs = kb.sb("sqs", [CH, D], scope=scope)
        mx = kb.sb("mx", [CH, D // 16], scope=scope)
        mxr = kb.sb("mxr", [CH, D // 16], scope=scope)
        s16 = kb.sb("s16", [CH, D // 16], scope=scope)
        w_ = kb.sb("w_", [CH, D], F16, scope=scope)
        p1 = kb.sb("p1", [CH, D], F16, scope=scope)
        t_a = kb.sb("t_a", [CH, D], F16, scope=scope)
        t_b = kb.sb("t_b", [CH, D], F16, scope=scope)
        p23 = kb.sb("p23", [CH, D], F16, scope=scope)
        xq_t = kb.sb("xq_t", [CH, D], BF16, scope=scope)
        pln = kb.ps("pln", [CH, 1024], scope=scope)

        xn3 = xn_fm[:].rearrange("p (j t) -> p j t", j=8)
        ps3 = pln[:].rearrange("p (j t) -> p j t", j=8)

        ld_tgt = {}
        a2_t = {}
        rcol_t = {}
        up_vd = {}
        diagr_vd = {}
        xn_ac = {}
        mm_pe_d = {}
        q.dma("sync", utile[0:CH, 0:D], u[0:CH, :], "ld_u")
        ld_tgt[0] = q.n("ld_u")
        q.dma("sync", utile[0:CH, D:2 * D], u[CH:2 * CH, :], "ld_u")
        ld_tgt[1] = q.n("ld_u")
        for it in range(17):
            rows = CH if it < 16 else HALO
            r0 = it * CH
            sl = slice(0, rows)
            pu = it % 2
            ut = utile[sl, pu * D:pu * D + D]
            q.wait("scalar", "ld_u", ld_tgt[it])
            if it >= 2:
                q.wait("scalar", "vd", diagr_vd[it - 2])
            q.emit("scalar", lambda e, sl=sl, ut=ut: e.activation(
                sqs[sl, :], ut, AF.Square, accum_out=ssraw[sl, :]), inc="ac")
            q.emit("scalar", lambda e, sl=sl, ut=ut: e.activation(
                sqs[sl, :], ut, AF.Copy, accum_out=mraw[sl, :]), inc="ac")
            a2_t[it] = q.n("ac")
            q.emit("scalar", lambda e: e.drain())
            q.emit("scalar", lambda e, sl=sl, pu=pu: e.activation(mcol[sl, pu:pu + 1], mraw[sl, :], AF.Copy, scale=1.0 / D))
            q.emit("scalar", lambda e, sl=sl: e.activation(ssd2[sl, :], ssraw[sl, :], AF.Copy, scale=1.0 / D, bias=EPS_LN))
            q.emit("scalar", lambda e: e.drain())
            q.emit("scalar", lambda e, sl=sl, pu=pu: e.activation(vcol[sl, :], mcol[sl, pu:pu + 1], AF.Square))
            q.emit("scalar", lambda e: e.drain())
            q.emit("scalar", lambda e, sl=sl: e.activation(vcol[sl, :], vcol[sl, :], AF.Ln, scale=-1.0, bias=ssd2[sl, :]))
            q.emit("scalar", lambda e: e.drain())
            q.emit("scalar", lambda e, sl=sl, pu=pu: e.activation(rcol[sl, pu:pu + 1], vcol[sl, :], AF.Exp, scale=-0.5))
            q.emit("scalar", lambda e: e.drain(), inc="ac")
            rcol_t[it] = q.n("ac")
            q.wait("vector", "ac", rcol_t[it])
            q.emit("vector", lambda e, sl=sl, ut=ut, pu=pu: e.tensor_scalar(up[sl, :], ut, mcol[sl, pu:pu + 1], None, ALU.subtract), inc="vd")
            up_vd[it] = q.n("vd")
            if it >= 1:
                q.wait("vector", "pe", mm_pe_d[it - 1])
            if it == 16:
                q.emit("vector", lambda e: e.memset(xh[:], 0.0), inc="vd")
            q.emit("vector", lambda e, sl=sl, pu=pu: e.tensor_scalar(xh[sl, :], up[sl, :], rcol[sl, pu:pu + 1], None, ALU.mult), inc="vd")
            diagr_vd[it] = q.n("vd")
            qr_t = equant_pre(kb, src=xh, sl=slice(0, CH), width=D, mx=mx)
            if it + 2 <= 16:
                rows2 = CH if it + 2 < 16 else HALO
                q.wait("sync", "ac", a2_t[it])
                q.wait("sync", "vd", up_vd[it])
                q.dma("sync", utile[0:rows2, pu * D:pu * D + D],
                      u[(it + 2) * CH:(it + 2) * CH + rows2, :], "ld_u")
                ld_tgt[it + 2] = q.n("ld_u")
            q.wait("tensor", "vd", diagr_vd[it])
            if it >= 1:
                q.wait("tensor", "vd", xn_ac[it - 1])
            for j in range(8):
                q.emit("tensor", lambda e, j=j, sl=sl:
                       e.matmul(pln[:, j * CH:(j + 1) * CH], xh[sl, j * CH:(j + 1) * CH],
                                kb.sbuf["identh"][sl, :], start=True, stop=True),
                       inc=("pe" if j == 7 else None))
            mm_pe_d[it] = q.n("pe")
            q.wait("vector", "pe", mm_pe_d[it])
            q.emit("vector", lambda e, it=it, rows=rows:
                   e.tensor_copy(xn3[:, :, r0_(it):r0_(it) + rows], ps3[:, :, 0:rows]), inc="vd")
            xn_ac[it] = q.n("vd")
            if it == 10:
                kb.tap("t_mcol", mcol[:])
                kb.tap("t_sscol", ssraw[:])
                kb.tap("t_rcol", rcol[:])

            p0 = HALO if it == 0 else 0
            q.wait("vector", "st_xq")
            equant_post(kb, src=xh, sl=slice(0, CH), width=D, mx=mx, mxr=mxr, s16=s16,
                        w_=w_, p1=p1, t_a=t_a, t_b=t_b, p23=p23, out=xq_t, qr_tgt=qr_t)
            q.wait("sync", "vd")
            q.dma("sync", dram["d_xq"][it * CH - HALO + p0: it * CH - HALO + rows, :],
                  xq_t[p0:rows, :], "st_xq")

        q.wait("sync", "vd")
        q.wait("sync", "ac")
        q.dma("sync", dram["d_xn"], xn_fm[:], "st_xn")
        kb.tap("t_xn", xn_fm[:])
        q.wait("sync", "st_xq")
        kb.tap("t_xq", dram["d_xq"])


def r0_(it):
    return it * CH


def equant_pre(kb, src, sl, width, mx, mx0=0, src0=0):
    q = kb.q
    nb = width // 16
    src3 = src[sl, src0:src0 + width].rearrange("p (b k) -> p b k", k=16)
    q.emit("vector", lambda e: e.tensor_reduce(mx[sl, mx0:mx0 + nb].unsqueeze(2), src3, AX.X,
                                               ALU.max, apply_absolute_value=True), inc="qr")
    return q.n("qr")


def equant_post(kb, src, sl, width, mx, mxr, s16, w_, p1, t_a, t_b, p23, out, qr_tgt, osl=None, mx0=0, src0=0):
    q = kb.q
    if osl is None:
        osl = slice(0, width)
    nb = width // 16
    rows = sl.stop - sl.start
    src3 = src[sl, src0:src0 + width].rearrange("p (b k) -> p b k", k=16)
    q.wait("vector", "qr", qr_tgt)
    q.emit("vector", lambda e: e.tensor_scalar(s16[sl, 0:nb], mx[sl, mx0:mx0 + nb], 6e-12, 1.0 / 6.0, ALU.max, ALU.mult), inc="qs")
    q.wait("vector", "qs", q.n("qs"))
    q.emit("vector", lambda e: e.reciprocal(mxr[sl, 0:nb], s16[sl, 0:nb]), inc="qs")
    q.wait("vector", "qs", q.n("qs"))
    r3 = mxr[sl, 0:nb].unsqueeze(2).broadcast_to([rows, nb, 16])
    s163 = s16[sl, 0:nb].unsqueeze(2).broadcast_to([rows, nb, 16])
    w3 = w_[sl, 0:width].rearrange("p (b k) -> p b k", k=16)
    q.emit("vector", lambda e: e.tensor_tensor(w3, src3, r3, ALU.mult), inc="vd")
    kb.srcfree_vd = q.n("vd")
    wf = w_[sl, 0:width]
    q.emit("vector", lambda e: e.tensor_scalar(p1[sl, 0:width], wf, -2.25, 2.25, ALU.max, ALU.min), inc="vd")
    q.emit("vector", lambda e: e.tensor_scalar(p1[sl, 0:width], p1[sl, 0:width], M_H, -M_H, ALU.add, ALU.add), inc="vd")
    q.emit("vector", lambda e: e.tensor_scalar(t_a[sl, 0:width], wf, M_1, -M_1, ALU.add, ALU.add), inc="vd")
    q.emit("vector", lambda e: e.tensor_scalar(t_b[sl, 0:width], t_a[sl, 0:width], 2.0, 4.0, ALU.max, ALU.min), inc="vd")
    q.emit("vector", lambda e: e.tensor_scalar(t_a[sl, 0:width], t_a[sl, 0:width], -4.0, -2.0, ALU.max, ALU.min), inc="vd")
    q.emit("vector", lambda e: e.tensor_tensor(p23[sl, 0:width], t_a[sl, 0:width], t_b[sl, 0:width], ALU.add), inc="vd")
    q.emit("vector", lambda e: e.tensor_tensor(p1[sl, 0:width], p1[sl, 0:width], p23[sl, 0:width], ALU.add), inc="vd")
    q.emit("vector", lambda e: e.tensor_scalar(t_a[sl, 0:width], wf, M_2, -M_2, ALU.add, ALU.add), inc="vd")
    q.emit("vector", lambda e: e.tensor_scalar(t_b[sl, 0:width], t_a[sl, 0:width], 4.0, 6.0, ALU.max, ALU.min), inc="vd")
    q.emit("vector", lambda e: e.tensor_scalar(t_a[sl, 0:width], t_a[sl, 0:width], -6.0, -4.0, ALU.max, ALU.min), inc="vd")
    q.emit("vector", lambda e: e.tensor_tensor(p23[sl, 0:width], t_a[sl, 0:width], t_b[sl, 0:width], ALU.add), inc="vd")
    q.emit("vector", lambda e: e.tensor_tensor(p1[sl, 0:width], p1[sl, 0:width], p23[sl, 0:width], ALU.add), inc="vd")
    o3 = out[sl, osl].rearrange("p (b k) -> p b k", k=16)
    p13 = p1[sl, 0:width].rearrange("p (b k) -> p b k", k=16)
    q.emit("vector", lambda e: e.tensor_tensor(o3, p13, s163, ALU.mult), inc="vd")


def phase_a(kb):
    nc, q, dram, sbuf = kb.nc, kb.q, kb.dram, kb.sbuf
    q.dma("gpsimd", dram["d_wz"], dram["w_zT"], "ldwz")
    q.dma("gpsimd", dram["d_wout"], dram["w_outT"], "ldwo")
    xn_fm = sbuf["xn_fm"]
    TH2 = 1024
    NC2 = 1027

    with contextlib.ExitStack() as sc:
        wxb = kb.sb("wxb", [CH, 2 * 8 * CH], BF16, scope=sc)
        cwall = kb.sb("cwall", [CH, 18 * 4], scope=sc)
        cball = kb.sb("cball", [CH, 18], scope=sc)
        ta = kb.sb("ta", [CH, TH2], BF16, scope=sc)
        tb = kb.sb("tb", [CH, 2 * TH2], BF16, scope=sc)
        xpost = kb.sb("xpost", [CH, 2 * TH2], BF16, scope=sc)
        dtraw = kb.sb("dtraw", [NH, TH2], scope=sc)
        dts1 = kb.sb("dts1", [NH, TH2], scope=sc)
        dts2 = kb.sb("dts2", [NH, TH2], scope=sc)
        dtv_bf = kb.sb("dtv_bf", [NH, TH2], BF16, scope=sc)
        a_f = kb.sb("a_f", [NH, TH2], scope=sc)
        x_tm = kb.sb("x_tm", [CH, 2 * DI], BF16, scope=sc)
        xfc = kb.sb("xfc", [CH, DI], BF16, scope=sc)
        xdc = kb.sb("xdc", [CH, DI], BF16, scope=sc)
        xdt = kb.sb("xdt", [CH, DI], BF16, scope=sc)
        bw = kb.sb("bw", [CH, NH * DS], BF16, scope=sc)
        mt = kb.sb("mt", [CH, NH * CH], BF16, scope=sc)
        ebuf = kb.sb("ebuf", [CH, NH * CH], BF16, scope=sc)
        d1 = kb.sb("d1", [CH, 3 * 512], scope=sc)
        rowp_hi = kb.sb("rowp_hi", [CH, NH * CH], F16, scope=sc)
        rowp_lo = kb.sb("rowp_lo", [CH, NH * CH], BF16, scope=sc)
        cumc = kb.sb("cumc", [CH, NH], scope=sc)
        wd_bf = kb.sb("wd_bf", [CH, 2 * NH], BF16, scope=sc)
        decs = kb.sb("decs", [CH, 2 * 64], scope=sc)
        rtmp = kb.sb("rtmp", [DS, NH * HD], scope=sc)
        ybt = kb.sb("ybt", [CH, 512], scope=sc)
        ytile = kb.sb("ytile", [CH, DI], BF16, scope=sc)
        scT = kb.sb("scT", [CH, CH], BF16, scope=sc)
        lsum = kb.sb("lsum", [NH, 1], scope=sc)
        ps1 = kb.ps("ps1", [CH, 2048], scope=sc)
        ps2 = kb.ps("ps2", [CH, 1536], scope=sc)
        psS = kb.ps("psS", [CH, 512], scope=sc)

        q.wait("gpsimd", "st_xq")
        q.wait("gpsimd", "st_xn")
        q.wait("gpsimd", "st_wq")
        q.dma("gpsimd", cwall[:].rearrange("p (c w) -> p c w", w=4),
              dram["conv_w4"][0:18 * CH, :].rearrange("(c p) w -> p c w", p=CH), "ldc2")
        q.dma("gpsimd", cball[:].rearrange("p (c w) -> p c w", w=1),
              dram["conv_b"][0:18 * CH, :].rearrange("(c p) w -> p c w", p=CH), "ldc2",
              allow_slow_non_contiguous=True)
        q.wait("vector", "ldc2")
        q.wait("scalar", "ldc2")
        pe_t = {}
        ldw_t = {}
        conv_vd = {}
        sil_ac = {}
        stx_t = {}
        kb.ssd_t = {}
        kb.ssd_sty = 0
        for half in range(2):
            q.tag = "a_inproj"
            t0 = half * TH2
            c0 = half * TH2
            q.wait("tensor", "vd")
            q.wait("tensor", "ac")
            q.wait("gpsimd", "ldw0", 16)
            for ec in range(19):
                width = CH if ec < 18 else NH
                e0 = ec * CH
                gec = half * 19 + ec
                wb = gec % 2
                reg = sbuf["ps1"] if gec % 2 == 0 else sbuf["ps2"]
                if gec >= 2:
                    q.wait("gpsimd", "pe", pe_t[gec - 2])
                q.dma("gpsimd", wxb[:, wb * 1024:wb * 1024 + 8 * width].rearrange("p (k w) -> p k w", k=8),
                      dram["d_wxb"][:, e0:e0 + width].rearrange("(k p) w -> p k w", p=CH),
                      "ld_w")
                ldw_t[gec] = q.n("ld_w")
                q.wait("tensor", "ld_w", ldw_t[gec])
                if half == 0 and ec == 0:
                    q.wait("sync", "ld_w")
                    kb.tap("t_wxb", wxb[:, 0:8 * CH])
                    kb.tap("t_xnfm", xn_fm[:])
                if gec >= 2 and (gec - 2) in conv_vd:
                    q.wait("tensor", "vd", conv_vd[gec - 2])
                for (tc0, tw) in [(0, 512), (512, 512), (1024, 3)]:
                    for k in range(8):
                        q.emit("tensor", lambda e, k=k, tc0=tc0, tw=tw, width=width, c0=c0, reg=reg, wb=wb:
                               e.matmul(reg[0:width, tc0:tc0 + tw],
                                        wxb[:, wb * 1024 + k * width:wb * 1024 + (k + 1) * width],
                                        xn_fm[:, k * TOKH + c0 + tc0: k * TOKH + c0 + tc0 + tw],
                                        start=(k == 0), stop=(k == 7)),
                               inc=("pe" if (k == 7 and tc0 == 1024) else None))
                pe_t[gec] = q.n("pe")
                if half == 0 and ec == 1 and "t_raw1" in kb.taps:
                    q.wait("scalar", "pe", pe_t[gec])
                    q.emit("scalar", lambda e, reg=reg: e.copy(sbuf["x_tm"][:, 0:NC2], reg[:, 0:NC2]), inc="ac")
                    q.wait("sync", "ac")
                    kb.tap("t_raw1", sbuf["x_tm"][:, 0:NC2])
                if ec < 18:
                    cwc = ec * 4
                    q.wait("vector", "pe", pe_t[gec])
                    if gec >= 2 and (gec - 2) in sil_ac:
                        q.wait("vector", "ac", sil_ac[gec - 2])
                    q.emit("vector", lambda e, reg=reg, cwc=cwc: e.tensor_scalar(
                        ta[:], reg[:, 0:TH2], cwall[:, cwc:cwc + 1], None, ALU.mult), inc="vd")
                    q.emit("vector", lambda e, reg=reg, cwc=cwc, wb=wb: e.scalar_tensor_tensor(
                        tb[:, wb * TH2:(wb + 1) * TH2], reg[:, 1:1 + TH2], cwall[:, cwc + 1:cwc + 2], ta[:], ALU.mult, ALU.add), inc="vd")
                    q.emit("vector", lambda e, reg=reg, cwc=cwc, wb=wb: e.scalar_tensor_tensor(
                        ta[:], reg[:, 2:2 + TH2], cwall[:, cwc + 2:cwc + 3], tb[:, wb * TH2:(wb + 1) * TH2], ALU.mult, ALU.add), inc="vd")
                    q.emit("vector", lambda e, reg=reg, cwc=cwc, wb=wb: e.scalar_tensor_tensor(
                        tb[:, wb * TH2:(wb + 1) * TH2], reg[:, 3:3 + TH2], cwall[:, cwc + 3:cwc + 4], ta[:], ALU.mult, ALU.add), inc="vd")
                    conv_vd[gec] = q.n("vd")
                    q.wait("scalar", "vd", conv_vd[gec])
                    if gec >= 2:
                        q.wait("scalar", "st_x", stx_t.get(gec - 2, 0))
                    q.emit("scalar", lambda e, wb=wb, ec=ec: e.activation(
                        xpost[:, wb * TH2:(wb + 1) * TH2], tb[:, wb * TH2:(wb + 1) * TH2],
                        AF.Silu, bias=cball[:, ec:ec + 1]), inc="ac")
                    sil_ac[gec] = q.n("ac")
                    if half == 0 and ec == 0:
                        q.wait("sync", "ac")
                        kb.tap("t_xpost", xpost[:, 0:TH2])
                    if ec <= 16:
                        q.wait("sync", "ac", sil_ac[gec])
                        q.dma("sync", dram["d_x"][e0:e0 + CH, t0:t0 + TH2],
                              xpost[:, wb * TH2:(wb + 1) * TH2], "st_x")
                        stx_t[gec] = q.n("st_x")
                    if ec == 16:
                        q.emit("scalar", lambda e, t0=t0, wb=wb: e.copy(
                            sbuf["B_fm"][:, t0:t0 + TH2], xpost[:, wb * TH2:(wb + 1) * TH2]), inc="ac")
                    if ec == 17:
                        q.emit("scalar", lambda e, t0=t0, wb=wb: e.copy(
                            sbuf["C_fm"][:, t0:t0 + TH2], xpost[:, wb * TH2:(wb + 1) * TH2]), inc="ac")
                else:
                    q.wait("scalar", "pe", pe_t[gec])
                    q.emit("scalar", lambda e, reg=reg: e.copy(dtraw[:], reg[0:NH, 3:3 + TH2]), inc="ac")
                    q.wait("scalar", "ac")
                    q.emit("scalar", lambda e: e.activation(dts1[:], dtraw[:], AF.Abs, bias=sbuf["dtb"][:, 0:1]))
                    q.emit("scalar", lambda e: e.drain())
                    q.emit("scalar", lambda e: e.activation(dts1[:], dts1[:], AF.Exp, scale=-1.0))
                    q.emit("scalar", lambda e: e.drain())
                    q.emit("scalar", lambda e: e.activation(dts1[:], dts1[:], AF.Ln, bias=sbuf["onecolf"][0:NH, 0:1]))
                    q.emit("scalar", lambda e: e.drain())
                    q.emit("scalar", lambda e: e.activation(dts2[:], dtraw[:], AF.Relu, bias=sbuf["dtb"][:, 0:1]), inc="ac")
                    q.wait("vector", "ac")
                    q.emit("vector", lambda e: e.tensor_tensor(dts2[:], dts1[:], dts2[:], ALU.add), inc="vd")
                    q.emit("vector", lambda e: e.tensor_copy(dtv_bf[:], dts2[:]), inc="vd")
                    q.emit("vector", lambda e: e.tensor_scalar(a_f[:], dts2[:], sbuf["acol"][0:NH, 0:1], None, ALU.mult), inc="vd")

            q.emit("vector", lambda e, half=half: e.tensor_reduce(
                sbuf["az_fm"][:, half * 8:(half + 1) * 8].unsqueeze(2),
                a_f[:].rearrange("p (c t) -> p c t", t=CH), AX.X, ALU.add))
            q.emit("vector", lambda e: e.drain(), inc="vd")
            kb.ssd_t = dict(kb.ssd_t)
            kb.ssd_t["psS_vd"] = q.n("vd")
            kb.ssd_t["psS_ac"] = q.n("ac")
            q.wait("scalar", "st_x")
            q.wait("scalar", "pe")
            q.wait("scalar", "vd")
            for tc in range(8):
                gc = half * 8 + tc
                q.dma("scalar", sbuf["B_tm"][:, gc * DS:(gc + 1) * DS],
                      dram["d_x"][DI:DI + DS, t0 + tc * CH:t0 + (tc + 1) * CH], "ld_xt",
                      transpose=True)
            ld_tgt = {}
            q.dma("scalar", x_tm[:, 0:DI],
                  dram["d_x"][0:DI, t0:t0 + CH], "ld_xc", transpose=True)
            ld_tgt[0] = q.n("ld_xc")

            q.tag = "a_ssd"
            vd_prev = 0
            for cc_ in range(8):
                if cc_ + 1 < 8:
                    if vd_prev:
                        q.wait("scalar", "vd", vd_prev)
                    q.dma("scalar", x_tm[:, ((cc_ + 1) % 2) * DI:((cc_ + 1) % 2 + 1) * DI],
                          dram["d_x"][0:DI, t0 + (cc_ + 1) * CH:t0 + (cc_ + 2) * CH],
                          "ld_xc", transpose=True)
                    ld_tgt[cc_ + 1] = q.n("ld_xc")
                if half * 8 + cc_ < SSD_NCHUNK:
                    ssd_chunk(kb, half, cc_, cc_ % 2, ld_tgt[cc_])
                vd_prev = q.n("vd")

        if SSD_NCHUNK < 16:
            kb.tap("t_dx", dram["d_x"])
            if SSD_NCHUNK > 0:
                kb.tap("t_R", sbuf["R"][:])
            return
        q.wait("sync", "vd")
        q.dma("sync", dram["cc_s_in"], sbuf["R"][:], "st_s")
        q.wait("vector", "ac")
        q.emit("vector", lambda e: e.tensor_reduce(lsum[:].unsqueeze(2), sbuf["az_fm"][:].unsqueeze(1), AX.X, ALU.add))
        q.emit("vector", lambda e: e.drain(), inc="vd")
        q.wait("sync", "vd")
        q.dma("sync", dram["cc_l_in"], lsum[:], "st_s")
        kb.tap("t_y", dram["d_y"])
        kb.tap("t_R", sbuf["R"][:])
        kb.tap("t_dx", dram["d_x"])


def ssd_chunk(kb, half, cc_, xbuf, ld_xc_tgt):
    nc, q, dram, sbuf = kb.nc, kb.q, kb.dram, kb.sbuf
    gc = half * 8 + cc_
    t0 = half * 1024
    tl0 = t0 + cc_ * CH
    ps1, ps2, psS = sbuf["ps1"], sbuf["ps2"], sbuf["psS"]
    a_f = sbuf["a_f"]
    acs = slice(cc_ * CH, (cc_ + 1) * CH)
    ahc = sbuf["a_hi"][:, gc * NH:(gc + 1) * NH]
    alc = sbuf["a_lo"][:, gc * NH:(gc + 1) * NH]
    dtc = sbuf["dt_tmb"][:, gc * NH:(gc + 1) * NH]
    par = gc % 2
    eacol = sbuf["decs"][:, par * 64:par * 64 + NH]
    pcb = sbuf["decs"][:, par * 64 + NH:par * 64 + 2 * NH]
    wd_bf = sbuf["wd_bf"][:, par * NH:(par + 1) * NH]
    T = kb.ssd_t

    q.wait("tensor", "vd", T.get("psS_vd", 0))
    q.wait("tensor", "ac", T.get("psS_ac", 0))
    q.emit("tensor", lambda e, tl0=tl0: e.matmul(psS[:, 192:320], sbuf["B_fm"][:, tl0:tl0 + CH],
                                                 sbuf["C_fm"][:, tl0:tl0 + CH], start=True, stop=True))
    q.emit("tensor", lambda e: e.transpose(psS[:, 0:NH], a_f[:, acs], sbuf["identf"][0:NH, 0:NH]))
    q.emit("tensor", lambda e: e.transpose(psS[:, 64:80].bitcast(BF16), sbuf["dtv_bf"][:, acs], sbuf["ident"][0:NH, 0:NH]), inc="pe")
    tr_pe = q.n("pe")
    q.wait("scalar", "vd", T.get("mt_vd", 0))
    q.wait("scalar", "pe", tr_pe)
    q.emit("scalar", lambda e: e.copy(sbuf["scT"][:], psS[:, 192:320]), inc="ac")
    sct_ac = q.n("ac")
    q.wait("vector", "pe", tr_pe)
    q.emit("vector", lambda e: e.tensor_copy(ahc, psS[:, 0:NH]), inc="vd")
    q.wait("vector", "vd")
    q.emit("vector", lambda e: e.tensor_tensor(alc, psS[:, 0:NH], ahc, ALU.subtract), inc="vd")
    alc_vd = q.n("vd")
    q.emit("vector", lambda e: e.tensor_copy(dtc, psS[:, 64:80].bitcast(BF16)), inc="vd")
    copies_vd = q.n("vd")

    q.wait("tensor", "vd", copies_vd)
    q.wait("tensor", "ac", sct_ac)
    q.emit("tensor", lambda e: e.matmul(psS[:, 96:96 + NH], sbuf["ut_incl"][:], ahc, start=True, stop=False))
    q.emit("tensor", lambda e: e.matmul(psS[:, 96:96 + NH], sbuf["ut_incl"][:], alc, start=False, stop=True))
    q.emit("tensor", lambda e: e.matmul(psS[:, 160:160 + NH], sbuf["lt_strict"][:], ahc, start=True, stop=False))
    q.emit("tensor", lambda e: e.matmul(psS[:, 160:160 + NH], sbuf["lt_strict"][:], alc, start=False, stop=True))
    q.emit("tensor", lambda e: e.matmul(psS[:, 128:128 + NH], sbuf["ones_full"][:], ahc, start=True, stop=False))
    q.emit("tensor", lambda e: e.matmul(psS[:, 128:128 + NH], sbuf["ones_full"][:], alc, start=False, stop=True), inc="pe")
    cum_pe = q.n("pe")
    q.wait("scalar", "pe", cum_pe)
    q.emit("scalar", lambda e, par=par: e.activation(
        sbuf["decs"][:, par * 64:(par + 1) * 64], psS[:, 96:160], AF.Exp), inc="ac")
    q.emit("scalar", lambda e, wd_bf=wd_bf: e.activation(wd_bf, psS[:, 160:160 + NH], AF.Exp), inc="ac")
    dec_ac = q.n("ac")

    q.wait("vector", "pe", T.get("diff_pe", 0))
    q.emit("vector", lambda e: e.tensor_tensor(
        sbuf["rowp_hi"][:].rearrange("p (h l) -> p h l", l=CH),
        ahc.unsqueeze(2).broadcast_to([CH, NH, CH]),
        sbuf["ut_incl"][:].unsqueeze(1).broadcast_to([CH, NH, CH]), ALU.mult), inc="vd")
    q.emit("vector", lambda e: e.tensor_tensor(
        sbuf["rowp_lo"][:].rearrange("p (h l) -> p h l", l=CH),
        alc.unsqueeze(2).broadcast_to([CH, NH, CH]),
        sbuf["ut_incl"][:].unsqueeze(1).broadcast_to([CH, NH, CH]), ALU.mult), inc="vd")
    q.wait("vector", "pe", cum_pe)
    q.emit("vector", lambda e: e.tensor_copy(sbuf["cumc"][:], psS[:, 96:96 + NH]), inc="vd")
    prod_vd = q.n("vd")

    q.wait("sync", "st_x")
    q.dma("sync", sbuf["xfc"][:].rearrange("p (j t) -> p j t", t=CH),
          dram["d_x"][0:DI, tl0:tl0 + CH].rearrange("(j p) t -> p j t", p=CH), "ld_xf")
    q.wait("vector", "ld_xf")
    q.wait("vector", "ac", dec_ac)
    q.emit("vector", lambda e: e.tensor_tensor(
        sbuf["xdc"][:].rearrange("p (j t) -> p j t", t=CH),
        sbuf["xfc"][:].rearrange("p (j t) -> p j t", t=CH),
        sbuf["dcol"][:].unsqueeze(2).broadcast_to([CH, 16, CH]), ALU.mult), inc="vd")
    q.wait("vector", "ld_xt")
    q.wait("vector", "ld_xc", ld_xc_tgt)
    btc = sbuf["B_tm"][:, gc * DS:(gc + 1) * DS]
    q.emit("vector", lambda e, wd_bf=wd_bf: e.tensor_tensor(
        sbuf["bw"][:].rearrange("p (h n) -> p h n", h=NH),
        btc.unsqueeze(1).broadcast_to([CH, NH, DS]),
        wd_bf.unsqueeze(2).broadcast_to([CH, NH, DS]), ALU.mult), inc="vd")
    xtc = sbuf["x_tm"][:, xbuf * DI:(xbuf + 1) * DI]
    q.emit("vector", lambda e: e.tensor_tensor(
        sbuf["xdt"][:].rearrange("p (h d) -> p h d", h=NH),
        xtc.rearrange("p (h d) -> p h d", h=NH),
        dtc.unsqueeze(2).broadcast_to([CH, NH, HD]), ALU.mult), inc="vd")
    build_vd = q.n("vd")

    q.wait("scalar", "pe", T.get("m4_pe", 0))
    q.wait("scalar", "vd", T.get("m4_vd", 0))
    q.emit("scalar", lambda e: e.copy(sbuf["Rb"][:], sbuf["R"][:]), inc="ac")
    rb_ac = q.n("ac")
    q.wait("tensor", "vd", build_vd)
    q.wait("tensor", "vd", T.get("m4_vd", 0))
    m3_pe = {}
    radd_vd = {}
    def m3_out(hh):
        if hh < 16:
            return ps2[:, hh * HD:(hh + 1) * HD]
        if hh < 24:
            return ps2[:, 1024 + (hh - 16) * HD:1024 + (hh - 15) * HD]
        return ps1[:, 1536 + (hh - 24) * HD:1536 + (hh - 23) * HD]
    for hh in range(32):
        q.emit("tensor", lambda e, hh=hh: e.matmul(
            m3_out(hh), sbuf["bw"][:, hh * DS:(hh + 1) * DS],
            sbuf["xdt"][:, hh * HD:(hh + 1) * HD], start=(hh % 8 == 0), stop=(hh % 8 == 7)),
            inc=("pe" if hh in (15, 31) else None))
        if hh == 15:
            m3_pe[0] = q.n("pe")
    m3_pe[1] = q.n("pe")
    q.wait("vector", "ac", dec_ac)
    for hhalf in range(2):
        hsl = slice(hhalf * 1024, (hhalf + 1) * 1024)
        q.emit("vector", lambda e, hhalf=hhalf, hsl=hsl, pcb=pcb: e.tensor_tensor(
            sbuf["rtmp"][:, hsl].rearrange("p (h d) -> p h d", d=HD),
            sbuf["R"][:, hsl].rearrange("p (h d) -> p h d", d=HD),
            pcb[:, hhalf * 16:(hhalf + 1) * 16].unsqueeze(2).broadcast_to([DS, 16, HD]),
            ALU.mult), inc="vd")
    q.wait("vector", "pe", m3_pe[0])
    q.wait("vector", "ac", rb_ac)
    q.wait("vector", "vd")
    q.emit("vector", lambda e: e.tensor_tensor(
        sbuf["R"][:, 0:1024], sbuf["rtmp"][:, 0:1024], ps2[:, 0:1024], ALU.add), inc="vd")
    radd_vd[0] = q.n("vd")
    q.wait("vector", "pe", m3_pe[1])
    q.emit("vector", lambda e: e.tensor_tensor(
        sbuf["R"][:, 1024:1536], sbuf["rtmp"][:, 1024:1536], ps2[:, 1024:1536], ALU.add), inc="vd")
    q.emit("vector", lambda e: e.tensor_tensor(
        sbuf["R"][:, 1536:2048], sbuf["rtmp"][:, 1536:2048], ps1[:, 1536:2048], ALU.add), inc="vd")
    radd_vd[1] = q.n("vd")
    scan_vd = q.n("vd")

    q.wait("tensor", "vd", prod_vd)
    q.wait("tensor", "vd", T.get("ytile_vd", 0))
    d1_t = {}
    exp_t = {}
    for p in range(8):
        bank = p % 3
        if p >= 3:
            q.wait("tensor", "vd", d1_t[p - 3])
        sl = slice(p * 512, (p + 1) * 512)
        q.emit("tensor", lambda e, sl=sl, bank=bank: e.matmul(
            ps1[:, bank * 512:(bank + 1) * 512], sbuf["ones_full"][:], sbuf["rowp_hi"][:, sl], start=True, stop=False))
        q.emit("tensor", lambda e, sl=sl, bank=bank: e.matmul(
            ps1[:, bank * 512:(bank + 1) * 512], sbuf["ones_full"][:], sbuf["rowp_lo"][:, sl], start=False, stop=True), inc="pe")
        pe_p = q.n("pe")
        q.wait("vector", "pe", pe_p)
        if p >= 3:
            q.wait("vector", "ac", exp_t[p - 3])
        q.emit("vector", lambda e, p=p, bank=bank: e.tensor_tensor(
            sbuf["d1"][:, bank * 512:(bank + 1) * 512].rearrange("p (a b) -> p a b", b=CH),
            ps1[:, bank * 512:(bank + 1) * 512].rearrange("p (a b) -> p a b", b=CH),
            sbuf["cumc"][:, 4 * p:4 * p + 4].unsqueeze(2).broadcast_to([CH, 4, CH]),
            ALU.subtract), inc="vd")
        q.emit("vector", lambda e, bank=bank: e.scalar_tensor_tensor(
            sbuf["d1"][:, bank * 512:(bank + 1) * 512].rearrange("p (a b) -> p a b", b=CH),
            sbuf["d1"][:, bank * 512:(bank + 1) * 512].rearrange("p (a b) -> p a b", b=CH), 0.0,
            sbuf["maskb"][:].unsqueeze(1).broadcast_to([CH, 4, CH]),
            ALU.min, ALU.add), inc="vd")
        d1_t[p] = q.n("vd")
        q.wait("scalar", "vd", d1_t[p])
        q.emit("scalar", lambda e, p=p, bank=bank: e.activation(
            sbuf["ebuf"][:, p * 512:(p + 1) * 512], sbuf["d1"][:, bank * 512:(bank + 1) * 512], AF.Exp), inc="ac")
        exp_t[p] = q.n("ac")
    diff_pe = q.n("pe")

    q.wait("vector", "ac", exp_t[7])
    q.wait("vector", "ac", sct_ac)
    q.emit("vector", lambda e: e.tensor_tensor(
        sbuf["mt"][:].rearrange("p (h l) -> p h l", h=NH),
        sbuf["ebuf"][:].rearrange("p (h l) -> p h l", h=NH),
        sbuf["scT"][:].unsqueeze(1).broadcast_to([CH, NH, CH]), ALU.mult), inc="vd")
    mt_vd = q.n("vd")

    q.wait("tensor", "vd", mt_vd)
    for h in range(NH):
        q.emit("tensor", lambda e, h=h: e.matmul(
            ps1[:, h * HD:(h + 1) * HD], sbuf["mt"][:, h * CH:(h + 1) * CH],
            sbuf["xdt"][:, h * HD:(h + 1) * HD], start=(h % 8 == 0), stop=False))
    for j in range(16):
        q.emit("tensor", lambda e, j=j: e.matmul(
            ps1[:, j * CH:(j + 1) * CH],
            sbuf["xdc"][:, j * CH:(j + 1) * CH],
            sbuf["ident"][:], start=False, stop=(j % 4 == 3)),
            inc=("pe" if j == 15 else None))
    y_pe = q.n("pe")

    q.wait("tensor", "vd", scan_vd)
    q.wait("tensor", "ac", rb_ac)
    ybt_vd = {}
    m4_pe = {}

    def m4_mm(qq):
        bank = qq % 3
        q.emit("tensor", lambda e, qq=qq, bank=bank, tl0=tl0: e.matmul(
            ps2[:, bank * 512:(bank + 1) * 512], sbuf["C_fm"][:, tl0:tl0 + CH],
            sbuf["Rb"][:, qq * 512:(qq + 1) * 512], start=True, stop=True), inc="pe")
        m4_pe[qq] = q.n("pe")

    def m4_ev(qq):
        bank = qq % 3
        q.wait("vector", "pe", m4_pe[qq])
        q.emit("vector", lambda e, qq=qq, bank=bank, eacol=eacol: e.tensor_tensor(
            sbuf["ybt"][:].rearrange("p (h d) -> p h d", d=HD),
            ps2[:, bank * 512:(bank + 1) * 512].rearrange("p (h d) -> p h d", d=HD),
            eacol[:, qq * 8:(qq + 1) * 8].unsqueeze(2).broadcast_to([CH, 8, HD]),
            ALU.mult), inc="vd")
        ybt_vd[qq] = q.n("vd")
        q.emit("vector", lambda e, qq=qq: e.scalar_tensor_tensor(
            sbuf["ytile"][:, qq * 512:(qq + 1) * 512], ps1[:, qq * 512:(qq + 1) * 512], 1.0,
            sbuf["ybt"][:], ALU.mult, ALU.add), inc="vd")

    m4_mm(0)
    m4_mm(1)
    m4_mm(2)
    q.wait("vector", "pe", y_pe)
    q.wait("vector", "st_y", kb.ssd_sty)
    m4_ev(0)
    q.wait("tensor", "vd", ybt_vd[0])
    m4_mm(3)
    m4_ev(1)
    m4_ev(2)
    m4_ev(3)
    ytile_vd = q.n("vd")
    q.wait("sync", "vd", ytile_vd)
    q.dma("sync", dram["d_y"][tl0:tl0 + CH, :], sbuf["ytile"][:], "st_y")
    kb.ssd_sty = q.n("st_y")

    kb.ssd_t = dict(
        psS_vd=copies_vd, psS_ac=dec_ac, mt_vd=mt_vd, bw_vd=build_vd,
        diff_pe=diff_pe, m4_pe=m4_pe[3], m4_vd=ytile_vd, ytile_vd=ytile_vd,
    )


def phase_gather_w(kb, sc):
    nc, q, dram, sbuf = kb.nc, kb.q, kb.dram, kb.sbuf
    wsh = kb.sb("wsh", [CH, 2 * FFI], F16, scope=sc)
    wqo = kb.sb("wqo", [CH, FFI], BF16, scope=sc)
    qmx = kb.sb("qmx", [CH, 2 * (FFI // 16)], scope=sc)
    qmxr = kb.sb("qmxr", [CH, FFI // 16], scope=sc)
    qs16 = kb.sb("qs16", [CH, FFI // 16], scope=sc)
    qw = kb.sb("qw", [CH, FFI], F16, scope=sc)
    qp1 = kb.sb("qp1", [CH, FFI], F16, scope=sc)
    qta = kb.sb("qta", [CH, FFI], F16, scope=sc)
    qtb = kb.sb("qtb", [CH, FFI], F16, scope=sc)
    qp23 = kb.sb("qp23", [CH, FFI], F16, scope=sc)

    tiles = []
    for src_t, cc_t, rows, width in (
        ("wg_sh", "cc_g_in", FFI // NCORES, D),
        ("wu_sh", "cc_u_in", FFI // NCORES, D),
        ("wd_sh", "cc_dn_in", D // NCORES, FFI),
    ):
        r0 = 0
        while r0 < rows:
            rr = min(CH, rows - r0)
            tiles.append((src_t, cc_t, r0, rr, width))
            r0 += rr
    srcfree = {}
    pend = None
    for ti, (src_t, cc_t, r0, rr, width) in enumerate(tiles):
        par = ti % 2
        if ti >= 2:
            q.wait("gpsimd", "vd", srcfree[ti - 2])
        q.dma("gpsimd", wsh[0:rr, par * FFI:par * FFI + width],
              dram[src_t][r0:r0 + rr, :], "ld_wq")
        if ti == 1:
            q.dma("gpsimd", kb.dram["d_wxb"], kb.dram["w_xbcT"], "ldw0")
        q.wait("vector", "ld_wq", q.n("ld_wq"))
        qr_t = equant_pre(kb, src=wsh, sl=slice(0, rr), width=width, mx=qmx,
                          mx0=par * (FFI // 16), src0=par * FFI)
        srcfree[ti] = q.n("vd")
        if pend is not None:
            (psrc, pcc, pr0, prr, pwidth, ppar, pqr, pti) = pend
            q.wait("vector", "st_wq")
            equant_post(kb, src=wsh, sl=slice(0, prr), width=pwidth, mx=qmx, mxr=qmxr,
                        s16=qs16, w_=qw, p1=qp1, t_a=qta, t_b=qtb, p23=qp23, out=wqo,
                        qr_tgt=pqr, mx0=ppar * (FFI // 16), src0=ppar * FFI)
            srcfree[pti] = kb.srcfree_vd
            q.wait("sync", "vd")
            q.dma("sync", dram[pcc][pr0:pr0 + prr, :], wqo[0:prr, 0:pwidth], "st_wq")
        pend = (src_t, cc_t, r0, rr, width, par, qr_t, ti)
    (psrc, pcc, pr0, prr, pwidth, ppar, pqr, pti) = pend
    q.wait("vector", "st_wq")
    equant_post(kb, src=wsh, sl=slice(0, prr), width=pwidth, mx=qmx, mxr=qmxr,
                s16=qs16, w_=qw, p1=qp1, t_a=qta, t_b=qtb, p23=qp23, out=wqo,
                qr_tgt=pqr, mx0=ppar * (FFI // 16), src0=ppar * FFI)
    q.wait("sync", "vd")
    q.dma("sync", dram[pcc][pr0:pr0 + prr, :], wqo[0:prr, 0:pwidth], "st_wq")
    q.wait("gpsimd", "st_wq")
    for kind, i, o in (("g", "cc_g_in", "cc_g_out"), ("u", "cc_u_in", "cc_u_out"),
                       ("d", "cc_dn_in", "cc_dn_out")):
        q.emit("gpsimd", lambda e, i=i, o=o: e.collective_compute(
            "AllGather", ALU.bypass, replica_groups=[list(range(NCORES))],
            ins=[dram[i]], outs=[dram[o]]), inc="ccd")


def phase_gather_sl(kb):
    nc, q, dram, sbuf = kb.nc, kb.q, kb.dram, kb.sbuf
    q.wait("gpsimd", "st_s")
    for kind, i, o in (("s", "cc_s_in", "cc_s_out"), ("l", "cc_l_in", "cc_l_out")):
        q.emit("gpsimd", lambda e, i=i, o=o: e.collective_compute(
            "AllGather", ALU.bypass, replica_groups=[list(range(NCORES))],
            ins=[dram[i]], outs=[dram[o]]), inc="ccd")
    q.wait("sync", "ccd")
    kb.tap("t_wg", dram["cc_g_out"])
    kb.tap("t_wdn", dram["cc_dn_out"])
    kb.tap("t_sg", dram["cc_s_out"])
    kb.tap("t_lg", dram["cc_l_out"])


def phase_ffn(kb):
    nc, q, dram, sbuf = kb.nc, kb.q, kb.dram, kb.sbuf
    OGS = [(0, 1024), (1024, 1024), (2048, 768)]
    with contextlib.ExitStack() as sc:
        xqf = kb.sb("xqf", [CH, 8 * TOK], BF16, scope=sc)
        wgt = kb.sb("wgt", [CH, 8 * 1024], BF16, scope=sc)
        wut = kb.sb("wut", [CH, 8 * 1024], BF16, scope=sc)
        sg = kb.sb("sg", [CH, 2 * 1024], F16, scope=sc)
        hbuf = kb.sb("hbuf", [CH, 2 * 1024], F16, scope=sc)
        hq = kb.sb("hq", [CH, 2 * 1024], BF16, scope=sc)
        fmx = kb.sb("fmx", [CH, 128], scope=sc)
        fmxr = kb.sb("fmxr", [CH, 64], scope=sc)
        fs16 = kb.sb("fs16", [CH, 64], scope=sc)
        fw = kb.sb("fw", [CH, 1024], F16, scope=sc)
        fp1 = kb.sb("fp1", [CH, 1024], F16, scope=sc)
        fta = kb.sb("fta", [CH, 1024], F16, scope=sc)
        ftb = kb.sb("ftb", [CH, 1024], F16, scope=sc)
        fp23 = kb.sb("fp23", [CH, 1024], F16, scope=sc)
        psg = kb.ps("psg", [CH, 2048], scope=sc)
        psu = kb.ps("psu", [CH, 2048], scope=sc)

        q.wait("scalar", "st_xq")
        q.wait("scalar", "ccd")
        q.wait("scalar", "st_wq")
        if "t_xq3" in kb.taps:
            q.wait("sync", "st_xq")
            q.dma("sync", hq[:, 0:D], dram["d_xq"][0:CH, :], "ld_dbg")
            q.wait("sync", "ld_dbg")
            kb.tap("t_xq3", hq[:, 0:D])
            q.wait("vector", "st_tap")
        for k in range(8):
            q.dma("scalar", xqf[:, k * TOK:(k + 1) * TOK],
                  dram["d_xq"][:, k * CH:(k + 1) * CH], "ld_ff", transpose=True)
        q.wait("sync", "ld_ff")
        kb.tap("t_xqf", xqf[:])
        ac_t = {}
        vd_t = {}
        pe_t = {}
        st_t = {}
        gtc = 0
        pend = None
        for og0, ogw in OGS:
            q.wait("scalar", "pe")
            for k in range(8):
                q.dma("scalar", wgt[:, k * 1024:k * 1024 + ogw],
                      dram["cc_g_out"][og0:og0 + ogw, k * CH:(k + 1) * CH], "ld_ff",
                      transpose=True)
                q.dma("scalar", wut[:, k * 1024:k * 1024 + ogw],
                      dram["cc_u_out"][og0:og0 + ogw, k * CH:(k + 1) * CH], "ld_ff",
                      transpose=True)
            q.wait("tensor", "ld_ff")
            for tc in range(16):
                p = gtc % 2
                if gtc >= 2:
                    q.wait("tensor", "ac", ac_t[gtc - 2])
                    q.wait("tensor", "vd", vd_t[gtc - 2])
                pieces = [(0, 512), (512, ogw - 512)]
                for k in range(8):
                    for f0, fw_ in pieces:
                        q.emit("tensor", lambda e, tc=tc, k=k, f0=f0, fw_=fw_, p=p:
                               e.matmul(psg[:, p * 1024 + f0:p * 1024 + f0 + fw_],
                                        xqf[:, k * TOK + tc * CH:k * TOK + (tc + 1) * CH],
                                        wgt[:, k * 1024 + f0:k * 1024 + f0 + fw_],
                                        start=(k == 0), stop=(k == 7)))
                for k in range(8):
                    for f0, fw_ in pieces:
                        q.emit("tensor", lambda e, tc=tc, k=k, f0=f0, fw_=fw_, p=p:
                               e.matmul(psu[:, p * 1024 + f0:p * 1024 + f0 + fw_],
                                        xqf[:, k * TOK + tc * CH:k * TOK + (tc + 1) * CH],
                                        wut[:, k * 1024 + f0:k * 1024 + f0 + fw_],
                                        start=(k == 0), stop=(k == 7)),
                               inc=("pe" if (k == 7 and f0 != 0) else None))
                pe_t[gtc] = q.n("pe")
                q.wait("scalar", "pe", pe_t[gtc])
                q.emit("scalar", lambda e, ogw=ogw, p=p: e.activation(
                    sg[:, p * 1024:p * 1024 + ogw], psg[:, p * 1024:p * 1024 + ogw],
                    AF.Silu), inc="ac")
                ac_t[gtc] = q.n("ac")
                q.wait("vector", "ac", ac_t[gtc])
                q.emit("vector", lambda e, ogw=ogw, p=p: e.tensor_tensor(
                    hbuf[:, p * 1024:p * 1024 + ogw], sg[:, p * 1024:p * 1024 + ogw],
                    psu[:, p * 1024:p * 1024 + ogw], ALU.mult), inc="vd")
                vd_t[gtc] = q.n("vd")
                qr_t = equant_pre(kb, src=hbuf, sl=slice(0, CH), width=ogw, mx=fmx,
                                  mx0=p * 64, src0=p * 1024)
                if pend is not None:
                    pgtc, ptc, pog0, pogw, pp, pqr = pend
                    q.wait("vector", "st_hq", st_t.get(pgtc - 2, 0))
                    equant_post(kb, src=hbuf, sl=slice(0, CH), width=pogw, mx=fmx,
                                mxr=fmxr, s16=fs16, w_=fw, p1=fp1, t_a=fta, t_b=ftb,
                                p23=fp23, out=hq, qr_tgt=pqr,
                                osl=slice(pp * 1024, pp * 1024 + pogw),
                                mx0=pp * 64, src0=pp * 1024)
                    q.wait("sync", "vd")
                    q.dma("sync", dram["d_hq"][ptc * CH:(ptc + 1) * CH, pog0:pog0 + pogw],
                          hq[0:CH, pp * 1024:pp * 1024 + pogw], "st_hq")
                    st_t[pgtc] = q.n("st_hq")
                pend = (gtc, tc, og0, ogw, p, qr_t)
                gtc += 1
        pgtc, ptc, pog0, pogw, pp, pqr = pend
        q.wait("vector", "st_hq", st_t.get(pgtc - 2, 0))
        equant_post(kb, src=hbuf, sl=slice(0, CH), width=pogw, mx=fmx,
                    mxr=fmxr, s16=fs16, w_=fw, p1=fp1, t_a=fta, t_b=ftb,
                    p23=fp23, out=hq, qr_tgt=pqr,
                    osl=slice(pp * 1024, pp * 1024 + pogw),
                    mx0=pp * 64, src0=pp * 1024)
        q.wait("sync", "vd")
        q.dma("sync", dram["d_hq"][ptc * CH:(ptc + 1) * CH, pog0:pog0 + pogw],
              hq[0:CH, pp * 1024:pp * 1024 + pogw], "st_hq")
        st_t[pgtc] = q.n("st_hq")
    q.wait("sync", "st_hq")
    kb.tap("t_hq", dram["d_hq"])

    q.tag = "ffn_dn"
    with contextlib.ExitStack() as sc:
        dnt = kb.sb("dnt", [CH, 22 * 1024], BF16, scope=sc)
        hqt = kb.sb("hqt", [CH, 2 * 512], BF16, scope=sc)
        psD = kb.ps("psD", [CH, 2048], scope=sc)
        psD2 = kb.ps("psD2", [CH, 2048], scope=sc)
        q.wait("scalar", "ccd")
        q.wait("scalar", "st_hq")
        q.wait("scalar", "vd")
        for o in range(22):
            q.dma("scalar", dnt[:, o * 1024:(o + 1) * 1024],
                  dram["cc_dn_out"][0:D, o * CH:(o + 1) * CH], "ld_dn", transpose=True)
        q.wait("tensor", "ld_dn")
        q.wait("tensor", "vd")
        dpe = {}
        dld = {}
        for t in range(4):
            q.wait("tensor", "ac")
            for o in range(22):
                buf = o % 2
                go = t * 22 + o
                if go >= 2:
                    q.wait("scalar", "pe", dpe[go - 2])
                q.dma("scalar", hqt[:, buf * 512:(buf + 1) * 512],
                      dram["d_hq"][t * 512:(t + 1) * 512, o * CH:(o + 1) * CH], "ld_hf",
                      transpose=True)
                dld[go] = q.n("ld_hf")
                q.wait("tensor", "ld_hf", dld[go])
                for ee in range(8):
                    ps = psD if ee < 4 else psD2
                    q.emit("tensor", lambda e, o=o, ee=ee, buf=buf, ps=ps:
                           e.matmul(ps[:, (ee % 4) * 512:(ee % 4 + 1) * 512],
                                    dnt[:, o * 1024 + ee * CH:o * 1024 + (ee + 1) * CH],
                                    hqt[:, buf * 512:(buf + 1) * 512],
                                    start=(o == 0), stop=(o == 21)),
                           inc=("pe" if ee == 7 else None))
                dpe[go] = q.n("pe")
            q.wait("scalar", "pe")
            for ee in range(8):
                ps = psD if ee < 4 else psD2
                q.emit("scalar", lambda e, t=t, ee=ee, ps=ps: e.activation(
                    sbuf["g_fm"][:, ee * TOK + t * 512:ee * TOK + (t + 1) * 512],
                    ps[:, (ee % 4) * 512:(ee % 4 + 1) * 512], AF.Silu), inc="ac")
    q.wait("sync", "st_hq")
    kb.tap("t_gfm", kb.sbuf["g_fm"][:])


def phase_tail(kb):
    nc, q, dram, sbuf = kb.nc, kb.q, kb.dram, kb.sbuf
    with contextlib.ExitStack() as sc:
        Lmat = kb.sb("Lmat", [NH, NCORES], scope=sc)
        etmp = kb.sb("etmp", [NH, NCORES * NCORES], scope=sc)
        esum = kb.sb("esum", [NH, NCORES], scope=sc)
        coef = kb.sb("coef", [NH, NCORES], scope=sc)
        crow = kb.sb("crow", [1, NCORES * NH], F16, scope=sc)
        cbc = kb.sb("cbc", [CH, NCORES * NH], scope=sc)
        Sj = kb.sb("Sj", [DS, 2 * NH * HD], scope=sc)
        Rtmp2 = kb.sb("Rtmp2", [DS, NH * HD], scope=sc)
        Rin = kb.sb("Rin", [DS, NH * HD], scope=sc)
        Rinb = kb.sb("Rinb", [DS, NH * HD], BF16, scope=sc)
        lcz = kb.sb("lcz2", [NH, NCH], scope=sc)
        lrow = kb.sb("lrow", [1, NCH * NH], F16, scope=sc)
        ead = kb.sb("ead", [CH, 2 * NH], scope=sc)
        ycorr = kb.sb("ycorr", [CH, 512], scope=sc)
        yload = kb.sb("yload", [CH, 2 * DI], BF16, scope=sc)
        psC = kb.ps("psC", [CH, 2048], scope=sc)
        psS2 = kb.ps("psS2", [CH, 512], scope=sc)

        q.wait("sync", "ccd")
        q.wait("sync", "pe")
        q.wait("sync", "ac")
        q.dma("sync", Lmat[:], dram["cc_l_out"].rearrange("(r h) b -> h (r b)", h=NH),
              "ld_t", allow_slow_non_contiguous=True)
        q.wait("vector", "ld_t")
        q.emit("vector", lambda e: e.tensor_tensor(
            etmp[:].rearrange("p (j i) -> p j i", j=NCORES),
            Lmat[:].unsqueeze(1).broadcast_to([NH, NCORES, NCORES]),
            sbuf["bmx_s"][:].rearrange("p (j i) -> p j i", j=NCORES), ALU.mult), inc="vd")
        q.wait("vector", "vd")
        q.emit("vector", lambda e: e.tensor_reduce(
            esum[:].unsqueeze(2), etmp[:].rearrange("p (j i) -> p j i", j=NCORES),
            AX.X, ALU.add))
        q.emit("vector", lambda e: e.drain(), inc="vd")
        q.wait("scalar", "vd")
        q.emit("scalar", lambda e: e.activation(esum[:], esum[:], AF.Exp))
        q.emit("scalar", lambda e: e.drain(), inc="ac")
        q.wait("vector", "ac")
        q.emit("vector", lambda e: e.tensor_tensor(coef[:], esum[:], sbuf["pmx_s"][:], ALU.mult), inc="vd")
        q.wait("sync", "vd")
        q.dma("sync", dram["d_coef"].rearrange("j h -> h j"), coef[:], "st_t",
              allow_slow_non_contiguous=True)
        q.wait("sync", "st_t")
        kb.tap("t_coef", dram["d_coef"])
        q.wait("gpsimd", "st_t")
        q.emit("gpsimd", lambda e: e.memset(Rin[:], 0.0), inc="gp")
        q.dma("gpsimd", crow[:], dram["d_coef"].rearrange("j h -> (j h)").unsqueeze(0),
              "ld_tg")
        q.wait("tensor", "ld_tg", q.n("ld_tg"))
        q.wait("tensor", "vd")
        for j in range(NCORES):
            q.emit("tensor", lambda e, j=j: e.matmul(
                psS2[:, j * NH:(j + 1) * NH], sbuf["ones1_bf"][:],
                crow[0:1, j * NH:(j + 1) * NH], start=True, stop=True),
                inc=("pe" if j == NCORES - 1 else None))
        cbc_pe = q.n("pe")
        sld = {}
        q.wait("sync", "vd")
        q.dma("sync", Sj[:, 0:NH * HD], dram["cc_s_out"][0:DS, :], "ld_t")
        sld[0] = q.n("ld_t")
        q.wait("vector", "pe", cbc_pe)
        q.wait("vector", "gp")
        q.emit("vector", lambda e: e.tensor_copy(cbc[:], psS2[:, 0:NCORES * NH]), inc="vd")
        rt_t = {}
        for j in range(NCORES):
            jp = j % 2
            if j + 1 < NCORES:
                if j >= 1:
                    q.wait("sync", "vd", rt_t[j - 1])
                q.dma("sync", Sj[:, (1 - jp) * NH * HD:(2 - jp) * NH * HD],
                      dram["cc_s_out"][(j + 1) * DS:(j + 2) * DS, :], "ld_t")
                sld[j + 1] = q.n("ld_t")
            q.wait("vector", "ld_t", sld[j])
            q.emit("vector", lambda e, j=j, jp=jp: e.tensor_tensor(
                Rtmp2[:].rearrange("p (h d) -> p h d", d=HD),
                Sj[:, jp * NH * HD:(jp + 1) * NH * HD].rearrange("p (h d) -> p h d", d=HD),
                cbc[:, j * NH:(j + 1) * NH].unsqueeze(2).broadcast_to([DS, NH, HD]),
                ALU.mult), inc="vd")
            rt_t[j] = q.n("vd")
            q.wait("vector", "vd")
            q.emit("vector", lambda e: e.tensor_tensor(Rin[:], Rin[:], Rtmp2[:], ALU.add), inc="vd")
        q.wait("vector", "vd")
        q.emit("vector", lambda e: e.tensor_copy(Rinb[:], Rin[:]), inc="vd")

        q.emit("vector", lambda e: e.memset(lcz[:, 0:1], 0.0), inc="vd")
        for c in range(1, NCH):
            q.emit("vector", lambda e, c=c: e.tensor_tensor(
                lcz[:, c:c + 1], lcz[:, c - 1:c], sbuf["az_fm"][:, c - 1:c], ALU.add))
            q.emit("vector", lambda e: e.drain(), inc="vd")
        q.wait("sync", "vd")
        q.dma("sync", dram["d_lcz"].rearrange("c h -> h c"), lcz[:], "st_t",
              allow_slow_non_contiguous=True)
        q.wait("sync", "st_t")
        kb.tap("t_lcz", dram["d_lcz"])
        q.wait("sync", "vd")
        kb.tap("t_rin", Rin[:])

        q.tag = "tail_corr"
        q.wait("gpsimd", "st_t")
        q.wait("gpsimd", "st_y")
        q.dma("gpsimd", lrow[:], dram["d_lcz"].rearrange("c h -> (c h)").unsqueeze(0),
              "ld_tg")
        lrow_t = q.n("ld_tg")
        q.dma("gpsimd", yload[:, 0:DI], dram["d_y"][0:CH, :], "ld_y")
        yl_t = {0: q.n("ld_y")}
        q.wait("tensor", "ld_tg", lrow_t)
        q.wait("tensor", "vd")
        exp_t = {}
        mm3_t = {}
        pc_t = {}
        yc_t = {}
        yc_half = {}
        styt = {}
        for gc in range(NCH):
            c2 = gc % 2
            tl0 = gc * CH
            ahc = sbuf["a_hi"][:, gc * NH:(gc + 1) * NH]
            alc = sbuf["a_lo"][:, gc * NH:(gc + 1) * NH]
            s0 = 256 + c2 * 64
            if gc >= 2:
                q.wait("tensor", "ac", exp_t[gc - 2])
            q.emit("tensor", lambda e, ahc=ahc, s0=s0: e.matmul(psS2[:, s0:s0 + NH], sbuf["ut_incl"][:], ahc, start=True, stop=False))
            q.emit("tensor", lambda e, alc=alc, s0=s0: e.matmul(psS2[:, s0:s0 + NH], sbuf["ut_incl"][:], alc, start=False, stop=False))
            q.emit("tensor", lambda e, gc=gc, s0=s0: e.matmul(psS2[:, s0:s0 + NH], sbuf["ones1_bf"][:], lrow[0:1, gc * NH:(gc + 1) * NH], start=False, stop=True), inc="pe")
            mm3_t[gc] = q.n("pe")
            q.wait("scalar", "pe", mm3_t[gc])
            if gc >= 2:
                q.wait("scalar", "vd", yc_t[gc - 2])
            q.emit("scalar", lambda e, c2=c2, s0=s0: e.activation(
                ead[:, c2 * NH:(c2 + 1) * NH], psS2[:, s0:s0 + NH], AF.Exp), inc="ac")
            exp_t[gc] = q.n("ac")
            if gc + 1 < NCH:
                if gc >= 1:
                    q.wait("gpsimd", "st_y", styt[gc - 1])
                q.dma("gpsimd", yload[:, (1 - c2) * DI:(2 - c2) * DI],
                      dram["d_y"][(gc + 1) * CH:(gc + 2) * CH, :], "ld_y")
                yl_t[gc + 1] = q.n("ld_y")
            for half in range(2):
                if gc >= 1:
                    q.wait("tensor", "vd", yc_half[(gc - 1, half)])
                for qq in (half * 2, half * 2 + 1):
                    q.emit("tensor", lambda e, qq=qq, tl0=tl0: e.matmul(
                        psC[:, qq * 512:(qq + 1) * 512], sbuf["C_fm"][:, tl0:tl0 + CH],
                        Rinb[:, qq * 512:(qq + 1) * 512], start=True, stop=True),
                        inc=("pe" if qq % 2 == 1 else None))
                pc_t[(gc, half)] = q.n("pe")
            q.wait("vector", "ac", exp_t[gc])
            q.wait("vector", "ld_y", yl_t[gc])
            for half in range(2):
                q.wait("vector", "pe", pc_t[(gc, half)])
                for qq in (half * 2, half * 2 + 1):
                    q.emit("vector", lambda e, qq=qq, c2=c2: e.tensor_tensor(
                        ycorr[:].rearrange("p (h d) -> p h d", d=HD),
                        psC[:, qq * 512:(qq + 1) * 512].rearrange("p (h d) -> p h d", d=HD),
                        ead[:, c2 * NH + qq * 8:c2 * NH + (qq + 1) * 8].unsqueeze(2).broadcast_to([CH, 8, HD]),
                        ALU.mult), inc="vd")
                    q.wait("vector", "vd")
                    q.emit("vector", lambda e, qq=qq, c2=c2: e.tensor_tensor(
                        yload[:, c2 * DI + qq * 512:c2 * DI + (qq + 1) * 512],
                        yload[:, c2 * DI + qq * 512:c2 * DI + (qq + 1) * 512],
                        ycorr[:], ALU.add), inc="vd")
                yc_half[(gc, half)] = q.n("vd")
            yc_t[gc] = q.n("vd")
            q.wait("sync", "vd", yc_t[gc])
            q.dma("sync", dram["d_y"][tl0:tl0 + CH, :], yload[:, c2 * DI:(c2 + 1) * DI], "st_y")
            styt[gc] = q.n("st_y")
        q.wait("sync", "st_y")
        kb.tap("t_y2", dram["d_y"])

    with contextlib.ExitStack() as sct:
        t1 = kb.sb("t1", [CH, 16 * TOK], BF16, scope=sct)
        ffff = phase_tail2(kb, sct, t1)


def phase_tail2(kb, sct, t1):
    nc, q, dram, sbuf = kb.nc, kb.q, kb.dram, kb.sbuf
    q.tag = "ztail"
    with contextlib.ExitStack() as sc:
        xn2 = kb.sb("xn2", [CH, 8 * TOKH], BF16, scope=sc)
        wz = kb.sb("wz", [CH, 2 * 8 * CH], BF16, scope=sc)
        szc = kb.sb("szc", [CH, TOK], F16, scope=sc)
        yfm = kb.sb("yfm", [CH, 2 * TOK], BF16, scope=sc)
        sq2 = kb.sb("sq2", [CH, 3 * 1024], BF16, scope=sc)
        ones32b = kb.sb("ones32b", [CH, NH], BF16, scope=sc)
        rrow = kb.sb("rrow", [NH, TOK], scope=sc)
        psZ = kb.ps("psZ", [CH, 2048], scope=sc)
        psR = kb.ps("psR", [CH, 2048], scope=sc)

        q.wait("gpsimd", "st_y")
        q.emit("gpsimd", lambda e: e.memset(ones32b[:], 1.0), inc="gp")
        q.wait("sync", "st_xn")
        q.wait("sync", "st_y")
        q.dma("sync", xn2[:], dram["d_xn"], "ld_t")
        xn2_t = q.n("ld_t")
        q.wait("sync", "ldwz", 16)
        q.wait("scalar", "st_y")
        zmm_t = {}
        silu_t = {}
        gate_t = {}
        sq2_t = {}
        rms_t = {}
        wz_t = {}
        yfm_t = {}
        for fc in range(16):
            pf = fc % 2
            if fc == 0:
                q.dma("sync", wz[:, 0:8 * CH].rearrange("p (k w) -> p k w", k=8),
                      dram["d_wz"][:, 0:CH].rearrange("(k p) w -> p k w", p=CH), "ld_wz")
                wz_t[0] = q.n("ld_wz")
                q.dma("scalar", yfm[:, 0:TOK], dram["d_y"][:, 0:CH], "ld_yf",
                      transpose=True)
                yfm_t[0] = q.n("ld_yf")
            if fc + 1 < 16:
                if fc >= 1:
                    q.wait("sync", "pe", zmm_t[2 * fc - 1])
                q.dma("sync", wz[:, (1 - pf) * 8 * CH:(2 - pf) * 8 * CH].rearrange("p (k w) -> p k w", k=8),
                      dram["d_wz"][:, (fc + 1) * CH:(fc + 2) * CH].rearrange("(k p) w -> p k w", p=CH),
                      "ld_wz")
                wz_t[fc + 1] = q.n("ld_wz")
                if fc >= 1:
                    q.wait("scalar", "vd", gate_t[2 * fc - 1])
                q.dma("scalar", yfm[:, (1 - pf) * TOK:(2 - pf) * TOK],
                      dram["d_y"][:, (fc + 1) * CH:(fc + 2) * CH], "ld_yf",
                      transpose=True)
                yfm_t[fc + 1] = q.n("ld_yf")
            for th in range(2):
                i = fc * 2 + th
                ip = i % 2
                if i == 0:
                    q.wait("tensor", "ld_t", xn2_t)
                    q.wait("tensor", "ac")
                    q.wait("tensor", "vd")
                if th == 0:
                    q.wait("tensor", "ld_wz", wz_t[fc])
                if i >= 2:
                    q.wait("tensor", "ac", silu_t[i - 2])
                for tt_ in range(2):
                    for k in range(8):
                        q.emit("tensor", lambda e, th=th, tt_=tt_, k=k, pf=pf, ip=ip:
                               e.matmul(psZ[:, ip * 1024 + tt_ * 512:ip * 1024 + (tt_ + 1) * 512],
                                        wz[:, pf * 1024 + k * CH:pf * 1024 + (k + 1) * CH],
                                        xn2[:, k * TOKH + HALO + th * 1024 + tt_ * 512: k * TOKH + HALO + th * 1024 + (tt_ + 1) * 512],
                                        start=(k == 0), stop=(k == 7)),
                               inc=("pe" if (k == 7 and tt_ == 1) else None))
                zmm_t[i] = q.n("pe")
                q.wait("scalar", "pe", zmm_t[i])
                if i >= 2:
                    q.wait("scalar", "vd", gate_t[i - 2])
                q.emit("scalar", lambda e, ip=ip: e.activation(
                    szc[:, ip * 1024:(ip + 1) * 1024], psZ[:, ip * 1024:(ip + 1) * 1024],
                    AF.Silu), inc="ac")
                silu_t[i] = q.n("ac")
                q.wait("vector", "ld_yf", yfm_t[fc])
                q.wait("vector", "ac", silu_t[i])
                q.emit("vector", lambda e, fc=fc, th=th, pf=pf, ip=ip: e.tensor_tensor(
                    t1[:, fc * TOK + th * 1024:fc * TOK + (th + 1) * 1024],
                    yfm[:, pf * TOK + th * 1024:pf * TOK + (th + 1) * 1024],
                    szc[:, ip * 1024:(ip + 1) * 1024], ALU.mult), inc="vd")
                gate_t[i] = q.n("vd")
                q.wait("scalar", "vd", gate_t[i])
                i3 = i % 3
                if i >= 3:
                    q.wait("scalar", "pe", rms_t[i - 3])
                q.emit("scalar", lambda e, fc=fc, th=th, i3=i3: e.activation(
                    sq2[:, i3 * 1024:(i3 + 1) * 1024],
                    t1[:, fc * TOK + th * 1024:fc * TOK + (th + 1) * 1024], AF.Square), inc="ac")
                sq2_t[i] = q.n("ac")
                for j in ([i - 2] if i >= 2 else []) + ([i - 1, i] if i == 31 else []):
                    jfc, jth, j3 = j // 2, j % 2, j % 3
                    q.wait("tensor", "ac", sq2_t[j])
                    for tt_ in range(2):
                        q.emit("tensor", lambda e, jfc=jfc, jth=jth, tt_=tt_, j3=j3: e.matmul(
                            psR[0:NH, jth * 1024 + tt_ * 512:jth * 1024 + (tt_ + 1) * 512],
                            ones32b[:], sq2[:, j3 * 1024 + tt_ * 512:j3 * 1024 + (tt_ + 1) * 512],
                            start=(jfc == 0), stop=(jfc == 15)),
                            inc=("pe" if tt_ == 1 else None))
                    rms_t[j] = q.n("pe")
        q.wait("vector", "pe")
        q.emit("vector", lambda e: e.tensor_copy(rrow[:], psR[0:NH, 0:TOK]), inc="vd")
        q.wait("vector", "vd")
        q.emit("vector", lambda e: e.tensor_scalar(rrow[:], rrow[:], 1.0 / DI, None, ALU.mult))
        q.emit("vector", lambda e: e.drain(), inc="vd")
        q.wait("scalar", "vd")
        q.emit("scalar", lambda e: e.activation(rrow[:], rrow[:], AF.Ln, bias=sbuf["epscol"][0:NH, 0:1]))
        q.emit("scalar", lambda e: e.drain())
        q.emit("scalar", lambda e: e.activation(rrow[:], rrow[:], AF.Exp, scale=-0.5))
        q.emit("scalar", lambda e: e.drain(), inc="ac")
        q.wait("sync", "ac")
        q.dma("sync", dram["d_rstd"], rrow[0:1, :], "st_t")
        q.wait("sync", "st_t")
        kb.tap("t_rstd", dram["d_rstd"])
        kb.tap("t_t1", t1[:])

    q.tag = "oproj"
    with contextlib.ExitStack() as sc2:
        wo = kb.sb("wo", [CH, 16 * D], BF16, scope=sc2)
        rfm = kb.sb("rfm", [CH, 8 * 256], BF16, scope=sc2)
        utile2 = kb.sb("utile2", [CH, 2 * D], scope=sc2)
        rcol2 = kb.sb("rcol2", [CH, 2], scope=sc2)
        otile = kb.sb("otile", [CH, 2 * D], scope=sc2)
        psO = kb.ps("psO", [CH, 2048], scope=sc2)
        psO2 = kb.ps("psO2", [CH, 2048], scope=sc2)
        q.wait("scalar", "st_t")
        q.wait("scalar", "ldwo", 16)
        q.dma("scalar", wo[:].rearrange("p (i w) -> p i w", i=16),
              dram["d_wout"].rearrange("(i p) w -> p i w", p=CH), "ld_wo")
        q.wait("vector", "ld_wo")
        for i in range(16):
            q.emit("vector", lambda e, i=i: e.tensor_scalar(
                wo[:, i * D:(i + 1) * D], wo[:, i * D:(i + 1) * D],
                sbuf["nwcol"][:, i:i + 1], None, ALU.mult), inc="vd")
        q.wait("vector", "ac")
        q.wait("gpsimd", "vd")
        q.wait("gpsimd", "st_t")
        u2t = {}
        for tc in (0, 1):
            q.dma("gpsimd", utile2[:, (tc % 2) * D:(tc % 2 + 1) * D],
                  dram["u"][HALO + tc * CH:HALO + (tc + 1) * CH, :], "ld_u2")
            q.dma("gpsimd", rcol2[:, tc % 2:tc % 2 + 1],
                  dram["d_rstd"][0:1, tc * CH:(tc + 1) * CH].rearrange("a p -> (a p)").unsqueeze(1), "ld_u2")
            u2t[tc] = q.n("ld_u2")
        rfm_t = {}
        trans_t = {}
        fin_t = {}
        sto_t = {}
        mmO_t = {}
        q.wait("tensor", "vd")
        q.wait("tensor", "ac")
        for s in range(8):
            if s >= 1:
                q.wait("tensor", "vd", rfm_t[s - 1])
            for ee in range(8):
                for i in range(16):
                    q.emit("tensor", lambda e, s=s, i=i, ee=ee: e.matmul(
                        psO[:, ee * 256:(ee + 1) * 256],
                        wo[:, i * D + ee * CH:i * D + (ee + 1) * CH],
                        t1[:, i * TOK + s * 256:i * TOK + (s + 1) * 256],
                        start=(i == 0), stop=(i == 15)),
                        inc=("pe" if (i == 15 and ee == 7) else None))
            mmO_t[s] = q.n("pe")
            q.wait("vector", "pe", mmO_t[s])
            for ee in range(8):
                q.emit("vector", lambda e, s=s, ee=ee: e.tensor_tensor(
                    rfm[:, ee * 256:(ee + 1) * 256],
                    psO[:, ee * 256:(ee + 1) * 256],
                    sbuf["g_fm"][:, ee * TOK + s * 256:ee * TOK + (s + 1) * 256],
                    ALU.mult), inc="vd")
            rfm_t[s] = q.n("vd")
            q.wait("tensor", "vd", rfm_t[s])
            if s >= 1:
                q.wait("tensor", "vd", fin_t[2 * s - 1])
            for tc2 in range(2):
                for ee in range(8):
                    q.emit("tensor", lambda e, tc2=tc2, ee=ee: e.matmul(
                        psO2[:, tc2 * 1024 + ee * CH:tc2 * 1024 + (ee + 1) * CH],
                        rfm[:, ee * 256 + tc2 * CH:ee * 256 + (tc2 + 1) * CH],
                        sbuf["ident"][:], start=True, stop=True),
                        inc=("pe" if ee == 7 else None))
                trans_t[(s, tc2)] = q.n("pe")
            for tc2 in range(2):
                tc = 2 * s + tc2
                q.wait("vector", "pe", trans_t[(s, tc2)])
                q.wait("vector", "ld_u2", u2t[tc])
                q.wait("vector", "st_out", sto_t.get(tc - 2, 0))
                q.emit("vector", lambda e, tc=tc, tc2=tc2: e.scalar_tensor_tensor(
                    otile[:, (tc % 2) * D:(tc % 2 + 1) * D], psO2[:, tc2 * 1024:tc2 * 1024 + D],
                    rcol2[:, tc % 2:tc % 2 + 1],
                    utile2[:, (tc % 2) * D:(tc % 2 + 1) * D], ALU.mult, ALU.add), inc="vd")
                fin_t[tc] = q.n("vd")
                q.wait("sync", "vd", fin_t[tc])
                q.dma("sync", dram["out"][tc * CH:(tc + 1) * CH, :],
                      otile[:, (tc % 2) * D:(tc % 2 + 1) * D], "st_out")
                sto_t[tc] = q.n("st_out")
                if tc + 2 < 16:
                    q.wait("gpsimd", "vd", fin_t[tc])
                    q.dma("gpsimd", utile2[:, (tc % 2) * D:(tc % 2 + 1) * D],
                          dram["u"][HALO + (tc + 2) * CH:HALO + (tc + 3) * CH, :], "ld_u2")
                    q.dma("gpsimd", rcol2[:, tc % 2:tc % 2 + 1],
                          dram["d_rstd"][0:1, (tc + 2) * CH:(tc + 3) * CH].rearrange("a p -> (a p)").unsqueeze(1), "ld_u2")
                    u2t[tc + 2] = q.n("ld_u2")


def _prep_inputs(inputs):
    u = np.asarray(inputs["L0_attn_u"])
    ipw = np.asarray(inputs["L0_attn_in_proj_w"])
    conv_w = np.asarray(inputs["L0_attn_conv_w"])
    conv_b = np.asarray(inputs["L0_attn_conv_b"])
    dt_bias = np.asarray(inputs["L0_attn_dt_bias"])
    a_log = np.asarray(inputs["L0_attn_A_log"])
    Dv = np.asarray(inputs["L0_attn_D"])
    norm_w = np.asarray(inputs["L0_attn_norm_w"])
    opw = np.asarray(inputs["L0_attn_out_proj_w"])
    wg = np.asarray(inputs["L0_ffn_w_gate"])
    wu = np.asarray(inputs["L0_ffn_w_up"])
    wdn = np.asarray(inputs["L0_ffn_w_down"])

    w_zT = np.ascontiguousarray(ipw[:DI].T.astype(np.float32))
    w_xbcT = np.ascontiguousarray(ipw[DI:].T.astype(np.float32))
    w_outT = np.ascontiguousarray(opw.T.astype(np.float32))
    conv_w4 = np.ascontiguousarray(conv_w[:, 0, :].astype(np.float32))
    dcol64 = np.ascontiguousarray(np.repeat(np.asarray(Dv, np.float32), HD).reshape(-1, 1))

    base = dict(
        w_xbcT=w_xbcT, w_zT=w_zT, w_outT=w_outT,
        conv_w4=conv_w4,
        conv_b=np.ascontiguousarray(conv_b.reshape(-1, 1), np.float32),
        dt_bias=np.ascontiguousarray(dt_bias.reshape(-1, 1), np.float32),
        a_log=np.ascontiguousarray(a_log.reshape(-1, 1), np.float32),
        dcol64=dcol64,
        norm_w=np.ascontiguousarray(norm_w.reshape(-1, 1), np.float32),
    )
    in_maps = []
    for c in range(NCORES):
        b, sidx = divmod(c, 4)
        s0 = sidx * TOK
        if s0 == 0:
            upad = np.concatenate([np.zeros((HALO, D), np.float32), u[b, :TOK]], 0)
        else:
            upad = u[b, s0 - HALO:s0 + TOK]
        grp0 = (c // 4) * 4
        prevmask = np.zeros((NCORES,), np.float32)
        binmask = np.zeros((NCORES, NCORES), np.float32)
        for j in range(grp0, c):
            prevmask[j] = 1.0
            for i in range(j + 1, c):
                binmask[j, i] = 1.0
        m = dict(base)
        m["u"] = np.ascontiguousarray(upad, np.float32)
        m["pmx"] = np.ascontiguousarray(np.repeat(prevmask.reshape(1, -1), NH, 0))
        m["bmx"] = np.ascontiguousarray(np.repeat(binmask.reshape(1, -1), NH, 0))
        shf = FFI // NCORES
        shd = D // NCORES
        m["wg_sh"] = np.ascontiguousarray(wg[c * shf:(c + 1) * shf], np.float32)
        m["wu_sh"] = np.ascontiguousarray(wu[c * shf:(c + 1) * shf], np.float32)
        m["wd_sh"] = np.ascontiguousarray(wdn[c * shd:(c + 1) * shd], np.float32)
        in_maps.append(m)
    return in_maps


def kernel(**inputs):
    in_maps = _prep_inputs(inputs)
    nc = build(stage=STAGE, debug_taps=DEBUG_TAPS)
    res = run_bass_kernel_spmd(nc, in_maps, core_ids=list(range(NCORES)))
    full = np.zeros((2, 8192, D), np.float32)
    for c in range(NCORES):
        b, sidx = divmod(c, 4)
        full[b, sidx * TOK:(sidx + 1) * TOK] = res.results[c]["out"]
    return full
